# version 07
# baseline (speedup 1.0000x reference)
import math
import numpy as np
import concourse.bass as bass
import concourse.mybir as mybir
from concourse.bass_utils import run_bass_kernel_spmd
from contextlib import ExitStack

F32 = mybir.dt.float32
BF16 = mybir.dt.bfloat16
I32 = mybir.dt.int32
AF = mybir.ActivationFunctionType
ALU = mybir.AluOpType

P = 128
D = 1024
NCH = 8
H = 16
SEG = 32
TOK = 128
NSEQ = 2
TSEQ = 2048
NBLK = TSEQ // TOK
NMT = NSEQ * NBLK
SEGB = TOK // SEG
EXPB = 20.0
LOG1E4 = math.log(10000.0)
TWO_PI = 2.0 * math.pi


def build_nc(taps=(), nmt=NMT, stop_after=None):
    nc = bass.Bass("TRN2", target_bir_lowering=False)
    x_h = nc.dram_tensor("x", [NSEQ, TSEQ, D], F32, kind="ExternalInput").ap()
    wqkv_h = nc.dram_tensor("w_qkv", [D, 3 * D], F32, kind="ExternalInput").ap()
    wout_h = nc.dram_tensor("w_out", [D, D], F32, kind="ExternalInput").ap()
    g_h = nc.dram_tensor("ln_g", [D], F32, kind="ExternalInput").ap()
    b_h = nc.dram_tensor("ln_b", [D], F32, kind="ExternalInput").ap()
    beta_h = nc.dram_tensor("mem_beta", [1, H], F32, kind="ExternalInput").ap()
    wr_h = nc.dram_tensor("W_r", [D, D], F32, kind="ExternalInput").ap()
    wk_h = nc.dram_tensor("W_k", [D, D], F32, kind="ExternalInput").ap()
    wv_h = nc.dram_tensor("W_v", [D, D], F32, kind="ExternalInput").ap()
    mur_h = nc.dram_tensor("mu_r", [D], F32, kind="ExternalInput").ap()
    muk_h = nc.dram_tensor("mu_k", [D], F32, kind="ExternalInput").ap()
    out_h = nc.dram_tensor("out", [NSEQ, TSEQ, D], F32, kind="ExternalOutput").ap()

    ctx = ExitStack()
    e = ctx.enter_context

    wq_all = e(nc.sbuf_tensor("wq_all", [P, NCH * 3072], BF16))
    wo_all = e(nc.sbuf_tensor("wo_all", [P, NCH * 1024], BF16))
    wrT = e(nc.sbuf_tensor("wrT", [P, NCH * 1024], BF16))
    wkT = e(nc.sbuf_tensor("wkT", [P, NCH * 1024], BF16))
    wvT = e(nc.sbuf_tensor("wvT", [P, NCH * 1024], BF16))
    cosT = e(nc.sbuf_tensor("cosT", [P, TSEQ], BF16))
    sinT = e(nc.sbuf_tensor("sinT", [P, TSEQ], BF16))
    mask01 = e(nc.sbuf_tensor("mask01", [P, P], BF16))
    smat = e(nc.sbuf_tensor("smat", [P, P], BF16))
    gcol = e(nc.sbuf_tensor("gcol", [P, 8], F32))
    bcol = e(nc.sbuf_tensor("bcol", [P, 8], BF16))
    murm1 = e(nc.sbuf_tensor("murm1", [P, 8], F32))
    mukm1 = e(nc.sbuf_tensor("mukm1", [P, 8], F32))
    murb = e(nc.sbuf_tensor("murb", [P, 8], BF16))
    mukb = e(nc.sbuf_tensor("mukb", [P, 8], BF16))
    ubqk = e(nc.sbuf_tensor("ubqk", [P, 16], F32))
    ubv = e(nc.sbuf_tensor("ubv", [P, 1024], BF16))
    btr = e(nc.sbuf_tensor("btr", [1, 16], F32))
    betab = e(nc.sbuf_tensor("betab", [1, 16], BF16))
    ombb = e(nc.sbuf_tensor("ombb", [1, 16], BF16))
    btile = e(nc.sbuf_tensor("btile", [P, 8], F32))
    omb = e(nc.sbuf_tensor("omb", [P, 16], F32))
    epst = e(nc.sbuf_tensor("epst", [P, 1], F32))
    cm20 = e(nc.sbuf_tensor("cm20", [P, 1], F32))
    ones1 = e(nc.sbuf_tensor("ones1", [1, P], BF16))
    pidx = e(nc.sbuf_tensor("pidx", [P, 1], F32))
    cidx = None
    par = e(nc.sbuf_tensor("par", [P, 2], F32))
    freq64 = e(nc.sbuf_tensor("freq64", [P, 1], F32))
    sti = e(nc.sbuf_tensor("sti", [P, 1], I32))
    gtmp = e(nc.sbuf_tensor("gtmp", [P, 8], F32))

    def perseq(name, cols, dt):
        return [e(nc.sbuf_tensor(f"{name}{s}", [P, cols], dt)) for s in range(NSEQ)]

    xs = perseq("xs", D, F32)
    xhat = perseq("xhat", D, BF16)
    xnfm = perseq("xnfm", D, BF16)
    qkfm = perseq("qkfm", 2 * D, BF16)
    qkbd = perseq("qkbd", 2 * D, BF16)
    skT = perseq("skT", D, BF16)
    vb = perseq("vb", 8 * 130, BF16)
    sdpa = perseq("sdpa", 16 * 65, BF16)
    shadow = perseq("shadow", 8 * 65, BF16)
    zf = perseq("zf", 8, F32)
    hh = perseq("hh", 8, BF16)
    otok = None
    htok = perseq("htok", D, F32)
    hbf = xhat
    otok = xhat
    hfm = perseq("hfm", D, BF16)
    skfm = xnfm
    kafm = perseq("kafm", D, BF16)
    h3 = None
    ofm = e(nc.sbuf_tensor("ofm", [P, D], BF16))
    ut = e(nc.sbuf_tensor("ut", [P, D], BF16))
    h3 = [qkbd[s_].bitcast(F32)[:, 0:1024] for s_ in range(NSEQ)]
    dtile_sv = [sdpa[s_][:, 0:1024] for s_ in range(NSEQ)]
    tanh_t = e(nc.sbuf_tensor("tanh_t", [P, D], F32))
    probs_sv = [[htok[s_].bitcast(BF16)[:, 0:512], htok[s_].bitcast(BF16)[:, 512:1024]]
                for s_ in range(NSEQ)]
    rop1 = e(nc.sbuf_tensor("rop1", [P, 2 * TOK], F32))
    rop2 = e(nc.sbuf_tensor("rop2", [P, 2 * TOK], F32))
    rop1w = rop1[:].rearrange("p (a f) -> p a f", f=TOK)
    rop2w = rop2[:].rearrange("p (a f) -> p a f", f=TOK)

    wsc = e(nc.sbuf_tensor("wsc", [P, 3072], BF16))
    wscf = wsc.bitcast(F32)
    xr_t = wsc[:, 0:1024]
    xk_t = wsc[:, 1024:2048]
    kact_t = wsc[:, 2048:3072]
    xr = [xr_t, xr_t]
    xk = [xk_t, xk_t]
    kact = [kact_t, kact_t]

    arena = e(nc.sbuf_tensor("arena", [P, 2048], BF16))
    emin = arena[:, 0:1024]
    eexp = arena[:, 1024:2048]
    lnsq = emin
    wraw = arena[:, 0:1536]
    ffsc = arena.bitcast(F32)[:, 0:512]
    krelu_of = [kafm[s_].bitcast(F32)[:, 0:512] for s_ in range(NSEQ)]

    ubvrow = ut[0:1, 0:1024]
    cidx = ut.bitcast(F32)[:, 0:128]
    ST = e(nc.sbuf_tensor("ST", [P, 9 * 6], F32))

    def stc(seq, lnid, j):
        c = 9 * (lnid + 3 * seq) + j
        return ST[:, c:c + 1]

    REG = e(nc.sbuf_tensor("REG", [P, 5 * 64], F32))
    W64 = 64
    r_nm, r_tt, r_yy, r_aa, r_hx = [REG[:, 64 * j:64 * j + 64] for j in range(5)]
    rc1A = e(nc.sbuf_tensor("rc1A", [P, 4], F32))
    rc1B = e(nc.sbuf_tensor("rc1B", [P, 4], F32))
    rc1 = rc1A
    zred = e(nc.sbuf_tensor("zred", [P, 8], F32))

    pA = e(nc.psum_tensor("pA", [P, 512], F32))
    pB = e(nc.psum_tensor("pB", [P, 512], F32))
    pC = e(nc.psum_tensor("pC", [P, 512], F32))
    pD = e(nc.psum_tensor("pD", [P, 512], F32))
    pE = e(nc.psum_tensor("pE", [P, 512], F32))
    pF = e(nc.psum_tensor("pF", [P, 512], F32))
    memPS = [e(nc.psum_tensor(f"memPS{s}", [P, 512], F32)) for s in range(NSEQ)]

    LOG2PHYS = {"pe": "pe", "dve": "dve", "act": "scalar", "tq": "scalar",
                "gps": "gps", "gq": "gps", "sq": "sync", "actd": "scalar"}
    PHYS = ["pe", "dve", "scalar", "gps", "sync"]
    sems = {k: e(nc.semaphore(f"s_{k}")) for k in ("pe", "dve", "act", "gps", "sq", "tq", "gq")}
    sems["actd"] = sems["act"]
    cnt = {k: 0 for k in sems}
    cnt.pop("actd"); cnt["actd"] = None
    waitmark = {ph: {k: 0 for k in sems} for ph in PHYS}
    ops = {ph: [] for ph in PHYS}
    tap_bar = {"v": 0}

    def emit(logical, fn, inc=False, waits=()):
        ph = LOG2PHYS[logical]
        waits = list(waits)
        if tap_bar["v"]:
            waits.append(("sq", tap_bar["v"]))
        wl = []
        for (pr, val) in waits:
            if val and val > waitmark[ph][pr]:
                waitmark[ph][pr] = val
                wl.append((sems[pr], val))
        step = 16 if logical in ("sq", "tq", "gq") else 1
        cl = "act" if logical == "actd" else logical
        sem = None
        if inc:
            cnt[cl] += step
            sem = sems[cl]
        ops[ph].append((wl, fn, sem, step, logical))
        return cnt[cl]

    def now(l):
        return cnt[l]


    def ddrain():
        emit("dve", lambda v: v.drain())

    tap_h = {}

    def tap(name, view, waits):
        th = nc.dram_tensor(f"tap_{name}", list(view.shape), view.dtype,
                            kind="ExternalOutput").ap()
        tap_h[f"tap_{name}"] = str(view.dtype)
        emit("sq", lambda s, th=th, view=view: s.dma_start(out=th, in_=view),
             inc=True, waits=waits)
        tap_bar["v"] = cnt["sq"]

    def maybe_tap(stage, m, view_fn, waits_fn):
        for (tname, tmt) in taps:
            if tname == stage and tmt == m:
                tap(f"{stage}_{m}", view_fn(), waits_fn())

    CK = {}
    xload = {}
    done = {}

    def load_x(m):
        seq, blk = m % 2, m // 2
        w = [("sq", done[("store", m - 2)])] if m >= 2 else []
        xload[m] = emit("sq", lambda s, seq=seq, blk=blk: s.dma_start(
            out=xs[seq][:], in_=x_h[seq, TOK * blk:TOK * (blk + 1), :]), inc=True, waits=w)

    load_x(0)
    load_x(1)

    def vec_to_cols(vh, dest, first_waits=()):
        mk = 0
        for c in range(8):
            src = vh[128 * c:128 * c + 128].rearrange("(a b) -> a b", b=1)
            mk = emit("sq", lambda s, d=dest[:, c:c + 1], src=src: s.dma_start(out=d, in_=src),
                      inc=True, waits=(list(first_waits) if c == 0 else []))
        return mk

    CK["g"] = vec_to_cols(g_h, gcol)
    CK["b"] = vec_to_cols(b_h, gtmp)
    CK["bc"] = emit("dve", lambda v: v.tensor_copy(out=bcol[:], in_=gtmp[:]), inc=True,
                    waits=[("sq", CK["b"])])
    CK["mur"] = vec_to_cols(mur_h, gtmp, first_waits=[("dve", CK["bc"])])
    emit("dve", lambda v: v.tensor_scalar(out=murm1[:], in0=gtmp[:], scalar1=1.0,
                                          scalar2=-1.0, op0=ALU.mult, op1=ALU.add),
         waits=[("sq", CK["mur"])])
    ddrain()
    CK["m1"] = emit("dve", lambda v: v.tensor_copy(out=murb[:], in_=murm1[:]), inc=True)
    CK["muk"] = vec_to_cols(muk_h, gtmp, first_waits=[("dve", CK["m1"])])
    emit("dve", lambda v: v.tensor_scalar(out=mukm1[:], in0=gtmp[:], scalar1=1.0,
                                          scalar2=-1.0, op0=ALU.mult, op1=ALU.add),
         waits=[("sq", CK["muk"])])
    ddrain()
    CK["m2"] = emit("dve", lambda v: v.tensor_copy(out=mukb[:], in_=mukm1[:]), inc=True)
    CK["beta"] = emit("sq", lambda s: s.dma_start(out=btr[:], in_=beta_h), inc=True)

    emit("gps", lambda g: g.iota(pidx[:], pattern=[[0, 1]], channel_multiplier=1,
                                 allow_small_or_imprecise_dtypes=True))
    emit("gps", lambda g: g.iota(cidx[:], pattern=[[1, P]], channel_multiplier=0,
                                 allow_small_or_imprecise_dtypes=True))
    emit("gps", lambda g: g.memset(mask01[:], 1.0))
    mview = mask01.rearrange("p (a b) -> p a b", a=4)
    emit("gps", lambda g: g.affine_select(out=mview, in_=mview, pattern=[[-32, 4], [0, 32]],
                                          compare_op=ALU.is_ge, fill=0.0, base=0,
                                          channel_multiplier=1))
    emit("gps", lambda g: g.affine_select(out=mview, in_=mview, pattern=[[32, 4], [0, 32]],
                                          compare_op=ALU.is_ge, fill=0.0, base=31,
                                          channel_multiplier=-1))
    emit("gps", lambda g: g.affine_select(out=mview, in_=mview, pattern=[[32, 4], [1, 32]],
                                          compare_op=ALU.is_ge, fill=0.0, base=0,
                                          channel_multiplier=-1))
    emit("gps", lambda g: g.memset(ones1[:], 1.0))
    for s_ in range(NSEQ):
        vbv = vb[s_].rearrange("p (c f) -> p c f", f=130)
        emit("gps", lambda g, vbv=vbv: g.memset(vbv[:, :, 64:65], 1.0))
        emit("gps", lambda g, vbv=vbv: g.memset(vbv[:, :, 129:130], 1.0))
        emit("gps", lambda g, s_=s_: g.memset(hh[s_][:], 0.0))
        emit("gps", lambda g, s_=s_: g.memset(zf[s_][:], 1e-6))
        shv = shadow[s_].rearrange("p (c f) -> p c f", f=65)
        emit("gps", lambda g, s_=s_: g.memset(shadow[s_][:], 0.0))
        emit("gps", lambda g, shv=shv: g.memset(shv[:, :, 64:65], 1e-6))
    emit("gps", lambda g: g.memset(cm20[:], -EXPB))
    emit("gps", lambda g: g.memset(epst[:], 0.0))
    emit("gps", lambda g: g.memset(epst[0:32, :], 1e-6))
    CK["gps0"] = emit("gps", lambda g: g.memset(epst[64:96, :], 1e-6), inc=True)
    emit("dve", lambda v: v.memset(memPS[0][:], 0.0))
    CK["pszero"] = emit("dve", lambda v: v.memset(memPS[1][:], 0.0), inc=True)

    st0 = stc(0, 0, 6)
    emit("dve", lambda v: v.tensor_scalar(out=st0, in0=pidx[:], scalar1=0.5, scalar2=-0.25,
                                          op0=ALU.mult, op1=ALU.add),
         waits=[("gps", CK["gps0"])])
    ddrain()
    emit("dve", lambda v: v.tensor_copy(out=sti[:], in_=st0))
    ddrain()
    CK["flr"] = emit("dve", lambda v: v.tensor_copy(out=st0, in_=sti[:]), inc=True)
    ddrain()
    CK["freq"] = emit("act", lambda a: a.activation(out=freq64[:], in_=st0, func=AF.Exp,
                                                    scale=-LOG1E4 / 32.0), inc=True,
                      waits=[("dve", CK["flr"])])
    emit("dve", lambda v: v.tensor_scalar(out=par[:, 0:1], in0=st0, scalar1=-2.0, scalar2=None,
                                          op0=ALU.mult))
    ddrain()
    emit("dve", lambda v: v.tensor_tensor(out=par[:, 0:1], in0=par[:, 0:1], in1=pidx[:],
                                          op=ALU.add))
    ddrain()
    emit("dve", lambda v: v.tensor_scalar(out=par[:, 1:2], in0=par[:, 0:1], scalar1=-1.0,
                                          scalar2=1.0, op0=ALU.mult, op1=ALU.add))
    ddrain()
    emit("dve", lambda v: v.tensor_scalar(out=cidx[:], in0=cidx[:], scalar1=pidx[:], scalar2=None,
                                          op0=ALU.subtract))
    emit("dve", lambda v: v.tensor_scalar(out=smat[:], in0=cidx[:], scalar1=1.0, scalar2=None,
                                          op0=ALU.is_equal))
    emit("dve", lambda v: v.tensor_scalar(out=smat[:], in0=smat[:], scalar1=par[:, 1:2],
                                          scalar2=None, op0=ALU.mult))
    emit("dve", lambda v: v.tensor_scalar(out=lnsq[:, 0:128], in0=cidx[:], scalar1=-1.0,
                                          scalar2=None, op0=ALU.is_equal))
    emit("dve", lambda v: v.tensor_scalar(out=lnsq[:, 0:128], in0=lnsq[:, 0:128],
                                          scalar1=par[:, 0:1], scalar2=None, op0=ALU.mult))
    CK["smat"] = emit("dve", lambda v: v.tensor_tensor(out=smat[:], in0=smat[:],
                                                       in1=lnsq[:, 0:128], op=ALU.subtract),
                      inc=True)

    wq_f = wq_all.bitcast(F32)
    pos64 = wq_f[0:64, 0:2048]
    ang64 = wq_f[0:64, 2048:4096]
    u_s = wq_f[0:64, 4096:6144]
    n_f = wq_f[0:64, 6144:8192]
    n_i = wq_f[0:64, 8192:10240].bitcast(I32)
    CK["pos"] = emit("gps", lambda g: g.iota(pos64, pattern=[[1, 2048]], channel_multiplier=0,
                                             allow_small_or_imprecise_dtypes=True), inc=True)
    emit("dve", lambda v: v.tensor_scalar(out=ang64, in0=pos64, scalar1=freq64[0:64, 0:1],
                                          scalar2=None, op0=ALU.mult),
         waits=[("gps", CK["pos"]), ("act", CK["freq"])])

    def table_build(phase, dest):
        emit("dve", lambda v: v.tensor_scalar(out=u_s, in0=ang64, scalar1=1.0 / TWO_PI,
                                              scalar2=phase, op0=ALU.mult, op1=ALU.add))
        emit("dve", lambda v: v.tensor_copy(out=n_i, in_=u_s))
        emit("dve", lambda v: v.tensor_copy(out=n_f, in_=n_i))
        cc = emit("dve", lambda v: v.tensor_tensor(out=u_s, in0=u_s, in1=n_f, op=ALU.subtract),
                  inc=True)
        ca = emit("act", lambda a: a.activation(out=dest[0:64, :], in_=u_s, func=AF.Sin,
                                                scale=TWO_PI), inc=True, waits=[("dve", cc)])
        emit("dve", lambda v: v.tensor_copy(out=dest[64:128, :], in_=dest[0:64, :]),
             waits=[("act", ca)])

    table_build(0.25, cosT)
    table_build(0.0, sinT)
    CK["tables"] = emit("dve", lambda v: None, inc=True)

    ca = emit("act", lambda a: a.activation(out=btr[:], in_=btr[:], func=AF.Tanh, scale=0.5),
              inc=True, waits=[("sq", CK["beta"])])
    emit("dve", lambda v: v.tensor_scalar(out=betab[:], in0=btr[:], scalar1=0.5, scalar2=0.5,
                                          op0=ALU.mult, op1=ALU.add), waits=[("act", ca)])
    CK["betab"] = emit("dve", lambda v: v.tensor_scalar(out=ombb[:], in0=btr[:], scalar1=-0.5,
                                                        scalar2=0.5, op0=ALU.mult, op1=ALU.add),
                       inc=True)
    emit("pe", lambda t: t.matmul(out=pA[:, 0:16], lhsT=ones1[:], rhs=ombb[:],
                                  start=True, stop=True),
         waits=[("dve", CK["betab"]), ("gps", CK["gps0"])])
    CK["pe_beta"] = emit("pe", lambda t: t.matmul(out=pB[0:32, 0:16], lhsT=ones1[:, 0:32],
                                                  rhs=betab[:], start=True, stop=True), inc=True)
    emit("dve", lambda v: v.tensor_copy(out=omb[:], in_=pA[:, 0:16]),
         waits=[("pe", CK["pe_beta"])])
    emit("dve", lambda v: v.memset(btile[:], 1.0))
    ddrain()
    emit("dve", lambda v: v.tensor_copy(out=btile[0:32, :], in_=pB[0:32, 0:16:2]))
    CK["btile"] = emit("dve", lambda v: v.tensor_copy(out=btile[64:96, :], in_=pB[0:32, 1:16:2]),
                      inc=True)

    prev_sc = 0
    prev_ca = 0
    last_pe = 0
    for c in range(NCH):
        for hf in range(2):
            wl = emit("sq", lambda s, c=c, hf=hf: s.dma_start(
                out=wscf[:, 0:1536],
                in_=wqkv_h[128 * c:128 * c + 128, 1536 * hf:1536 * (hf + 1)]),
                inc=True, waits=[("dve", max(CK["tables"], prev_sc)), ("act", prev_ca),
                                 ("pe", last_pe)])
            prev_ca = emit("act", lambda a: a.activation(out=wraw, in_=wscf[:, 0:1536],
                                                         func=AF.Copy), inc=True,
                           waits=[("sq", wl)])
            prev_sc = emit("dve", lambda v, c=c, hf=hf: v.tensor_scalar(
                out=wq_all[:, 3072 * c + 1536 * hf:3072 * c + 1536 * (hf + 1)],
                in0=wscf[:, 0:1536], scalar1=gcol[:, c:c + 1], scalar2=None,
                op0=ALU.mult), inc=True,
                waits=[("sq", wl), ("sq", CK["g"]), ("dve", CK["tables"])])
            if hf == 0:
                mcs = range(0, 12)
            else:
                mcs = range(12, 16)
            for mc in mcs:
                off = 128 * mc - 1536 * hf
                emit("pe", lambda t, c=c, mc=mc, off=off: t.matmul(
                    out=pC[:, 32 + mc:33 + mc], lhsT=wraw[:, off:off + 128],
                    rhs=bcol[:, c:c + 1], start=(c == 0 and mc == 0),
                    stop=(c == NCH - 1)),
                    waits=[("act", prev_ca), ("dve", CK["bc"]),
                           ("dve", CK["btile"] if c == 0 else 0)])
            if hf == 1:
                emit("pe", lambda t, c=c: t.matmul(out=pD[0:1, 0:512], lhsT=bcol[:, c:c + 1],
                                                   rhs=wraw[:, 512:1024], start=(c == 0),
                                                   stop=(c == NCH - 1)))
                last_pe = emit("pe", lambda t, c=c: t.matmul(
                    out=pE[0:1, 0:512], lhsT=bcol[:, c:c + 1],
                    rhs=wraw[:, 1024:1536], start=(c == 0), stop=(c == NCH - 1)), inc=True)
    CK["pe_ub"] = last_pe
    emit("dve", lambda v: v.tensor_copy(out=ubqk[:], in_=pC[:, 32:48]),
         waits=[("pe", CK["pe_ub"])])
    emit("dve", lambda v: v.tensor_copy(out=ubvrow[0:1, 0:512], in_=pD[0:1, 0:512]))
    CK["ubvr"] = emit("dve", lambda v: v.tensor_copy(out=ubvrow[0:1, 512:1024],
                                                     in_=pE[0:1, 0:512]), inc=True)
    emit("pe", lambda t: t.matmul(out=pD[:, 0:512], lhsT=ones1[:], rhs=ubvrow[0:1, 0:512],
                                  start=True, stop=True), waits=[("dve", CK["ubvr"])])
    CK["pe_ubv"] = emit("pe", lambda t: t.matmul(out=pE[:, 0:512], lhsT=ones1[:],
                                                 rhs=ubvrow[0:1, 512:1024], start=True,
                                                 stop=True), inc=True)
    emit("dve", lambda v: v.tensor_copy(out=ubv[:, 0:512], in_=pD[:, 0:512]),
         waits=[("pe", CK["pe_ubv"])])
    CK["ubv"] = emit("dve", lambda v: v.tensor_copy(out=ubv[:, 512:1024], in_=pE[:, 0:512]),
                     inc=True)

    prev_gq = 0
    for c in range(NCH):
        prev_gq = emit("gq", lambda g, c=c: g.dma_start(
            out=wo_all[:, 1024 * c:1024 * (c + 1)], in_=wout_h[128 * c:128 * c + 128, :]),
            inc=True, waits=[("gps", CK["gps0"])])
    prev_tq = 0
    prev_dv = 0
    stg0 = qkbd[0][:, 0:1024]
    stg1 = qkbd[0][:, 1024:2048]
    for (whh, dstT, halfscale) in ((wr_h, wrT, False), (wk_h, wkT, False), (wv_h, wvT, True)):
        for dc in range(NCH):
            wl = emit("gq", lambda g, whh=whh, dc=dc: g.dma_start(
                out=stg0, in_=whh[128 * dc:128 * dc + 128, :]), inc=True,
                waits=[("tq", prev_tq), ("dve", prev_dv)])
            src = stg0
            if halfscale:
                prev_dv = emit("dve", lambda v: v.tensor_scalar(
                    out=stg1, in0=stg0, scalar1=0.5, scalar2=None, op0=ALU.mult), inc=True,
                    waits=[("gq", wl)])
                src = stg1
            dst = dstT.rearrange("p (a b f) -> p a b f", a=8, f=128)[:, :, dc, :]
            prev_tq = emit("tq", lambda s, dst=dst, src=src.rearrange(
                "p (c f) -> p c f", c=8): s.dma_start(out=dst, in_=src, transpose=True),
                inc=True, waits=([("gq", wl)] if not halfscale else [("dve", prev_dv)]))
    CK["wffn_dve"] = max(prev_sc, prev_dv)
    CK["wffn_gq"] = prev_gq
    CK["wffn_tq"] = prev_tq
    CK["wsc_free_dve"] = prev_sc
    CK["wsc_free_act"] = prev_ca
    CK["arena_free_pe"] = CK["pe_ubv"]


    def dget(key, default=0):
        return done.get(key, default)

    pe_big = [pA, pB, pC, pD]
    big_i = [0]
    big_evac_d = {id(pA): ("dve", CK["btile"]), id(pB): ("dve", CK["btile"]),
                  id(pC): ("dve", CK["ubv"]), id(pD): ("dve", CK["ubv"]),
                  id(pE): ("dve", CK["ubv"]), id(pF): ("dve", 0)}

    class _BE:
        def __getitem__(self, b):
            return big_evac_d[id(b)]

        def __setitem__(self, b, v):
            big_evac_d[id(b)] = v

    big_evac = _BE()

    def next_big():
        b = pe_big[big_i[0] % 2]
        big_i[0] += 1
        return b

    for m in range(nmt):
        seq, blk = m % 2, m // 2

        r1, sq_ = stc(seq, 0, 0), stc(seq, 0, 1)
        wx = xload[m]
        emit("actd", lambda a, seq=seq, r1=r1: a.activation(
            out=h3[seq][:], in_=xs[seq][:], func=AF.Copy, accum_out=r1),
            inc=True,
            waits=[("sq", wx), ("pe", dget(("phb_done", m - 2), CK["arena_free_pe"])),
                   ("tq", CK["wffn_tq"] if m < 2 else 0)])
        c_sqa = emit("actd", lambda a, seq=seq, sq_=sq_: a.activation(
            out=qkbd[seq][:, 0:1024], in_=xs[seq][:], func=AF.Square, accum_out=sq_),
            inc=True)

        def rstd_chain(lnid, seq, steps, xtr_waits=()):
            r1, sq_ = stc(seq, lnid, 0), stc(seq, lnid, 1)
            emit("dve", lambda v: v.tensor_scalar(
                out=r_nm, in0=r1.broadcast_to([P, W64]), scalar1=-1.0 / D,
                scalar2=None, op0=ALU.mult), waits=list(xtr_waits))
            emit("dve", lambda v: v.tensor_scalar(
                out=r_tt, in0=sq_.broadcast_to([P, W64]), scalar1=1.0 / D,
                scalar2=1e-5, op0=ALU.mult, op1=ALU.add))
            emit("dve", lambda v: v.tensor_tensor(out=r_aa, in0=r_nm, in1=r_nm, op=ALU.mult))
            emit("dve", lambda v: v.tensor_tensor(out=r_tt, in0=r_tt, in1=r_aa,
                                                  op=ALU.subtract))
            emit("dve", lambda v: v.tensor_scalar(out=r_aa, in0=r_tt, scalar1=0.5, scalar2=0.5,
                                                  op0=ALU.mult, op1=ALU.add))
            emit("dve", lambda v: v.reciprocal(out=r_yy, in_=r_aa))
            for _ in range(steps):
                emit("dve", lambda v: v.tensor_tensor(out=r_aa, in0=r_yy, in1=r_yy,
                                                      op=ALU.mult))
                emit("dve", lambda v: v.tensor_tensor(out=r_hx, in0=r_aa, in1=r_tt,
                                                      op=ALU.mult))
                emit("dve", lambda v: v.tensor_scalar(out=r_aa, in0=r_hx, scalar1=-0.5,
                                                      scalar2=1.5, op0=ALU.mult, op1=ALU.add))
                emit("dve", lambda v: v.tensor_tensor(out=r_yy, in0=r_yy, in1=r_aa,
                                                      op=ALU.mult))

        rstd_chain(0, seq, 2, xtr_waits=[("act", c_sqa)])
        c_xh = emit("dve", lambda v, seq=seq: v.tensor_scalar(
            out=xhat[seq][:], in0=xs[seq][:], scalar1=r_nm[:, 0:1], scalar2=r_yy[:, 0:1],
            op0=ALU.add, op1=ALU.mult), inc=True, waits=[("tq", dget(("ht", m - 2)))])
        maybe_tap("xhat", m, lambda: xhat[seq][:], lambda: [("dve", c_xh)])

        if stop_after == "ln1":
            break
        done[("xt", m)] = emit("tq", lambda s, seq=seq: s.dma_start(
            out=xnfm[seq].rearrange("p (c f) -> p c f", c=8),
            in_=xhat[seq].rearrange("p (c f) -> p c f", c=8), transpose=True), inc=True,
            waits=[("dve", c_xh), ("pe", dget(("qkv_done", m - 2))),
                   ("tq", dget(("skt", m - 2)))])
        maybe_tap("xnfm", m, lambda: xnfm[seq][:], lambda: [("tq", done[("xt", m)])])

        if stop_after == "xt":
            break
        for mp in range(8):
            bank = next_big()
            beng, bval = big_evac[bank]
            for half in range(2):
                mc = 2 * mp + half
                for kc in range(NCH):
                    wts = []
                    if half == 0 and kc == 0:
                        wts = [("tq", done[("xt", m)]), (beng, bval)]
                        if m < 2 and mp == 0:
                            wts.append(("dve", CK["wffn_dve"]))
                    emit("pe", lambda t, mc=mc, kc=kc, half=half, seq=seq, bank=bank: t.matmul(
                        out=bank[:, 128 * half:128 * (half + 1)],
                        lhsT=wq_all[:, 3072 * kc + 128 * mc:3072 * kc + 128 * (mc + 1)],
                        rhs=xnfm[seq][:, 128 * kc:128 * (kc + 1)],
                        start=(kc == 0), stop=(kc == NCH - 1)), waits=wts)
            cpe = emit("pe", lambda t: None, inc=True)
            ubv2 = ubqk[:, 2 * mp:2 * mp + 2].unsqueeze(2).broadcast_to([P, 2, 128])
            cev = emit("dve", lambda v, mp=mp, seq=seq, bank=bank, ubv2=ubv2: v.tensor_tensor(
                out=qkfm[seq][:, 256 * mp:256 * (mp + 1)].rearrange("p (a f) -> p a f", f=128),
                in0=bank[:, 0:256].rearrange("p (a f) -> p a f", f=128),
                in1=ubv2, op=ALU.add), inc=True,
                waits=[("pe", cpe), ("pe", dget(("scores_done", m - 2)))])
            big_evac[bank] = ("dve", cev)
        done[("qk_evac", m)] = cev
        for nh in range(2):
            bank = next_big()
            beng, bval = big_evac[bank]
            for kc in range(NCH):
                wts = []
                if kc == 0:
                    wts = [("tq", done[("xt", m)]), (beng, bval)]
                emit("pe", lambda t, nh=nh, kc=kc, seq=seq, bank=bank: t.matmul(
                    out=bank[:, 0:512], lhsT=xnfm[seq][:, 128 * kc:128 * (kc + 1)],
                    rhs=wq_all[:, 3072 * kc + 2048 + 512 * nh:3072 * kc + 2048 + 512 * (nh + 1)],
                    start=(kc == 0), stop=(kc == NCH - 1)), waits=wts)
            cpe = emit("pe", lambda t: None, inc=True)
            vdst = vb[seq].rearrange("p (c g f) -> p c g f", g=2, f=65)[:, 4 * nh:4 * (nh + 1), :, 0:64]
            vsrc = bank[:, 0:512].rearrange("p (c g f) -> p c g f", c=4, g=2)
            usrc = ubv[:, 512 * nh:512 * (nh + 1)].rearrange("p (c g f) -> p c g f", c=4, g=2)
            cev = emit("dve", lambda v, vdst=vdst, vsrc=vsrc, usrc=usrc: v.tensor_tensor(
                out=vdst, in0=vsrc, in1=usrc, op=ALU.add), inc=True,
                waits=[("pe", cpe), ("dve", dget(("usub", m - 2, SEGB - 1))),
                       ("pe", dget(("sdpa_done", m - 2))), ("dve", CK["ubv"] if m < 2 else 0)])
            big_evac[bank] = ("dve", cev)
        done[("v_evac", m)] = cev
        done[("qkv_done", m)] = now("pe")
        maybe_tap("qkfm", m, lambda: qkfm[seq][:], lambda: [("dve", done[("qk_evac", m)])])
        maybe_tap("vb", m, lambda: vb[seq][:], lambda: [("dve", done[("v_evac", m)])])

        if stop_after == "qkv":
            break
        c_expq = emit("act", lambda a, seq=seq: a.activation(out=eexp[:], in_=qkfm[seq][:, 0:1024],
                                                             func=AF.Exp), inc=True,
                      waits=[("dve", done[("qk_evac", m)]), ("dve", dget(("elu_stt", m - 1))),
                             ("pe", CK["arena_free_pe"]),
                             ("dve", dget(("mix_done", m - 1)))])
        emit("dve", lambda v: v.tensor_scalar(out=emin[:], in0=eexp[:], scalar1=1.0,
                                              scalar2=None, op0=ALU.min),
             waits=[("act", c_expq)])
        qdst = qkbd[seq].rearrange("p (c s g f) -> p c s g f", c=8, s=4, g=2)[:, :, :, 0, :]
        c_sttq = emit("dve", lambda v, seq=seq, qdst=qdst: v.scalar_tensor_tensor(
            out=qdst, in0=qkfm[seq][:, 0:1024].rearrange("p (c s f) -> p c s f", c=8, f=32),
            scalar=0.0, in1=emin.rearrange("p (c s f) -> p c s f", c=8, f=32),
            op0=ALU.max, op1=ALU.add), inc=True,
            waits=[("pe", dget(("phb_done", m - 2))),
                   ("tq", CK["wffn_tq"] if m < 2 else 0)])
        c_expk = emit("act", lambda a, seq=seq: a.activation(out=eexp[:],
                                                             in_=qkfm[seq][:, 1024:2048],
                                                             func=AF.Exp), inc=True,
                      waits=[("dve", c_sttq)])
        done[("elu_exp", m)] = c_expk
        c_min = c_expk
        emit("dve", lambda v: v.tensor_scalar(out=emin[:], in0=eexp[:], scalar1=1.0,
                                              scalar2=None, op0=ALU.min),
             waits=[("act", c_expk)])
        emit("dve", lambda v, seq=seq: v.scalar_tensor_tensor(
            out=skfm[seq][:], in0=qkfm[seq][:, 1024:2048], scalar=0.0,
            in1=emin[:], op0=ALU.max, op1=ALU.add),
            waits=[("pe", done[("qkv_done", m)])])
        kdst = qkbd[seq].rearrange("p (c s g f) -> p c s g f", c=8, s=4, g=2)[:, :, :, 1, :]
        c3 = emit("dve", lambda v, seq=seq, kdst=kdst: v.tensor_copy(
            out=kdst, in_=skfm[seq].rearrange("p (c s f) -> p c s f", c=8, f=32)), inc=True,
            waits=[("tq", CK["wffn_tq"] if m < 2 else 0)])
        done[("elu_stt", m)] = c3
        maybe_tap("qkbd", m, lambda: qkbd[seq][:], lambda: [("dve", c3)])
        done[("skt", m)] = emit("tq", lambda s, seq=seq: s.dma_start(
            out=skT[seq].rearrange("p (c f) -> p c f", c=8),
            in_=skfm[seq].rearrange("p (c f) -> p c f", c=8), transpose=True), inc=True,
            waits=[("dve", c3), ("pe", dget(("phb_done", m - 2)))])

        if stop_after == "elu":
            break
        cosv = cosT[:, TOK * blk:TOK * (blk + 1)]
        sinv = sinT[:, TOK * blk:TOK * (blk + 1)]
        prev_rope_dve = 0
        for half in range(2):
            base = 1024 * half
            for cp in range(4):
                bank = next_big()
                beng, bval = big_evac[bank]
                for half2 in range(2):
                    c = 2 * cp + half2
                    cpe = emit("pe", lambda t, seq=seq, base=base, c=c, half2=half2, bank=bank: t.matmul(
                        out=bank[:, 128 * half2:128 * (half2 + 1)], lhsT=smat[:],
                        rhs=qkfm[seq][:, base + 128 * c:base + 128 * (c + 1)],
                        start=True, stop=True), inc=(half2 == 1),
                        waits=([("dve", done[("elu_stt", m)]), (beng, bval),
                                ("act", c_min), ("dve", CK["smat"])] if half2 == 0 else []))
                c2 = 2 * cp
                cosv2 = cosv.unsqueeze(1).broadcast_to([P, 2, TOK])
                sinv2 = sinv.unsqueeze(1).broadcast_to([P, 2, TOK])
                emit("dve", lambda v, bank=bank, sinv2=sinv2: v.tensor_tensor(
                    out=rop2[:].rearrange("p (a f) -> p a f", f=TOK) if False else rop2w,
                    in0=bank[:, 0:256].rearrange("p (a f) -> p a f", f=TOK),
                    in1=sinv2, op=ALU.mult), waits=[("pe", cpe)])
                emit("dve", lambda v, seq=seq, base=base, c2=c2, cosv2=cosv2: v.tensor_tensor(
                    out=rop1w,
                    in0=qkfm[seq][:, base + 128 * c2:base + 128 * (c2 + 2)].rearrange(
                        "p (a f) -> p a f", f=TOK),
                    in1=cosv2, op=ALU.mult))
                cev = emit("dve", lambda v, seq=seq, base=base, c2=c2: v.tensor_tensor(
                    out=qkfm[seq][:, base + 128 * c2:base + 128 * (c2 + 2)].rearrange(
                        "p (a f) -> p a f", f=TOK),
                    in0=rop2w, in1=rop1w, op=ALU.add), inc=True)
                big_evac[bank] = ("dve", cev)
                prev_rope_dve = cev
        done[("rope", m)] = prev_rope_dve
        maybe_tap("qkrope", m, lambda: qkfm[seq][:], lambda: [("dve", done[("rope", m)])])

        if stop_after == "rope":
            break
        pa_lvl = {"pa_scores": 1, "pa_exp": 2, "pa_mask": 3, "pa_sdpa": 4,
                  "pa_recip": 5}.get(stop_after, 99)
        probs = probs_sv[seq]
        pend_evac = [None]

        def pa_evac(v_unused=None):
            pass

        for r in range(4):
            hf = r % 2
            c0 = 4 * (r // 2)
            sbank = pC if r % 2 == 0 else pD
            obank = pE if r % 2 == 0 else pF
            rcX = rc1A if r % 2 == 0 else rc1B
            sbeng, sbval = big_evac[sbank]
            for ci in range(4):
                ch = c0 + ci
                wts = []
                if ci == 0:
                    wts = [("dve", done[("rope", m)]), (sbeng, sbval),
                           ("act", dget(("exp_done", m, r - 2), dget(("exp_done", m - 1, r + 2))))]
                cpe = emit("pe", lambda t, seq=seq, ch=ch, hf=hf, ci=ci, sbank=sbank: t.matmul(
                    out=sbank[:, 128 * ci:128 * (ci + 1)],
                    lhsT=qkfm[seq][64 * hf:64 * (hf + 1), 1024 + 128 * ch:1024 + 128 * (ch + 1)],
                    rhs=qkfm[seq][64 * hf:64 * (hf + 1), 128 * ch:128 * (ch + 1)],
                    start=True, stop=True, tile_position=(64 * hf, 0)),
                    inc=(ci == 3), waits=wts)
            done[("scorespe", m, r)] = cpe
            if pa_lvl < 2:
                continue
            c_ex = emit("act", lambda a, r=r, sbank=sbank: a.activation(
                out=probs[r % 2][:], in_=sbank[:], func=AF.Exp, scale=0.125, bias=cm20[:]),
                inc=True, waits=[("pe", cpe),
                                 ("dve", dget(("msk_done", m, r - 2),
                                              dget(("msk_done", m - 1, r + 2)))),
                                 ("dve", dget(("ln3", m - 2))),
                                 ("dve", done[("elu_stt", m)] if r == 0 else 0),
                                 ("gps", CK["gps0"] if m < 2 else 0)])
            done[("exp_done", m, r)] = c_ex
            big_evac[sbank] = ("act", c_ex)
            if pa_lvl < 3:
                continue
            c_mk = emit("dve", lambda v, r=r: v.tensor_tensor(
                out=probs[r % 2].rearrange("p (a f) -> p a f", f=128),
                in0=probs[r % 2].rearrange("p (a f) -> p a f", f=128),
                in1=mask01[:].unsqueeze(1).broadcast_to([P, 4, P]),
                op=ALU.mult), inc=True, waits=[("act", c_ex),
                                               ("gps", CK["gps0"] if m < 2 else 0)])
            done[("msk_done", m, r)] = c_mk
            if pa_lvl < 4:
                continue
            obeng, obval = big_evac[obank]
            for ci in range(4):
                ch = c0 + ci
                wts = []
                if ci == 0:
                    wts = [("dve", c_mk), ("dve", done[("v_evac", m)]), (obeng, obval),
                           ("dve", dget(("nds_mult", m - 1, SEGB - 1))),
                           ("gps", CK["gps0"] if m < 2 else 0)]
                cpe = emit("pe", lambda t, seq=seq, ch=ch, hf=hf, ci=ci, r=r, obank=obank: t.matmul(
                    out=obank[:, 65 * ci:65 * (ci + 1)],
                    lhsT=probs[r % 2][:, 128 * ci:128 * (ci + 1)],
                    rhs=vb[seq][:, 130 * ch + 65 * hf:130 * ch + 65 * (hf + 1)],
                    start=True, stop=True), inc=(ci == 3), waits=wts)
            done[("sdpape", m, r)] = cpe
            if pa_lvl < 5:
                continue
            emit("dve", lambda v, obank=obank, rcX=rcX: v.reciprocal(
                out=rcX[:], in_=obank[:, 0:260].rearrange("p (a b) -> p a b", b=65)[:, :, 64]),
                waits=[("pe", cpe), ("dve", CK["btile"] if m == 0 else 0)])
            if pend_evac[0] is None:
                ddrain()
            else:
                (pr, prcX, probank) = pend_evac[0]
                sdst = sdpa[seq].rearrange("p (c g b) -> p c g b", g=2, b=65)[
                    :, 4 * (pr // 2):4 * (pr // 2) + 4, pr % 2, :]
                c_se = emit("dve", lambda v, sdst=sdst, probank=probank, prcX=prcX: v.tensor_tensor(
                    out=sdst, in0=probank[:, 0:260].rearrange("p (a b) -> p a b", b=65),
                    in1=prcX[:].unsqueeze(2).broadcast_to([P, 4, 65]), op=ALU.mult),
                    inc=True, waits=[("dve", dget(("blend", m - 2, SEGB - 1)))])
                done[("sdpa_evac", m, pr)] = c_se
                big_evac[probank] = ("dve", c_se)
            ombv = omb.rearrange("p (c g) -> p c g", g=2)[:, c0:c0 + 4, hf]
            emit("dve", lambda v, ombv=ombv, rcX=rcX: v.tensor_tensor(
                out=rcX[:], in0=rcX[:], in1=ombv, op=ALU.mult))
            pend_evac[0] = (r, rcX, obank)
        done[("sdpa_done", m)] = now("pe")
        done[("scores_done", m)] = now("pe")
        if pa_lvl >= 5 and pend_evac[0] is not None:
            (pr, prcX, probank) = pend_evac[0]
            ddrain()
            sdst = sdpa[seq].rearrange("p (c g b) -> p c g b", g=2, b=65)[
                :, 4 * (pr // 2):4 * (pr // 2) + 4, pr % 2, :]
            c_se = emit("dve", lambda v, sdst=sdst, probank=probank, prcX=prcX: v.tensor_tensor(
                out=sdst, in0=probank[:, 0:260].rearrange("p (a b) -> p a b", b=65),
                in1=prcX[:].unsqueeze(2).broadcast_to([P, 4, 65]), op=ALU.mult),
                inc=True, waits=[("dve", dget(("blend", m - 2, SEGB - 1)))])
            done[("sdpa_evac", m, pr)] = c_se
            big_evac[probank] = ("dve", c_se)
        if pa_lvl < 99:
            break
        maybe_tap("sdpa", m, lambda: sdpa[seq][:],
                  lambda: [("dve", done[("sdpa_evac", m, 3)])])

        if stop_after == "phasea":
            break
        for s_ in range(SEGB):
            gseg = SEGB * blk + s_
            r0 = 32 * s_
            shdst = shadow[seq].rearrange("p (c f) -> p c f", f=65)[:, :, 0:64]
            if gseg > 0:
                c_sh = emit("act", lambda a, seq=seq, shdst=shdst: a.activation(
                    out=shdst, in_=memPS[seq].rearrange("p (c f) -> p c f", f=64),
                    func=AF.Copy), inc=True,
                    waits=[("pe", dget(("delta", m, s_ - 1), dget(("delta", m - 2, SEGB - 1))))])
            else:
                c_sh = now("act")
            done[("shadow", m, s_)] = c_sh
            emit("dve", lambda v, seq=seq, s_=s_: v.tensor_reduce(
                out=zred[:], in_=qkbd[seq].rearrange("p (c s g f) -> p c s g f",
                                                     c=8, s=4, g=2)[:, :, s_, 1, :],
                axis=mybir.AxisListType.X, op=ALU.add),
                waits=[("dve", done[("elu_stt", m)]),
                       ("gps", CK["gps0"] if m < 2 else 0)])
            for jj in range(8):
                bank = pE if jj < 4 else pF
                col = 65 * (jj % 4)
                for hf in range(2):
                    wts = []
                    if jj == 0 and hf == 0:
                        wts = [("dve", max(done[("elu_stt", m)],
                                           done[("sdpa_evac", m, 3)])),
                               ("act", c_sh),
                               ("dve", dget(("nds_mult", m, s_ - 1))),
                               ("gps", CK["gps0"] if m < 2 else 0)]
                    cpe = emit("pe", lambda t, seq=seq, s_=s_, jj=jj, hf=hf, bank=bank,
                               col=col: t.matmul(
                        out=bank[64 * hf:64 * (hf + 1), col:col + 65],
                        lhsT=qkbd[seq][64 * hf:64 * hf + 64,
                                       256 * jj + 64 * s_:256 * jj + 64 * (s_ + 1)],
                        rhs=shadow[seq][64 * hf:64 * hf + 64, 65 * jj:65 * (jj + 1)],
                        start=True, stop=True, tile_position=(64 * hf, 64 * hf)),
                        inc=(jj in (3, 7) and hf == 1), waits=wts)
            done[("numskm", m, s_)] = cpe
            emit("dve", lambda v: v.reciprocal(
                out=rc1A[:], in_=pE[:, 0:260].rearrange("p (a b) -> p a b", b=65)[:, :, 64]),
                waits=[("pe", done[("numskm", m, s_)]),
                       ("dve", dget(("usub", m, s_ - 1), dget(("usub", m - 1, SEGB - 1)))),
                       ("dve", dget(("blend", m, s_ - 1)))])
            emit("dve", lambda v: v.reciprocal(
                out=rc1B[:], in_=pF[:, 0:260].rearrange("p (a b) -> p a b", b=65)[:, :, 64]))
            emit("dve", lambda v: v.tensor_tensor(out=rc1A[:], in0=rc1A[:],
                                                  in1=btile[:, 0:4], op=ALU.mult))
            emit("dve", lambda v: v.tensor_tensor(out=rc1B[:], in0=rc1B[:],
                                                  in1=btile[:, 4:8], op=ALU.mult))
            emit("dve", lambda v, seq=seq: v.tensor_tensor(out=zf[seq][:], in0=zf[seq][:],
                                                           in1=zred[:], op=ALU.add))
            c_nds = 0
            for half in range(2):
                bank = pE if half == 0 else pF
                rcX = rc1A if half == 0 else rc1B
                for hf in range(2):
                    nrow = 64 * hf
                    prow = 32 + 64 * hf
                    oview = otok[seq][r0:r0 + 32, :].rearrange(
                        "p (c g f) -> p c g f", g=2, f=64)[:, 4 * half:4 * half + 4, hf, :]
                    uview = ut[r0:r0 + 32, :].rearrange(
                        "p (c g f) -> p c g f", g=2, f=64)[:, 4 * half:4 * half + 4, hf, :]
                    emit("dve", lambda v, bank=bank, nrow=nrow, oview=oview, rcX=rcX: v.tensor_tensor(
                        out=oview,
                        in0=bank[nrow:nrow + 32, 0:260].rearrange("p (a b) -> p a b", b=65)[:, :, 0:64],
                        in1=rcX[nrow:nrow + 32, :].unsqueeze(2).broadcast_to([32, 4, 64]),
                        op=ALU.mult))
                    c_nds = emit("dve", lambda v, bank=bank, prow=prow, uview=uview, rcX=rcX: v.tensor_tensor(
                        out=uview,
                        in0=bank[prow:prow + 32, 0:260].rearrange("p (a b) -> p a b", b=65)[:, :, 0:64],
                        in1=rcX[prow:prow + 32, :].unsqueeze(2).broadcast_to([32, 4, 64]),
                        op=ALU.mult), inc=(half == 1 and hf == 1))
                    if half == 0 and hf == 0:
                        emit("dve", lambda v, seq=seq: v.tensor_copy(
                            out=shadow[seq].rearrange("p (c f) -> p c f", f=65)[:, :, 64],
                            in_=zf[seq][:]))
            done[("nds_mult", m, s_)] = c_nds
            uin = ut[r0:r0 + 32, :].rearrange("p (c g f) -> p c g f", g=2, f=64)
            vsrc = vb[seq][r0:r0 + 32, :].rearrange("p (c g f) -> p c g f", g=2, f=65)[:, :, :, 0:64]
            cu = emit("dve", lambda v, vsrc=vsrc, uin=uin: v.tensor_tensor(
                out=uin, in0=vsrc, in1=uin, op=ALU.subtract), inc=True,
                waits=[("pe", dget(("delta", m, s_ - 1)))])
            done[("usub", m, s_)] = cu
            oview = otok[seq][r0:r0 + 32, :].rearrange("p (c g f) -> p c g f", g=2, f=64)
            sdv = sdpa[seq][r0:r0 + 32, :].rearrange("p (c g b) -> p c g b", g=2, b=65)[:, :, :, 0:64]
            cb = emit("dve", lambda v, oview=oview, sdv=sdv: v.tensor_tensor(
                out=oview, in0=oview, in1=sdv, op=ALU.add), inc=True)
            done[("blend", m, s_)] = cb
            first = (gseg == 0)
            for jj in range(8):
                for hf in range(2):
                    wts = []
                    if jj == 0 and hf == 0:
                        wts = [("dve", cu), ("act", c_sh), ("tq", done[("skt", m)]),
                               ("dve", CK["pszero"] if m < 2 else 0)]
                    cpe = emit("pe", lambda t, seq=seq, s_=s_, jj=jj, hf=hf, first=first: t.matmul(
                        out=memPS[seq][64 * hf:64 * (hf + 1), 64 * jj:64 * (jj + 1)],
                        lhsT=skT[seq][32 * s_:32 * (s_ + 1),
                                      128 * jj + 64 * hf:128 * jj + 64 * (hf + 1)],
                        rhs=ut[32 * s_:32 * (s_ + 1),
                               128 * jj + 64 * hf:128 * jj + 64 * (hf + 1)],
                        start=False, stop=False,
                        tile_position=(32 * s_, 64 * hf), skip_group_check=True),
                        inc=(jj == 7 and hf == 1), waits=wts)
            done[("delta", m, s_)] = cpe
        done[("phb_done", m)] = now("pe")
        maybe_tap("otok", m, lambda: otok[seq][:],
                  lambda: [("dve", done[("blend", m, SEGB - 1)])])
        maybe_tap("shadow", m, lambda: shadow[seq][:],
                  lambda: [("act", done[("shadow", m, SEGB - 1)])])

        if stop_after == "phaseb":
            break
        done[("ot", m)] = emit("tq", lambda s, seq=seq: s.dma_start(
            out=ofm.rearrange("p (c f) -> p c f", c=8),
            in_=otok[seq].rearrange("p (c f) -> p c f", c=8), transpose=True), inc=True,
            waits=[("dve", done[("blend", m, SEGB - 1)]), ("pe", dget(("wout_done", m - 1)))])

        if stop_after == "ot":
            break
        for nh in range(2):
            bank = next_big()
            beng, bval = big_evac[bank]
            for ic in range(NCH):
                wts = []
                if ic == 0:
                    wts = [("tq", done[("ot", m)]), (beng, bval)]
                emit("pe", lambda t, nh=nh, ic=ic, bank=bank: t.matmul(
                    out=bank[:, 0:512], lhsT=ofm[:, 128 * ic:128 * (ic + 1)],
                    rhs=wo_all[:, 1024 * ic + 512 * nh:1024 * ic + 512 * (nh + 1)],
                    start=(ic == 0), stop=(ic == NCH - 1)), waits=wts)
            cpe = emit("pe", lambda t: None, inc=True)
            cev = emit("dve", lambda v, seq=seq, nh=nh, bank=bank: v.scalar_tensor_tensor(
                out=xs[seq][:, 512 * nh:512 * (nh + 1)], in0=bank[:, 0:512], scalar=1.0,
                in1=xs[seq][:, 512 * nh:512 * (nh + 1)], op0=ALU.mult, op1=ALU.add), inc=True,
                waits=[("pe", cpe)])
            big_evac[bank] = ("dve", cev)
        done[("wout_done", m)] = now("pe")
        c_s2 = cev
        maybe_tap("s2", m, lambda: xs[seq][:], lambda: [("dve", c_s2)])
        r1, sq_ = stc(seq, 1, 0), stc(seq, 1, 1)
        emit("actd", lambda a, seq=seq, r1=r1: a.activation(
            out=xnfm[seq][:], in_=xs[seq][:], func=AF.Copy, accum_out=r1), inc=True,
            waits=[("dve", c_s2), ("tq", done[("skt", m)])])
        c_sqa2 = emit("actd", lambda a, seq=seq, sq_=sq_: a.activation(
            out=xnfm[seq][:], in_=xs[seq][:], func=AF.Square, accum_out=sq_), inc=True)
        rstd_chain(1, seq, 3, xtr_waits=[("act", c_sqa2)])
        c_h = emit("dve", lambda v, seq=seq: v.tensor_scalar(
            out=htok[seq][:], in0=xs[seq][:], scalar1=r_nm[:, 0:1], scalar2=r_yy[:, 0:1],
            op0=ALU.add, op1=ALU.mult), inc=True, waits=[("dve", dget(("ln3", m - 2)))])
        c_hb = emit("act", lambda a, seq=seq: a.activation(
            out=hbf[seq][:], in_=htok[seq][:], func=AF.Copy), inc=True,
            waits=[("dve", c_h), ("tq", done[("ot", m)])])
        maybe_tap("htok", m, lambda: htok[seq][:], lambda: [("dve", c_h)])

        if stop_after == "ln2":
            break
        if blk > 0:
            c_halo = emit("dve", lambda v, seq=seq: v.tensor_copy(
                out=hh[seq][:], in_=hfm[seq].rearrange("p (c f) -> p c f", f=128)[:, :, 127]),
                inc=True, waits=[("dve", dget(("mix_done", m - 2)))])
        else:
            c_halo = now("dve")
        done[("ht", m)] = emit("tq", lambda s, seq=seq: s.dma_start(
            out=hfm[seq].rearrange("p (c f) -> p c f", c=8),
            in_=hbf[seq].rearrange("p (c f) -> p c f", c=8), transpose=True), inc=True,
            waits=[("act", c_hb), ("dve", c_halo)])

        if stop_after == "ht":
            break
        dtile = emin
        c_d2 = emit("dve", lambda v, seq=seq: v.tensor_tensor(
            out=dtile[:, 0:1024].rearrange("p (c f) -> p c f", f=128)[:, :, 0],
            in0=hfm[seq].rearrange("p (c f) -> p c f", f=128)[:, :, 0],
            in1=hh[seq][:], op=ALU.subtract), inc=True,
            waits=[("tq", done[("ht", m)]), ("act", done[("elu_exp", m)]),
                   ("dve", done[("elu_stt", m)]),
                   ("gps", CK["gps0"] if m < 2 else 0)])
        emit("dve", lambda v, seq=seq: v.tensor_tensor(
            out=dtile[:, 0:1024].rearrange("p (c f) -> p c f", f=128)[:, :, 1:128],
            in0=hfm[seq].rearrange("p (c f) -> p c f", f=128)[:, :, 1:128],
            in1=hfm[seq].rearrange("p (c f) -> p c f", f=128)[:, :, 0:127], op=ALU.subtract))
        murb_v = murb[:].unsqueeze(2).broadcast_to([P, 8, 128])
        mukb_v = mukb[:].unsqueeze(2).broadcast_to([P, 8, 128])
        dv = dtile[:, 0:1024].rearrange("p (c f) -> p c f", f=128)
        emit("dve", lambda v, dv=dv, murb_v=murb_v: v.tensor_tensor(
            out=xr[0].rearrange("p (c f) -> p c f", f=128), in0=dv, in1=murb_v,
            op=ALU.mult),
            waits=[("dve", c_d2), ("dve", CK["wsc_free_dve"] if m < 2 else 0),
                   ("act", CK["wsc_free_act"] if m < 2 else 0),
                   ("pe", dget(("ffnmm_done", m - 1))),
                   ("act", dget(("tanh", m - 1, 1)))])
        emit("dve", lambda v, dv=dv, mukb_v=mukb_v: v.tensor_tensor(
            out=xk[0].rearrange("p (c f) -> p c f", f=128), in0=dv, in1=mukb_v,
            op=ALU.mult))
        emit("dve", lambda v, seq=seq: v.tensor_tensor(
            out=xr[0][:], in0=xr[0][:], in1=hfm[seq][:], op=ALU.add))
        cm = emit("dve", lambda v, seq=seq: v.tensor_tensor(
            out=xk[0][:], in0=xk[0][:], in1=hfm[seq][:], op=ALU.add), inc=True)
        done[("mix_done", m)] = cm
        maybe_tap("xr", m, lambda: xr[seq][:], lambda: [("dve", cm)])

        if stop_after == "mix":
            break
        for nh in range(2):
            bank = next_big()
            beng, bval = big_evac[bank]
            for cc in range(NCH):
                wts = []
                if cc == 0:
                    wts = [("dve", cm), (beng, bval)]
                emit("pe", lambda t, seq=seq, nh=nh, cc=cc, bank=bank: t.matmul(
                    out=bank[:, 0:512], lhsT=xr[seq][:, 128 * cc:128 * (cc + 1)],
                    rhs=wrT[:, 1024 * cc + 512 * nh:1024 * cc + 512 * (nh + 1)],
                    start=(cc == 0), stop=(cc == NCH - 1)), waits=wts)
            cpe = emit("pe", lambda t: None, inc=True)
            ca = emit("act", lambda a, nh=nh, bank=bank: a.activation(
                out=tanh_t[:, 512 * nh:512 * (nh + 1)], in_=bank[:, 0:512], func=AF.Tanh,
                scale=0.5), inc=True,
                waits=[("pe", cpe), ("dve", dget(("ffno", m - 1)))])
            done[("tanh", m, nh)] = ca
            big_evac[bank] = ("act", ca)
        for nh in range(2):
            bank = next_big()
            beng, bval = big_evac[bank]
            for cc in range(NCH):
                wts = []
                if cc == 0:
                    wts = [("dve", cm), (beng, bval)]
                emit("pe", lambda t, seq=seq, nh=nh, cc=cc, bank=bank: t.matmul(
                    out=bank[:, 0:512], lhsT=xk[seq][:, 128 * cc:128 * (cc + 1)],
                    rhs=wkT[:, 1024 * cc + 512 * nh:1024 * cc + 512 * (nh + 1)],
                    start=(cc == 0), stop=(cc == NCH - 1)), waits=wts)
            cpe = emit("pe", lambda t: None, inc=True)
            emit("dve", lambda v, seq=seq, bank=bank: v.tensor_scalar(
                out=krelu_of[seq][:], in0=bank[:, 0:512], scalar1=0.0, scalar2=None,
                op0=ALU.max), waits=[("pe", cpe), ("pe", dget(("kv_done", m - 2))),
                                     ("tq", dget(("kt", m - 1))),
                                     ("dve", CK["wsc_free_dve"] if m < 2 else 0),
                                     ("act", CK["wsc_free_act"] if m < 2 else 0)])
            cev = emit("dve", lambda v, seq=seq, nh=nh: v.tensor_tensor(
                out=kact[seq][:, 512 * nh:512 * (nh + 1)], in0=krelu_of[seq][:],
                in1=krelu_of[seq][:], op=ALU.mult), inc=True)
            big_evac[bank] = ("dve", cev)
        c_ka = cev
        done[("kact", m)] = c_ka
        maybe_tap("kact", m, lambda: kact[seq][:], lambda: [("dve", c_ka)])
        done[("kt", m)] = emit("tq", lambda s, seq=seq: s.dma_start(
            out=kafm[seq].rearrange("p (c f) -> p c f", c=8),
            in_=kact[seq].rearrange("p (c f) -> p c f", c=8), transpose=True), inc=True,
            waits=[("dve", c_ka)])

        if stop_after == "kact":
            break
        for nh in range(2):
            bank = next_big()
            beng, bval = big_evac[bank]
            for cc in range(NCH):
                wts = []
                if cc == 0:
                    wts = [("tq", done[("kt", m)]), (beng, bval)]
                emit("pe", lambda t, seq=seq, nh=nh, cc=cc, bank=bank: t.matmul(
                    out=bank[:, 0:512], lhsT=kafm[seq][:, 128 * cc:128 * (cc + 1)],
                    rhs=wvT[:, 1024 * cc + 512 * nh:1024 * cc + 512 * (nh + 1)],
                    start=(cc == 0), stop=(cc == NCH - 1)), waits=wts)
            cpe = emit("pe", lambda t: None, inc=True)
            emit("dve", lambda v, nh=nh, bank=bank: v.scalar_tensor_tensor(
                out=tanh_t[:, 512 * nh:512 * (nh + 1)], in0=tanh_t[:, 512 * nh:512 * (nh + 1)],
                scalar=1.0, in1=bank[:, 0:512], op0=ALU.add, op1=ALU.mult),
                waits=[("pe", cpe), ("act", done[("tanh", m, nh)])])
            cev = emit("dve", lambda v, seq=seq, nh=nh: v.tensor_tensor(
                out=htok[seq][:, 512 * nh:512 * (nh + 1)],
                in0=htok[seq][:, 512 * nh:512 * (nh + 1)],
                in1=tanh_t[:, 512 * nh:512 * (nh + 1)], op=ALU.add), inc=True,
                waits=[("tq", done[("ht", m)])])
            big_evac[bank] = ("dve", cev)
        done[("ffnmm_done", m)] = now("pe")
        done[("kv_done", m)] = now("pe")
        done[("ffno", m)] = cev
        c_s3 = cev
        maybe_tap("s3", m, lambda: htok[seq][:], lambda: [("dve", c_s3)])
        r1, sq_ = stc(seq, 2, 0), stc(seq, 2, 1)
        emit("actd", lambda a, seq=seq, r1=r1: a.activation(
            out=xnfm[seq][:], in_=htok[seq][:], func=AF.Copy, accum_out=r1), inc=True,
            waits=[("dve", c_s3)])
        c_sqa3 = emit("actd", lambda a, seq=seq, sq_=sq_: a.activation(
            out=xnfm[seq][:], in_=htok[seq][:], func=AF.Square, accum_out=sq_), inc=True)
        rstd_chain(2, seq, 3, xtr_waits=[("act", c_sqa3)])
        c_o = emit("dve", lambda v, seq=seq: v.tensor_scalar(
            out=h3[seq][:], in0=htok[seq][:], scalar1=r_nm[:, 0:1], scalar2=r_yy[:, 0:1],
            op0=ALU.add, op1=ALU.mult), inc=True, waits=[("sq", dget(("store", m - 2)))])
        done[("ln3", m)] = c_o
        maybe_tap("h3", m, lambda: h3[seq][:], lambda: [("dve", c_o)])
        done[("store", m)] = emit("sq", lambda s, seq=seq, blk=blk: s.dma_start(
            out=out_h[seq, TOK * blk:TOK * (blk + 1), :], in_=h3[seq][:]), inc=True,
            waits=[("dve", c_o)])

        if m + 2 < NMT:
            load_x(m + 2)

    total_sq = cnt["sq"]
    emit("sq", lambda s: None, waits=[("sq", total_sq)])

    with nc.Block() as block:
        block.tensor(_mk_runner(ops["pe"]))
        block.vector(_mk_runner(ops["dve"]))
        block.scalar(_mk_runner(ops["scalar"]))
        block.gpsimd(_mk_runner(ops["gps"]))
        block.sync(_mk_runner(ops["sync"]))

    ctx.close()
    return nc, tap_h


def _mk_runner(stream):
    def _run(eng):
        last_instr = [None]
        for (wl, fn, sem, step, logical) in stream:
            for (s, v) in wl:
                eng.wait_ge(s, v)
            instr = fn(eng)
            if instr is not None:
                last_instr[0] = instr
            if sem is not None:
                if logical == "actd":
                    eng.drain().then_inc(sem, step)
                else:
                    tgt = instr if instr is not None else last_instr[0]
                    tgt.then_inc(sem, step)
    return _run


_NC_CACHE = {}


def kernel(L0_attn_x, L0_attn_w_qkv, L0_attn_w_out, L0_attn_ln_g, L0_attn_ln_b,
           L0_attn_mem_beta, L0_ffn_W_r, L0_ffn_W_k, L0_ffn_W_v, L0_ffn_mu_r, L0_ffn_mu_k):
    x = np.ascontiguousarray(np.asarray(L0_attn_x, dtype=np.float32))
    if "nc" not in _NC_CACHE:
        _NC_CACHE["nc"] = build_nc()
    nc, _ = _NC_CACHE["nc"]
    shared = {
        "w_qkv": np.ascontiguousarray(np.asarray(L0_attn_w_qkv, np.float32)),
        "w_out": np.ascontiguousarray(np.asarray(L0_attn_w_out, np.float32)),
        "ln_g": np.ascontiguousarray(np.asarray(L0_attn_ln_g, np.float32)),
        "ln_b": np.ascontiguousarray(np.asarray(L0_attn_ln_b, np.float32)),
        "mem_beta": np.ascontiguousarray(np.asarray(L0_attn_mem_beta, np.float32).reshape(1, H)),
        "W_r": np.ascontiguousarray(np.asarray(L0_ffn_W_r, np.float32)),
        "W_k": np.ascontiguousarray(np.asarray(L0_ffn_W_k, np.float32)),
        "W_v": np.ascontiguousarray(np.asarray(L0_ffn_W_v, np.float32)),
        "mu_r": np.ascontiguousarray(np.asarray(L0_ffn_mu_r, np.float32)),
        "mu_k": np.ascontiguousarray(np.asarray(L0_ffn_mu_k, np.float32)),
    }
    in_maps = []
    for core in range(8):
        im = dict(shared)
        im["x"] = np.ascontiguousarray(x[2 * core:2 * core + 2])
        in_maps.append(im)
    res = run_bass_kernel_spmd(nc, in_maps, core_ids=list(range(8)))
    out = np.concatenate([r["out"] for r in res.results], axis=0)
    return np.ascontiguousarray(out.reshape(16, TSEQ, D).astype(np.float32))


# version 10
# speedup vs baseline: 1.1595x; 1.1595x over previous
import math
import numpy as np
import concourse.bass as bass
import concourse.mybir as mybir
from concourse.bass_utils import run_bass_kernel_spmd
from contextlib import ExitStack

F32 = mybir.dt.float32
BF16 = mybir.dt.bfloat16
I32 = mybir.dt.int32
AF = mybir.ActivationFunctionType
ALU = mybir.AluOpType

P = 128
D = 1024
NCH = 8
H = 16
SEG = 32
TOK = 128
NSEQ = 2
TSEQ = 2048
NBLK = TSEQ // TOK
NMT = NSEQ * NBLK
SEGB = TOK // SEG
EXPB = 20.0
LOG1E4 = math.log(10000.0)
TWO_PI = 2.0 * math.pi


def build_nc(taps=(), nmt=NMT, stop_after=None):
    nc = bass.Bass("TRN2", target_bir_lowering=False)
    x_h = nc.dram_tensor("x", [NSEQ, TSEQ, D], F32, kind="ExternalInput").ap()
    wqkv_h = nc.dram_tensor("w_qkv", [D, 3 * D], F32, kind="ExternalInput").ap()
    wout_h = nc.dram_tensor("w_out", [D, D], F32, kind="ExternalInput").ap()
    g_h = nc.dram_tensor("ln_g", [D], F32, kind="ExternalInput").ap()
    b_h = nc.dram_tensor("ln_b", [D], F32, kind="ExternalInput").ap()
    beta_h = nc.dram_tensor("mem_beta", [1, H], F32, kind="ExternalInput").ap()
    wr_h = nc.dram_tensor("W_r", [D, D], F32, kind="ExternalInput").ap()
    wk_h = nc.dram_tensor("W_k", [D, D], F32, kind="ExternalInput").ap()
    wv_h = nc.dram_tensor("W_v", [D, D], F32, kind="ExternalInput").ap()
    mur_h = nc.dram_tensor("mu_r", [D], F32, kind="ExternalInput").ap()
    muk_h = nc.dram_tensor("mu_k", [D], F32, kind="ExternalInput").ap()
    out_h = nc.dram_tensor("out", [NSEQ, TSEQ, D], F32, kind="ExternalOutput").ap()

    ctx = ExitStack()
    e = ctx.enter_context

    wq_all = e(nc.sbuf_tensor("wq_all", [P, NCH * 3072], BF16))
    wo_all = e(nc.sbuf_tensor("wo_all", [P, NCH * 1024], BF16))
    wrT = e(nc.sbuf_tensor("wrT", [P, NCH * 1024], BF16))
    wkT = e(nc.sbuf_tensor("wkT", [P, NCH * 1024], BF16))
    wvT = e(nc.sbuf_tensor("wvT", [P, NCH * 1024], BF16))
    cosT = e(nc.sbuf_tensor("cosT", [P, TSEQ], BF16))
    sinT = e(nc.sbuf_tensor("sinT", [P, TSEQ], BF16))
    mask01 = e(nc.sbuf_tensor("mask01", [P, P], BF16))
    smat = e(nc.sbuf_tensor("smat", [P, P], BF16))
    gcol = e(nc.sbuf_tensor("gcol", [P, 8], F32))
    bcol = e(nc.sbuf_tensor("bcol", [P, 8], BF16))
    murm1 = e(nc.sbuf_tensor("murm1", [P, 8], F32))
    mukm1 = e(nc.sbuf_tensor("mukm1", [P, 8], F32))
    murb = e(nc.sbuf_tensor("murb", [P, 8], BF16))
    mukb = e(nc.sbuf_tensor("mukb", [P, 8], BF16))
    ubqk = e(nc.sbuf_tensor("ubqk", [P, 16], F32))
    ubv = e(nc.sbuf_tensor("ubv", [P, 1024], BF16))
    btr = e(nc.sbuf_tensor("btr", [1, 16], F32))
    betab = e(nc.sbuf_tensor("betab", [1, 16], BF16))
    ombb = e(nc.sbuf_tensor("ombb", [1, 16], BF16))
    btile = e(nc.sbuf_tensor("btile", [P, 8], F32))
    omb = e(nc.sbuf_tensor("omb", [P, 16], F32))
    epst = e(nc.sbuf_tensor("epst", [P, 1], F32))
    cm20 = e(nc.sbuf_tensor("cm20", [P, 1], F32))
    ones1 = e(nc.sbuf_tensor("ones1", [1, P], BF16))
    pidx = e(nc.sbuf_tensor("pidx", [P, 1], F32))
    cidx = None
    par = e(nc.sbuf_tensor("par", [P, 2], F32))
    freq64 = e(nc.sbuf_tensor("freq64", [P, 1], F32))
    sti = e(nc.sbuf_tensor("sti", [P, 1], I32))
    gtmp = e(nc.sbuf_tensor("gtmp", [P, 8], F32))

    def perseq(name, cols, dt):
        return [e(nc.sbuf_tensor(f"{name}{s}", [P, cols], dt)) for s in range(NSEQ)]

    xs = perseq("xs", D, F32)
    xhat = perseq("xhat", D, BF16)
    xnfm = perseq("xnfm", D, BF16)
    qkfm = perseq("qkfm", 2 * D, BF16)
    qkbd = perseq("qkbd", 2 * D, BF16)
    skT = perseq("skT", D, BF16)
    vb = perseq("vb", 8 * 130, BF16)
    sdpa = perseq("sdpa", 16 * 65, BF16)
    shadow = perseq("shadow", 8 * 65, BF16)
    zf = perseq("zf", 8, F32)
    hh = perseq("hh", 8, BF16)
    otok = None
    htok = perseq("htok", D, F32)
    hbf = xhat
    otok = xhat
    hfm = perseq("hfm", D, BF16)
    skfm = xnfm
    kafm = perseq("kafm", D, BF16)
    h3 = None
    ofm = e(nc.sbuf_tensor("ofm", [P, D], BF16))
    ut = e(nc.sbuf_tensor("ut", [P, D], BF16))
    h3 = [qkbd[s_].bitcast(F32)[:, 0:1024] for s_ in range(NSEQ)]
    dtile_sv = [sdpa[s_][:, 0:1024] for s_ in range(NSEQ)]
    tanh_t = e(nc.sbuf_tensor("tanh_t", [P, D], F32))
    probs_sv = [[htok[s_].bitcast(BF16)[:, 0:512], htok[s_].bitcast(BF16)[:, 512:1024]]
                for s_ in range(NSEQ)]
    rop1 = e(nc.sbuf_tensor("rop1", [P, 2 * TOK], F32))
    rop2 = e(nc.sbuf_tensor("rop2", [P, 2 * TOK], F32))
    rop1w = rop1[:].rearrange("p (a f) -> p a f", f=TOK)
    rop2w = rop2[:].rearrange("p (a f) -> p a f", f=TOK)

    wsc = e(nc.sbuf_tensor("wsc", [P, 3072], BF16))
    wscf = wsc.bitcast(F32)
    xr_t = wsc[:, 0:1024]
    xk_t = wsc[:, 1024:2048]
    kact_t = wsc[:, 2048:3072]
    xr = [xr_t, xr_t]
    xk = [xk_t, xk_t]
    kact = [kact_t, kact_t]

    arena = e(nc.sbuf_tensor("arena", [P, 2048], BF16))
    emin = arena[:, 0:1024]
    eexp = arena[:, 1024:2048]
    lnsq = emin
    wraw = arena[:, 0:1536]
    ffsc = arena.bitcast(F32)[:, 0:512]
    krelu_of = [kafm[s_].bitcast(F32)[:, 0:512] for s_ in range(NSEQ)]

    ubvrow = ut[0:1, 0:1024]
    cidx = ut.bitcast(F32)[:, 0:128]
    ST = e(nc.sbuf_tensor("ST", [P, 9 * 6], F32))

    def stc(seq, lnid, j):
        c = 9 * (lnid + 3 * seq) + j
        return ST[:, c:c + 1]

    REG = e(nc.sbuf_tensor("REG", [P, 5 * 64], F32))
    W64 = 64
    r_nm, r_tt, r_yy, r_aa, r_hx = [REG[:, 64 * j:64 * j + 64] for j in range(5)]
    rc1A = e(nc.sbuf_tensor("rc1A", [P, 4], F32))
    rc1B = e(nc.sbuf_tensor("rc1B", [P, 4], F32))
    rc1 = rc1A
    zred = e(nc.sbuf_tensor("zred", [P, 8], F32))

    pA = e(nc.psum_tensor("pA", [P, 512], F32))
    pB = e(nc.psum_tensor("pB", [P, 512], F32))
    pC = e(nc.psum_tensor("pC", [P, 512], F32))
    pD = e(nc.psum_tensor("pD", [P, 512], F32))
    pE = e(nc.psum_tensor("pE", [P, 512], F32))
    pF = e(nc.psum_tensor("pF", [P, 512], F32))
    memPS = [e(nc.psum_tensor(f"memPS{s}", [P, 512], F32)) for s in range(NSEQ)]

    LOG2PHYS = {"pe": "pe", "dve": "dve", "act": "scalar", "tq": "sync",
                "gps": "gps", "gq": "gps", "sq": "sync", "actd": "scalar"}
    PHYS = ["pe", "dve", "scalar", "gps", "sync"]
    sems = {k: e(nc.semaphore(f"s_{k}")) for k in ("pe", "dve", "act", "gps", "sq", "tq", "gq")}
    sems["actd"] = sems["act"]
    cnt = {k: 0 for k in sems}
    cnt.pop("actd"); cnt["actd"] = None
    waitmark = {ph: {k: 0 for k in sems} for ph in PHYS}
    ops = {ph: [] for ph in PHYS}
    tap_bar = {"v": 0}

    def emit(logical, fn, inc=False, waits=()):
        ph = LOG2PHYS[logical]
        waits = list(waits)
        if tap_bar["v"]:
            waits.append(("sq", tap_bar["v"]))
        wl = []
        for (pr, val) in waits:
            if val and val > waitmark[ph][pr]:
                waitmark[ph][pr] = val
                wl.append((sems[pr], val))
        step = 16 if logical in ("sq", "tq", "gq") else 1
        cl = "act" if logical == "actd" else logical
        sem = None
        if inc:
            cnt[cl] += step
            sem = sems[cl]
        ops[ph].append((wl, fn, sem, step, logical))
        return cnt[cl]

    def now(l):
        return cnt[l]


    def ddrain():
        emit("dve", lambda v: v.drain())

    tap_h = {}

    def tap(name, view, waits):
        th = nc.dram_tensor(f"tap_{name}", list(view.shape), view.dtype,
                            kind="ExternalOutput").ap()
        tap_h[f"tap_{name}"] = str(view.dtype)
        emit("sq", lambda s, th=th, view=view: s.dma_start(out=th, in_=view),
             inc=True, waits=waits)
        tap_bar["v"] = cnt["sq"]

    def maybe_tap(stage, m, view_fn, waits_fn):
        for (tname, tmt) in taps:
            if tname == stage and tmt == m:
                tap(f"{stage}_{m}", view_fn(), waits_fn())

    CK = {}
    xload = {}
    done = {}

    def load_x(m, w=()):
        seq, blk = m % 2, m // 2
        xload[m] = emit("sq", lambda s, seq=seq, blk=blk: s.dma_start(
            out=xs[seq][:], in_=x_h[seq, TOK * blk:TOK * (blk + 1), :]), inc=True,
            waits=list(w))

    load_x(0)
    load_x(1)

    def vec_to_cols(vh, dest, first_waits=()):
        mk = 0
        for c in range(8):
            src = vh[128 * c:128 * c + 128].rearrange("(a b) -> a b", b=1)
            mk = emit("sq", lambda s, d=dest[:, c:c + 1], src=src: s.dma_start(out=d, in_=src),
                      inc=True, waits=(list(first_waits) if c == 0 else []))
        return mk

    CK["g"] = vec_to_cols(g_h, gcol)
    CK["b"] = vec_to_cols(b_h, gtmp)
    CK["bc"] = emit("dve", lambda v: v.tensor_copy(out=bcol[:], in_=gtmp[:]), inc=True,
                    waits=[("sq", CK["b"])])
    CK["mur"] = vec_to_cols(mur_h, gtmp, first_waits=[("dve", CK["bc"])])
    emit("dve", lambda v: v.tensor_scalar(out=murm1[:], in0=gtmp[:], scalar1=1.0,
                                          scalar2=-1.0, op0=ALU.mult, op1=ALU.add),
         waits=[("sq", CK["mur"])])
    ddrain()
    CK["m1"] = emit("dve", lambda v: v.tensor_copy(out=murb[:], in_=murm1[:]), inc=True)
    CK["muk"] = vec_to_cols(muk_h, gtmp, first_waits=[("dve", CK["m1"])])
    emit("dve", lambda v: v.tensor_scalar(out=mukm1[:], in0=gtmp[:], scalar1=1.0,
                                          scalar2=-1.0, op0=ALU.mult, op1=ALU.add),
         waits=[("sq", CK["muk"])])
    ddrain()
    CK["m2"] = emit("dve", lambda v: v.tensor_copy(out=mukb[:], in_=mukm1[:]), inc=True)
    CK["beta"] = emit("sq", lambda s: s.dma_start(out=btr[:], in_=beta_h), inc=True)

    emit("gps", lambda g: g.iota(pidx[:], pattern=[[0, 1]], channel_multiplier=1,
                                 allow_small_or_imprecise_dtypes=True))
    emit("gps", lambda g: g.iota(cidx[:], pattern=[[1, P]], channel_multiplier=0,
                                 allow_small_or_imprecise_dtypes=True))
    emit("gps", lambda g: g.memset(mask01[:], 1.0))
    mview = mask01.rearrange("p (a b) -> p a b", a=4)
    emit("gps", lambda g: g.affine_select(out=mview, in_=mview, pattern=[[-32, 4], [0, 32]],
                                          compare_op=ALU.is_ge, fill=0.0, base=0,
                                          channel_multiplier=1))
    emit("gps", lambda g: g.affine_select(out=mview, in_=mview, pattern=[[32, 4], [0, 32]],
                                          compare_op=ALU.is_ge, fill=0.0, base=31,
                                          channel_multiplier=-1))
    emit("gps", lambda g: g.affine_select(out=mview, in_=mview, pattern=[[32, 4], [1, 32]],
                                          compare_op=ALU.is_ge, fill=0.0, base=0,
                                          channel_multiplier=-1))
    emit("gps", lambda g: g.memset(ones1[:], 1.0))
    for s_ in range(NSEQ):
        vbv = vb[s_].rearrange("p (c f) -> p c f", f=130)
        emit("gps", lambda g, vbv=vbv: g.memset(vbv[:, :, 64:65], 1.0))
        emit("gps", lambda g, vbv=vbv: g.memset(vbv[:, :, 129:130], 1.0))
        emit("gps", lambda g, s_=s_: g.memset(hh[s_][:], 0.0))
        emit("gps", lambda g, s_=s_: g.memset(zf[s_][:], 1e-6))
        shv = shadow[s_].rearrange("p (c f) -> p c f", f=65)
        emit("gps", lambda g, s_=s_: g.memset(shadow[s_][:], 0.0))
        emit("gps", lambda g, shv=shv: g.memset(shv[:, :, 64:65], 1e-6))
    emit("gps", lambda g: g.memset(cm20[:], -EXPB))
    emit("gps", lambda g: g.memset(epst[:], 0.0))
    emit("gps", lambda g: g.memset(epst[0:32, :], 1e-6))
    CK["gps0"] = emit("gps", lambda g: g.memset(epst[64:96, :], 1e-6), inc=True)
    emit("dve", lambda v: v.memset(memPS[0][:], 0.0))
    CK["pszero"] = emit("dve", lambda v: v.memset(memPS[1][:], 0.0), inc=True)

    st0 = stc(0, 0, 6)
    emit("dve", lambda v: v.tensor_scalar(out=st0, in0=pidx[:], scalar1=0.5, scalar2=-0.25,
                                          op0=ALU.mult, op1=ALU.add),
         waits=[("gps", CK["gps0"])])
    ddrain()
    emit("dve", lambda v: v.tensor_copy(out=sti[:], in_=st0))
    ddrain()
    CK["flr"] = emit("dve", lambda v: v.tensor_copy(out=st0, in_=sti[:]), inc=True)
    ddrain()
    CK["freq"] = emit("act", lambda a: a.activation(out=freq64[:], in_=st0, func=AF.Exp,
                                                    scale=-LOG1E4 / 32.0), inc=True,
                      waits=[("dve", CK["flr"])])
    emit("dve", lambda v: v.tensor_scalar(out=par[:, 0:1], in0=st0, scalar1=-2.0, scalar2=None,
                                          op0=ALU.mult))
    ddrain()
    emit("dve", lambda v: v.tensor_tensor(out=par[:, 0:1], in0=par[:, 0:1], in1=pidx[:],
                                          op=ALU.add))
    ddrain()
    emit("dve", lambda v: v.tensor_scalar(out=par[:, 1:2], in0=par[:, 0:1], scalar1=-1.0,
                                          scalar2=1.0, op0=ALU.mult, op1=ALU.add))
    ddrain()
    emit("dve", lambda v: v.tensor_scalar(out=cidx[:], in0=cidx[:], scalar1=pidx[:], scalar2=None,
                                          op0=ALU.subtract))
    emit("dve", lambda v: v.tensor_scalar(out=smat[:], in0=cidx[:], scalar1=1.0, scalar2=None,
                                          op0=ALU.is_equal))
    emit("dve", lambda v: v.tensor_scalar(out=smat[:], in0=smat[:], scalar1=par[:, 1:2],
                                          scalar2=None, op0=ALU.mult))
    emit("dve", lambda v: v.tensor_scalar(out=lnsq[:, 0:128], in0=cidx[:], scalar1=-1.0,
                                          scalar2=None, op0=ALU.is_equal))
    emit("dve", lambda v: v.tensor_scalar(out=lnsq[:, 0:128], in0=lnsq[:, 0:128],
                                          scalar1=par[:, 0:1], scalar2=None, op0=ALU.mult))
    CK["smat"] = emit("dve", lambda v: v.tensor_tensor(out=smat[:], in0=smat[:],
                                                       in1=lnsq[:, 0:128], op=ALU.subtract),
                      inc=True)

    wq_f = wq_all.bitcast(F32)
    pos64 = wq_f[0:64, 0:2048]
    ang64 = wq_f[0:64, 2048:4096]
    u_s = wq_f[0:64, 4096:6144]
    n_f = wq_f[0:64, 6144:8192]
    n_i = wq_f[0:64, 8192:10240].bitcast(I32)
    CK["pos"] = emit("gps", lambda g: g.iota(pos64, pattern=[[1, 2048]], channel_multiplier=0,
                                             allow_small_or_imprecise_dtypes=True), inc=True)
    emit("dve", lambda v: v.tensor_scalar(out=ang64, in0=pos64, scalar1=freq64[0:64, 0:1],
                                          scalar2=None, op0=ALU.mult),
         waits=[("gps", CK["pos"]), ("act", CK["freq"])])

    def table_build(phase, dest):
        emit("dve", lambda v: v.tensor_scalar(out=u_s, in0=ang64, scalar1=1.0 / TWO_PI,
                                              scalar2=phase, op0=ALU.mult, op1=ALU.add))
        emit("dve", lambda v: v.tensor_copy(out=n_i, in_=u_s))
        emit("dve", lambda v: v.tensor_copy(out=n_f, in_=n_i))
        cc = emit("dve", lambda v: v.tensor_tensor(out=u_s, in0=u_s, in1=n_f, op=ALU.subtract),
                  inc=True)
        ca = emit("act", lambda a: a.activation(out=dest[0:64, :], in_=u_s, func=AF.Sin,
                                                scale=TWO_PI), inc=True, waits=[("dve", cc)])
        emit("dve", lambda v: v.tensor_copy(out=dest[64:128, :], in_=dest[0:64, :]),
             waits=[("act", ca)])

    table_build(0.25, cosT)
    table_build(0.0, sinT)
    CK["tables"] = emit("dve", lambda v: None, inc=True)

    ca = emit("act", lambda a: a.activation(out=btr[:], in_=btr[:], func=AF.Tanh, scale=0.5),
              inc=True, waits=[("sq", CK["beta"])])
    emit("dve", lambda v: v.tensor_scalar(out=betab[:], in0=btr[:], scalar1=0.5, scalar2=0.5,
                                          op0=ALU.mult, op1=ALU.add), waits=[("act", ca)])
    CK["betab"] = emit("dve", lambda v: v.tensor_scalar(out=ombb[:], in0=btr[:], scalar1=-0.5,
                                                        scalar2=0.5, op0=ALU.mult, op1=ALU.add),
                       inc=True)
    emit("pe", lambda t: t.matmul(out=pA[:, 0:16], lhsT=ones1[:], rhs=ombb[:],
                                  start=True, stop=True),
         waits=[("dve", CK["betab"]), ("gps", CK["gps0"])])
    CK["pe_beta"] = emit("pe", lambda t: t.matmul(out=pB[0:32, 0:16], lhsT=ones1[:, 0:32],
                                                  rhs=betab[:], start=True, stop=True), inc=True)
    emit("dve", lambda v: v.tensor_copy(out=omb[:], in_=pA[:, 0:16]),
         waits=[("pe", CK["pe_beta"])])
    emit("dve", lambda v: v.memset(btile[:], 1.0))
    ddrain()
    emit("dve", lambda v: v.tensor_copy(out=btile[0:32, :], in_=pB[0:32, 0:16:2]))
    CK["btile"] = emit("dve", lambda v: v.tensor_copy(out=btile[64:96, :], in_=pB[0:32, 1:16:2]),
                      inc=True)

    prev_sc = 0
    prev_ca = 0
    last_pe = 0
    for c in range(NCH):
        for hf in range(2):
            wl = emit("sq", lambda s, c=c, hf=hf: s.dma_start(
                out=wscf[:, 0:1536],
                in_=wqkv_h[128 * c:128 * c + 128, 1536 * hf:1536 * (hf + 1)]),
                inc=True, waits=[("dve", max(CK["tables"], prev_sc)), ("act", prev_ca),
                                 ("pe", last_pe)])
            prev_ca = emit("act", lambda a: a.activation(out=wraw, in_=wscf[:, 0:1536],
                                                         func=AF.Copy), inc=True,
                           waits=[("sq", wl)])
            prev_sc = emit("dve", lambda v, c=c, hf=hf: v.tensor_scalar(
                out=wq_all[:, 3072 * c + 1536 * hf:3072 * c + 1536 * (hf + 1)],
                in0=wscf[:, 0:1536], scalar1=gcol[:, c:c + 1], scalar2=None,
                op0=ALU.mult), inc=True,
                waits=[("sq", wl), ("sq", CK["g"]), ("dve", CK["tables"])])
            if hf == 0:
                mcs = range(0, 12)
            else:
                mcs = range(12, 16)
            for mc in mcs:
                off = 128 * mc - 1536 * hf
                emit("pe", lambda t, c=c, mc=mc, off=off: t.matmul(
                    out=pC[:, 32 + mc:33 + mc], lhsT=wraw[:, off:off + 128],
                    rhs=bcol[:, c:c + 1], start=(c == 0 and mc == 0),
                    stop=(c == NCH - 1)),
                    waits=[("act", prev_ca), ("dve", CK["bc"]),
                           ("dve", CK["btile"] if c == 0 else 0)])
            if hf == 1:
                emit("pe", lambda t, c=c: t.matmul(out=pD[0:1, 0:512], lhsT=bcol[:, c:c + 1],
                                                   rhs=wraw[:, 512:1024], start=(c == 0),
                                                   stop=(c == NCH - 1)))
                last_pe = emit("pe", lambda t, c=c: t.matmul(
                    out=pE[0:1, 0:512], lhsT=bcol[:, c:c + 1],
                    rhs=wraw[:, 1024:1536], start=(c == 0), stop=(c == NCH - 1)), inc=True)
    CK["pe_ub"] = last_pe
    emit("dve", lambda v: v.tensor_copy(out=ubqk[:], in_=pC[:, 32:48]),
         waits=[("pe", CK["pe_ub"])])
    emit("dve", lambda v: v.tensor_copy(out=ubvrow[0:1, 0:512], in_=pD[0:1, 0:512]))
    CK["ubvr"] = emit("dve", lambda v: v.tensor_copy(out=ubvrow[0:1, 512:1024],
                                                     in_=pE[0:1, 0:512]), inc=True)
    emit("pe", lambda t: t.matmul(out=pD[:, 0:512], lhsT=ones1[:], rhs=ubvrow[0:1, 0:512],
                                  start=True, stop=True), waits=[("dve", CK["ubvr"])])
    CK["pe_ubv"] = emit("pe", lambda t: t.matmul(out=pE[:, 0:512], lhsT=ones1[:],
                                                 rhs=ubvrow[0:1, 512:1024], start=True,
                                                 stop=True), inc=True)
    emit("dve", lambda v: v.tensor_copy(out=ubv[:, 0:512], in_=pD[:, 0:512]),
         waits=[("pe", CK["pe_ubv"])])
    CK["ubv"] = emit("dve", lambda v: v.tensor_copy(out=ubv[:, 512:1024], in_=pE[:, 0:512]),
                     inc=True)

    prev_gq = 0
    for c in range(NCH):
        prev_gq = emit("gq", lambda g, c=c: g.dma_start(
            out=wo_all[:, 1024 * c:1024 * (c + 1)], in_=wout_h[128 * c:128 * c + 128, :]),
            inc=True, waits=[("gps", CK["gps0"])])
    prev_tq = 0
    prev_dv = 0
    stg0 = qkbd[0][:, 0:1024]
    stg1 = qkbd[0][:, 1024:2048]
    for (whh, dstT, halfscale) in ((wr_h, wrT, False), (wk_h, wkT, False), (wv_h, wvT, True)):
        for dc in range(NCH):
            wl = emit("gq", lambda g, whh=whh, dc=dc: g.dma_start(
                out=stg0, in_=whh[128 * dc:128 * dc + 128, :]), inc=True,
                waits=[("tq", prev_tq), ("dve", prev_dv)])
            src = stg0
            if halfscale:
                prev_dv = emit("dve", lambda v: v.tensor_scalar(
                    out=stg1, in0=stg0, scalar1=0.5, scalar2=None, op0=ALU.mult), inc=True,
                    waits=[("gq", wl)])
                src = stg1
            dst = dstT.rearrange("p (a b f) -> p a b f", a=8, f=128)[:, :, dc, :]
            prev_tq = emit("tq", lambda s, dst=dst, src=src.rearrange(
                "p (c f) -> p c f", c=8): s.dma_start(out=dst, in_=src, transpose=True),
                inc=True, waits=([("gq", wl)] if not halfscale else [("dve", prev_dv)]))
    CK["wffn_dve"] = max(prev_sc, prev_dv)
    CK["wffn_gq"] = prev_gq
    CK["wffn_tq"] = prev_tq
    CK["wsc_free_dve"] = prev_sc
    CK["wsc_free_act"] = prev_ca
    CK["arena_free_pe"] = CK["pe_ubv"]


    def dget(key, default=0):
        return done.get(key, default)

    pe_big = [pA, pB, pC, pD]
    big_i = [0]
    big_evac_d = {id(pA): ("dve", CK["btile"]), id(pB): ("dve", CK["btile"]),
                  id(pC): ("dve", CK["ubv"]), id(pD): ("dve", CK["ubv"]),
                  id(pE): ("dve", CK["ubv"]), id(pF): ("dve", 0)}

    class _BE:
        def __getitem__(self, b):
            return big_evac_d[id(b)]

        def __setitem__(self, b, v):
            big_evac_d[id(b)] = v

    big_evac = _BE()

    def next_big():
        b = pe_big[big_i[0] % 2]
        big_i[0] += 1
        return b

    def rstd_chain(lnid, seq, steps, xtr_waits=()):
            r1, sq_ = stc(seq, lnid, 0), stc(seq, lnid, 1)
            emit("dve", lambda v: v.tensor_scalar(
                out=r_nm, in0=r1.broadcast_to([P, W64]), scalar1=-1.0 / D,
                scalar2=None, op0=ALU.mult), waits=list(xtr_waits))
            emit("dve", lambda v: v.tensor_scalar(
                out=r_tt, in0=sq_.broadcast_to([P, W64]), scalar1=1.0 / D,
                scalar2=1e-5, op0=ALU.mult, op1=ALU.add))
            emit("dve", lambda v: v.tensor_tensor(out=r_aa, in0=r_nm, in1=r_nm, op=ALU.mult))
            emit("dve", lambda v: v.tensor_tensor(out=r_tt, in0=r_tt, in1=r_aa,
                                                  op=ALU.subtract))
            emit("dve", lambda v: v.tensor_scalar(out=r_aa, in0=r_tt, scalar1=0.5, scalar2=0.5,
                                                  op0=ALU.mult, op1=ALU.add))
            emit("dve", lambda v: v.reciprocal(out=r_yy, in_=r_aa))
            for _ in range(steps):
                emit("dve", lambda v: v.tensor_tensor(out=r_aa, in0=r_yy, in1=r_yy,
                                                      op=ALU.mult))
                emit("dve", lambda v: v.tensor_tensor(out=r_hx, in0=r_aa, in1=r_tt,
                                                      op=ALU.mult))
                emit("dve", lambda v: v.tensor_scalar(out=r_aa, in0=r_hx, scalar1=-0.5,
                                                      scalar2=1.5, op0=ALU.mult, op1=ALU.add))
                emit("dve", lambda v: v.tensor_tensor(out=r_yy, in0=r_yy, in1=r_aa,
                                                      op=ALU.mult))

    def emit_F1(m):
        seq, blk = m % 2, m // 2
        r1, sq_ = stc(seq, 0, 0), stc(seq, 0, 1)
        wx = xload[m]
        emit("act", lambda a, seq=seq, r1=r1: a.activation(
            out=xhat[seq][:], in_=xs[seq][:], func=AF.Copy, accum_out=r1),
            waits=[("sq", wx),
                   ("tq", max(dget(("ht", m - 2)), dget(("xt", m - 2))))])
        c_sqa = emit("actd", lambda a, seq=seq, sq_=sq_: a.activation(
            out=xhat[seq][:], in_=xs[seq][:], func=AF.Square, accum_out=sq_),
            inc=True)

        rstd_chain(0, seq, 2, xtr_waits=[("act", c_sqa)])
        c_xh = emit("dve", lambda v, seq=seq: v.tensor_scalar(
            out=xhat[seq][:], in0=xs[seq][:], scalar1=r_nm[:, 0:1], scalar2=r_yy[:, 0:1],
            op0=ALU.add, op1=ALU.mult), inc=True, waits=[("tq", dget(("ht", m - 2)))])
        maybe_tap("xhat", m, lambda: xhat[seq][:], lambda: [("dve", c_xh)])

        if stop_after == "ln1":
            return True
        done[("xt", m)] = emit("tq", lambda s, seq=seq: s.dma_start(
            out=xnfm[seq].rearrange("p (c f) -> p c f", c=8),
            in_=xhat[seq].rearrange("p (c f) -> p c f", c=8), transpose=True), inc=True,
            waits=[("dve", c_xh), ("pe", dget(("qkv_done", m - 2))),
                   ("tq", dget(("skt", m - 2)))])
        maybe_tap("xnfm", m, lambda: xnfm[seq][:], lambda: [("tq", done[("xt", m)])])

        return stop_after == "xt"

    def emit_F2qk(m, mp0, mp1):
        seq, blk = m % 2, m // 2
        cev = 0
        for mp in range(mp0, mp1):
            bank = next_big()
            beng, bval = big_evac[bank]
            for half in range(2):
                mc = 2 * mp + half
                for kc in range(NCH):
                    wts = []
                    if half == 0 and kc == 0:
                        wts = [("tq", done[("xt", m)]), (beng, bval)]
                        if m < 2 and mp == 0:
                            wts.append(("dve", CK["wffn_dve"]))
                    emit("pe", lambda t, mc=mc, kc=kc, half=half, seq=seq, bank=bank: t.matmul(
                        out=bank[:, 128 * half:128 * (half + 1)],
                        lhsT=wq_all[:, 3072 * kc + 128 * mc:3072 * kc + 128 * (mc + 1)],
                        rhs=xnfm[seq][:, 128 * kc:128 * (kc + 1)],
                        start=(kc == 0), stop=(kc == NCH - 1)), waits=wts)
            cpe = emit("pe", lambda t: None, inc=True)
            ubv2 = ubqk[:, 2 * mp:2 * mp + 2].unsqueeze(2).broadcast_to([P, 2, 128])
            cev = emit("dve", lambda v, mp=mp, seq=seq, bank=bank, ubv2=ubv2: v.tensor_tensor(
                out=qkfm[seq][:, 256 * mp:256 * (mp + 1)].rearrange("p (a f) -> p a f", f=128),
                in0=bank[:, 0:256].rearrange("p (a f) -> p a f", f=128),
                in1=ubv2, op=ALU.add), inc=True,
                waits=[("pe", cpe), ("pe", dget(("scores_done", m - 2)))])
            big_evac[bank] = ("dve", cev)
        return cev

    def emit_F2a(m):
        emit_F2qk(m, 0, 4)
        return False

    def emit_F2b(m):
        done[("qk_evac", m)] = emit_F2qk(m, 4, 8)
        return False

    def emit_F2c(m):
        seq, blk = m % 2, m // 2
        for nh in range(2):
            bank = next_big()
            beng, bval = big_evac[bank]
            for kc in range(NCH):
                wts = []
                if kc == 0:
                    wts = [("tq", done[("xt", m)]), (beng, bval)]
                emit("pe", lambda t, nh=nh, kc=kc, seq=seq, bank=bank: t.matmul(
                    out=bank[:, 0:512], lhsT=xnfm[seq][:, 128 * kc:128 * (kc + 1)],
                    rhs=wq_all[:, 3072 * kc + 2048 + 512 * nh:3072 * kc + 2048 + 512 * (nh + 1)],
                    start=(kc == 0), stop=(kc == NCH - 1)), waits=wts)
            cpe = emit("pe", lambda t: None, inc=True)
            vdst = vb[seq].rearrange("p (c g f) -> p c g f", g=2, f=65)[:, 4 * nh:4 * (nh + 1), :, 0:64]
            vsrc = bank[:, 0:512].rearrange("p (c g f) -> p c g f", c=4, g=2)
            usrc = ubv[:, 512 * nh:512 * (nh + 1)].rearrange("p (c g f) -> p c g f", c=4, g=2)
            cev = emit("dve", lambda v, vdst=vdst, vsrc=vsrc, usrc=usrc: v.tensor_tensor(
                out=vdst, in0=vsrc, in1=usrc, op=ALU.add), inc=True,
                waits=[("pe", cpe), ("dve", dget(("usub", m - 2, SEGB - 1))),
                       ("pe", dget(("sdpa_done", m - 2))), ("dve", CK["ubv"] if m < 2 else 0)])
            big_evac[bank] = ("dve", cev)
        done[("v_evac", m)] = cev
        done[("qkv_done", m)] = now("pe")
        maybe_tap("qkfm", m, lambda: qkfm[seq][:], lambda: [("dve", done[("qk_evac", m)])])
        maybe_tap("vb", m, lambda: vb[seq][:], lambda: [("dve", done[("v_evac", m)])])

        return stop_after == "qkv"

    def emit_F2(m):
        return emit_F2a(m) or emit_F2b(m) or emit_F2c(m)

    def emit_F3(m):
        seq, blk = m % 2, m // 2
        c_expq = emit("act", lambda a, seq=seq: a.activation(out=eexp[:], in_=qkfm[seq][:, 0:1024],
                                                             func=AF.Exp), inc=True,
                      waits=[("dve", done[("qk_evac", m)]), ("dve", dget(("elu_stt", m - 1))),
                             ("pe", CK["arena_free_pe"]),
                             ("dve", dget(("mix_done", m - 1)))])
        emit("dve", lambda v: v.tensor_scalar(out=emin[:], in0=eexp[:], scalar1=1.0,
                                              scalar2=None, op0=ALU.min),
             waits=[("act", c_expq)])
        qdst = qkbd[seq].rearrange("p (c s g f) -> p c s g f", c=8, s=4, g=2)[:, :, :, 0, :]
        c_sttq = emit("dve", lambda v, seq=seq, qdst=qdst: v.scalar_tensor_tensor(
            out=qdst, in0=qkfm[seq][:, 0:1024].rearrange("p (c s f) -> p c s f", c=8, f=32),
            scalar=0.0, in1=emin.rearrange("p (c s f) -> p c s f", c=8, f=32),
            op0=ALU.max, op1=ALU.add), inc=True,
            waits=[("pe", dget(("phb_done", m - 2))),
                   ("sq", dget(("store", m - 2))),
                   ("tq", CK["wffn_tq"] if m < 2 else 0)])
        c_expk = emit("act", lambda a, seq=seq: a.activation(out=eexp[:],
                                                             in_=qkfm[seq][:, 1024:2048],
                                                             func=AF.Exp), inc=True,
                      waits=[("dve", c_sttq)])
        done[("elu_exp", m)] = c_expk
        c_min = c_expk
        emit("dve", lambda v: v.tensor_scalar(out=emin[:], in0=eexp[:], scalar1=1.0,
                                              scalar2=None, op0=ALU.min),
             waits=[("act", c_expk)])
        emit("dve", lambda v, seq=seq: v.scalar_tensor_tensor(
            out=skfm[seq][:], in0=qkfm[seq][:, 1024:2048], scalar=0.0,
            in1=emin[:], op0=ALU.max, op1=ALU.add),
            waits=[("pe", done[("qkv_done", m)])])
        kdst = qkbd[seq].rearrange("p (c s g f) -> p c s g f", c=8, s=4, g=2)[:, :, :, 1, :]
        c3 = emit("dve", lambda v, seq=seq, kdst=kdst: v.tensor_copy(
            out=kdst, in_=skfm[seq].rearrange("p (c s f) -> p c s f", c=8, f=32)), inc=True,
            waits=[("tq", CK["wffn_tq"] if m < 2 else 0)])
        done[("elu_stt", m)] = c3
        maybe_tap("qkbd", m, lambda: qkbd[seq][:], lambda: [("dve", c3)])
        done[("skt", m)] = emit("tq", lambda s, seq=seq: s.dma_start(
            out=skT[seq].rearrange("p (c f) -> p c f", c=8),
            in_=skfm[seq].rearrange("p (c f) -> p c f", c=8), transpose=True), inc=True,
            waits=[("dve", c3), ("pe", dget(("phb_done", m - 2)))])

        return stop_after == "elu"

    def emit_F4(m):
        seq, blk = m % 2, m // 2
        c_min = done[("elu_exp", m)]
        cosv = cosT[:, TOK * blk:TOK * (blk + 1)]
        sinv = sinT[:, TOK * blk:TOK * (blk + 1)]
        prev_rope_dve = 0
        for half in range(2):
            base = 1024 * half
            for cp in range(4):
                bank = next_big()
                beng, bval = big_evac[bank]
                for half2 in range(2):
                    c = 2 * cp + half2
                    cpe = emit("pe", lambda t, seq=seq, base=base, c=c, half2=half2, bank=bank: t.matmul(
                        out=bank[:, 128 * half2:128 * (half2 + 1)], lhsT=smat[:],
                        rhs=qkfm[seq][:, base + 128 * c:base + 128 * (c + 1)],
                        start=True, stop=True), inc=(half2 == 1),
                        waits=([("dve", done[("elu_stt", m)]), (beng, bval),
                                ("act", c_min), ("dve", CK["smat"])] if half2 == 0 else []))
                c2 = 2 * cp
                cosv2 = cosv.unsqueeze(1).broadcast_to([P, 2, TOK])
                sinv2 = sinv.unsqueeze(1).broadcast_to([P, 2, TOK])
                emit("dve", lambda v, bank=bank, sinv2=sinv2: v.tensor_tensor(
                    out=rop2[:].rearrange("p (a f) -> p a f", f=TOK) if False else rop2w,
                    in0=bank[:, 0:256].rearrange("p (a f) -> p a f", f=TOK),
                    in1=sinv2, op=ALU.mult), waits=[("pe", cpe)])
                emit("dve", lambda v, seq=seq, base=base, c2=c2, cosv2=cosv2: v.tensor_tensor(
                    out=rop1w,
                    in0=qkfm[seq][:, base + 128 * c2:base + 128 * (c2 + 2)].rearrange(
                        "p (a f) -> p a f", f=TOK),
                    in1=cosv2, op=ALU.mult))
                cev = emit("dve", lambda v, seq=seq, base=base, c2=c2: v.tensor_tensor(
                    out=qkfm[seq][:, base + 128 * c2:base + 128 * (c2 + 2)].rearrange(
                        "p (a f) -> p a f", f=TOK),
                    in0=rop2w, in1=rop1w, op=ALU.add), inc=True)
                big_evac[bank] = ("dve", cev)
                prev_rope_dve = cev
        done[("rope", m)] = prev_rope_dve
        maybe_tap("qkrope", m, lambda: qkfm[seq][:], lambda: [("dve", done[("rope", m)])])

        return stop_after == "rope"

    pa_state = {}

    def pa_lvl_of():
        return {"pa_scores": 1, "pa_exp": 2, "pa_mask": 3, "pa_sdpa": 4,
                "pa_recip": 5}.get(stop_after, 99)

    def emit_Ar(m, r):
        seq, blk = m % 2, m // 2
        pa_lvl = pa_lvl_of()
        probs = probs_sv[seq]
        pend_evac = pa_state.setdefault(m, [None])
        if True:
            hf = r % 2
            c0 = 4 * (r // 2)
            sbank = pC if r % 2 == 0 else pD
            obank = pE if r % 2 == 0 else pF
            rcX = rc1A if r % 2 == 0 else rc1B
            sbeng, sbval = big_evac[sbank]
            for ci in range(4):
                ch = c0 + ci
                wts = []
                if ci == 0:
                    wts = [("dve", done[("rope", m)]), (sbeng, sbval),
                           ("act", dget(("exp_done", m, r - 2), dget(("exp_done", m - 1, r + 2))))]
                cpe = emit("pe", lambda t, seq=seq, ch=ch, hf=hf, ci=ci, sbank=sbank: t.matmul(
                    out=sbank[:, 128 * ci:128 * (ci + 1)],
                    lhsT=qkfm[seq][64 * hf:64 * (hf + 1), 1024 + 128 * ch:1024 + 128 * (ch + 1)],
                    rhs=qkfm[seq][64 * hf:64 * (hf + 1), 128 * ch:128 * (ch + 1)],
                    start=True, stop=True, tile_position=(64 * hf, 0)),
                    inc=(ci == 3), waits=wts)
            done[("scorespe", m, r)] = cpe
            if pa_lvl < 2:
                return
            c_ex = emit("act", lambda a, r=r, sbank=sbank: a.activation(
                out=probs[r % 2][:], in_=sbank[:], func=AF.Exp, scale=0.125, bias=cm20[:]),
                inc=True, waits=[("pe", cpe),
                                 ("dve", dget(("msk_done", m, r - 2),
                                              dget(("msk_done", m - 1, r + 2)))),
                                 ("dve", dget(("ln3", m - 2))),
                                 ("dve", done[("elu_stt", m)] if r == 0 else 0),
                                 ("gps", CK["gps0"] if m < 2 else 0)])
            done[("exp_done", m, r)] = c_ex
            big_evac[sbank] = ("act", c_ex)
            if pa_lvl < 3:
                return
            c_mk = emit("dve", lambda v, r=r: v.tensor_tensor(
                out=probs[r % 2].rearrange("p (a f) -> p a f", f=128),
                in0=probs[r % 2].rearrange("p (a f) -> p a f", f=128),
                in1=mask01[:].unsqueeze(1).broadcast_to([P, 4, P]),
                op=ALU.mult), inc=True, waits=[("act", c_ex),
                                               ("gps", CK["gps0"] if m < 2 else 0)])
            done[("msk_done", m, r)] = c_mk
            if pa_lvl < 4:
                return
            obeng, obval = big_evac[obank]
            for ci in range(4):
                ch = c0 + ci
                wts = []
                if ci == 0:
                    wts = [("dve", c_mk), ("dve", done[("v_evac", m)]), (obeng, obval),
                           ("dve", dget(("nds_mult", m - 1, SEGB - 1))),
                           ("gps", CK["gps0"] if m < 2 else 0)]
                cpe = emit("pe", lambda t, seq=seq, ch=ch, hf=hf, ci=ci, r=r, obank=obank: t.matmul(
                    out=obank[:, 65 * ci:65 * (ci + 1)],
                    lhsT=probs[r % 2][:, 128 * ci:128 * (ci + 1)],
                    rhs=vb[seq][:, 130 * ch + 65 * hf:130 * ch + 65 * (hf + 1)],
                    start=True, stop=True), inc=(ci == 3), waits=wts)
            done[("sdpape", m, r)] = cpe
            if pa_lvl < 5:
                return
            emit("dve", lambda v, obank=obank, rcX=rcX: v.reciprocal(
                out=rcX[:], in_=obank[:, 0:260].rearrange("p (a b) -> p a b", b=65)[:, :, 64]),
                waits=[("pe", cpe), ("dve", CK["btile"] if m == 0 else 0)])
            if pend_evac[0] is None:
                ddrain()
            else:
                (pr, prcX, probank) = pend_evac[0]
                sdst = sdpa[seq].rearrange("p (c g b) -> p c g b", g=2, b=65)[
                    :, 4 * (pr // 2):4 * (pr // 2) + 4, pr % 2, :]
                c_se = emit("dve", lambda v, sdst=sdst, probank=probank, prcX=prcX: v.tensor_tensor(
                    out=sdst, in0=probank[:, 0:260].rearrange("p (a b) -> p a b", b=65),
                    in1=prcX[:].unsqueeze(2).broadcast_to([P, 4, 65]), op=ALU.mult),
                    inc=True, waits=[("dve", dget(("blend", m - 2, SEGB - 1)))])
                done[("sdpa_evac", m, pr)] = c_se
                big_evac[probank] = ("dve", c_se)
            ombv = omb.rearrange("p (c g) -> p c g", g=2)[:, c0:c0 + 4, hf]
            emit("dve", lambda v, ombv=ombv, rcX=rcX: v.tensor_tensor(
                out=rcX[:], in0=rcX[:], in1=ombv, op=ALU.mult))
            pend_evac[0] = (r, rcX, obank)

    def emit_Aflush(m):
        seq, blk = m % 2, m // 2
        pa_lvl = pa_lvl_of()
        pend_evac = pa_state.setdefault(m, [None])
        done[("sdpa_done", m)] = now("pe")
        done[("scores_done", m)] = now("pe")
        if pa_lvl >= 5 and pend_evac[0] is not None:
            (pr, prcX, probank) = pend_evac[0]
            ddrain()
            sdst = sdpa[seq].rearrange("p (c g b) -> p c g b", g=2, b=65)[
                :, 4 * (pr // 2):4 * (pr // 2) + 4, pr % 2, :]
            c_se = emit("dve", lambda v, sdst=sdst, probank=probank, prcX=prcX: v.tensor_tensor(
                out=sdst, in0=probank[:, 0:260].rearrange("p (a b) -> p a b", b=65),
                in1=prcX[:].unsqueeze(2).broadcast_to([P, 4, 65]), op=ALU.mult),
                inc=True, waits=[("dve", dget(("blend", m - 2, SEGB - 1)))])
            done[("sdpa_evac", m, pr)] = c_se
            big_evac[probank] = ("dve", c_se)
        if pa_lvl >= 99:
            maybe_tap("sdpa", m, lambda: sdpa[seq][:],
                      lambda: [("dve", done[("sdpa_evac", m, 3)])])

    def emit_As(m, s_):
        seq, blk = m % 2, m // 2
        if True:
            gseg = SEGB * blk + s_
            r0 = 32 * s_
            shdst = shadow[seq].rearrange("p (c f) -> p c f", f=65)[:, :, 0:64]
            if gseg > 0:
                c_sh = emit("act", lambda a, seq=seq, shdst=shdst: a.activation(
                    out=shdst, in_=memPS[seq].rearrange("p (c f) -> p c f", f=64),
                    func=AF.Copy), inc=True,
                    waits=[("pe", dget(("delta", m, s_ - 1), dget(("delta", m - 2, SEGB - 1))))])
            else:
                c_sh = now("act")
            done[("shadow", m, s_)] = c_sh
            emit("dve", lambda v, seq=seq, s_=s_: v.tensor_reduce(
                out=zred[:], in_=qkbd[seq].rearrange("p (c s g f) -> p c s g f",
                                                     c=8, s=4, g=2)[:, :, s_, 1, :],
                axis=mybir.AxisListType.X, op=ALU.add),
                waits=[("dve", done[("elu_stt", m)]),
                       ("gps", CK["gps0"] if m < 2 else 0)])
            for jj in range(8):
                bank = pE if jj < 4 else pF
                col = 65 * (jj % 4)
                for hf in range(2):
                    wts = []
                    if jj == 0 and hf == 0:
                        wts = [("dve", max(done[("elu_stt", m)],
                                           done[("sdpa_evac", m, 3)])),
                               ("act", c_sh),
                               ("dve", dget(("nds_mult", m, s_ - 1))),
                               ("gps", CK["gps0"] if m < 2 else 0)]
                    cpe = emit("pe", lambda t, seq=seq, s_=s_, jj=jj, hf=hf, bank=bank,
                               col=col: t.matmul(
                        out=bank[64 * hf:64 * (hf + 1), col:col + 65],
                        lhsT=qkbd[seq][64 * hf:64 * hf + 64,
                                       256 * jj + 64 * s_:256 * jj + 64 * (s_ + 1)],
                        rhs=shadow[seq][64 * hf:64 * hf + 64, 65 * jj:65 * (jj + 1)],
                        start=True, stop=True, tile_position=(64 * hf, 64 * hf)),
                        inc=(jj in (3, 7) and hf == 1), waits=wts)
            done[("numskm", m, s_)] = cpe
            emit("dve", lambda v: v.reciprocal(
                out=rc1A[:], in_=pE[:, 0:260].rearrange("p (a b) -> p a b", b=65)[:, :, 64]),
                waits=[("pe", done[("numskm", m, s_)]),
                       ("dve", dget(("usub", m, s_ - 1), dget(("usub", m - 1, SEGB - 1)))),
                       ("dve", dget(("blend", m, s_ - 1)))])
            emit("dve", lambda v: v.reciprocal(
                out=rc1B[:], in_=pF[:, 0:260].rearrange("p (a b) -> p a b", b=65)[:, :, 64]))
            emit("dve", lambda v: v.tensor_tensor(out=rc1A[:], in0=rc1A[:],
                                                  in1=btile[:, 0:4], op=ALU.mult))
            emit("dve", lambda v: v.tensor_tensor(out=rc1B[:], in0=rc1B[:],
                                                  in1=btile[:, 4:8], op=ALU.mult))
            emit("dve", lambda v, seq=seq: v.tensor_tensor(out=zf[seq][:], in0=zf[seq][:],
                                                           in1=zred[:], op=ALU.add))
            c_nds = 0
            for half in range(2):
                bank = pE if half == 0 else pF
                rcX = rc1A if half == 0 else rc1B
                for hf in range(2):
                    nrow = 64 * hf
                    prow = 32 + 64 * hf
                    oview = otok[seq][r0:r0 + 32, :].rearrange(
                        "p (c g f) -> p c g f", g=2, f=64)[:, 4 * half:4 * half + 4, hf, :]
                    uview = ut[r0:r0 + 32, :].rearrange(
                        "p (c g f) -> p c g f", g=2, f=64)[:, 4 * half:4 * half + 4, hf, :]
                    emit("dve", lambda v, bank=bank, nrow=nrow, oview=oview, rcX=rcX: v.tensor_tensor(
                        out=oview,
                        in0=bank[nrow:nrow + 32, 0:260].rearrange("p (a b) -> p a b", b=65)[:, :, 0:64],
                        in1=rcX[nrow:nrow + 32, :].unsqueeze(2).broadcast_to([32, 4, 64]),
                        op=ALU.mult))
                    c_nds = emit("dve", lambda v, bank=bank, prow=prow, uview=uview, rcX=rcX: v.tensor_tensor(
                        out=uview,
                        in0=bank[prow:prow + 32, 0:260].rearrange("p (a b) -> p a b", b=65)[:, :, 0:64],
                        in1=rcX[prow:prow + 32, :].unsqueeze(2).broadcast_to([32, 4, 64]),
                        op=ALU.mult), inc=(half == 1 and hf == 1))
                    if half == 0 and hf == 0:
                        emit("dve", lambda v, seq=seq: v.tensor_copy(
                            out=shadow[seq].rearrange("p (c f) -> p c f", f=65)[:, :, 64],
                            in_=zf[seq][:]))
            done[("nds_mult", m, s_)] = c_nds
            uin = ut[r0:r0 + 32, :].rearrange("p (c g f) -> p c g f", g=2, f=64)
            vsrc = vb[seq][r0:r0 + 32, :].rearrange("p (c g f) -> p c g f", g=2, f=65)[:, :, :, 0:64]
            cu = emit("dve", lambda v, vsrc=vsrc, uin=uin: v.tensor_tensor(
                out=uin, in0=vsrc, in1=uin, op=ALU.subtract), inc=True,
                waits=[("pe", dget(("delta", m, s_ - 1)))])
            done[("usub", m, s_)] = cu
            oview = otok[seq][r0:r0 + 32, :].rearrange("p (c g f) -> p c g f", g=2, f=64)
            sdv = sdpa[seq][r0:r0 + 32, :].rearrange("p (c g b) -> p c g b", g=2, b=65)[:, :, :, 0:64]
            cb = emit("dve", lambda v, oview=oview, sdv=sdv: v.tensor_tensor(
                out=oview, in0=oview, in1=sdv, op=ALU.add), inc=True)
            done[("blend", m, s_)] = cb
            first = (gseg == 0)
            for jj in range(8):
                for hf in range(2):
                    wts = []
                    if jj == 0 and hf == 0:
                        wts = [("dve", cu), ("act", c_sh), ("tq", done[("skt", m)]),
                               ("dve", CK["pszero"] if m < 2 else 0)]
                    cpe = emit("pe", lambda t, seq=seq, s_=s_, jj=jj, hf=hf, first=first: t.matmul(
                        out=memPS[seq][64 * hf:64 * (hf + 1), 64 * jj:64 * (jj + 1)],
                        lhsT=skT[seq][32 * s_:32 * (s_ + 1),
                                      128 * jj + 64 * hf:128 * jj + 64 * (hf + 1)],
                        rhs=ut[32 * s_:32 * (s_ + 1),
                               128 * jj + 64 * hf:128 * jj + 64 * (hf + 1)],
                        start=False, stop=False,
                        tile_position=(32 * s_, 64 * hf), skip_group_check=True),
                        inc=(jj == 7 and hf == 1), waits=wts)
            done[("delta", m, s_)] = cpe

    def emit_Atail(m):
        seq, blk = m % 2, m // 2
        done[("phb_done", m)] = now("pe")
        maybe_tap("otok", m, lambda: otok[seq][:],
                  lambda: [("dve", done[("blend", m, SEGB - 1)])])
        maybe_tap("shadow", m, lambda: shadow[seq][:],
                  lambda: [("act", done[("shadow", m, SEGB - 1)])])
        done[("ot", m)] = emit("tq", lambda s, seq=seq: s.dma_start(
            out=ofm.rearrange("p (c f) -> p c f", c=8),
            in_=otok[seq].rearrange("p (c f) -> p c f", c=8), transpose=True), inc=True,
            waits=[("dve", done[("blend", m, SEGB - 1)]), ("pe", dget(("wout_done", m - 1)))])
        return stop_after == "ot"

    def emit_scan(m):
        for r in range(4):
            emit_Ar(m, r)
        emit_Aflush(m)
        if pa_lvl_of() < 99 or stop_after == "phasea":
            return True
        for s_ in range(SEGB):
            emit_As(m, s_)
        if stop_after == "phaseb":
            return True
        return emit_Atail(m)

    def emit_B1(m):
        seq, blk = m % 2, m // 2
        for nh in range(2):
            bank = next_big()
            beng, bval = big_evac[bank]
            for ic in range(NCH):
                wts = []
                if ic == 0:
                    wts = [("tq", done[("ot", m)]), (beng, bval)]
                emit("pe", lambda t, nh=nh, ic=ic, bank=bank: t.matmul(
                    out=bank[:, 0:512], lhsT=ofm[:, 128 * ic:128 * (ic + 1)],
                    rhs=wo_all[:, 1024 * ic + 512 * nh:1024 * ic + 512 * (nh + 1)],
                    start=(ic == 0), stop=(ic == NCH - 1)), waits=wts)
            cpe = emit("pe", lambda t: None, inc=True)
            cev = emit("dve", lambda v, seq=seq, nh=nh, bank=bank: v.scalar_tensor_tensor(
                out=xs[seq][:, 512 * nh:512 * (nh + 1)], in0=bank[:, 0:512], scalar=1.0,
                in1=xs[seq][:, 512 * nh:512 * (nh + 1)], op0=ALU.mult, op1=ALU.add), inc=True,
                waits=[("pe", cpe)])
            big_evac[bank] = ("dve", cev)
        done[("wout_done", m)] = now("pe")
        done[("s2", m)] = cev
        maybe_tap("s2", m, lambda: xs[seq][:], lambda: [("dve", cev)])
        return False

    def emit_B1b(m):
        seq, blk = m % 2, m // 2
        c_s2 = done[("s2", m)]
        r1, sq_ = stc(seq, 1, 0), stc(seq, 1, 1)
        emit("act", lambda a, seq=seq, r1=r1: a.activation(
            out=xnfm[seq][:], in_=xs[seq][:], func=AF.Copy, accum_out=r1),
            waits=[("dve", c_s2), ("tq", done[("skt", m)])])
        c_sqa2 = emit("actd", lambda a, seq=seq, sq_=sq_: a.activation(
            out=xnfm[seq][:], in_=xs[seq][:], func=AF.Square, accum_out=sq_), inc=True)
        rstd_chain(1, seq, 3, xtr_waits=[("act", c_sqa2)])
        c_h = emit("dve", lambda v, seq=seq: v.tensor_scalar(
            out=htok[seq][:], in0=xs[seq][:], scalar1=r_nm[:, 0:1], scalar2=r_yy[:, 0:1],
            op0=ALU.add, op1=ALU.mult), inc=True, waits=[("dve", dget(("ln3", m - 2)))])
        c_hb = emit("act", lambda a, seq=seq: a.activation(
            out=hbf[seq][:], in_=htok[seq][:], func=AF.Copy), inc=True,
            waits=[("dve", c_h), ("tq", done[("ot", m)])])
        done[("chb", m)] = c_hb
        maybe_tap("htok", m, lambda: htok[seq][:], lambda: [("dve", c_h)])
        if m + 2 < nmt:
            load_x(m + 2, w=[("dve", c_h), ("act", c_sqa2)])

        return stop_after == "ln2"

    def emit_B2(m):
        seq, blk = m % 2, m // 2
        if blk > 0:
            c_halo = emit("dve", lambda v, seq=seq: v.tensor_copy(
                out=hh[seq][:], in_=hfm[seq].rearrange("p (c f) -> p c f", f=128)[:, :, 127]),
                inc=True, waits=[("dve", dget(("mix_done", m - 2)))])
        else:
            c_halo = now("dve")
        done[("ht", m)] = emit("tq", lambda s, seq=seq: s.dma_start(
            out=hfm[seq].rearrange("p (c f) -> p c f", c=8),
            in_=hbf[seq].rearrange("p (c f) -> p c f", c=8), transpose=True), inc=True,
            waits=[("act", done[("chb", m)]), ("dve", c_halo)])

        if stop_after == "ht":
            return True
        dtile = dtile_sv[seq]
        c_d2 = emit("dve", lambda v, seq=seq, dtile=dtile: v.tensor_tensor(
            out=dtile[:, 0:1024].rearrange("p (c f) -> p c f", f=128)[:, :, 0],
            in0=hfm[seq].rearrange("p (c f) -> p c f", f=128)[:, :, 0],
            in1=hh[seq][:], op=ALU.subtract), inc=True,
            waits=[("tq", done[("ht", m)]), ("dve", done[("blend", m, SEGB - 1)]),
                   ("gps", CK["gps0"] if m < 2 else 0)])
        emit("dve", lambda v, seq=seq, dtile=dtile: v.tensor_tensor(
            out=dtile[:, 0:1024].rearrange("p (c f) -> p c f", f=128)[:, :, 1:128],
            in0=hfm[seq].rearrange("p (c f) -> p c f", f=128)[:, :, 1:128],
            in1=hfm[seq].rearrange("p (c f) -> p c f", f=128)[:, :, 0:127], op=ALU.subtract))
        murb_v = murb[:].unsqueeze(2).broadcast_to([P, 8, 128])
        mukb_v = mukb[:].unsqueeze(2).broadcast_to([P, 8, 128])
        dv = dtile[:, 0:1024].rearrange("p (c f) -> p c f", f=128)
        emit("dve", lambda v, dv=dv, murb_v=murb_v: v.tensor_tensor(
            out=xr[0].rearrange("p (c f) -> p c f", f=128), in0=dv, in1=murb_v,
            op=ALU.mult),
            waits=[("dve", c_d2), ("dve", CK["wsc_free_dve"] if m < 2 else 0),
                   ("act", CK["wsc_free_act"] if m < 2 else 0),
                   ("pe", dget(("ffnmm_done", m - 1))),
                   ("act", dget(("tanh", m - 1, 1)))])
        emit("dve", lambda v, dv=dv, mukb_v=mukb_v: v.tensor_tensor(
            out=xk[0].rearrange("p (c f) -> p c f", f=128), in0=dv, in1=mukb_v,
            op=ALU.mult))
        emit("dve", lambda v, seq=seq: v.tensor_tensor(
            out=xr[0][:], in0=xr[0][:], in1=hfm[seq][:], op=ALU.add))
        cm = emit("dve", lambda v, seq=seq: v.tensor_tensor(
            out=xk[0][:], in0=xk[0][:], in1=hfm[seq][:], op=ALU.add), inc=True)
        done[("mix_done", m)] = cm
        maybe_tap("xr", m, lambda: xr[seq][:], lambda: [("dve", cm)])

        return stop_after == "mix"

    def emit_B3(m):
        seq, blk = m % 2, m // 2
        cm = done[("mix_done", m)]
        for nh in range(2):
            bank = next_big()
            beng, bval = big_evac[bank]
            for cc in range(NCH):
                wts = []
                if cc == 0:
                    wts = [("dve", cm), (beng, bval)]
                emit("pe", lambda t, seq=seq, nh=nh, cc=cc, bank=bank: t.matmul(
                    out=bank[:, 0:512], lhsT=xr[seq][:, 128 * cc:128 * (cc + 1)],
                    rhs=wrT[:, 1024 * cc + 512 * nh:1024 * cc + 512 * (nh + 1)],
                    start=(cc == 0), stop=(cc == NCH - 1)), waits=wts)
            cpe = emit("pe", lambda t: None, inc=True)
            ca = emit("act", lambda a, nh=nh, bank=bank: a.activation(
                out=tanh_t[:, 512 * nh:512 * (nh + 1)], in_=bank[:, 0:512], func=AF.Tanh,
                scale=0.5), inc=True,
                waits=[("pe", cpe), ("dve", dget(("ffno", m - 1)))])
            done[("tanh", m, nh)] = ca
            big_evac[bank] = ("act", ca)
        return False

    def emit_B4(m):
        seq, blk = m % 2, m // 2
        cm = done[("mix_done", m)]
        for nh in range(2):
            bank = next_big()
            beng, bval = big_evac[bank]
            for cc in range(NCH):
                wts = []
                if cc == 0:
                    wts = [("dve", cm), (beng, bval)]
                emit("pe", lambda t, seq=seq, nh=nh, cc=cc, bank=bank: t.matmul(
                    out=bank[:, 0:512], lhsT=xk[seq][:, 128 * cc:128 * (cc + 1)],
                    rhs=wkT[:, 1024 * cc + 512 * nh:1024 * cc + 512 * (nh + 1)],
                    start=(cc == 0), stop=(cc == NCH - 1)), waits=wts)
            cpe = emit("pe", lambda t: None, inc=True)
            emit("dve", lambda v, seq=seq, bank=bank: v.tensor_scalar(
                out=krelu_of[seq][:], in0=bank[:, 0:512], scalar1=0.0, scalar2=None,
                op0=ALU.max), waits=[("pe", cpe), ("pe", dget(("kv_done", m - 2))),
                                     ("tq", dget(("kt", m - 1))),
                                     ("act", dget(("ln3sq", m - 2))),
                                     ("dve", CK["wsc_free_dve"] if m < 2 else 0),
                                     ("act", CK["wsc_free_act"] if m < 2 else 0)])
            cev = emit("dve", lambda v, seq=seq, nh=nh: v.tensor_tensor(
                out=kact[seq][:, 512 * nh:512 * (nh + 1)], in0=krelu_of[seq][:],
                in1=krelu_of[seq][:], op=ALU.mult), inc=True)
            big_evac[bank] = ("dve", cev)
        c_ka = cev
        done[("kact", m)] = c_ka
        maybe_tap("kact", m, lambda: kact[seq][:], lambda: [("dve", c_ka)])
        done[("kt", m)] = emit("tq", lambda s, seq=seq: s.dma_start(
            out=kafm[seq].rearrange("p (c f) -> p c f", c=8),
            in_=kact[seq].rearrange("p (c f) -> p c f", c=8), transpose=True), inc=True,
            waits=[("dve", c_ka)])

        return stop_after == "kact"

    def emit_B5(m):
        seq, blk = m % 2, m // 2
        for nh in range(2):
            bank = next_big()
            beng, bval = big_evac[bank]
            for cc in range(NCH):
                wts = []
                if cc == 0:
                    wts = [("tq", done[("kt", m)]), (beng, bval)]
                emit("pe", lambda t, seq=seq, nh=nh, cc=cc, bank=bank: t.matmul(
                    out=bank[:, 0:512], lhsT=kafm[seq][:, 128 * cc:128 * (cc + 1)],
                    rhs=wvT[:, 1024 * cc + 512 * nh:1024 * cc + 512 * (nh + 1)],
                    start=(cc == 0), stop=(cc == NCH - 1)), waits=wts)
            cpe = emit("pe", lambda t: None, inc=True)
            emit("dve", lambda v, nh=nh, bank=bank: v.scalar_tensor_tensor(
                out=tanh_t[:, 512 * nh:512 * (nh + 1)], in0=tanh_t[:, 512 * nh:512 * (nh + 1)],
                scalar=1.0, in1=bank[:, 0:512], op0=ALU.add, op1=ALU.mult),
                waits=[("pe", cpe), ("act", done[("tanh", m, nh)])])
            cev = emit("dve", lambda v, seq=seq, nh=nh: v.tensor_tensor(
                out=htok[seq][:, 512 * nh:512 * (nh + 1)],
                in0=htok[seq][:, 512 * nh:512 * (nh + 1)],
                in1=tanh_t[:, 512 * nh:512 * (nh + 1)], op=ALU.add), inc=True,
                waits=[("tq", done[("ht", m)])])
            big_evac[bank] = ("dve", cev)
        done[("ffnmm_done", m)] = now("pe")
        done[("kv_done", m)] = now("pe")
        done[("ffno", m)] = cev
        c_s3 = cev
        maybe_tap("s3", m, lambda: htok[seq][:], lambda: [("dve", c_s3)])
        r1, sq_ = stc(seq, 2, 0), stc(seq, 2, 1)
        emit("act", lambda a, seq=seq, r1=r1: a.activation(
            out=kafm[seq][:], in_=htok[seq][:], func=AF.Copy, accum_out=r1),
            waits=[("dve", c_s3), ("pe", done[("kv_done", m)])])
        c_sqa3 = emit("actd", lambda a, seq=seq, sq_=sq_: a.activation(
            out=kafm[seq][:], in_=htok[seq][:], func=AF.Square, accum_out=sq_), inc=True)
        done[("ln3sq", m)] = c_sqa3
        rstd_chain(2, seq, 3, xtr_waits=[("act", c_sqa3)])
        c_o = emit("dve", lambda v, seq=seq: v.tensor_scalar(
            out=h3[seq][:], in0=htok[seq][:], scalar1=r_nm[:, 0:1], scalar2=r_yy[:, 0:1],
            op0=ALU.add, op1=ALU.mult), inc=True,
            waits=[("sq", dget(("store", m - 2))), ("pe", done[("phb_done", m)])])
        done[("ln3", m)] = c_o
        maybe_tap("h3", m, lambda: h3[seq][:], lambda: [("dve", c_o)])
        done[("store", m)] = emit("sq", lambda s, seq=seq, blk=blk: s.dma_start(
            out=out_h[seq, TOK * blk:TOK * (blk + 1), :], in_=h3[seq][:]), inc=True,
            waits=[("dve", c_o)])
        return False

    if stop_after is None:
        def emit_front_all(m):
            emit_F1(m); emit_F2a(m); emit_F2b(m); emit_F2c(m); emit_F3(m); emit_F4(m)

        if nmt >= 1:
            emit_front_all(0)
        for it in range(nmt + 1):
            b = it - 1
            f = it + 1
            Bv = 0 <= b < nmt
            Fv = f < nmt
            if it < nmt:
                emit_Ar(it, 0)
                if Bv:
                    emit_B1(b)
                emit_Ar(it, 1)
                emit_Ar(it, 2)
                emit_Ar(it, 3)
                emit_Aflush(it)
                emit_As(it, 0)
                if Bv:
                    emit_B1b(b)
                    emit_B2(b)
                emit_As(it, 1)
                if Fv:
                    emit_F1(f)
                if Bv:
                    emit_B3(b)
                if Fv:
                    emit_F2a(f)
                emit_As(it, 2)
                if Bv:
                    emit_B4(b)
                if Fv:
                    emit_F2b(f)
                if Bv:
                    emit_B5(b)
                emit_As(it, 3)
                if Fv:
                    emit_F2c(f)
                emit_Atail(it)
                if Fv:
                    emit_F3(f)
                    emit_F4(f)
            else:
                if Bv:
                    emit_B1(b); emit_B1b(b); emit_B2(b); emit_B3(b)
                    emit_B4(b); emit_B5(b)
    else:
        for m in range(nmt):
            if (emit_F1(m) or emit_F2(m) or emit_F3(m) or emit_F4(m)
                    or emit_scan(m) or emit_B1(m) or emit_B1b(m) or emit_B2(m)
                    or emit_B3(m) or emit_B4(m) or emit_B5(m)):
                break

    total_sq = cnt["sq"]
    emit("sq", lambda s: None, waits=[("sq", total_sq)])

    with nc.Block() as block:
        block.tensor(_mk_runner(ops["pe"]))
        block.vector(_mk_runner(ops["dve"]))
        block.scalar(_mk_runner(ops["scalar"]))
        block.gpsimd(_mk_runner(ops["gps"]))
        block.sync(_mk_runner(ops["sync"]))

    ctx.close()
    return nc, tap_h


def _mk_runner(stream):
    def _run(eng):
        last_instr = [None]
        for (wl, fn, sem, step, logical) in stream:
            for (s, v) in wl:
                eng.wait_ge(s, v)
            instr = fn(eng)
            if instr is not None:
                last_instr[0] = instr
            if sem is not None:
                if logical == "actd":
                    eng.drain().then_inc(sem, step)
                else:
                    tgt = instr if instr is not None else last_instr[0]
                    tgt.then_inc(sem, step)
    return _run


_NC_CACHE = {}


def kernel(L0_attn_x, L0_attn_w_qkv, L0_attn_w_out, L0_attn_ln_g, L0_attn_ln_b,
           L0_attn_mem_beta, L0_ffn_W_r, L0_ffn_W_k, L0_ffn_W_v, L0_ffn_mu_r, L0_ffn_mu_k):
    x = np.ascontiguousarray(np.asarray(L0_attn_x, dtype=np.float32))
    if "nc" not in _NC_CACHE:
        _NC_CACHE["nc"] = build_nc()
    nc, _ = _NC_CACHE["nc"]
    shared = {
        "w_qkv": np.ascontiguousarray(np.asarray(L0_attn_w_qkv, np.float32)),
        "w_out": np.ascontiguousarray(np.asarray(L0_attn_w_out, np.float32)),
        "ln_g": np.ascontiguousarray(np.asarray(L0_attn_ln_g, np.float32)),
        "ln_b": np.ascontiguousarray(np.asarray(L0_attn_ln_b, np.float32)),
        "mem_beta": np.ascontiguousarray(np.asarray(L0_attn_mem_beta, np.float32).reshape(1, H)),
        "W_r": np.ascontiguousarray(np.asarray(L0_ffn_W_r, np.float32)),
        "W_k": np.ascontiguousarray(np.asarray(L0_ffn_W_k, np.float32)),
        "W_v": np.ascontiguousarray(np.asarray(L0_ffn_W_v, np.float32)),
        "mu_r": np.ascontiguousarray(np.asarray(L0_ffn_mu_r, np.float32)),
        "mu_k": np.ascontiguousarray(np.asarray(L0_ffn_mu_k, np.float32)),
    }
    in_maps = []
    for core in range(8):
        im = dict(shared)
        im["x"] = np.ascontiguousarray(x[2 * core:2 * core + 2])
        in_maps.append(im)
    res = run_bass_kernel_spmd(nc, in_maps, core_ids=list(range(8)))
    out = np.concatenate([r["out"] for r in res.results], axis=0)
    return np.ascontiguousarray(out.reshape(16, TSEQ, D).astype(np.float32))


# version 11
# speedup vs baseline: 1.2352x; 1.0654x over previous
import math
import numpy as np
import concourse.bass as bass
import concourse.mybir as mybir
from concourse.bass_utils import run_bass_kernel_spmd
from contextlib import ExitStack

F32 = mybir.dt.float32
BF16 = mybir.dt.bfloat16
I32 = mybir.dt.int32
AF = mybir.ActivationFunctionType
ALU = mybir.AluOpType

P = 128
D = 1024
NCH = 8
H = 16
SEG = 32
TOK = 128
NSEQ = 2
TSEQ = 2048
NBLK = TSEQ // TOK
NMT = NSEQ * NBLK
SEGB = TOK // SEG
EXPB = 20.0
LOG1E4 = math.log(10000.0)
TWO_PI = 2.0 * math.pi


def build_nc(taps=(), nmt=NMT, stop_after=None):
    nc = bass.Bass("TRN2", target_bir_lowering=False)
    x_h = nc.dram_tensor("x", [NSEQ, TSEQ, D], F32, kind="ExternalInput").ap()
    wqkv_h = nc.dram_tensor("w_qkv", [D, 3 * D], F32, kind="ExternalInput").ap()
    wout_h = nc.dram_tensor("w_out", [D, D], F32, kind="ExternalInput").ap()
    g_h = nc.dram_tensor("ln_g", [D], F32, kind="ExternalInput").ap()
    b_h = nc.dram_tensor("ln_b", [D], F32, kind="ExternalInput").ap()
    beta_h = nc.dram_tensor("mem_beta", [1, H], F32, kind="ExternalInput").ap()
    wr_h = nc.dram_tensor("W_r", [D, D], F32, kind="ExternalInput").ap()
    wk_h = nc.dram_tensor("W_k", [D, D], F32, kind="ExternalInput").ap()
    wv_h = nc.dram_tensor("W_v", [D, D], F32, kind="ExternalInput").ap()
    mur_h = nc.dram_tensor("mu_r", [D], F32, kind="ExternalInput").ap()
    muk_h = nc.dram_tensor("mu_k", [D], F32, kind="ExternalInput").ap()
    out_h = nc.dram_tensor("out", [NSEQ, TSEQ, D], F32, kind="ExternalOutput").ap()

    ctx = ExitStack()
    e = ctx.enter_context

    wq_all = e(nc.sbuf_tensor("wq_all", [P, NCH * 3072], BF16))
    wo_all = e(nc.sbuf_tensor("wo_all", [P, NCH * 1024], BF16))
    wrT = e(nc.sbuf_tensor("wrT", [P, NCH * 1024], BF16))
    wkT = e(nc.sbuf_tensor("wkT", [P, NCH * 1024], BF16))
    wvT = e(nc.sbuf_tensor("wvT", [P, NCH * 1024], BF16))
    cosT = e(nc.sbuf_tensor("cosT", [P, TSEQ], BF16))
    sinT = e(nc.sbuf_tensor("sinT", [P, TSEQ], BF16))
    mask01 = e(nc.sbuf_tensor("mask01", [P, P], BF16))
    smat = e(nc.sbuf_tensor("smat", [P, P], BF16))
    gcol = e(nc.sbuf_tensor("gcol", [P, 8], F32))
    bcol = e(nc.sbuf_tensor("bcol", [P, 8], BF16))
    murm1 = e(nc.sbuf_tensor("murm1", [P, 8], F32))
    mukm1 = e(nc.sbuf_tensor("mukm1", [P, 8], F32))
    murb = e(nc.sbuf_tensor("murb", [P, 8], BF16))
    mukb = e(nc.sbuf_tensor("mukb", [P, 8], BF16))
    ubqk = e(nc.sbuf_tensor("ubqk", [P, 16], F32))
    ubv = e(nc.sbuf_tensor("ubv", [P, 1024], BF16))
    btr = e(nc.sbuf_tensor("btr", [1, 16], F32))
    betab = e(nc.sbuf_tensor("betab", [1, 16], BF16))
    ombb = e(nc.sbuf_tensor("ombb", [1, 16], BF16))
    btile = e(nc.sbuf_tensor("btile", [P, 8], F32))
    omb = e(nc.sbuf_tensor("omb", [P, 16], F32))
    epst = e(nc.sbuf_tensor("epst", [P, 1], F32))
    cm20 = e(nc.sbuf_tensor("cm20", [P, 1], F32))
    ones1 = e(nc.sbuf_tensor("ones1", [1, P], BF16))
    pidx = e(nc.sbuf_tensor("pidx", [P, 1], F32))
    cidx = None
    par = e(nc.sbuf_tensor("par", [P, 2], F32))
    freq64 = e(nc.sbuf_tensor("freq64", [P, 1], F32))
    sti = e(nc.sbuf_tensor("sti", [P, 1], I32))
    gtmp = e(nc.sbuf_tensor("gtmp", [P, 8], F32))

    def perseq(name, cols, dt):
        return [e(nc.sbuf_tensor(f"{name}{s}", [P, cols], dt)) for s in range(NSEQ)]

    xs = perseq("xs", D, F32)
    xhat = perseq("xhat", D, BF16)
    xnfm = perseq("xnfm", D, BF16)
    qkfm = perseq("qkfm", 2 * D, BF16)
    qkbd = perseq("qkbd", 2 * D, BF16)
    skT = perseq("skT", D, BF16)
    vb = perseq("vb", 8 * 130, BF16)
    sdpa = perseq("sdpa", 16 * 65, BF16)
    shadow = perseq("shadow", 8 * 65, BF16)
    zf = perseq("zf", 8, F32)
    hh = perseq("hh", 8, BF16)
    otok = None
    htok = perseq("htok", D, F32)
    hbf = xhat
    otok = xhat
    hfm = perseq("hfm", D, BF16)
    skfm = xnfm
    kafm = perseq("kafm", D, BF16)
    h3 = None
    ofm = e(nc.sbuf_tensor("ofm", [P, D], BF16))
    ut = e(nc.sbuf_tensor("ut", [P, D], BF16))
    h3 = [qkbd[s_].bitcast(F32)[:, 0:1024] for s_ in range(NSEQ)]
    dtile_sv = [sdpa[s_][:, 0:1024] for s_ in range(NSEQ)]
    tanh_t = e(nc.sbuf_tensor("tanh_t", [P, D], F32))
    probs_sv = [[htok[s_].bitcast(BF16)[:, 0:512], htok[s_].bitcast(BF16)[:, 512:1024]]
                for s_ in range(NSEQ)]
    rop1 = e(nc.sbuf_tensor("rop1", [P, 2 * TOK], F32))
    rop2 = e(nc.sbuf_tensor("rop2", [P, 2 * TOK], F32))
    rop1w = rop1[:].rearrange("p (a f) -> p a f", f=TOK)
    rop2w = rop2[:].rearrange("p (a f) -> p a f", f=TOK)

    wsc = e(nc.sbuf_tensor("wsc", [P, 3072], BF16))
    wscf = wsc.bitcast(F32)
    xr_t = wsc[:, 0:1024]
    xk_t = wsc[:, 1024:2048]
    kact_t = wsc[:, 2048:3072]
    xr = [xr_t, xr_t]
    xk = [xk_t, xk_t]
    kact = [kact_t, kact_t]

    arena = e(nc.sbuf_tensor("arena", [P, 2048], BF16))
    emin = arena[:, 0:1024]
    eexp = arena[:, 1024:2048]
    lnsq = emin
    wraw = arena[:, 0:1536]
    ffsc = arena.bitcast(F32)[:, 0:512]
    krelu_of = [kafm[s_].bitcast(F32)[:, 0:512] for s_ in range(NSEQ)]

    ubvrow = ut[0:1, 0:1024]
    cidx = ut.bitcast(F32)[:, 0:128]
    ST = e(nc.sbuf_tensor("ST", [P, 9 * 6], F32))

    def stc(seq, lnid, j):
        c = 9 * (lnid + 3 * seq) + j
        return ST[:, c:c + 1]

    REG = e(nc.sbuf_tensor("REG", [P, 5 * 64], F32))
    W64 = 64
    r_nm, r_tt, r_yy, r_aa, r_hx = [REG[:, 64 * j:64 * j + 64] for j in range(5)]
    rc1A = e(nc.sbuf_tensor("rc1A", [P, 4], F32))
    rc1B = e(nc.sbuf_tensor("rc1B", [P, 4], F32))
    rc1 = rc1A
    zred = e(nc.sbuf_tensor("zred", [P, 8], F32))

    pA = e(nc.psum_tensor("pA", [P, 512], F32))
    pB = e(nc.psum_tensor("pB", [P, 512], F32))
    pC = e(nc.psum_tensor("pC", [P, 512], F32))
    pD = e(nc.psum_tensor("pD", [P, 512], F32))
    pE = e(nc.psum_tensor("pE", [P, 512], F32))
    pF = e(nc.psum_tensor("pF", [P, 512], F32))
    memPS = [e(nc.psum_tensor(f"memPS{s}", [P, 512], F32)) for s in range(NSEQ)]

    LOG2PHYS = {"pe": "pe", "dve": "dve", "act": "scalar", "tq": "sync",
                "gps": "gps", "gq": "gps", "sq": "sync", "actd": "scalar"}
    PHYS = ["pe", "dve", "scalar", "gps", "sync"]
    sems = {k: e(nc.semaphore(f"s_{k}")) for k in ("pe", "dve", "act", "gps", "sq", "tq", "gq")}
    sems["actd"] = sems["act"]
    cnt = {k: 0 for k in sems}
    cnt.pop("actd"); cnt["actd"] = None
    waitmark = {ph: {k: 0 for k in sems} for ph in PHYS}
    ops = {ph: [] for ph in PHYS}
    tap_bar = {"v": 0}

    def emit(logical, fn, inc=False, waits=()):
        ph = LOG2PHYS[logical]
        waits = list(waits)
        if tap_bar["v"]:
            waits.append(("sq", tap_bar["v"]))
        wl = []
        for (pr, val) in waits:
            if val and val > waitmark[ph][pr]:
                waitmark[ph][pr] = val
                wl.append((sems[pr], val))
        step = 16 if logical in ("sq", "tq", "gq") else 1
        cl = "act" if logical == "actd" else logical
        sem = None
        if inc:
            cnt[cl] += step
            sem = sems[cl]
        ops[ph].append((wl, fn, sem, step, logical))
        return cnt[cl]

    def now(l):
        return cnt[l]


    def ddrain():
        emit("dve", lambda v: v.drain())

    tap_h = {}

    def tap(name, view, waits):
        th = nc.dram_tensor(f"tap_{name}", list(view.shape), view.dtype,
                            kind="ExternalOutput").ap()
        tap_h[f"tap_{name}"] = str(view.dtype)
        emit("sq", lambda s, th=th, view=view: s.dma_start(out=th, in_=view),
             inc=True, waits=waits)
        tap_bar["v"] = cnt["sq"]

    def maybe_tap(stage, m, view_fn, waits_fn):
        for (tname, tmt) in taps:
            if tname == stage and tmt == m:
                tap(f"{stage}_{m}", view_fn(), waits_fn())

    CK = {}
    xload = {}
    done = {}

    def load_x(m, w=()):
        seq, blk = m % 2, m // 2
        xload[m] = emit("sq", lambda s, seq=seq, blk=blk: s.dma_start(
            out=xs[seq][:], in_=x_h[seq, TOK * blk:TOK * (blk + 1), :]), inc=True,
            waits=list(w))

    load_x(0)
    load_x(1)

    def vec_to_cols(vh, dest, first_waits=()):
        mk = 0
        for c in range(8):
            src = vh[128 * c:128 * c + 128].rearrange("(a b) -> a b", b=1)
            mk = emit("sq", lambda s, d=dest[:, c:c + 1], src=src: s.dma_start(out=d, in_=src),
                      inc=True, waits=(list(first_waits) if c == 0 else []))
        return mk

    CK["g"] = vec_to_cols(g_h, gcol)
    CK["b"] = vec_to_cols(b_h, gtmp)
    CK["bc"] = emit("dve", lambda v: v.tensor_copy(out=bcol[:], in_=gtmp[:]), inc=True,
                    waits=[("sq", CK["b"])])
    CK["mur"] = vec_to_cols(mur_h, gtmp, first_waits=[("dve", CK["bc"])])
    emit("dve", lambda v: v.tensor_scalar(out=murm1[:], in0=gtmp[:], scalar1=1.0,
                                          scalar2=-1.0, op0=ALU.mult, op1=ALU.add),
         waits=[("sq", CK["mur"])])
    ddrain()
    CK["m1"] = emit("dve", lambda v: v.tensor_copy(out=murb[:], in_=murm1[:]), inc=True)
    CK["muk"] = vec_to_cols(muk_h, gtmp, first_waits=[("dve", CK["m1"])])
    emit("dve", lambda v: v.tensor_scalar(out=mukm1[:], in0=gtmp[:], scalar1=1.0,
                                          scalar2=-1.0, op0=ALU.mult, op1=ALU.add),
         waits=[("sq", CK["muk"])])
    ddrain()
    CK["m2"] = emit("dve", lambda v: v.tensor_copy(out=mukb[:], in_=mukm1[:]), inc=True)
    CK["beta"] = emit("sq", lambda s: s.dma_start(out=btr[:], in_=beta_h), inc=True)

    emit("gps", lambda g: g.iota(pidx[:], pattern=[[0, 1]], channel_multiplier=1,
                                 allow_small_or_imprecise_dtypes=True))
    emit("gps", lambda g: g.iota(cidx[:], pattern=[[1, P]], channel_multiplier=0,
                                 allow_small_or_imprecise_dtypes=True))
    emit("gps", lambda g: g.memset(mask01[:], 1.0))
    mview = mask01.rearrange("p (a b) -> p a b", a=4)
    emit("gps", lambda g: g.affine_select(out=mview, in_=mview, pattern=[[-32, 4], [0, 32]],
                                          compare_op=ALU.is_ge, fill=0.0, base=0,
                                          channel_multiplier=1))
    emit("gps", lambda g: g.affine_select(out=mview, in_=mview, pattern=[[32, 4], [0, 32]],
                                          compare_op=ALU.is_ge, fill=0.0, base=31,
                                          channel_multiplier=-1))
    emit("gps", lambda g: g.affine_select(out=mview, in_=mview, pattern=[[32, 4], [1, 32]],
                                          compare_op=ALU.is_ge, fill=0.0, base=0,
                                          channel_multiplier=-1))
    emit("gps", lambda g: g.memset(ones1[:], 1.0))
    for s_ in range(NSEQ):
        vbv = vb[s_].rearrange("p (c f) -> p c f", f=130)
        emit("gps", lambda g, vbv=vbv: g.memset(vbv[:, :, 64:65], 1.0))
        emit("gps", lambda g, vbv=vbv: g.memset(vbv[:, :, 129:130], 1.0))
        emit("gps", lambda g, s_=s_: g.memset(hh[s_][:], 0.0))
        emit("gps", lambda g, s_=s_: g.memset(zf[s_][:], 1e-6))
        shv = shadow[s_].rearrange("p (c f) -> p c f", f=65)
        emit("gps", lambda g, s_=s_: g.memset(shadow[s_][:], 0.0))
        emit("gps", lambda g, shv=shv: g.memset(shv[:, :, 64:65], 1e-6))
    emit("gps", lambda g: g.memset(cm20[:], -EXPB))
    emit("gps", lambda g: g.memset(epst[:], 0.0))
    emit("gps", lambda g: g.memset(epst[0:32, :], 1e-6))
    CK["gps0"] = emit("gps", lambda g: g.memset(epst[64:96, :], 1e-6), inc=True)
    emit("dve", lambda v: v.memset(memPS[0][:], 0.0))
    CK["pszero"] = emit("dve", lambda v: v.memset(memPS[1][:], 0.0), inc=True)

    st0 = stc(0, 0, 6)
    emit("dve", lambda v: v.tensor_scalar(out=st0, in0=pidx[:], scalar1=0.5, scalar2=-0.25,
                                          op0=ALU.mult, op1=ALU.add),
         waits=[("gps", CK["gps0"])])
    ddrain()
    emit("dve", lambda v: v.tensor_copy(out=sti[:], in_=st0))
    ddrain()
    CK["flr"] = emit("dve", lambda v: v.tensor_copy(out=st0, in_=sti[:]), inc=True)
    ddrain()
    CK["freq"] = emit("act", lambda a: a.activation(out=freq64[:], in_=st0, func=AF.Exp,
                                                    scale=-LOG1E4 / 32.0), inc=True,
                      waits=[("dve", CK["flr"])])
    emit("dve", lambda v: v.tensor_scalar(out=par[:, 0:1], in0=st0, scalar1=-2.0, scalar2=None,
                                          op0=ALU.mult))
    ddrain()
    emit("dve", lambda v: v.tensor_tensor(out=par[:, 0:1], in0=par[:, 0:1], in1=pidx[:],
                                          op=ALU.add))
    ddrain()
    emit("dve", lambda v: v.tensor_scalar(out=par[:, 1:2], in0=par[:, 0:1], scalar1=-1.0,
                                          scalar2=1.0, op0=ALU.mult, op1=ALU.add))
    ddrain()
    emit("dve", lambda v: v.tensor_scalar(out=cidx[:], in0=cidx[:], scalar1=pidx[:], scalar2=None,
                                          op0=ALU.subtract))
    emit("dve", lambda v: v.tensor_scalar(out=smat[:], in0=cidx[:], scalar1=1.0, scalar2=None,
                                          op0=ALU.is_equal))
    emit("dve", lambda v: v.tensor_scalar(out=smat[:], in0=smat[:], scalar1=par[:, 1:2],
                                          scalar2=None, op0=ALU.mult))
    emit("dve", lambda v: v.tensor_scalar(out=lnsq[:, 0:128], in0=cidx[:], scalar1=-1.0,
                                          scalar2=None, op0=ALU.is_equal))
    emit("dve", lambda v: v.tensor_scalar(out=lnsq[:, 0:128], in0=lnsq[:, 0:128],
                                          scalar1=par[:, 0:1], scalar2=None, op0=ALU.mult))
    CK["smat"] = emit("dve", lambda v: v.tensor_tensor(out=smat[:], in0=smat[:],
                                                       in1=lnsq[:, 0:128], op=ALU.subtract),
                      inc=True)

    wq_f = wq_all.bitcast(F32)
    pos64 = wq_f[0:64, 0:2048]
    ang64 = wq_f[0:64, 2048:4096]
    u_s = wq_f[0:64, 4096:6144]
    n_f = wq_f[0:64, 6144:8192]
    n_i = wq_f[0:64, 8192:10240].bitcast(I32)
    CK["pos"] = emit("gps", lambda g: g.iota(pos64, pattern=[[1, 2048]], channel_multiplier=0,
                                             allow_small_or_imprecise_dtypes=True), inc=True)
    emit("dve", lambda v: v.tensor_scalar(out=ang64, in0=pos64, scalar1=freq64[0:64, 0:1],
                                          scalar2=None, op0=ALU.mult),
         waits=[("gps", CK["pos"]), ("act", CK["freq"])])

    def table_build(phase, dest):
        emit("dve", lambda v: v.tensor_scalar(out=u_s, in0=ang64, scalar1=1.0 / TWO_PI,
                                              scalar2=phase, op0=ALU.mult, op1=ALU.add))
        emit("dve", lambda v: v.tensor_copy(out=n_i, in_=u_s))
        emit("dve", lambda v: v.tensor_copy(out=n_f, in_=n_i))
        cc = emit("dve", lambda v: v.tensor_tensor(out=u_s, in0=u_s, in1=n_f, op=ALU.subtract),
                  inc=True)
        ca = emit("act", lambda a: a.activation(out=dest[0:64, :], in_=u_s, func=AF.Sin,
                                                scale=TWO_PI), inc=True, waits=[("dve", cc)])
        emit("dve", lambda v: v.tensor_copy(out=dest[64:128, :], in_=dest[0:64, :]),
             waits=[("act", ca)])

    table_build(0.25, cosT)
    table_build(0.0, sinT)
    CK["tables"] = emit("dve", lambda v: None, inc=True)

    ca = emit("act", lambda a: a.activation(out=btr[:], in_=btr[:], func=AF.Tanh, scale=0.5),
              inc=True, waits=[("sq", CK["beta"])])
    emit("dve", lambda v: v.tensor_scalar(out=betab[:], in0=btr[:], scalar1=0.5, scalar2=0.5,
                                          op0=ALU.mult, op1=ALU.add), waits=[("act", ca)])
    CK["betab"] = emit("dve", lambda v: v.tensor_scalar(out=ombb[:], in0=btr[:], scalar1=-0.5,
                                                        scalar2=0.5, op0=ALU.mult, op1=ALU.add),
                       inc=True)
    emit("pe", lambda t: t.matmul(out=pA[:, 0:16], lhsT=ones1[:], rhs=ombb[:],
                                  start=True, stop=True),
         waits=[("dve", CK["betab"]), ("gps", CK["gps0"])])
    CK["pe_beta"] = emit("pe", lambda t: t.matmul(out=pB[0:32, 0:16], lhsT=ones1[:, 0:32],
                                                  rhs=betab[:], start=True, stop=True), inc=True)
    emit("dve", lambda v: v.tensor_copy(out=omb[:], in_=pA[:, 0:16]),
         waits=[("pe", CK["pe_beta"])])
    emit("dve", lambda v: v.memset(btile[:], 1.0))
    ddrain()
    emit("dve", lambda v: v.tensor_copy(out=btile[0:32, :], in_=pB[0:32, 0:16:2]))
    CK["btile"] = emit("dve", lambda v: v.tensor_copy(out=btile[64:96, :], in_=pB[0:32, 1:16:2]),
                      inc=True)

    prev_sc = 0
    prev_ca = 0
    last_pe = 0
    for c in range(NCH):
        for hf in range(2):
            wl = emit("sq", lambda s, c=c, hf=hf: s.dma_start(
                out=wscf[:, 0:1536],
                in_=wqkv_h[128 * c:128 * c + 128, 1536 * hf:1536 * (hf + 1)]),
                inc=True, waits=[("dve", max(CK["tables"], prev_sc)), ("act", prev_ca),
                                 ("pe", last_pe)])
            prev_ca = emit("act", lambda a: a.activation(out=wraw, in_=wscf[:, 0:1536],
                                                         func=AF.Copy), inc=True,
                           waits=[("sq", wl)])
            prev_sc = emit("dve", lambda v, c=c, hf=hf: v.tensor_scalar(
                out=wq_all[:, 3072 * c + 1536 * hf:3072 * c + 1536 * (hf + 1)],
                in0=wscf[:, 0:1536], scalar1=gcol[:, c:c + 1], scalar2=None,
                op0=ALU.mult), inc=True,
                waits=[("sq", wl), ("sq", CK["g"]), ("dve", CK["tables"])])
            if hf == 0:
                mcs = range(0, 12)
            else:
                mcs = range(12, 16)
            for mc in mcs:
                off = 128 * mc - 1536 * hf
                emit("pe", lambda t, c=c, mc=mc, off=off: t.matmul(
                    out=pC[:, 32 + mc:33 + mc], lhsT=wraw[:, off:off + 128],
                    rhs=bcol[:, c:c + 1], start=(c == 0 and mc == 0),
                    stop=(c == NCH - 1)),
                    waits=[("act", prev_ca), ("dve", CK["bc"]),
                           ("dve", CK["btile"] if c == 0 else 0)])
            if hf == 1:
                emit("pe", lambda t, c=c: t.matmul(out=pD[0:1, 0:512], lhsT=bcol[:, c:c + 1],
                                                   rhs=wraw[:, 512:1024], start=(c == 0),
                                                   stop=(c == NCH - 1)))
                last_pe = emit("pe", lambda t, c=c: t.matmul(
                    out=pE[0:1, 0:512], lhsT=bcol[:, c:c + 1],
                    rhs=wraw[:, 1024:1536], start=(c == 0), stop=(c == NCH - 1)), inc=True)
    CK["pe_ub"] = last_pe
    emit("dve", lambda v: v.tensor_copy(out=ubqk[:], in_=pC[:, 32:48]),
         waits=[("pe", CK["pe_ub"])])
    emit("dve", lambda v: v.tensor_copy(out=ubvrow[0:1, 0:512], in_=pD[0:1, 0:512]))
    CK["ubvr"] = emit("dve", lambda v: v.tensor_copy(out=ubvrow[0:1, 512:1024],
                                                     in_=pE[0:1, 0:512]), inc=True)
    emit("pe", lambda t: t.matmul(out=pD[:, 0:512], lhsT=ones1[:], rhs=ubvrow[0:1, 0:512],
                                  start=True, stop=True), waits=[("dve", CK["ubvr"])])
    CK["pe_ubv"] = emit("pe", lambda t: t.matmul(out=pE[:, 0:512], lhsT=ones1[:],
                                                 rhs=ubvrow[0:1, 512:1024], start=True,
                                                 stop=True), inc=True)
    emit("dve", lambda v: v.tensor_copy(out=ubv[:, 0:512], in_=pD[:, 0:512]),
         waits=[("pe", CK["pe_ubv"])])
    CK["ubv"] = emit("dve", lambda v: v.tensor_copy(out=ubv[:, 512:1024], in_=pE[:, 0:512]),
                     inc=True)

    prev_gq = 0
    for c in range(NCH):
        prev_gq = emit("gq", lambda g, c=c: g.dma_start(
            out=wo_all[:, 1024 * c:1024 * (c + 1)], in_=wout_h[128 * c:128 * c + 128, :]),
            inc=True, waits=[("gps", CK["gps0"])])
    prev_tq = 0
    prev_dv = 0
    stg0 = qkbd[0][:, 0:1024]
    stg1 = qkbd[0][:, 1024:2048]
    for (whh, dstT, halfscale) in ((wr_h, wrT, False), (wk_h, wkT, False), (wv_h, wvT, True)):
        for dc in range(NCH):
            wl = emit("gq", lambda g, whh=whh, dc=dc: g.dma_start(
                out=stg0, in_=whh[128 * dc:128 * dc + 128, :]), inc=True,
                waits=[("tq", prev_tq), ("dve", prev_dv)])
            src = stg0
            if halfscale:
                prev_dv = emit("dve", lambda v: v.tensor_scalar(
                    out=stg1, in0=stg0, scalar1=0.5, scalar2=None, op0=ALU.mult), inc=True,
                    waits=[("gq", wl)])
                src = stg1
            dst = dstT.rearrange("p (a b f) -> p a b f", a=8, f=128)[:, :, dc, :]
            prev_tq = emit("tq", lambda s, dst=dst, src=src.rearrange(
                "p (c f) -> p c f", c=8): s.dma_start(out=dst, in_=src, transpose=True),
                inc=True, waits=([("gq", wl)] if not halfscale else [("dve", prev_dv)]))
    CK["wffn_dve"] = max(prev_sc, prev_dv)
    CK["wffn_gq"] = prev_gq
    CK["wffn_tq"] = prev_tq
    CK["wsc_free_dve"] = prev_sc
    CK["wsc_free_act"] = prev_ca
    CK["arena_free_pe"] = CK["pe_ubv"]


    def dget(key, default=0):
        return done.get(key, default)

    pe_big = [pA, pB, pC, pD]
    big_i = [0]
    big_evac_d = {id(pA): ("dve", CK["btile"]), id(pB): ("dve", CK["btile"]),
                  id(pC): ("dve", CK["ubv"]), id(pD): ("dve", CK["ubv"]),
                  id(pE): ("dve", CK["ubv"]), id(pF): ("dve", 0)}

    class _BE:
        def __getitem__(self, b):
            return big_evac_d[id(b)]

        def __setitem__(self, b, v):
            big_evac_d[id(b)] = v

    big_evac = _BE()

    def next_big():
        b = pe_big[big_i[0] % 2]
        big_i[0] += 1
        return b

    def rstd_chain(lnid, seq, steps, xtr_waits=()):
            r1, sq_ = stc(seq, lnid, 0), stc(seq, lnid, 1)
            emit("dve", lambda v: v.tensor_scalar(
                out=r_nm, in0=r1.broadcast_to([P, W64]), scalar1=-1.0 / D,
                scalar2=None, op0=ALU.mult), waits=list(xtr_waits))
            emit("dve", lambda v: v.tensor_scalar(
                out=r_tt, in0=sq_.broadcast_to([P, W64]), scalar1=1.0 / D,
                scalar2=1e-5, op0=ALU.mult, op1=ALU.add))
            emit("dve", lambda v: v.tensor_tensor(out=r_aa, in0=r_nm, in1=r_nm, op=ALU.mult))
            emit("dve", lambda v: v.tensor_tensor(out=r_tt, in0=r_tt, in1=r_aa,
                                                  op=ALU.subtract))
            emit("dve", lambda v: v.tensor_scalar(out=r_aa, in0=r_tt, scalar1=0.5, scalar2=0.5,
                                                  op0=ALU.mult, op1=ALU.add))
            emit("dve", lambda v: v.reciprocal(out=r_yy, in_=r_aa))
            for _ in range(steps):
                emit("dve", lambda v: v.tensor_tensor(out=r_aa, in0=r_yy, in1=r_yy,
                                                      op=ALU.mult))
                emit("dve", lambda v: v.tensor_tensor(out=r_hx, in0=r_aa, in1=r_tt,
                                                      op=ALU.mult))
                emit("dve", lambda v: v.tensor_scalar(out=r_aa, in0=r_hx, scalar1=-0.5,
                                                      scalar2=1.5, op0=ALU.mult, op1=ALU.add))
                emit("dve", lambda v: v.tensor_tensor(out=r_yy, in0=r_yy, in1=r_aa,
                                                      op=ALU.mult))

    def emit_F1(m):
        seq, blk = m % 2, m // 2
        r1, sq_ = stc(seq, 0, 0), stc(seq, 0, 1)
        wx = xload[m]
        emit("act", lambda a, seq=seq, r1=r1: a.activation(
            out=xhat[seq][:], in_=xs[seq][:], func=AF.Copy, accum_out=r1),
            waits=[("sq", wx),
                   ("tq", max(dget(("ht", m - 2)), dget(("xt", m - 2))))])
        c_sqa = emit("actd", lambda a, seq=seq, sq_=sq_: a.activation(
            out=xhat[seq][:], in_=xs[seq][:], func=AF.Square, accum_out=sq_),
            inc=True)

        rstd_chain(0, seq, 2, xtr_waits=[("act", c_sqa)])
        c_xh = emit("dve", lambda v, seq=seq: v.tensor_scalar(
            out=xhat[seq][:], in0=xs[seq][:], scalar1=r_nm[:, 0:1], scalar2=r_yy[:, 0:1],
            op0=ALU.add, op1=ALU.mult), inc=True, waits=[("tq", dget(("ht", m - 2)))])
        maybe_tap("xhat", m, lambda: xhat[seq][:], lambda: [("dve", c_xh)])

        if stop_after == "ln1":
            return True
        done[("xt", m)] = emit("tq", lambda s, seq=seq: s.dma_start(
            out=xnfm[seq].rearrange("p (c f) -> p c f", c=8),
            in_=xhat[seq].rearrange("p (c f) -> p c f", c=8), transpose=True), inc=True,
            waits=[("dve", c_xh), ("pe", dget(("qkv_done", m - 2))),
                   ("tq", dget(("skt", m - 2)))])
        maybe_tap("xnfm", m, lambda: xnfm[seq][:], lambda: [("tq", done[("xt", m)])])

        return stop_after == "xt"

    def emit_F2qk(m, mp0, mp1):
        seq, blk = m % 2, m // 2
        cev = 0
        for mp in range(mp0, mp1):
            bank = next_big()
            beng, bval = big_evac[bank]
            for half in range(2):
                mc = 2 * mp + half
                for kc in range(NCH):
                    wts = []
                    if half == 0 and kc == 0:
                        wts = [("tq", done[("xt", m)]), (beng, bval)]
                        if m < 2 and mp == 0:
                            wts.append(("dve", CK["wffn_dve"]))
                    emit("pe", lambda t, mc=mc, kc=kc, half=half, seq=seq, bank=bank: t.matmul(
                        out=bank[:, 128 * half:128 * (half + 1)],
                        lhsT=wq_all[:, 3072 * kc + 128 * mc:3072 * kc + 128 * (mc + 1)],
                        rhs=xnfm[seq][:, 128 * kc:128 * (kc + 1)],
                        start=(kc == 0), stop=(kc == NCH - 1)), waits=wts)
            cpe = emit("pe", lambda t: None, inc=True)
            ubv2 = ubqk[:, 2 * mp:2 * mp + 2].unsqueeze(2).broadcast_to([P, 2, 128])
            cev = emit("dve", lambda v, mp=mp, seq=seq, bank=bank, ubv2=ubv2: v.tensor_tensor(
                out=qkfm[seq][:, 256 * mp:256 * (mp + 1)].rearrange("p (a f) -> p a f", f=128),
                in0=bank[:, 0:256].rearrange("p (a f) -> p a f", f=128),
                in1=ubv2, op=ALU.add), inc=True,
                waits=[("pe", cpe), ("pe", dget(("scores_done", m - 2)))])
            big_evac[bank] = ("dve", cev)
        return cev

    def emit_F2a(m):
        emit_F2qk(m, 0, 4)
        return False

    def emit_F2b(m):
        done[("qk_evac", m)] = emit_F2qk(m, 4, 8)
        return False

    def emit_F2c(m):
        seq, blk = m % 2, m // 2
        for nh in range(2):
            bank = next_big()
            beng, bval = big_evac[bank]
            for kc in range(NCH):
                wts = []
                if kc == 0:
                    wts = [("tq", done[("xt", m)]), (beng, bval)]
                emit("pe", lambda t, nh=nh, kc=kc, seq=seq, bank=bank: t.matmul(
                    out=bank[:, 0:512], lhsT=xnfm[seq][:, 128 * kc:128 * (kc + 1)],
                    rhs=wq_all[:, 3072 * kc + 2048 + 512 * nh:3072 * kc + 2048 + 512 * (nh + 1)],
                    start=(kc == 0), stop=(kc == NCH - 1)), waits=wts)
            cpe = emit("pe", lambda t: None, inc=True)
            vdst = vb[seq].rearrange("p (c g f) -> p c g f", g=2, f=65)[:, 4 * nh:4 * (nh + 1), :, 0:64]
            vsrc = bank[:, 0:512].rearrange("p (c g f) -> p c g f", c=4, g=2)
            usrc = ubv[:, 512 * nh:512 * (nh + 1)].rearrange("p (c g f) -> p c g f", c=4, g=2)
            cev = emit("dve", lambda v, vdst=vdst, vsrc=vsrc, usrc=usrc: v.tensor_tensor(
                out=vdst, in0=vsrc, in1=usrc, op=ALU.add), inc=True,
                waits=[("pe", cpe), ("dve", dget(("usub", m - 2, SEGB - 1))),
                       ("pe", dget(("sdpa_done", m - 2))), ("dve", CK["ubv"] if m < 2 else 0)])
            big_evac[bank] = ("dve", cev)
        done[("v_evac", m)] = cev
        done[("qkv_done", m)] = now("pe")
        maybe_tap("qkfm", m, lambda: qkfm[seq][:], lambda: [("dve", done[("qk_evac", m)])])
        maybe_tap("vb", m, lambda: vb[seq][:], lambda: [("dve", done[("v_evac", m)])])

        return stop_after == "qkv"

    def emit_F2(m):
        return emit_F2a(m) or emit_F2b(m) or emit_F2c(m)

    def emit_F3(m):
        seq, blk = m % 2, m // 2
        c_expq = emit("act", lambda a, seq=seq: a.activation(out=eexp[:], in_=qkfm[seq][:, 0:1024],
                                                             func=AF.Exp), inc=True,
                      waits=[("dve", done[("qk_evac", m)]), ("dve", dget(("elu_stt", m - 1))),
                             ("pe", CK["arena_free_pe"]),
                             ("dve", dget(("mix_done", m - 1)))])
        emit("dve", lambda v: v.tensor_scalar(out=emin[:], in0=eexp[:], scalar1=1.0,
                                              scalar2=None, op0=ALU.min),
             waits=[("act", c_expq)])
        qdst = qkbd[seq].rearrange("p (c s g f) -> p c s g f", c=8, s=4, g=2)[:, :, :, 0, :]
        c_sttq = emit("dve", lambda v, seq=seq, qdst=qdst: v.scalar_tensor_tensor(
            out=qdst, in0=qkfm[seq][:, 0:1024].rearrange("p (c s f) -> p c s f", c=8, f=32),
            scalar=0.0, in1=emin.rearrange("p (c s f) -> p c s f", c=8, f=32),
            op0=ALU.max, op1=ALU.add), inc=True,
            waits=[("pe", dget(("phb_done", m - 2))),
                   ("sq", dget(("store", m - 2))),
                   ("tq", CK["wffn_tq"] if m < 2 else 0)])
        c_expk = emit("act", lambda a, seq=seq: a.activation(out=eexp[:],
                                                             in_=qkfm[seq][:, 1024:2048],
                                                             func=AF.Exp), inc=True,
                      waits=[("dve", c_sttq)])
        done[("elu_exp", m)] = c_expk
        c_min = c_expk
        emit("dve", lambda v: v.tensor_scalar(out=emin[:], in0=eexp[:], scalar1=1.0,
                                              scalar2=None, op0=ALU.min),
             waits=[("act", c_expk)])
        emit("dve", lambda v, seq=seq: v.scalar_tensor_tensor(
            out=skfm[seq][:], in0=qkfm[seq][:, 1024:2048], scalar=0.0,
            in1=emin[:], op0=ALU.max, op1=ALU.add),
            waits=[("pe", done[("qkv_done", m)])])
        kdst = qkbd[seq].rearrange("p (c s g f) -> p c s g f", c=8, s=4, g=2)[:, :, :, 1, :]
        c3 = emit("dve", lambda v, seq=seq, kdst=kdst: v.tensor_copy(
            out=kdst, in_=skfm[seq].rearrange("p (c s f) -> p c s f", c=8, f=32)), inc=True,
            waits=[("tq", CK["wffn_tq"] if m < 2 else 0)])
        done[("elu_stt", m)] = c3
        maybe_tap("qkbd", m, lambda: qkbd[seq][:], lambda: [("dve", c3)])
        done[("skt", m)] = emit("tq", lambda s, seq=seq: s.dma_start(
            out=skT[seq].rearrange("p (c f) -> p c f", c=8),
            in_=skfm[seq].rearrange("p (c f) -> p c f", c=8), transpose=True), inc=True,
            waits=[("dve", c3), ("pe", dget(("phb_done", m - 2)))])

        return stop_after == "elu"

    def emit_F4(m):
        seq, blk = m % 2, m // 2
        c_min = done[("elu_exp", m)]
        cosv = cosT[:, TOK * blk:TOK * (blk + 1)]
        sinv = sinT[:, TOK * blk:TOK * (blk + 1)]
        prev_rope_dve = 0
        for half in range(2):
            base = 1024 * half
            for cp in range(4):
                bank = next_big()
                beng, bval = big_evac[bank]
                for half2 in range(2):
                    c = 2 * cp + half2
                    cpe = emit("pe", lambda t, seq=seq, base=base, c=c, half2=half2, bank=bank: t.matmul(
                        out=bank[:, 128 * half2:128 * (half2 + 1)], lhsT=smat[:],
                        rhs=qkfm[seq][:, base + 128 * c:base + 128 * (c + 1)],
                        start=True, stop=True), inc=(half2 == 1),
                        waits=([("dve", done[("elu_stt", m)]), (beng, bval),
                                ("act", c_min), ("dve", CK["smat"])] if half2 == 0 else []))
                c2 = 2 * cp
                cosv2 = cosv.unsqueeze(1).broadcast_to([P, 2, TOK])
                sinv2 = sinv.unsqueeze(1).broadcast_to([P, 2, TOK])
                emit("dve", lambda v, bank=bank, sinv2=sinv2: v.tensor_tensor(
                    out=rop2[:].rearrange("p (a f) -> p a f", f=TOK) if False else rop2w,
                    in0=bank[:, 0:256].rearrange("p (a f) -> p a f", f=TOK),
                    in1=sinv2, op=ALU.mult), waits=[("pe", cpe)])
                emit("dve", lambda v, seq=seq, base=base, c2=c2, cosv2=cosv2: v.tensor_tensor(
                    out=rop1w,
                    in0=qkfm[seq][:, base + 128 * c2:base + 128 * (c2 + 2)].rearrange(
                        "p (a f) -> p a f", f=TOK),
                    in1=cosv2, op=ALU.mult))
                cev = emit("dve", lambda v, seq=seq, base=base, c2=c2: v.tensor_tensor(
                    out=qkfm[seq][:, base + 128 * c2:base + 128 * (c2 + 2)].rearrange(
                        "p (a f) -> p a f", f=TOK),
                    in0=rop2w, in1=rop1w, op=ALU.add), inc=True)
                big_evac[bank] = ("dve", cev)
                prev_rope_dve = cev
        done[("rope", m)] = prev_rope_dve
        maybe_tap("qkrope", m, lambda: qkfm[seq][:], lambda: [("dve", done[("rope", m)])])

        return stop_after == "rope"

    pa_state = {}

    def pa_lvl_of():
        return {"pa_scores": 1, "pa_exp": 2, "pa_mask": 3, "pa_sdpa": 4,
                "pa_recip": 5}.get(stop_after, 99)

    def emit_Ascores(m, r):
        seq, blk = m % 2, m // 2
        hf = r % 2
        c0 = 4 * (r // 2)
        sbank = pC if r % 2 == 0 else pD
        sbeng, sbval = big_evac[sbank]
        for ci in range(4):
            ch = c0 + ci
            wts = []
            if ci == 0:
                wts = [("dve", done[("rope", m)]), (sbeng, sbval),
                       ("act", dget(("exp_done", m, r - 2), dget(("exp_done", m - 1, r + 2))))]
            cpe = emit("pe", lambda t, seq=seq, ch=ch, hf=hf, ci=ci, sbank=sbank: t.matmul(
                out=sbank[:, 128 * ci:128 * (ci + 1)],
                lhsT=qkfm[seq][64 * hf:64 * (hf + 1), 1024 + 128 * ch:1024 + 128 * (ch + 1)],
                rhs=qkfm[seq][64 * hf:64 * (hf + 1), 128 * ch:128 * (ch + 1)],
                start=True, stop=True, tile_position=(64 * hf, 0)),
                inc=(ci == 3), waits=wts)
        done[("scorespe", m, r)] = cpe

    def emit_Ar(m, r):
        seq, blk = m % 2, m // 2
        pa_lvl = pa_lvl_of()
        probs = probs_sv[seq]
        pend_evac = pa_state.setdefault(m, [None])
        if True:
            hf = r % 2
            c0 = 4 * (r // 2)
            sbank = pC if r % 2 == 0 else pD
            obank = pE if r % 2 == 0 else pF
            rcX = rc1A if r % 2 == 0 else rc1B
            cpe = done[("scorespe", m, r)]
            if pa_lvl < 2:
                return
            c_ex = emit("act", lambda a, r=r, sbank=sbank: a.activation(
                out=probs[r % 2][:], in_=sbank[:], func=AF.Exp, scale=0.125, bias=cm20[:]),
                inc=True, waits=[("pe", cpe),
                                 ("dve", dget(("msk_done", m, r - 2),
                                              dget(("msk_done", m - 1, r + 2)))),
                                 ("dve", dget(("ln3", m - 2))),
                                 ("dve", done[("elu_stt", m)] if r == 0 else 0),
                                 ("gps", CK["gps0"] if m < 2 else 0)])
            done[("exp_done", m, r)] = c_ex
            big_evac[sbank] = ("act", c_ex)
            if pa_lvl < 3:
                return
            c_mk = emit("dve", lambda v, r=r: v.tensor_tensor(
                out=probs[r % 2].rearrange("p (a f) -> p a f", f=128),
                in0=probs[r % 2].rearrange("p (a f) -> p a f", f=128),
                in1=mask01[:].unsqueeze(1).broadcast_to([P, 4, P]),
                op=ALU.mult), inc=True, waits=[("act", c_ex),
                                               ("gps", CK["gps0"] if m < 2 else 0)])
            done[("msk_done", m, r)] = c_mk
            if pa_lvl < 4:
                return
            obeng, obval = big_evac[obank]
            for ci in range(4):
                ch = c0 + ci
                wts = []
                if ci == 0:
                    wts = [("dve", c_mk), ("dve", done[("v_evac", m)]), (obeng, obval),
                           ("dve", dget(("nds_mult", m - 1, SEGB - 1))),
                           ("gps", CK["gps0"] if m < 2 else 0)]
                cpe = emit("pe", lambda t, seq=seq, ch=ch, hf=hf, ci=ci, r=r, obank=obank: t.matmul(
                    out=obank[:, 65 * ci:65 * (ci + 1)],
                    lhsT=probs[r % 2][:, 128 * ci:128 * (ci + 1)],
                    rhs=vb[seq][:, 130 * ch + 65 * hf:130 * ch + 65 * (hf + 1)],
                    start=True, stop=True), inc=(ci == 3), waits=wts)
            done[("sdpape", m, r)] = cpe
            if r < 2:
                emit_Ascores(m, r + 2)
            if pa_lvl < 5:
                return
            emit("dve", lambda v, obank=obank, rcX=rcX: v.reciprocal(
                out=rcX[:], in_=obank[:, 0:260].rearrange("p (a b) -> p a b", b=65)[:, :, 64]),
                waits=[("pe", cpe), ("dve", CK["btile"] if m == 0 else 0)])
            if pend_evac[0] is None:
                ddrain()
            else:
                (pr, prcX, probank) = pend_evac[0]
                sdst = sdpa[seq].rearrange("p (c g b) -> p c g b", g=2, b=65)[
                    :, 4 * (pr // 2):4 * (pr // 2) + 4, pr % 2, :]
                c_se = emit("dve", lambda v, sdst=sdst, probank=probank, prcX=prcX: v.tensor_tensor(
                    out=sdst, in0=probank[:, 0:260].rearrange("p (a b) -> p a b", b=65),
                    in1=prcX[:].unsqueeze(2).broadcast_to([P, 4, 65]), op=ALU.mult),
                    inc=True, waits=[("dve", dget(("blend", m - 2, SEGB - 1)))])
                done[("sdpa_evac", m, pr)] = c_se
                big_evac[probank] = ("dve", c_se)
            ombv = omb.rearrange("p (c g) -> p c g", g=2)[:, c0:c0 + 4, hf]
            emit("dve", lambda v, ombv=ombv, rcX=rcX: v.tensor_tensor(
                out=rcX[:], in0=rcX[:], in1=ombv, op=ALU.mult))
            pend_evac[0] = (r, rcX, obank)

    def emit_Aflush(m):
        seq, blk = m % 2, m // 2
        pa_lvl = pa_lvl_of()
        pend_evac = pa_state.setdefault(m, [None])
        done[("sdpa_done", m)] = now("pe")
        done[("scores_done", m)] = now("pe")
        if pa_lvl >= 5 and pend_evac[0] is not None:
            (pr, prcX, probank) = pend_evac[0]
            ddrain()
            sdst = sdpa[seq].rearrange("p (c g b) -> p c g b", g=2, b=65)[
                :, 4 * (pr // 2):4 * (pr // 2) + 4, pr % 2, :]
            c_se = emit("dve", lambda v, sdst=sdst, probank=probank, prcX=prcX: v.tensor_tensor(
                out=sdst, in0=probank[:, 0:260].rearrange("p (a b) -> p a b", b=65),
                in1=prcX[:].unsqueeze(2).broadcast_to([P, 4, 65]), op=ALU.mult),
                inc=True, waits=[("dve", dget(("blend", m - 2, SEGB - 1)))])
            done[("sdpa_evac", m, pr)] = c_se
            big_evac[probank] = ("dve", c_se)
        if pa_lvl >= 99:
            maybe_tap("sdpa", m, lambda: sdpa[seq][:],
                      lambda: [("dve", done[("sdpa_evac", m, 3)])])

    def emit_As(m, s_):
        seq, blk = m % 2, m // 2
        if True:
            gseg = SEGB * blk + s_
            r0 = 32 * s_
            shdst = shadow[seq].rearrange("p (c f) -> p c f", f=65)[:, :, 0:64]
            if gseg > 0:
                c_sh = emit("act", lambda a, seq=seq, shdst=shdst: a.activation(
                    out=shdst, in_=memPS[seq].rearrange("p (c f) -> p c f", f=64),
                    func=AF.Copy), inc=True,
                    waits=[("pe", dget(("delta", m, s_ - 1), dget(("delta", m - 2, SEGB - 1))))])
            else:
                c_sh = now("act")
            done[("shadow", m, s_)] = c_sh
            emit("dve", lambda v, seq=seq, s_=s_: v.tensor_reduce(
                out=zred[:], in_=qkbd[seq].rearrange("p (c s g f) -> p c s g f",
                                                     c=8, s=4, g=2)[:, :, s_, 1, :],
                axis=mybir.AxisListType.X, op=ALU.add),
                waits=[("dve", done[("elu_stt", m)]),
                       ("gps", CK["gps0"] if m < 2 else 0)])
            for jj in range(8):
                bank = pE if jj < 4 else pF
                col = 65 * (jj % 4)
                for hf in range(2):
                    wts = []
                    if jj == 0 and hf == 0:
                        wts = [("dve", max(done[("elu_stt", m)],
                                           done[("sdpa_evac", m, 3)])),
                               ("act", c_sh),
                               ("dve", dget(("nds_mult", m, s_ - 1))),
                               ("gps", CK["gps0"] if m < 2 else 0)]
                    cpe = emit("pe", lambda t, seq=seq, s_=s_, jj=jj, hf=hf, bank=bank,
                               col=col: t.matmul(
                        out=bank[64 * hf:64 * (hf + 1), col:col + 65],
                        lhsT=qkbd[seq][64 * hf:64 * hf + 64,
                                       256 * jj + 64 * s_:256 * jj + 64 * (s_ + 1)],
                        rhs=shadow[seq][64 * hf:64 * hf + 64, 65 * jj:65 * (jj + 1)],
                        start=True, stop=True, tile_position=(64 * hf, 64 * hf)),
                        inc=(jj in (3, 7) and hf == 1), waits=wts)
            done[("numskm", m, s_)] = cpe
            emit("dve", lambda v: v.reciprocal(
                out=rc1A[:], in_=pE[:, 0:260].rearrange("p (a b) -> p a b", b=65)[:, :, 64]),
                waits=[("pe", done[("numskm", m, s_)]),
                       ("dve", dget(("usub", m, s_ - 1), dget(("usub", m - 1, SEGB - 1)))),
                       ("dve", dget(("blend", m, s_ - 1)))])
            emit("dve", lambda v: v.reciprocal(
                out=rc1B[:], in_=pF[:, 0:260].rearrange("p (a b) -> p a b", b=65)[:, :, 64]))
            emit("dve", lambda v: v.tensor_tensor(out=rc1A[:], in0=rc1A[:],
                                                  in1=btile[:, 0:4], op=ALU.mult))
            emit("dve", lambda v: v.tensor_tensor(out=rc1B[:], in0=rc1B[:],
                                                  in1=btile[:, 4:8], op=ALU.mult))
            emit("dve", lambda v, seq=seq: v.tensor_tensor(out=zf[seq][:], in0=zf[seq][:],
                                                           in1=zred[:], op=ALU.add))
            for half in range(2):
                bank = pE if half == 0 else pF
                rcX = rc1A if half == 0 else rc1B
                for hf in range(2):
                    prow = 32 + 64 * hf
                    uview = ut[r0:r0 + 32, :].rearrange(
                        "p (c g f) -> p c g f", g=2, f=64)[:, 4 * half:4 * half + 4, hf, :]
                    emit("dve", lambda v, bank=bank, prow=prow, uview=uview, rcX=rcX: v.tensor_tensor(
                        out=uview,
                        in0=bank[prow:prow + 32, 0:260].rearrange("p (a b) -> p a b", b=65)[:, :, 0:64],
                        in1=rcX[prow:prow + 32, :].unsqueeze(2).broadcast_to([32, 4, 64]),
                        op=ALU.mult))
            uin = ut[r0:r0 + 32, :].rearrange("p (c g f) -> p c g f", g=2, f=64)
            vsrc = vb[seq][r0:r0 + 32, :].rearrange("p (c g f) -> p c g f", g=2, f=65)[:, :, :, 0:64]
            cu = emit("dve", lambda v, vsrc=vsrc, uin=uin: v.tensor_tensor(
                out=uin, in0=vsrc, in1=uin, op=ALU.subtract), inc=True,
                waits=[("pe", dget(("delta", m, s_ - 1)))])
            done[("usub", m, s_)] = cu
            emit("dve", lambda v, seq=seq: v.tensor_copy(
                out=shadow[seq].rearrange("p (c f) -> p c f", f=65)[:, :, 64],
                in_=zf[seq][:]))
            c_nds = 0
            for half in range(2):
                bank = pE if half == 0 else pF
                rcX = rc1A if half == 0 else rc1B
                for hf in range(2):
                    nrow = 64 * hf
                    oview = otok[seq][r0:r0 + 32, :].rearrange(
                        "p (c g f) -> p c g f", g=2, f=64)[:, 4 * half:4 * half + 4, hf, :]
                    c_nds = emit("dve", lambda v, bank=bank, nrow=nrow, oview=oview, rcX=rcX: v.tensor_tensor(
                        out=oview,
                        in0=bank[nrow:nrow + 32, 0:260].rearrange("p (a b) -> p a b", b=65)[:, :, 0:64],
                        in1=rcX[nrow:nrow + 32, :].unsqueeze(2).broadcast_to([32, 4, 64]),
                        op=ALU.mult), inc=(half == 1 and hf == 1))
            done[("nds_mult", m, s_)] = c_nds
            oview = otok[seq][r0:r0 + 32, :].rearrange("p (c g f) -> p c g f", g=2, f=64)
            sdv = sdpa[seq][r0:r0 + 32, :].rearrange("p (c g b) -> p c g b", g=2, b=65)[:, :, :, 0:64]
            cb = emit("dve", lambda v, oview=oview, sdv=sdv: v.tensor_tensor(
                out=oview, in0=oview, in1=sdv, op=ALU.add), inc=True)
            done[("blend", m, s_)] = cb
            first = (gseg == 0)
            for jj in range(8):
                for hf in range(2):
                    wts = []
                    if jj == 0 and hf == 0:
                        wts = [("dve", cu), ("act", c_sh), ("tq", done[("skt", m)]),
                               ("dve", CK["pszero"] if m < 2 else 0)]
                    cpe = emit("pe", lambda t, seq=seq, s_=s_, jj=jj, hf=hf, first=first: t.matmul(
                        out=memPS[seq][64 * hf:64 * (hf + 1), 64 * jj:64 * (jj + 1)],
                        lhsT=skT[seq][32 * s_:32 * (s_ + 1),
                                      128 * jj + 64 * hf:128 * jj + 64 * (hf + 1)],
                        rhs=ut[32 * s_:32 * (s_ + 1),
                               128 * jj + 64 * hf:128 * jj + 64 * (hf + 1)],
                        start=False, stop=False,
                        tile_position=(32 * s_, 64 * hf), skip_group_check=True),
                        inc=(jj == 7 and hf == 1), waits=wts)
            done[("delta", m, s_)] = cpe

    def emit_Atail(m):
        seq, blk = m % 2, m // 2
        done[("phb_done", m)] = now("pe")
        maybe_tap("otok", m, lambda: otok[seq][:],
                  lambda: [("dve", done[("blend", m, SEGB - 1)])])
        maybe_tap("shadow", m, lambda: shadow[seq][:],
                  lambda: [("act", done[("shadow", m, SEGB - 1)])])
        done[("ot", m)] = emit("tq", lambda s, seq=seq: s.dma_start(
            out=ofm.rearrange("p (c f) -> p c f", c=8),
            in_=otok[seq].rearrange("p (c f) -> p c f", c=8), transpose=True), inc=True,
            waits=[("dve", done[("blend", m, SEGB - 1)]), ("pe", dget(("wout_done", m - 1)))])
        return stop_after == "ot"

    def emit_scan(m):
        emit_Ascores(m, 0)
        emit_Ascores(m, 1)
        for r in range(4):
            emit_Ar(m, r)
        emit_Aflush(m)
        if pa_lvl_of() < 99 or stop_after == "phasea":
            return True
        for s_ in range(SEGB):
            emit_As(m, s_)
        if stop_after == "phaseb":
            return True
        return emit_Atail(m)

    def emit_B1(m):
        seq, blk = m % 2, m // 2
        for nh in range(2):
            bank = next_big()
            beng, bval = big_evac[bank]
            for ic in range(NCH):
                wts = []
                if ic == 0:
                    wts = [("tq", done[("ot", m)]), (beng, bval)]
                emit("pe", lambda t, nh=nh, ic=ic, bank=bank: t.matmul(
                    out=bank[:, 0:512], lhsT=ofm[:, 128 * ic:128 * (ic + 1)],
                    rhs=wo_all[:, 1024 * ic + 512 * nh:1024 * ic + 512 * (nh + 1)],
                    start=(ic == 0), stop=(ic == NCH - 1)), waits=wts)
            cpe = emit("pe", lambda t: None, inc=True)
            cev = emit("dve", lambda v, seq=seq, nh=nh, bank=bank: v.scalar_tensor_tensor(
                out=xs[seq][:, 512 * nh:512 * (nh + 1)], in0=bank[:, 0:512], scalar=1.0,
                in1=xs[seq][:, 512 * nh:512 * (nh + 1)], op0=ALU.mult, op1=ALU.add), inc=True,
                waits=[("pe", cpe)])
            big_evac[bank] = ("dve", cev)
        done[("wout_done", m)] = now("pe")
        done[("s2", m)] = cev
        maybe_tap("s2", m, lambda: xs[seq][:], lambda: [("dve", cev)])
        return False

    def emit_B1b(m):
        seq, blk = m % 2, m // 2
        c_s2 = done[("s2", m)]
        r1, sq_ = stc(seq, 1, 0), stc(seq, 1, 1)
        emit("act", lambda a, seq=seq, r1=r1: a.activation(
            out=xnfm[seq][:], in_=xs[seq][:], func=AF.Copy, accum_out=r1),
            waits=[("dve", c_s2), ("tq", done[("skt", m)])])
        c_sqa2 = emit("actd", lambda a, seq=seq, sq_=sq_: a.activation(
            out=xnfm[seq][:], in_=xs[seq][:], func=AF.Square, accum_out=sq_), inc=True)
        rstd_chain(1, seq, 3, xtr_waits=[("act", c_sqa2)])
        c_h = emit("dve", lambda v, seq=seq: v.tensor_scalar(
            out=htok[seq][:], in0=xs[seq][:], scalar1=r_nm[:, 0:1], scalar2=r_yy[:, 0:1],
            op0=ALU.add, op1=ALU.mult), inc=True, waits=[("dve", dget(("ln3", m - 2)))])
        c_hb = emit("act", lambda a, seq=seq: a.activation(
            out=hbf[seq][:], in_=htok[seq][:], func=AF.Copy), inc=True,
            waits=[("dve", c_h), ("tq", done[("ot", m)])])
        done[("chb", m)] = c_hb
        maybe_tap("htok", m, lambda: htok[seq][:], lambda: [("dve", c_h)])
        if m + 2 < nmt:
            load_x(m + 2, w=[("dve", c_h), ("act", c_sqa2)])

        return stop_after == "ln2"

    def emit_B2(m):
        seq, blk = m % 2, m // 2
        if blk > 0:
            c_halo = emit("dve", lambda v, seq=seq: v.tensor_copy(
                out=hh[seq][:], in_=hfm[seq].rearrange("p (c f) -> p c f", f=128)[:, :, 127]),
                inc=True, waits=[("dve", dget(("mix_done", m - 2)))])
        else:
            c_halo = now("dve")
        done[("ht", m)] = emit("tq", lambda s, seq=seq: s.dma_start(
            out=hfm[seq].rearrange("p (c f) -> p c f", c=8),
            in_=hbf[seq].rearrange("p (c f) -> p c f", c=8), transpose=True), inc=True,
            waits=[("act", done[("chb", m)]), ("dve", c_halo)])

        if stop_after == "ht":
            return True
        dtile = dtile_sv[seq]
        c_d2 = emit("dve", lambda v, seq=seq, dtile=dtile: v.tensor_tensor(
            out=dtile[:, 0:1024].rearrange("p (c f) -> p c f", f=128)[:, :, 0],
            in0=hfm[seq].rearrange("p (c f) -> p c f", f=128)[:, :, 0],
            in1=hh[seq][:], op=ALU.subtract), inc=True,
            waits=[("tq", done[("ht", m)]), ("dve", done[("blend", m, SEGB - 1)]),
                   ("gps", CK["gps0"] if m < 2 else 0)])
        emit("dve", lambda v, seq=seq, dtile=dtile: v.tensor_tensor(
            out=dtile[:, 0:1024].rearrange("p (c f) -> p c f", f=128)[:, :, 1:128],
            in0=hfm[seq].rearrange("p (c f) -> p c f", f=128)[:, :, 1:128],
            in1=hfm[seq].rearrange("p (c f) -> p c f", f=128)[:, :, 0:127], op=ALU.subtract))
        murb_v = murb[:].unsqueeze(2).broadcast_to([P, 8, 128])
        mukb_v = mukb[:].unsqueeze(2).broadcast_to([P, 8, 128])
        dv = dtile[:, 0:1024].rearrange("p (c f) -> p c f", f=128)
        emit("dve", lambda v, dv=dv, murb_v=murb_v: v.tensor_tensor(
            out=xr[0].rearrange("p (c f) -> p c f", f=128), in0=dv, in1=murb_v,
            op=ALU.mult),
            waits=[("dve", c_d2), ("dve", CK["wsc_free_dve"] if m < 2 else 0),
                   ("act", CK["wsc_free_act"] if m < 2 else 0),
                   ("pe", dget(("ffnmm_done", m - 1))),
                   ("act", dget(("tanh", m - 1, 1)))])
        emit("dve", lambda v, dv=dv, mukb_v=mukb_v: v.tensor_tensor(
            out=xk[0].rearrange("p (c f) -> p c f", f=128), in0=dv, in1=mukb_v,
            op=ALU.mult))
        emit("dve", lambda v, seq=seq: v.tensor_tensor(
            out=xr[0][:], in0=xr[0][:], in1=hfm[seq][:], op=ALU.add))
        cm = emit("dve", lambda v, seq=seq: v.tensor_tensor(
            out=xk[0][:], in0=xk[0][:], in1=hfm[seq][:], op=ALU.add), inc=True)
        done[("mix_done", m)] = cm
        maybe_tap("xr", m, lambda: xr[seq][:], lambda: [("dve", cm)])

        return stop_after == "mix"

    def emit_B3(m):
        seq, blk = m % 2, m // 2
        cm = done[("mix_done", m)]
        for nh in range(2):
            bank = next_big()
            beng, bval = big_evac[bank]
            for cc in range(NCH):
                wts = []
                if cc == 0:
                    wts = [("dve", cm), (beng, bval)]
                emit("pe", lambda t, seq=seq, nh=nh, cc=cc, bank=bank: t.matmul(
                    out=bank[:, 0:512], lhsT=xr[seq][:, 128 * cc:128 * (cc + 1)],
                    rhs=wrT[:, 1024 * cc + 512 * nh:1024 * cc + 512 * (nh + 1)],
                    start=(cc == 0), stop=(cc == NCH - 1)), waits=wts)
            cpe = emit("pe", lambda t: None, inc=True)
            ca = emit("act", lambda a, nh=nh, bank=bank: a.activation(
                out=tanh_t[:, 512 * nh:512 * (nh + 1)], in_=bank[:, 0:512], func=AF.Tanh,
                scale=0.5), inc=True,
                waits=[("pe", cpe), ("dve", dget(("ffno", m - 1)))])
            done[("tanh", m, nh)] = ca
            big_evac[bank] = ("act", ca)
        return False

    def emit_B4(m):
        seq, blk = m % 2, m // 2
        cm = done[("mix_done", m)]
        for nh in range(2):
            bank = next_big()
            beng, bval = big_evac[bank]
            for cc in range(NCH):
                wts = []
                if cc == 0:
                    wts = [("dve", cm), (beng, bval)]
                emit("pe", lambda t, seq=seq, nh=nh, cc=cc, bank=bank: t.matmul(
                    out=bank[:, 0:512], lhsT=xk[seq][:, 128 * cc:128 * (cc + 1)],
                    rhs=wkT[:, 1024 * cc + 512 * nh:1024 * cc + 512 * (nh + 1)],
                    start=(cc == 0), stop=(cc == NCH - 1)), waits=wts)
            cpe = emit("pe", lambda t: None, inc=True)
            emit("dve", lambda v, seq=seq, bank=bank: v.tensor_scalar(
                out=krelu_of[seq][:], in0=bank[:, 0:512], scalar1=0.0, scalar2=None,
                op0=ALU.max), waits=[("pe", cpe), ("pe", dget(("kv_done", m - 2))),
                                     ("tq", dget(("kt", m - 1))),
                                     ("act", dget(("ln3sq", m - 2))),
                                     ("dve", CK["wsc_free_dve"] if m < 2 else 0),
                                     ("act", CK["wsc_free_act"] if m < 2 else 0)])
            cev = emit("dve", lambda v, seq=seq, nh=nh: v.tensor_tensor(
                out=kact[seq][:, 512 * nh:512 * (nh + 1)], in0=krelu_of[seq][:],
                in1=krelu_of[seq][:], op=ALU.mult), inc=True)
            big_evac[bank] = ("dve", cev)
        c_ka = cev
        done[("kact", m)] = c_ka
        maybe_tap("kact", m, lambda: kact[seq][:], lambda: [("dve", c_ka)])
        done[("kt", m)] = emit("tq", lambda s, seq=seq: s.dma_start(
            out=kafm[seq].rearrange("p (c f) -> p c f", c=8),
            in_=kact[seq].rearrange("p (c f) -> p c f", c=8), transpose=True), inc=True,
            waits=[("dve", c_ka)])

        return stop_after == "kact"

    def emit_B5(m):
        seq, blk = m % 2, m // 2
        for nh in range(2):
            bank = next_big()
            beng, bval = big_evac[bank]
            for cc in range(NCH):
                wts = []
                if cc == 0:
                    wts = [("tq", done[("kt", m)]), (beng, bval)]
                emit("pe", lambda t, seq=seq, nh=nh, cc=cc, bank=bank: t.matmul(
                    out=bank[:, 0:512], lhsT=kafm[seq][:, 128 * cc:128 * (cc + 1)],
                    rhs=wvT[:, 1024 * cc + 512 * nh:1024 * cc + 512 * (nh + 1)],
                    start=(cc == 0), stop=(cc == NCH - 1)), waits=wts)
            cpe = emit("pe", lambda t: None, inc=True)
            emit("dve", lambda v, nh=nh, bank=bank: v.scalar_tensor_tensor(
                out=tanh_t[:, 512 * nh:512 * (nh + 1)], in0=tanh_t[:, 512 * nh:512 * (nh + 1)],
                scalar=1.0, in1=bank[:, 0:512], op0=ALU.add, op1=ALU.mult),
                waits=[("pe", cpe), ("act", done[("tanh", m, nh)])])
            cev = emit("dve", lambda v, seq=seq, nh=nh: v.tensor_tensor(
                out=htok[seq][:, 512 * nh:512 * (nh + 1)],
                in0=htok[seq][:, 512 * nh:512 * (nh + 1)],
                in1=tanh_t[:, 512 * nh:512 * (nh + 1)], op=ALU.add), inc=True,
                waits=[("tq", done[("ht", m)])])
            big_evac[bank] = ("dve", cev)
        done[("ffnmm_done", m)] = now("pe")
        done[("kv_done", m)] = now("pe")
        done[("ffno", m)] = cev
        maybe_tap("s3", m, lambda: htok[seq][:], lambda: [("dve", cev)])
        return False

    def emit_B5b(m):
        seq, blk = m % 2, m // 2
        c_s3 = done[("ffno", m)]
        r1, sq_ = stc(seq, 2, 0), stc(seq, 2, 1)
        emit("act", lambda a, seq=seq, r1=r1: a.activation(
            out=kafm[seq][:], in_=htok[seq][:], func=AF.Copy, accum_out=r1),
            waits=[("dve", c_s3), ("pe", done[("kv_done", m)])])
        c_sqa3 = emit("actd", lambda a, seq=seq, sq_=sq_: a.activation(
            out=kafm[seq][:], in_=htok[seq][:], func=AF.Square, accum_out=sq_), inc=True)
        done[("ln3sq", m)] = c_sqa3
        rstd_chain(2, seq, 3, xtr_waits=[("act", c_sqa3)])
        c_o = emit("dve", lambda v, seq=seq: v.tensor_scalar(
            out=h3[seq][:], in0=htok[seq][:], scalar1=r_nm[:, 0:1], scalar2=r_yy[:, 0:1],
            op0=ALU.add, op1=ALU.mult), inc=True,
            waits=[("sq", dget(("store", m - 2))), ("pe", done[("phb_done", m)])])
        done[("ln3", m)] = c_o
        maybe_tap("h3", m, lambda: h3[seq][:], lambda: [("dve", c_o)])
        done[("store", m)] = emit("sq", lambda s, seq=seq, blk=blk: s.dma_start(
            out=out_h[seq, TOK * blk:TOK * (blk + 1), :], in_=h3[seq][:]), inc=True,
            waits=[("dve", c_o)])
        return False

    if stop_after is None:
        def emit_front_all(m):
            emit_F1(m); emit_F2a(m); emit_F2b(m); emit_F2c(m); emit_F3(m); emit_F4(m)

        if nmt >= 1:
            emit_front_all(0)
        for it in range(nmt + 1):
            b = it - 1
            f = it + 1
            Bv = 0 <= b < nmt
            Fv = f < nmt
            if it < nmt:
                emit_Ascores(it, 0)
                emit_Ascores(it, 1)
                emit_Ar(it, 0)
                emit_Ar(it, 1)
                emit_Ar(it, 2)
                emit_Ar(it, 3)
                emit_Aflush(it)
                if Bv:
                    emit_B1(b)
                emit_As(it, 0)
                if Bv:
                    emit_B1b(b)
                    emit_B2(b)
                emit_As(it, 1)
                if Fv:
                    emit_F1(f)
                if Bv:
                    emit_B3(b)
                if Fv:
                    emit_F2a(f)
                emit_As(it, 2)
                if Bv:
                    emit_B4(b)
                if Fv:
                    emit_F2b(f)
                if Bv:
                    emit_B5(b)
                emit_As(it, 3)
                if Fv:
                    emit_F2c(f)
                emit_Atail(it)
                if Bv:
                    emit_B5b(b)
                if Fv:
                    emit_F3(f)
                    emit_F4(f)
            else:
                if Bv:
                    emit_B1(b); emit_B1b(b); emit_B2(b); emit_B3(b)
                    emit_B4(b); emit_B5(b); emit_B5b(b)
    else:
        for m in range(nmt):
            if (emit_F1(m) or emit_F2(m) or emit_F3(m) or emit_F4(m)
                    or emit_scan(m) or emit_B1(m) or emit_B1b(m) or emit_B2(m)
                    or emit_B3(m) or emit_B4(m) or emit_B5(m) or emit_B5b(m)):
                break

    total_sq = cnt["sq"]
    emit("sq", lambda s: None, waits=[("sq", total_sq)])

    with nc.Block() as block:
        block.tensor(_mk_runner(ops["pe"]))
        block.vector(_mk_runner(ops["dve"]))
        block.scalar(_mk_runner(ops["scalar"]))
        block.gpsimd(_mk_runner(ops["gps"]))
        block.sync(_mk_runner(ops["sync"]))

    ctx.close()
    return nc, tap_h


def _mk_runner(stream):
    def _run(eng):
        last_instr = [None]
        for (wl, fn, sem, step, logical) in stream:
            for (s, v) in wl:
                eng.wait_ge(s, v)
            instr = fn(eng)
            if instr is not None:
                last_instr[0] = instr
            if sem is not None:
                if logical == "actd":
                    eng.drain().then_inc(sem, step)
                else:
                    tgt = instr if instr is not None else last_instr[0]
                    tgt.then_inc(sem, step)
    return _run


_NC_CACHE = {}


def kernel(L0_attn_x, L0_attn_w_qkv, L0_attn_w_out, L0_attn_ln_g, L0_attn_ln_b,
           L0_attn_mem_beta, L0_ffn_W_r, L0_ffn_W_k, L0_ffn_W_v, L0_ffn_mu_r, L0_ffn_mu_k):
    x = np.ascontiguousarray(np.asarray(L0_attn_x, dtype=np.float32))
    if "nc" not in _NC_CACHE:
        _NC_CACHE["nc"] = build_nc()
    nc, _ = _NC_CACHE["nc"]
    shared = {
        "w_qkv": np.ascontiguousarray(np.asarray(L0_attn_w_qkv, np.float32)),
        "w_out": np.ascontiguousarray(np.asarray(L0_attn_w_out, np.float32)),
        "ln_g": np.ascontiguousarray(np.asarray(L0_attn_ln_g, np.float32)),
        "ln_b": np.ascontiguousarray(np.asarray(L0_attn_ln_b, np.float32)),
        "mem_beta": np.ascontiguousarray(np.asarray(L0_attn_mem_beta, np.float32).reshape(1, H)),
        "W_r": np.ascontiguousarray(np.asarray(L0_ffn_W_r, np.float32)),
        "W_k": np.ascontiguousarray(np.asarray(L0_ffn_W_k, np.float32)),
        "W_v": np.ascontiguousarray(np.asarray(L0_ffn_W_v, np.float32)),
        "mu_r": np.ascontiguousarray(np.asarray(L0_ffn_mu_r, np.float32)),
        "mu_k": np.ascontiguousarray(np.asarray(L0_ffn_mu_k, np.float32)),
    }
    in_maps = []
    for core in range(8):
        im = dict(shared)
        im["x"] = np.ascontiguousarray(x[2 * core:2 * core + 2])
        in_maps.append(im)
    res = run_bass_kernel_spmd(nc, in_maps, core_ids=list(range(8)))
    out = np.concatenate([r["out"] for r in res.results], axis=0)
    return np.ascontiguousarray(out.reshape(16, TSEQ, D).astype(np.float32))


# version 12
# speedup vs baseline: 1.2931x; 1.0468x over previous
import math
import numpy as np
import concourse.bass as bass
import concourse.mybir as mybir
from concourse.bass_utils import run_bass_kernel_spmd
from contextlib import ExitStack

F32 = mybir.dt.float32
BF16 = mybir.dt.bfloat16
I32 = mybir.dt.int32
AF = mybir.ActivationFunctionType
ALU = mybir.AluOpType

P = 128
D = 1024
NCH = 8
H = 16
SEG = 32
TOK = 128
NSEQ = 2
TSEQ = 2048
NBLK = TSEQ // TOK
NMT = NSEQ * NBLK
SEGB = TOK // SEG
EXPB = 20.0
LOG1E4 = math.log(10000.0)
TWO_PI = 2.0 * math.pi


def build_nc(taps=(), nmt=NMT, stop_after=None):
    nc = bass.Bass("TRN2", target_bir_lowering=False)
    x_h = nc.dram_tensor("x", [NSEQ, TSEQ, D], F32, kind="ExternalInput").ap()
    wqkv_h = nc.dram_tensor("w_qkv", [D, 3 * D], F32, kind="ExternalInput").ap()
    wout_h = nc.dram_tensor("w_out", [D, D], F32, kind="ExternalInput").ap()
    g_h = nc.dram_tensor("ln_g", [D], F32, kind="ExternalInput").ap()
    b_h = nc.dram_tensor("ln_b", [D], F32, kind="ExternalInput").ap()
    beta_h = nc.dram_tensor("mem_beta", [1, H], F32, kind="ExternalInput").ap()
    wr_h = nc.dram_tensor("W_r", [D, D], F32, kind="ExternalInput").ap()
    wk_h = nc.dram_tensor("W_k", [D, D], F32, kind="ExternalInput").ap()
    wv_h = nc.dram_tensor("W_v", [D, D], F32, kind="ExternalInput").ap()
    mur_h = nc.dram_tensor("mu_r", [D], F32, kind="ExternalInput").ap()
    muk_h = nc.dram_tensor("mu_k", [D], F32, kind="ExternalInput").ap()
    out_h = nc.dram_tensor("out", [NSEQ, TSEQ, D], F32, kind="ExternalOutput").ap()

    ctx = ExitStack()
    e = ctx.enter_context

    wq_all = e(nc.sbuf_tensor("wq_all", [P, NCH * 3072], BF16))
    wo_all = e(nc.sbuf_tensor("wo_all", [P, NCH * 1024], BF16))
    wrT = e(nc.sbuf_tensor("wrT", [P, NCH * 1024], BF16))
    wkT = e(nc.sbuf_tensor("wkT", [P, NCH * 1024], BF16))
    wvT = e(nc.sbuf_tensor("wvT", [P, NCH * 1024], BF16))
    cosT = e(nc.sbuf_tensor("cosT", [P, TSEQ], BF16))
    sinT = e(nc.sbuf_tensor("sinT", [P, TSEQ], BF16))
    mask01 = e(nc.sbuf_tensor("mask01", [P, P], BF16))
    smat = e(nc.sbuf_tensor("smat", [P, P], BF16))
    gcol = e(nc.sbuf_tensor("gcol", [P, 8], F32))
    bcol = e(nc.sbuf_tensor("bcol", [P, 8], BF16))
    murm1 = e(nc.sbuf_tensor("murm1", [P, 8], F32))
    mukm1 = e(nc.sbuf_tensor("mukm1", [P, 8], F32))
    murb = e(nc.sbuf_tensor("murb", [P, 8], BF16))
    mukb = e(nc.sbuf_tensor("mukb", [P, 8], BF16))
    ubqk = e(nc.sbuf_tensor("ubqk", [P, 16], F32))
    ubv = e(nc.sbuf_tensor("ubv", [P, 1024], BF16))
    btr = e(nc.sbuf_tensor("btr", [1, 16], F32))
    betab = e(nc.sbuf_tensor("betab", [1, 16], BF16))
    ombb = e(nc.sbuf_tensor("ombb", [1, 16], BF16))
    btile = e(nc.sbuf_tensor("btile", [P, 8], F32))
    omb = e(nc.sbuf_tensor("omb", [P, 16], F32))
    epst = e(nc.sbuf_tensor("epst", [P, 1], F32))
    cm20 = e(nc.sbuf_tensor("cm20", [P, 1], F32))
    ones1 = e(nc.sbuf_tensor("ones1", [1, P], BF16))
    pidx = e(nc.sbuf_tensor("pidx", [P, 1], F32))
    cidx = None
    par = e(nc.sbuf_tensor("par", [P, 2], F32))
    freq64 = e(nc.sbuf_tensor("freq64", [P, 1], F32))
    sti = e(nc.sbuf_tensor("sti", [P, 1], I32))
    gtmp = e(nc.sbuf_tensor("gtmp", [P, 8], F32))

    def perseq(name, cols, dt):
        return [e(nc.sbuf_tensor(f"{name}{s}", [P, cols], dt)) for s in range(NSEQ)]

    xs = perseq("xs", D, F32)
    xhat = perseq("xhat", D, BF16)
    xnfm = perseq("xnfm", D, BF16)
    qkfm = perseq("qkfm", 2 * D, BF16)
    qkbd = perseq("qkbd", 2 * D, BF16)
    skT = perseq("skT", D, BF16)
    vb = perseq("vb", 8 * 130, BF16)
    sdpa = perseq("sdpa", 16 * 65, BF16)
    shadow = perseq("shadow", 8 * 65, BF16)
    zf = perseq("zf", 8, F32)
    hh = perseq("hh", 8, BF16)
    otok = None
    htok = perseq("htok", D, F32)
    hbf = xhat
    otok = xhat
    hfm = perseq("hfm", D, BF16)
    skfm = xnfm
    kafm = perseq("kafm", D, BF16)
    h3 = None
    ofm = e(nc.sbuf_tensor("ofm", [P, D], BF16))
    ut = e(nc.sbuf_tensor("ut", [P, D], BF16))
    h3 = [qkbd[s_].bitcast(F32)[:, 0:1024] for s_ in range(NSEQ)]
    dtile_sv = [sdpa[s_][:, 0:1024] for s_ in range(NSEQ)]
    tanh_t = e(nc.sbuf_tensor("tanh_t", [P, D], F32))
    rop1 = e(nc.sbuf_tensor("rop1", [P, 2 * TOK], F32))
    rop2 = e(nc.sbuf_tensor("rop2", [P, 2 * TOK], F32))
    rop1w = rop1[:].rearrange("p (a f) -> p a f", f=TOK)
    rop2w = rop2[:].rearrange("p (a f) -> p a f", f=TOK)

    wsc = e(nc.sbuf_tensor("wsc", [P, 3072], BF16))
    wscf = wsc.bitcast(F32)
    xr_t = wsc[:, 0:1024]
    xk_t = wsc[:, 1024:2048]
    kact_t = wsc[:, 2048:3072]
    xr = [xr_t, xr_t]
    xk = [xk_t, xk_t]
    kact = [kact_t, kact_t]

    arena = e(nc.sbuf_tensor("arena", [P, 2048], BF16))
    emin = arena[:, 0:1024]
    eexp = arena[:, 1024:2048]
    lnsq = emin
    wraw = arena[:, 0:1536]
    probs_sv = [[arena[:, 0:512], arena[:, 512:1024]] for s_ in range(NSEQ)]
    ffsc = arena.bitcast(F32)[:, 0:512]
    krelu_of = [kafm[s_].bitcast(F32)[:, 0:512] for s_ in range(NSEQ)]

    ubvrow = ut[0:1, 0:1024]
    cidx = ut.bitcast(F32)[:, 0:128]
    ST = e(nc.sbuf_tensor("ST", [P, 9 * 6], F32))

    def stc(seq, lnid, j):
        c = 9 * (lnid + 3 * seq) + j
        return ST[:, c:c + 1]

    REG = e(nc.sbuf_tensor("REG", [P, 5 * 64], F32))
    W64 = 64
    r_nm, r_tt, r_yy, r_aa, r_hx = [REG[:, 64 * j:64 * j + 64] for j in range(5)]
    rc1A = e(nc.sbuf_tensor("rc1A", [P, 4], F32))
    rc1B = e(nc.sbuf_tensor("rc1B", [P, 4], F32))
    rc1 = rc1A
    zred = e(nc.sbuf_tensor("zred", [P, 8], F32))

    pA = e(nc.psum_tensor("pA", [P, 512], F32))
    pB = e(nc.psum_tensor("pB", [P, 512], F32))
    pC = e(nc.psum_tensor("pC", [P, 512], F32))
    pD = e(nc.psum_tensor("pD", [P, 512], F32))
    pE = e(nc.psum_tensor("pE", [P, 512], F32))
    pF = e(nc.psum_tensor("pF", [P, 512], F32))
    memPS = [e(nc.psum_tensor(f"memPS{s}", [P, 512], F32)) for s in range(NSEQ)]

    LOG2PHYS = {"pe": "pe", "dve": "dve", "act": "scalar", "tq": "sync",
                "gps": "gps", "gq": "gps", "sq": "sync", "actd": "scalar"}
    PHYS = ["pe", "dve", "scalar", "gps", "sync"]
    sems = {k: e(nc.semaphore(f"s_{k}")) for k in ("pe", "dve", "act", "gps", "sq", "tq", "gq")}
    sems["actd"] = sems["act"]
    cnt = {k: 0 for k in sems}
    cnt.pop("actd"); cnt["actd"] = None
    waitmark = {ph: {k: 0 for k in sems} for ph in PHYS}
    ops = {ph: [] for ph in PHYS}
    tap_bar = {"v": 0}

    def emit(logical, fn, inc=False, waits=()):
        ph = LOG2PHYS[logical]
        waits = list(waits)
        if tap_bar["v"]:
            waits.append(("sq", tap_bar["v"]))
        wl = []
        for (pr, val) in waits:
            if val and val > waitmark[ph][pr]:
                waitmark[ph][pr] = val
                wl.append((sems[pr], val))
        step = 16 if logical in ("sq", "tq", "gq") else 1
        cl = "act" if logical == "actd" else logical
        sem = None
        if inc:
            cnt[cl] += step
            sem = sems[cl]
        ops[ph].append((wl, fn, sem, step, logical))
        return cnt[cl]

    def now(l):
        return cnt[l]


    def ddrain():
        emit("dve", lambda v: v.drain())

    tap_h = {}

    def tap(name, view, waits):
        th = nc.dram_tensor(f"tap_{name}", list(view.shape), view.dtype,
                            kind="ExternalOutput").ap()
        tap_h[f"tap_{name}"] = str(view.dtype)
        emit("sq", lambda s, th=th, view=view: s.dma_start(out=th, in_=view),
             inc=True, waits=waits)
        tap_bar["v"] = cnt["sq"]

    def maybe_tap(stage, m, view_fn, waits_fn):
        for (tname, tmt) in taps:
            if tname == stage and tmt == m:
                tap(f"{stage}_{m}", view_fn(), waits_fn())

    CK = {}
    xload = {}
    done = {}

    def load_x(m, w=()):
        seq, blk = m % 2, m // 2
        xload[m] = emit("sq", lambda s, seq=seq, blk=blk: s.dma_start(
            out=xs[seq][:], in_=x_h[seq, TOK * blk:TOK * (blk + 1), :]), inc=True,
            waits=list(w))

    load_x(0)
    load_x(1)

    def vec_to_cols(vh, dest, first_waits=()):
        mk = 0
        for c in range(8):
            src = vh[128 * c:128 * c + 128].rearrange("(a b) -> a b", b=1)
            mk = emit("sq", lambda s, d=dest[:, c:c + 1], src=src: s.dma_start(out=d, in_=src),
                      inc=True, waits=(list(first_waits) if c == 0 else []))
        return mk

    CK["g"] = vec_to_cols(g_h, gcol)
    CK["b"] = vec_to_cols(b_h, gtmp)
    CK["bc"] = emit("dve", lambda v: v.tensor_copy(out=bcol[:], in_=gtmp[:]), inc=True,
                    waits=[("sq", CK["b"])])
    CK["mur"] = vec_to_cols(mur_h, gtmp, first_waits=[("dve", CK["bc"])])
    emit("dve", lambda v: v.tensor_scalar(out=murm1[:], in0=gtmp[:], scalar1=1.0,
                                          scalar2=-1.0, op0=ALU.mult, op1=ALU.add),
         waits=[("sq", CK["mur"])])
    ddrain()
    CK["m1"] = emit("dve", lambda v: v.tensor_copy(out=murb[:], in_=murm1[:]), inc=True)
    CK["muk"] = vec_to_cols(muk_h, gtmp, first_waits=[("dve", CK["m1"])])
    emit("dve", lambda v: v.tensor_scalar(out=mukm1[:], in0=gtmp[:], scalar1=1.0,
                                          scalar2=-1.0, op0=ALU.mult, op1=ALU.add),
         waits=[("sq", CK["muk"])])
    ddrain()
    CK["m2"] = emit("dve", lambda v: v.tensor_copy(out=mukb[:], in_=mukm1[:]), inc=True)
    CK["beta"] = emit("sq", lambda s: s.dma_start(out=btr[:], in_=beta_h), inc=True)

    emit("gps", lambda g: g.iota(pidx[:], pattern=[[0, 1]], channel_multiplier=1,
                                 allow_small_or_imprecise_dtypes=True))
    emit("gps", lambda g: g.iota(cidx[:], pattern=[[1, P]], channel_multiplier=0,
                                 allow_small_or_imprecise_dtypes=True))
    emit("gps", lambda g: g.memset(mask01[:], 1.0))
    mview = mask01.rearrange("p (a b) -> p a b", a=4)
    emit("gps", lambda g: g.affine_select(out=mview, in_=mview, pattern=[[-32, 4], [0, 32]],
                                          compare_op=ALU.is_ge, fill=0.0, base=0,
                                          channel_multiplier=1))
    emit("gps", lambda g: g.affine_select(out=mview, in_=mview, pattern=[[32, 4], [0, 32]],
                                          compare_op=ALU.is_ge, fill=0.0, base=31,
                                          channel_multiplier=-1))
    emit("gps", lambda g: g.affine_select(out=mview, in_=mview, pattern=[[32, 4], [1, 32]],
                                          compare_op=ALU.is_ge, fill=0.0, base=0,
                                          channel_multiplier=-1))
    emit("gps", lambda g: g.memset(ones1[:], 1.0))
    for s_ in range(NSEQ):
        vbv = vb[s_].rearrange("p (c f) -> p c f", f=130)
        emit("gps", lambda g, vbv=vbv: g.memset(vbv[:, :, 64:65], 1.0))
        emit("gps", lambda g, vbv=vbv: g.memset(vbv[:, :, 129:130], 1.0))
        emit("gps", lambda g, s_=s_: g.memset(hh[s_][:], 0.0))
        emit("gps", lambda g, s_=s_: g.memset(zf[s_][:], 1e-6))
        shv = shadow[s_].rearrange("p (c f) -> p c f", f=65)
        emit("gps", lambda g, s_=s_: g.memset(shadow[s_][:], 0.0))
        emit("gps", lambda g, shv=shv: g.memset(shv[:, :, 64:65], 1e-6))
    emit("gps", lambda g: g.memset(cm20[:], -EXPB))
    emit("gps", lambda g: g.memset(epst[:], 0.0))
    emit("gps", lambda g: g.memset(epst[0:32, :], 1e-6))
    CK["gps0"] = emit("gps", lambda g: g.memset(epst[64:96, :], 1e-6), inc=True)
    emit("dve", lambda v: v.memset(memPS[0][:], 0.0))
    CK["pszero"] = emit("dve", lambda v: v.memset(memPS[1][:], 0.0), inc=True)

    st0 = stc(0, 0, 6)
    emit("dve", lambda v: v.tensor_scalar(out=st0, in0=pidx[:], scalar1=0.5, scalar2=-0.25,
                                          op0=ALU.mult, op1=ALU.add),
         waits=[("gps", CK["gps0"])])
    ddrain()
    emit("dve", lambda v: v.tensor_copy(out=sti[:], in_=st0))
    ddrain()
    CK["flr"] = emit("dve", lambda v: v.tensor_copy(out=st0, in_=sti[:]), inc=True)
    ddrain()
    CK["freq"] = emit("act", lambda a: a.activation(out=freq64[:], in_=st0, func=AF.Exp,
                                                    scale=-LOG1E4 / 32.0), inc=True,
                      waits=[("dve", CK["flr"])])
    emit("dve", lambda v: v.tensor_scalar(out=par[:, 0:1], in0=st0, scalar1=-2.0, scalar2=None,
                                          op0=ALU.mult))
    ddrain()
    emit("dve", lambda v: v.tensor_tensor(out=par[:, 0:1], in0=par[:, 0:1], in1=pidx[:],
                                          op=ALU.add))
    ddrain()
    emit("dve", lambda v: v.tensor_scalar(out=par[:, 1:2], in0=par[:, 0:1], scalar1=-1.0,
                                          scalar2=1.0, op0=ALU.mult, op1=ALU.add))
    ddrain()
    emit("dve", lambda v: v.tensor_scalar(out=cidx[:], in0=cidx[:], scalar1=pidx[:], scalar2=None,
                                          op0=ALU.subtract))
    emit("dve", lambda v: v.tensor_scalar(out=smat[:], in0=cidx[:], scalar1=1.0, scalar2=None,
                                          op0=ALU.is_equal))
    emit("dve", lambda v: v.tensor_scalar(out=smat[:], in0=smat[:], scalar1=par[:, 1:2],
                                          scalar2=None, op0=ALU.mult))
    emit("dve", lambda v: v.tensor_scalar(out=lnsq[:, 0:128], in0=cidx[:], scalar1=-1.0,
                                          scalar2=None, op0=ALU.is_equal))
    emit("dve", lambda v: v.tensor_scalar(out=lnsq[:, 0:128], in0=lnsq[:, 0:128],
                                          scalar1=par[:, 0:1], scalar2=None, op0=ALU.mult))
    CK["smat"] = emit("dve", lambda v: v.tensor_tensor(out=smat[:], in0=smat[:],
                                                       in1=lnsq[:, 0:128], op=ALU.subtract),
                      inc=True)

    wq_f = wq_all.bitcast(F32)
    pos64 = wq_f[0:64, 0:2048]
    ang64 = wq_f[0:64, 2048:4096]
    u_s = wq_f[0:64, 4096:6144]
    n_f = wq_f[0:64, 6144:8192]
    n_i = wq_f[0:64, 8192:10240].bitcast(I32)
    CK["pos"] = emit("gps", lambda g: g.iota(pos64, pattern=[[1, 2048]], channel_multiplier=0,
                                             allow_small_or_imprecise_dtypes=True), inc=True)
    emit("dve", lambda v: v.tensor_scalar(out=ang64, in0=pos64, scalar1=freq64[0:64, 0:1],
                                          scalar2=None, op0=ALU.mult),
         waits=[("gps", CK["pos"]), ("act", CK["freq"])])

    def table_build(phase, dest):
        emit("dve", lambda v: v.tensor_scalar(out=u_s, in0=ang64, scalar1=1.0 / TWO_PI,
                                              scalar2=phase, op0=ALU.mult, op1=ALU.add))
        emit("dve", lambda v: v.tensor_copy(out=n_i, in_=u_s))
        emit("dve", lambda v: v.tensor_copy(out=n_f, in_=n_i))
        cc = emit("dve", lambda v: v.tensor_tensor(out=u_s, in0=u_s, in1=n_f, op=ALU.subtract),
                  inc=True)
        ca = emit("act", lambda a: a.activation(out=dest[0:64, :], in_=u_s, func=AF.Sin,
                                                scale=TWO_PI), inc=True, waits=[("dve", cc)])
        emit("dve", lambda v: v.tensor_copy(out=dest[64:128, :], in_=dest[0:64, :]),
             waits=[("act", ca)])

    table_build(0.25, cosT)
    table_build(0.0, sinT)
    CK["tables"] = emit("dve", lambda v: None, inc=True)

    ca = emit("act", lambda a: a.activation(out=btr[:], in_=btr[:], func=AF.Tanh, scale=0.5),
              inc=True, waits=[("sq", CK["beta"])])
    emit("dve", lambda v: v.tensor_scalar(out=betab[:], in0=btr[:], scalar1=0.5, scalar2=0.5,
                                          op0=ALU.mult, op1=ALU.add), waits=[("act", ca)])
    CK["betab"] = emit("dve", lambda v: v.tensor_scalar(out=ombb[:], in0=btr[:], scalar1=-0.5,
                                                        scalar2=0.5, op0=ALU.mult, op1=ALU.add),
                       inc=True)
    emit("pe", lambda t: t.matmul(out=pA[:, 0:16], lhsT=ones1[:], rhs=ombb[:],
                                  start=True, stop=True),
         waits=[("dve", CK["betab"]), ("gps", CK["gps0"])])
    CK["pe_beta"] = emit("pe", lambda t: t.matmul(out=pB[0:32, 0:16], lhsT=ones1[:, 0:32],
                                                  rhs=betab[:], start=True, stop=True), inc=True)
    emit("dve", lambda v: v.tensor_copy(out=omb[:], in_=pA[:, 0:16]),
         waits=[("pe", CK["pe_beta"])])
    emit("dve", lambda v: v.memset(btile[:], 1.0))
    ddrain()
    emit("dve", lambda v: v.tensor_copy(out=btile[0:32, :], in_=pB[0:32, 0:16:2]))
    CK["btile"] = emit("dve", lambda v: v.tensor_copy(out=btile[64:96, :], in_=pB[0:32, 1:16:2]),
                      inc=True)

    prev_sc = 0
    prev_ca = 0
    last_pe = 0
    for c in range(NCH):
        for hf in range(2):
            wl = emit("sq", lambda s, c=c, hf=hf: s.dma_start(
                out=wscf[:, 0:1536],
                in_=wqkv_h[128 * c:128 * c + 128, 1536 * hf:1536 * (hf + 1)]),
                inc=True, waits=[("dve", max(CK["tables"], prev_sc)), ("act", prev_ca),
                                 ("pe", last_pe)])
            prev_ca = emit("act", lambda a: a.activation(out=wraw, in_=wscf[:, 0:1536],
                                                         func=AF.Copy), inc=True,
                           waits=[("sq", wl)])
            prev_sc = emit("dve", lambda v, c=c, hf=hf: v.tensor_scalar(
                out=wq_all[:, 3072 * c + 1536 * hf:3072 * c + 1536 * (hf + 1)],
                in0=wscf[:, 0:1536], scalar1=gcol[:, c:c + 1], scalar2=None,
                op0=ALU.mult), inc=True,
                waits=[("sq", wl), ("sq", CK["g"]), ("dve", CK["tables"])])
            if hf == 0:
                mcs = range(0, 12)
            else:
                mcs = range(12, 16)
            for mc in mcs:
                off = 128 * mc - 1536 * hf
                emit("pe", lambda t, c=c, mc=mc, off=off: t.matmul(
                    out=pC[:, 32 + mc:33 + mc], lhsT=wraw[:, off:off + 128],
                    rhs=bcol[:, c:c + 1], start=(c == 0 and mc == 0),
                    stop=(c == NCH - 1)),
                    waits=[("act", prev_ca), ("dve", CK["bc"]),
                           ("dve", CK["btile"] if c == 0 else 0)])
            if hf == 1:
                emit("pe", lambda t, c=c: t.matmul(out=pD[0:1, 0:512], lhsT=bcol[:, c:c + 1],
                                                   rhs=wraw[:, 512:1024], start=(c == 0),
                                                   stop=(c == NCH - 1)))
                last_pe = emit("pe", lambda t, c=c: t.matmul(
                    out=pE[0:1, 0:512], lhsT=bcol[:, c:c + 1],
                    rhs=wraw[:, 1024:1536], start=(c == 0), stop=(c == NCH - 1)), inc=True)
    CK["pe_ub"] = last_pe
    emit("dve", lambda v: v.tensor_copy(out=ubqk[:], in_=pC[:, 32:48]),
         waits=[("pe", CK["pe_ub"])])
    emit("dve", lambda v: v.tensor_copy(out=ubvrow[0:1, 0:512], in_=pD[0:1, 0:512]))
    CK["ubvr"] = emit("dve", lambda v: v.tensor_copy(out=ubvrow[0:1, 512:1024],
                                                     in_=pE[0:1, 0:512]), inc=True)
    emit("pe", lambda t: t.matmul(out=pD[:, 0:512], lhsT=ones1[:], rhs=ubvrow[0:1, 0:512],
                                  start=True, stop=True), waits=[("dve", CK["ubvr"])])
    CK["pe_ubv"] = emit("pe", lambda t: t.matmul(out=pE[:, 0:512], lhsT=ones1[:],
                                                 rhs=ubvrow[0:1, 512:1024], start=True,
                                                 stop=True), inc=True)
    emit("dve", lambda v: v.tensor_copy(out=ubv[:, 0:512], in_=pD[:, 0:512]),
         waits=[("pe", CK["pe_ubv"])])
    CK["ubv"] = emit("dve", lambda v: v.tensor_copy(out=ubv[:, 512:1024], in_=pE[:, 0:512]),
                     inc=True)

    prev_gq = 0
    for c in range(NCH):
        prev_gq = emit("gq", lambda g, c=c: g.dma_start(
            out=wo_all[:, 1024 * c:1024 * (c + 1)], in_=wout_h[128 * c:128 * c + 128, :]),
            inc=True, waits=[("gps", CK["gps0"])])
    prev_tq = 0
    prev_dv = 0
    stg0 = qkbd[0][:, 0:1024]
    stg1 = qkbd[0][:, 1024:2048]
    for (whh, dstT, halfscale) in ((wr_h, wrT, False), (wk_h, wkT, False), (wv_h, wvT, True)):
        for dc in range(NCH):
            wl = emit("gq", lambda g, whh=whh, dc=dc: g.dma_start(
                out=stg0, in_=whh[128 * dc:128 * dc + 128, :]), inc=True,
                waits=[("tq", prev_tq), ("dve", prev_dv)])
            src = stg0
            if halfscale:
                prev_dv = emit("dve", lambda v: v.tensor_scalar(
                    out=stg1, in0=stg0, scalar1=0.5, scalar2=None, op0=ALU.mult), inc=True,
                    waits=[("gq", wl)])
                src = stg1
            dst = dstT.rearrange("p (a b f) -> p a b f", a=8, f=128)[:, :, dc, :]
            prev_tq = emit("tq", lambda s, dst=dst, src=src.rearrange(
                "p (c f) -> p c f", c=8): s.dma_start(out=dst, in_=src, transpose=True),
                inc=True, waits=([("gq", wl)] if not halfscale else [("dve", prev_dv)]))
    CK["wffn_dve"] = max(prev_sc, prev_dv)
    CK["wffn_gq"] = prev_gq
    CK["wffn_tq"] = prev_tq
    CK["wsc_free_dve"] = prev_sc
    CK["wsc_free_act"] = prev_ca
    CK["arena_free_pe"] = CK["pe_ubv"]


    def dget(key, default=0):
        return done.get(key, default)

    pe_big = [pA, pB, pC, pD]
    big_i = [0]
    big_evac_d = {id(pA): ("dve", CK["btile"]), id(pB): ("dve", CK["btile"]),
                  id(pC): ("dve", CK["ubv"]), id(pD): ("dve", CK["ubv"]),
                  id(pE): ("dve", CK["ubv"]), id(pF): ("dve", 0)}

    class _BE:
        def __getitem__(self, b):
            return big_evac_d[id(b)]

        def __setitem__(self, b, v):
            big_evac_d[id(b)] = v

    big_evac = _BE()

    def next_big():
        b = pe_big[big_i[0] % 2]
        big_i[0] += 1
        return b

    def rstd_chain(lnid, seq, steps, xtr_waits=()):
            r1, sq_ = stc(seq, lnid, 0), stc(seq, lnid, 1)
            emit("dve", lambda v: v.tensor_scalar(
                out=r_nm, in0=r1.broadcast_to([P, W64]), scalar1=-1.0 / D,
                scalar2=None, op0=ALU.mult), waits=list(xtr_waits))
            emit("dve", lambda v: v.tensor_scalar(
                out=r_tt, in0=sq_.broadcast_to([P, W64]), scalar1=1.0 / D,
                scalar2=1e-5, op0=ALU.mult, op1=ALU.add))
            emit("dve", lambda v: v.tensor_tensor(out=r_aa, in0=r_nm, in1=r_nm, op=ALU.mult))
            emit("dve", lambda v: v.tensor_tensor(out=r_tt, in0=r_tt, in1=r_aa,
                                                  op=ALU.subtract))
            emit("dve", lambda v: v.tensor_scalar(out=r_aa, in0=r_tt, scalar1=0.5, scalar2=0.5,
                                                  op0=ALU.mult, op1=ALU.add))
            emit("dve", lambda v: v.reciprocal(out=r_yy, in_=r_aa))
            for _ in range(steps):
                emit("dve", lambda v: v.tensor_tensor(out=r_aa, in0=r_yy, in1=r_yy,
                                                      op=ALU.mult))
                emit("dve", lambda v: v.tensor_tensor(out=r_hx, in0=r_aa, in1=r_tt,
                                                      op=ALU.mult))
                emit("dve", lambda v: v.tensor_scalar(out=r_aa, in0=r_hx, scalar1=-0.5,
                                                      scalar2=1.5, op0=ALU.mult, op1=ALU.add))
                emit("dve", lambda v: v.tensor_tensor(out=r_yy, in0=r_yy, in1=r_aa,
                                                      op=ALU.mult))

    def emit_F1(m):
        seq, blk = m % 2, m // 2
        r1, sq_ = stc(seq, 0, 0), stc(seq, 0, 1)
        wx = xload[m]
        emit("act", lambda a, seq=seq, r1=r1: a.activation(
            out=xhat[seq][:], in_=xs[seq][:], func=AF.Copy, accum_out=r1),
            waits=[("sq", wx),
                   ("tq", max(dget(("ht", m - 2)), dget(("xt", m - 2))))])
        c_sqa = emit("actd", lambda a, seq=seq, sq_=sq_: a.activation(
            out=xhat[seq][:], in_=xs[seq][:], func=AF.Square, accum_out=sq_),
            inc=True)

        rstd_chain(0, seq, 2, xtr_waits=[("act", c_sqa)])
        c_xh = emit("dve", lambda v, seq=seq: v.tensor_scalar(
            out=xhat[seq][:], in0=xs[seq][:], scalar1=r_nm[:, 0:1], scalar2=r_yy[:, 0:1],
            op0=ALU.add, op1=ALU.mult), inc=True, waits=[("tq", dget(("ht", m - 2)))])
        maybe_tap("xhat", m, lambda: xhat[seq][:], lambda: [("dve", c_xh)])

        if stop_after == "ln1":
            return True
        done[("xt", m)] = emit("tq", lambda s, seq=seq: s.dma_start(
            out=xnfm[seq].rearrange("p (c f) -> p c f", c=8),
            in_=xhat[seq].rearrange("p (c f) -> p c f", c=8), transpose=True), inc=True,
            waits=[("dve", c_xh), ("pe", dget(("qkv_done", m - 2))),
                   ("tq", dget(("skt", m - 2)))])
        maybe_tap("xnfm", m, lambda: xnfm[seq][:], lambda: [("tq", done[("xt", m)])])

        return stop_after == "xt"

    def emit_F2qk(m, mp0, mp1):
        seq, blk = m % 2, m // 2
        cev = 0
        for mp in range(mp0, mp1):
            bank = next_big()
            beng, bval = big_evac[bank]
            for half in range(2):
                mc = 2 * mp + half
                for kc in range(NCH):
                    wts = []
                    if half == 0 and kc == 0:
                        wts = [("tq", done[("xt", m)]), (beng, bval)]
                        if m < 2 and mp == 0:
                            wts.append(("dve", CK["wffn_dve"]))
                    emit("pe", lambda t, mc=mc, kc=kc, half=half, seq=seq, bank=bank: t.matmul(
                        out=bank[:, 128 * half:128 * (half + 1)],
                        lhsT=wq_all[:, 3072 * kc + 128 * mc:3072 * kc + 128 * (mc + 1)],
                        rhs=xnfm[seq][:, 128 * kc:128 * (kc + 1)],
                        start=(kc == 0), stop=(kc == NCH - 1)), waits=wts)
            cpe = emit("pe", lambda t: None, inc=True)
            ubv2 = ubqk[:, 2 * mp:2 * mp + 2].unsqueeze(2).broadcast_to([P, 2, 128])
            cev = emit("dve", lambda v, mp=mp, seq=seq, bank=bank, ubv2=ubv2: v.tensor_tensor(
                out=qkfm[seq][:, 256 * mp:256 * (mp + 1)].rearrange("p (a f) -> p a f", f=128),
                in0=bank[:, 0:256].rearrange("p (a f) -> p a f", f=128),
                in1=ubv2, op=ALU.add), inc=True,
                waits=[("pe", cpe), ("pe", dget(("scores_done", m - 2)))])
            big_evac[bank] = ("dve", cev)
        return cev

    def emit_F2a(m):
        emit_F2qk(m, 0, 4)
        return False

    def emit_F2b(m):
        done[("qk_evac", m)] = emit_F2qk(m, 4, 8)
        return False

    def emit_F2c(m):
        seq, blk = m % 2, m // 2
        for nh in range(2):
            bank = next_big()
            beng, bval = big_evac[bank]
            for kc in range(NCH):
                wts = []
                if kc == 0:
                    wts = [("tq", done[("xt", m)]), (beng, bval)]
                emit("pe", lambda t, nh=nh, kc=kc, seq=seq, bank=bank: t.matmul(
                    out=bank[:, 0:512], lhsT=xnfm[seq][:, 128 * kc:128 * (kc + 1)],
                    rhs=wq_all[:, 3072 * kc + 2048 + 512 * nh:3072 * kc + 2048 + 512 * (nh + 1)],
                    start=(kc == 0), stop=(kc == NCH - 1)), waits=wts)
            cpe = emit("pe", lambda t: None, inc=True)
            vdst = vb[seq].rearrange("p (c g f) -> p c g f", g=2, f=65)[:, 4 * nh:4 * (nh + 1), :, 0:64]
            vsrc = bank[:, 0:512].rearrange("p (c g f) -> p c g f", c=4, g=2)
            usrc = ubv[:, 512 * nh:512 * (nh + 1)].rearrange("p (c g f) -> p c g f", c=4, g=2)
            cev = emit("dve", lambda v, vdst=vdst, vsrc=vsrc, usrc=usrc: v.tensor_tensor(
                out=vdst, in0=vsrc, in1=usrc, op=ALU.add), inc=True,
                waits=[("pe", cpe), ("dve", dget(("usub", m - 2, SEGB - 1))),
                       ("pe", dget(("sdpa_done", m - 2))), ("dve", CK["ubv"] if m < 2 else 0)])
            big_evac[bank] = ("dve", cev)
        done[("v_evac", m)] = cev
        done[("qkv_done", m)] = now("pe")
        maybe_tap("qkfm", m, lambda: qkfm[seq][:], lambda: [("dve", done[("qk_evac", m)])])
        maybe_tap("vb", m, lambda: vb[seq][:], lambda: [("dve", done[("v_evac", m)])])

        return stop_after == "qkv"

    def emit_F2(m):
        return emit_F2a(m) or emit_F2b(m) or emit_F2c(m)

    def emit_F3(m):
        seq, blk = m % 2, m // 2
        c_expq = emit("act", lambda a, seq=seq: a.activation(out=eexp[:], in_=qkfm[seq][:, 0:1024],
                                                             func=AF.Exp), inc=True,
                      waits=[("dve", done[("qk_evac", m)]), ("dve", dget(("elu_stt", m - 1))),
                             ("pe", CK["arena_free_pe"]),
                             ("dve", dget(("mix_done", m - 1)))])
        emit("dve", lambda v: v.tensor_scalar(out=emin[:], in0=eexp[:], scalar1=1.0,
                                              scalar2=None, op0=ALU.min),
             waits=[("act", c_expq), ("pe", dget(("sdpa_done", m - 1)))])
        qdst = qkbd[seq].rearrange("p (c s g f) -> p c s g f", c=8, s=4, g=2)[:, :, :, 0, :]
        c_sttq = emit("dve", lambda v, seq=seq, qdst=qdst: v.scalar_tensor_tensor(
            out=qdst, in0=qkfm[seq][:, 0:1024].rearrange("p (c s f) -> p c s f", c=8, f=32),
            scalar=0.0, in1=emin.rearrange("p (c s f) -> p c s f", c=8, f=32),
            op0=ALU.max, op1=ALU.add), inc=True,
            waits=[("pe", dget(("phb_done", m - 2))),
                   ("tq", CK["wffn_tq"] if m < 2 else 0)])
        c_expk = emit("act", lambda a, seq=seq: a.activation(out=eexp[:],
                                                             in_=qkfm[seq][:, 1024:2048],
                                                             func=AF.Exp), inc=True,
                      waits=[("dve", c_sttq)])
        done[("elu_exp", m)] = c_expk
        c_min = c_expk
        emit("dve", lambda v: v.tensor_scalar(out=emin[:], in0=eexp[:], scalar1=1.0,
                                              scalar2=None, op0=ALU.min),
             waits=[("act", c_expk)])
        emit("dve", lambda v, seq=seq: v.scalar_tensor_tensor(
            out=skfm[seq][:], in0=qkfm[seq][:, 1024:2048], scalar=0.0,
            in1=emin[:], op0=ALU.max, op1=ALU.add),
            waits=[("pe", done[("qkv_done", m)])])
        kdst = qkbd[seq].rearrange("p (c s g f) -> p c s g f", c=8, s=4, g=2)[:, :, :, 1, :]
        c3 = emit("dve", lambda v, seq=seq, kdst=kdst: v.tensor_copy(
            out=kdst, in_=skfm[seq].rearrange("p (c s f) -> p c s f", c=8, f=32)), inc=True,
            waits=[("tq", CK["wffn_tq"] if m < 2 else 0)])
        done[("elu_stt", m)] = c3
        maybe_tap("qkbd", m, lambda: qkbd[seq][:], lambda: [("dve", c3)])
        done[("skt", m)] = emit("tq", lambda s, seq=seq: s.dma_start(
            out=skT[seq].rearrange("p (c f) -> p c f", c=8),
            in_=skfm[seq].rearrange("p (c f) -> p c f", c=8), transpose=True), inc=True,
            waits=[("dve", c3), ("pe", dget(("phb_done", m - 2)))])

        return stop_after == "elu"

    def emit_F4(m):
        seq, blk = m % 2, m // 2
        c_min = done[("elu_exp", m)]
        cosv = cosT[:, TOK * blk:TOK * (blk + 1)]
        sinv = sinT[:, TOK * blk:TOK * (blk + 1)]
        prev_rope_dve = 0
        for half in range(2):
            base = 1024 * half
            for cp in range(4):
                bank = next_big()
                beng, bval = big_evac[bank]
                for half2 in range(2):
                    c = 2 * cp + half2
                    cpe = emit("pe", lambda t, seq=seq, base=base, c=c, half2=half2, bank=bank: t.matmul(
                        out=bank[:, 128 * half2:128 * (half2 + 1)], lhsT=smat[:],
                        rhs=qkfm[seq][:, base + 128 * c:base + 128 * (c + 1)],
                        start=True, stop=True), inc=(half2 == 1),
                        waits=([("dve", done[("elu_stt", m)]), (beng, bval),
                                ("act", c_min), ("dve", CK["smat"])] if half2 == 0 else []))
                c2 = 2 * cp
                cosv2 = cosv.unsqueeze(1).broadcast_to([P, 2, TOK])
                sinv2 = sinv.unsqueeze(1).broadcast_to([P, 2, TOK])
                emit("dve", lambda v, bank=bank, sinv2=sinv2: v.tensor_tensor(
                    out=rop2[:].rearrange("p (a f) -> p a f", f=TOK) if False else rop2w,
                    in0=bank[:, 0:256].rearrange("p (a f) -> p a f", f=TOK),
                    in1=sinv2, op=ALU.mult), waits=[("pe", cpe)])
                emit("dve", lambda v, seq=seq, base=base, c2=c2, cosv2=cosv2: v.tensor_tensor(
                    out=rop1w,
                    in0=qkfm[seq][:, base + 128 * c2:base + 128 * (c2 + 2)].rearrange(
                        "p (a f) -> p a f", f=TOK),
                    in1=cosv2, op=ALU.mult))
                cev = emit("dve", lambda v, seq=seq, base=base, c2=c2: v.tensor_tensor(
                    out=qkfm[seq][:, base + 128 * c2:base + 128 * (c2 + 2)].rearrange(
                        "p (a f) -> p a f", f=TOK),
                    in0=rop2w, in1=rop1w, op=ALU.add), inc=True)
                big_evac[bank] = ("dve", cev)
                prev_rope_dve = cev
        done[("rope", m)] = prev_rope_dve
        maybe_tap("qkrope", m, lambda: qkfm[seq][:], lambda: [("dve", done[("rope", m)])])

        return stop_after == "rope"

    pa_state = {}

    def pa_lvl_of():
        return {"pa_scores": 1, "pa_exp": 2, "pa_mask": 3, "pa_sdpa": 4,
                "pa_recip": 5}.get(stop_after, 99)

    def emit_Ascores(m, r):
        seq, blk = m % 2, m // 2
        hf = r % 2
        c0 = 4 * (r // 2)
        sbank = pC if r % 2 == 0 else pD
        sbeng, sbval = big_evac[sbank]
        for ci in range(4):
            ch = c0 + ci
            wts = []
            if ci == 0:
                wts = [("dve", done[("rope", m)]), (sbeng, sbval),
                       ("act", dget(("exp_done", m, r - 2), dget(("exp_done", m - 1, r + 2))))]
            cpe = emit("pe", lambda t, seq=seq, ch=ch, hf=hf, ci=ci, sbank=sbank: t.matmul(
                out=sbank[:, 128 * ci:128 * (ci + 1)],
                lhsT=qkfm[seq][64 * hf:64 * (hf + 1), 1024 + 128 * ch:1024 + 128 * (ch + 1)],
                rhs=qkfm[seq][64 * hf:64 * (hf + 1), 128 * ch:128 * (ch + 1)],
                start=True, stop=True, tile_position=(64 * hf, 0)),
                inc=(ci == 3), waits=wts)
        done[("scorespe", m, r)] = cpe

    def emit_Ar(m, r):
        seq, blk = m % 2, m // 2
        pa_lvl = pa_lvl_of()
        probs = probs_sv[seq]
        pend_evac = pa_state.setdefault(m, [None])
        if True:
            hf = r % 2
            c0 = 4 * (r // 2)
            sbank = pC if r % 2 == 0 else pD
            obank = pE if r % 2 == 0 else pF
            rcX = rc1A if r % 2 == 0 else rc1B
            cpe = done[("scorespe", m, r)]
            if pa_lvl < 2:
                return
            c_ex = emit("act", lambda a, r=r, sbank=sbank: a.activation(
                out=probs[r % 2][:], in_=sbank[:], func=AF.Exp, scale=0.125, bias=cm20[:]),
                inc=True, waits=[("pe", cpe),
                                 ("dve", dget(("msk_done", m, r - 2),
                                              dget(("msk_done", m - 1, r + 2)))),
                                 ("dve", done[("elu_stt", m)] if r == 0 else 0),
                                 ("pe", CK["arena_free_pe"] if m < 2 else 0),
                                 ("gps", CK["gps0"] if m < 2 else 0)])
            done[("exp_done", m, r)] = c_ex
            big_evac[sbank] = ("act", c_ex)
            if pa_lvl < 3:
                return
            c_mk = emit("dve", lambda v, r=r: v.tensor_tensor(
                out=probs[r % 2].rearrange("p (a f) -> p a f", f=128),
                in0=probs[r % 2].rearrange("p (a f) -> p a f", f=128),
                in1=mask01[:].unsqueeze(1).broadcast_to([P, 4, P]),
                op=ALU.mult), inc=True, waits=[("act", c_ex),
                                               ("gps", CK["gps0"] if m < 2 else 0)])
            done[("msk_done", m, r)] = c_mk
            if pa_lvl < 4:
                return
            obeng, obval = big_evac[obank]
            for ci in range(4):
                ch = c0 + ci
                wts = []
                if ci == 0:
                    wts = [("dve", c_mk), ("dve", done[("v_evac", m)]), (obeng, obval),
                           ("dve", dget(("nds_mult", m - 1, SEGB - 1))),
                           ("gps", CK["gps0"] if m < 2 else 0)]
                cpe = emit("pe", lambda t, seq=seq, ch=ch, hf=hf, ci=ci, r=r, obank=obank: t.matmul(
                    out=obank[:, 65 * ci:65 * (ci + 1)],
                    lhsT=probs[r % 2][:, 128 * ci:128 * (ci + 1)],
                    rhs=vb[seq][:, 130 * ch + 65 * hf:130 * ch + 65 * (hf + 1)],
                    start=True, stop=True), inc=(ci == 3), waits=wts)
            done[("sdpape", m, r)] = cpe
            if r < 2:
                emit_Ascores(m, r + 2)
            if pa_lvl < 5:
                return
            emit("dve", lambda v, obank=obank, rcX=rcX: v.reciprocal(
                out=rcX[:], in_=obank[:, 0:260].rearrange("p (a b) -> p a b", b=65)[:, :, 64]),
                waits=[("pe", cpe), ("dve", CK["btile"] if m == 0 else 0)])
            if pend_evac[0] is None:
                ddrain()
            else:
                (pr, prcX, probank) = pend_evac[0]
                sdst = sdpa[seq].rearrange("p (c g b) -> p c g b", g=2, b=65)[
                    :, 4 * (pr // 2):4 * (pr // 2) + 4, pr % 2, :]
                c_se = emit("dve", lambda v, sdst=sdst, probank=probank, prcX=prcX: v.tensor_tensor(
                    out=sdst, in0=probank[:, 0:260].rearrange("p (a b) -> p a b", b=65),
                    in1=prcX[:].unsqueeze(2).broadcast_to([P, 4, 65]), op=ALU.mult),
                    inc=True, waits=[("dve", dget(("blend", m - 2, SEGB - 1)))])
                done[("sdpa_evac", m, pr)] = c_se
                big_evac[probank] = ("dve", c_se)
            ombv = omb.rearrange("p (c g) -> p c g", g=2)[:, c0:c0 + 4, hf]
            emit("dve", lambda v, ombv=ombv, rcX=rcX: v.tensor_tensor(
                out=rcX[:], in0=rcX[:], in1=ombv, op=ALU.mult))
            pend_evac[0] = (r, rcX, obank)

    def emit_Aflush(m):
        seq, blk = m % 2, m // 2
        pa_lvl = pa_lvl_of()
        pend_evac = pa_state.setdefault(m, [None])
        done[("sdpa_done", m)] = now("pe")
        done[("scores_done", m)] = now("pe")
        if pa_lvl >= 5 and pend_evac[0] is not None:
            (pr, prcX, probank) = pend_evac[0]
            ddrain()
            sdst = sdpa[seq].rearrange("p (c g b) -> p c g b", g=2, b=65)[
                :, 4 * (pr // 2):4 * (pr // 2) + 4, pr % 2, :]
            c_se = emit("dve", lambda v, sdst=sdst, probank=probank, prcX=prcX: v.tensor_tensor(
                out=sdst, in0=probank[:, 0:260].rearrange("p (a b) -> p a b", b=65),
                in1=prcX[:].unsqueeze(2).broadcast_to([P, 4, 65]), op=ALU.mult),
                inc=True, waits=[("dve", dget(("blend", m - 2, SEGB - 1)))])
            done[("sdpa_evac", m, pr)] = c_se
            big_evac[probank] = ("dve", c_se)
        if pa_lvl >= 99:
            maybe_tap("sdpa", m, lambda: sdpa[seq][:],
                      lambda: [("dve", done[("sdpa_evac", m, 3)])])

    def emit_As(m, s_):
        seq, blk = m % 2, m // 2
        if True:
            gseg = SEGB * blk + s_
            r0 = 32 * s_
            shdst = shadow[seq].rearrange("p (c f) -> p c f", f=65)[:, :, 0:64]
            if gseg > 0:
                c_sh = emit("act", lambda a, seq=seq, shdst=shdst: a.activation(
                    out=shdst, in_=memPS[seq].rearrange("p (c f) -> p c f", f=64),
                    func=AF.Copy), inc=True,
                    waits=[("pe", dget(("delta", m, s_ - 1), dget(("delta", m - 2, SEGB - 1))))])
            else:
                c_sh = now("act")
            done[("shadow", m, s_)] = c_sh
            emit("dve", lambda v, seq=seq, s_=s_: v.tensor_reduce(
                out=zred[:], in_=qkbd[seq].rearrange("p (c s g f) -> p c s g f",
                                                     c=8, s=4, g=2)[:, :, s_, 1, :],
                axis=mybir.AxisListType.X, op=ALU.add),
                waits=[("dve", done[("elu_stt", m)]),
                       ("gps", CK["gps0"] if m < 2 else 0)])
            for jj in range(8):
                bank = pE if jj < 4 else pF
                col = 65 * (jj % 4)
                for hf in range(2):
                    wts = []
                    if jj == 0 and hf == 0:
                        wts = [("dve", max(done[("elu_stt", m)],
                                           done[("sdpa_evac", m, 3)])),
                               ("act", c_sh),
                               ("dve", dget(("nds_mult", m, s_ - 1))),
                               ("gps", CK["gps0"] if m < 2 else 0)]
                    cpe = emit("pe", lambda t, seq=seq, s_=s_, jj=jj, hf=hf, bank=bank,
                               col=col: t.matmul(
                        out=bank[64 * hf:64 * (hf + 1), col:col + 65],
                        lhsT=qkbd[seq][64 * hf:64 * hf + 64,
                                       256 * jj + 64 * s_:256 * jj + 64 * (s_ + 1)],
                        rhs=shadow[seq][64 * hf:64 * hf + 64, 65 * jj:65 * (jj + 1)],
                        start=True, stop=True, tile_position=(64 * hf, 64 * hf)),
                        inc=(jj in (3, 7) and hf == 1), waits=wts)
            done[("numskm", m, s_)] = cpe
            emit("dve", lambda v: v.reciprocal(
                out=rc1A[:], in_=pE[:, 0:260].rearrange("p (a b) -> p a b", b=65)[:, :, 64]),
                waits=[("pe", done[("numskm", m, s_)]),
                       ("dve", dget(("usub", m, s_ - 1), dget(("usub", m - 1, SEGB - 1)))),
                       ("dve", dget(("blend", m, s_ - 1)))])
            emit("dve", lambda v: v.reciprocal(
                out=rc1B[:], in_=pF[:, 0:260].rearrange("p (a b) -> p a b", b=65)[:, :, 64]))
            emit("dve", lambda v: v.tensor_tensor(out=rc1A[:], in0=rc1A[:],
                                                  in1=btile[:, 0:4], op=ALU.mult))
            emit("dve", lambda v: v.tensor_tensor(out=rc1B[:], in0=rc1B[:],
                                                  in1=btile[:, 4:8], op=ALU.mult))
            emit("dve", lambda v, seq=seq: v.tensor_tensor(out=zf[seq][:], in0=zf[seq][:],
                                                           in1=zred[:], op=ALU.add))
            for half in range(2):
                bank = pE if half == 0 else pF
                rcX = rc1A if half == 0 else rc1B
                for hf in range(2):
                    prow = 32 + 64 * hf
                    uview = ut[r0:r0 + 32, :].rearrange(
                        "p (c g f) -> p c g f", g=2, f=64)[:, 4 * half:4 * half + 4, hf, :]
                    emit("dve", lambda v, bank=bank, prow=prow, uview=uview, rcX=rcX: v.tensor_tensor(
                        out=uview,
                        in0=bank[prow:prow + 32, 0:260].rearrange("p (a b) -> p a b", b=65)[:, :, 0:64],
                        in1=rcX[prow:prow + 32, :].unsqueeze(2).broadcast_to([32, 4, 64]),
                        op=ALU.mult))
            uin = ut[r0:r0 + 32, :].rearrange("p (c g f) -> p c g f", g=2, f=64)
            vsrc = vb[seq][r0:r0 + 32, :].rearrange("p (c g f) -> p c g f", g=2, f=65)[:, :, :, 0:64]
            cu = emit("dve", lambda v, vsrc=vsrc, uin=uin: v.tensor_tensor(
                out=uin, in0=vsrc, in1=uin, op=ALU.subtract), inc=True,
                waits=[("pe", dget(("delta", m, s_ - 1)))])
            done[("usub", m, s_)] = cu
            emit("dve", lambda v, seq=seq: v.tensor_copy(
                out=shadow[seq].rearrange("p (c f) -> p c f", f=65)[:, :, 64],
                in_=zf[seq][:]))
            c_nds = 0
            for half in range(2):
                bank = pE if half == 0 else pF
                rcX = rc1A if half == 0 else rc1B
                for hf in range(2):
                    nrow = 64 * hf
                    oview = otok[seq][r0:r0 + 32, :].rearrange(
                        "p (c g f) -> p c g f", g=2, f=64)[:, 4 * half:4 * half + 4, hf, :]
                    c_nds = emit("dve", lambda v, bank=bank, nrow=nrow, oview=oview, rcX=rcX: v.tensor_tensor(
                        out=oview,
                        in0=bank[nrow:nrow + 32, 0:260].rearrange("p (a b) -> p a b", b=65)[:, :, 0:64],
                        in1=rcX[nrow:nrow + 32, :].unsqueeze(2).broadcast_to([32, 4, 64]),
                        op=ALU.mult), inc=(half == 1 and hf == 1))
            done[("nds_mult", m, s_)] = c_nds
            oview = otok[seq][r0:r0 + 32, :].rearrange("p (c g f) -> p c g f", g=2, f=64)
            sdv = sdpa[seq][r0:r0 + 32, :].rearrange("p (c g b) -> p c g b", g=2, b=65)[:, :, :, 0:64]
            cb = emit("dve", lambda v, oview=oview, sdv=sdv: v.tensor_tensor(
                out=oview, in0=oview, in1=sdv, op=ALU.add), inc=True)
            done[("blend", m, s_)] = cb
            first = (gseg == 0)
            for jj in range(8):
                for hf in range(2):
                    wts = []
                    if jj == 0 and hf == 0:
                        wts = [("dve", cu), ("act", c_sh), ("tq", done[("skt", m)]),
                               ("dve", CK["pszero"] if m < 2 else 0)]
                    cpe = emit("pe", lambda t, seq=seq, s_=s_, jj=jj, hf=hf, first=first: t.matmul(
                        out=memPS[seq][64 * hf:64 * (hf + 1), 64 * jj:64 * (jj + 1)],
                        lhsT=skT[seq][32 * s_:32 * (s_ + 1),
                                      128 * jj + 64 * hf:128 * jj + 64 * (hf + 1)],
                        rhs=ut[32 * s_:32 * (s_ + 1),
                               128 * jj + 64 * hf:128 * jj + 64 * (hf + 1)],
                        start=False, stop=False,
                        tile_position=(32 * s_, 64 * hf), skip_group_check=True),
                        inc=(jj == 7 and hf == 1), waits=wts)
            done[("delta", m, s_)] = cpe

    def emit_Atail(m):
        seq, blk = m % 2, m // 2
        done[("phb_done", m)] = now("pe")
        maybe_tap("otok", m, lambda: otok[seq][:],
                  lambda: [("dve", done[("blend", m, SEGB - 1)])])
        maybe_tap("shadow", m, lambda: shadow[seq][:],
                  lambda: [("act", done[("shadow", m, SEGB - 1)])])
        done[("ot", m)] = emit("tq", lambda s, seq=seq: s.dma_start(
            out=ofm.rearrange("p (c f) -> p c f", c=8),
            in_=otok[seq].rearrange("p (c f) -> p c f", c=8), transpose=True), inc=True,
            waits=[("dve", done[("blend", m, SEGB - 1)]), ("pe", dget(("wout_done", m - 1)))])
        return stop_after == "ot"

    def emit_scan(m):
        emit_Ascores(m, 0)
        emit_Ascores(m, 1)
        for r in range(4):
            emit_Ar(m, r)
        emit_Aflush(m)
        if pa_lvl_of() < 99 or stop_after == "phasea":
            return True
        for s_ in range(SEGB):
            emit_As(m, s_)
        if stop_after == "phaseb":
            return True
        return emit_Atail(m)

    def emit_B1(m):
        seq, blk = m % 2, m // 2
        for nh in range(2):
            bank = next_big()
            beng, bval = big_evac[bank]
            for ic in range(NCH):
                wts = []
                if ic == 0:
                    wts = [("tq", done[("ot", m)]), (beng, bval)]
                emit("pe", lambda t, nh=nh, ic=ic, bank=bank: t.matmul(
                    out=bank[:, 0:512], lhsT=ofm[:, 128 * ic:128 * (ic + 1)],
                    rhs=wo_all[:, 1024 * ic + 512 * nh:1024 * ic + 512 * (nh + 1)],
                    start=(ic == 0), stop=(ic == NCH - 1)), waits=wts)
            cpe = emit("pe", lambda t: None, inc=True)
            cev = emit("dve", lambda v, seq=seq, nh=nh, bank=bank: v.scalar_tensor_tensor(
                out=xs[seq][:, 512 * nh:512 * (nh + 1)], in0=bank[:, 0:512], scalar=1.0,
                in1=xs[seq][:, 512 * nh:512 * (nh + 1)], op0=ALU.mult, op1=ALU.add), inc=True,
                waits=[("pe", cpe)])
            big_evac[bank] = ("dve", cev)
        done[("wout_done", m)] = now("pe")
        done[("s2", m)] = cev
        maybe_tap("s2", m, lambda: xs[seq][:], lambda: [("dve", cev)])
        return False

    def emit_B1b(m):
        seq, blk = m % 2, m // 2
        c_s2 = done[("s2", m)]
        r1, sq_ = stc(seq, 1, 0), stc(seq, 1, 1)
        emit("act", lambda a, seq=seq, r1=r1: a.activation(
            out=xnfm[seq][:], in_=xs[seq][:], func=AF.Copy, accum_out=r1),
            waits=[("dve", c_s2), ("tq", done[("skt", m)])])
        c_sqa2 = emit("actd", lambda a, seq=seq, sq_=sq_: a.activation(
            out=xnfm[seq][:], in_=xs[seq][:], func=AF.Square, accum_out=sq_), inc=True)
        rstd_chain(1, seq, 3, xtr_waits=[("act", c_sqa2)])
        c_h = emit("dve", lambda v, seq=seq: v.tensor_scalar(
            out=htok[seq][:], in0=xs[seq][:], scalar1=r_nm[:, 0:1], scalar2=r_yy[:, 0:1],
            op0=ALU.add, op1=ALU.mult), inc=True,
            waits=[("dve", dget(("ln3", m - 2))), ("sq", dget(("store", m - 2)))])
        c_hb = emit("act", lambda a, seq=seq: a.activation(
            out=hbf[seq][:], in_=htok[seq][:], func=AF.Copy), inc=True,
            waits=[("dve", c_h), ("tq", done[("ot", m)])])
        done[("chb", m)] = c_hb
        maybe_tap("htok", m, lambda: htok[seq][:], lambda: [("dve", c_h)])
        if m + 2 < nmt:
            load_x(m + 2, w=[("dve", c_h), ("act", c_sqa2)])

        return stop_after == "ln2"

    def emit_B2(m):
        seq, blk = m % 2, m // 2
        if blk > 0:
            c_halo = emit("dve", lambda v, seq=seq: v.tensor_copy(
                out=hh[seq][:], in_=hfm[seq].rearrange("p (c f) -> p c f", f=128)[:, :, 127]),
                inc=True, waits=[("dve", dget(("mix_done", m - 2)))])
        else:
            c_halo = now("dve")
        done[("ht", m)] = emit("tq", lambda s, seq=seq: s.dma_start(
            out=hfm[seq].rearrange("p (c f) -> p c f", c=8),
            in_=hbf[seq].rearrange("p (c f) -> p c f", c=8), transpose=True), inc=True,
            waits=[("act", done[("chb", m)]), ("dve", c_halo)])

        if stop_after == "ht":
            return True
        dtile = dtile_sv[seq]
        c_d2 = emit("dve", lambda v, seq=seq, dtile=dtile: v.tensor_tensor(
            out=dtile[:, 0:1024].rearrange("p (c f) -> p c f", f=128)[:, :, 0],
            in0=hfm[seq].rearrange("p (c f) -> p c f", f=128)[:, :, 0],
            in1=hh[seq][:], op=ALU.subtract), inc=True,
            waits=[("tq", done[("ht", m)]), ("dve", done[("blend", m, SEGB - 1)]),
                   ("gps", CK["gps0"] if m < 2 else 0)])
        emit("dve", lambda v, seq=seq, dtile=dtile: v.tensor_tensor(
            out=dtile[:, 0:1024].rearrange("p (c f) -> p c f", f=128)[:, :, 1:128],
            in0=hfm[seq].rearrange("p (c f) -> p c f", f=128)[:, :, 1:128],
            in1=hfm[seq].rearrange("p (c f) -> p c f", f=128)[:, :, 0:127], op=ALU.subtract))
        murb_v = murb[:].unsqueeze(2).broadcast_to([P, 8, 128])
        mukb_v = mukb[:].unsqueeze(2).broadcast_to([P, 8, 128])
        dv = dtile[:, 0:1024].rearrange("p (c f) -> p c f", f=128)
        emit("dve", lambda v, dv=dv, murb_v=murb_v: v.tensor_tensor(
            out=xr[0].rearrange("p (c f) -> p c f", f=128), in0=dv, in1=murb_v,
            op=ALU.mult),
            waits=[("dve", c_d2), ("dve", CK["wsc_free_dve"] if m < 2 else 0),
                   ("act", CK["wsc_free_act"] if m < 2 else 0),
                   ("pe", dget(("ffnmm_done", m - 1))),
                   ("act", dget(("tanh", m - 1, 1)))])
        emit("dve", lambda v, dv=dv, mukb_v=mukb_v: v.tensor_tensor(
            out=xk[0].rearrange("p (c f) -> p c f", f=128), in0=dv, in1=mukb_v,
            op=ALU.mult))
        emit("dve", lambda v, seq=seq: v.tensor_tensor(
            out=xr[0][:], in0=xr[0][:], in1=hfm[seq][:], op=ALU.add))
        cm = emit("dve", lambda v, seq=seq: v.tensor_tensor(
            out=xk[0][:], in0=xk[0][:], in1=hfm[seq][:], op=ALU.add), inc=True)
        done[("mix_done", m)] = cm
        maybe_tap("xr", m, lambda: xr[seq][:], lambda: [("dve", cm)])

        return stop_after == "mix"

    def emit_B3(m):
        seq, blk = m % 2, m // 2
        cm = done[("mix_done", m)]
        for nh in range(2):
            bank = next_big()
            beng, bval = big_evac[bank]
            for cc in range(NCH):
                wts = []
                if cc == 0:
                    wts = [("dve", cm), (beng, bval)]
                emit("pe", lambda t, seq=seq, nh=nh, cc=cc, bank=bank: t.matmul(
                    out=bank[:, 0:512], lhsT=xr[seq][:, 128 * cc:128 * (cc + 1)],
                    rhs=wrT[:, 1024 * cc + 512 * nh:1024 * cc + 512 * (nh + 1)],
                    start=(cc == 0), stop=(cc == NCH - 1)), waits=wts)
            cpe = emit("pe", lambda t: None, inc=True)
            ca = emit("act", lambda a, nh=nh, bank=bank: a.activation(
                out=tanh_t[:, 512 * nh:512 * (nh + 1)], in_=bank[:, 0:512], func=AF.Tanh,
                scale=0.5), inc=True,
                waits=[("pe", cpe), ("dve", dget(("ffno", m - 1)))])
            done[("tanh", m, nh)] = ca
            big_evac[bank] = ("act", ca)
        return False

    def emit_B4(m):
        seq, blk = m % 2, m // 2
        cm = done[("mix_done", m)]
        for nh in range(2):
            bank = next_big()
            beng, bval = big_evac[bank]
            for cc in range(NCH):
                wts = []
                if cc == 0:
                    wts = [("dve", cm), (beng, bval)]
                emit("pe", lambda t, seq=seq, nh=nh, cc=cc, bank=bank: t.matmul(
                    out=bank[:, 0:512], lhsT=xk[seq][:, 128 * cc:128 * (cc + 1)],
                    rhs=wkT[:, 1024 * cc + 512 * nh:1024 * cc + 512 * (nh + 1)],
                    start=(cc == 0), stop=(cc == NCH - 1)), waits=wts)
            cpe = emit("pe", lambda t: None, inc=True)
            emit("dve", lambda v, seq=seq, bank=bank: v.tensor_scalar(
                out=krelu_of[seq][:], in0=bank[:, 0:512], scalar1=0.0, scalar2=None,
                op0=ALU.max), waits=[("pe", cpe), ("pe", dget(("kv_done", m - 2))),
                                     ("tq", dget(("kt", m - 1))),
                                     ("act", dget(("ln3sq", m - 2))),
                                     ("dve", CK["wsc_free_dve"] if m < 2 else 0),
                                     ("act", CK["wsc_free_act"] if m < 2 else 0)])
            cev = emit("dve", lambda v, seq=seq, nh=nh: v.tensor_tensor(
                out=kact[seq][:, 512 * nh:512 * (nh + 1)], in0=krelu_of[seq][:],
                in1=krelu_of[seq][:], op=ALU.mult), inc=True)
            big_evac[bank] = ("dve", cev)
        c_ka = cev
        done[("kact", m)] = c_ka
        maybe_tap("kact", m, lambda: kact[seq][:], lambda: [("dve", c_ka)])
        done[("kt", m)] = emit("tq", lambda s, seq=seq: s.dma_start(
            out=kafm[seq].rearrange("p (c f) -> p c f", c=8),
            in_=kact[seq].rearrange("p (c f) -> p c f", c=8), transpose=True), inc=True,
            waits=[("dve", c_ka)])

        return stop_after == "kact"

    def emit_B5(m):
        seq, blk = m % 2, m // 2
        for nh in range(2):
            bank = next_big()
            beng, bval = big_evac[bank]
            for cc in range(NCH):
                wts = []
                if cc == 0:
                    wts = [("tq", done[("kt", m)]), (beng, bval)]
                emit("pe", lambda t, seq=seq, nh=nh, cc=cc, bank=bank: t.matmul(
                    out=bank[:, 0:512], lhsT=kafm[seq][:, 128 * cc:128 * (cc + 1)],
                    rhs=wvT[:, 1024 * cc + 512 * nh:1024 * cc + 512 * (nh + 1)],
                    start=(cc == 0), stop=(cc == NCH - 1)), waits=wts)
            cpe = emit("pe", lambda t: None, inc=True)
            emit("dve", lambda v, nh=nh, bank=bank: v.scalar_tensor_tensor(
                out=tanh_t[:, 512 * nh:512 * (nh + 1)], in0=tanh_t[:, 512 * nh:512 * (nh + 1)],
                scalar=1.0, in1=bank[:, 0:512], op0=ALU.add, op1=ALU.mult),
                waits=[("pe", cpe), ("act", done[("tanh", m, nh)])])
            cev = emit("dve", lambda v, seq=seq, nh=nh: v.tensor_tensor(
                out=htok[seq][:, 512 * nh:512 * (nh + 1)],
                in0=htok[seq][:, 512 * nh:512 * (nh + 1)],
                in1=tanh_t[:, 512 * nh:512 * (nh + 1)], op=ALU.add), inc=True,
                waits=[("tq", done[("ht", m)])])
            big_evac[bank] = ("dve", cev)
        done[("ffnmm_done", m)] = now("pe")
        done[("kv_done", m)] = now("pe")
        done[("ffno", m)] = cev
        maybe_tap("s3", m, lambda: htok[seq][:], lambda: [("dve", cev)])
        return False

    def emit_B5b(m):
        seq, blk = m % 2, m // 2
        c_s3 = done[("ffno", m)]
        r1, sq_ = stc(seq, 2, 0), stc(seq, 2, 1)
        emit("act", lambda a, seq=seq, r1=r1: a.activation(
            out=kafm[seq][:], in_=htok[seq][:], func=AF.Copy, accum_out=r1),
            waits=[("dve", c_s3), ("pe", done[("kv_done", m)])])
        c_sqa3 = emit("actd", lambda a, seq=seq, sq_=sq_: a.activation(
            out=kafm[seq][:], in_=htok[seq][:], func=AF.Square, accum_out=sq_), inc=True)
        done[("ln3sq", m)] = c_sqa3
        rstd_chain(2, seq, 3, xtr_waits=[("act", c_sqa3)])
        c_o = emit("dve", lambda v, seq=seq: v.tensor_scalar(
            out=htok[seq][:], in0=htok[seq][:], scalar1=r_nm[:, 0:1], scalar2=r_yy[:, 0:1],
            op0=ALU.add, op1=ALU.mult), inc=True,
            waits=[("sq", dget(("store", m - 2)))])
        done[("ln3", m)] = c_o
        maybe_tap("h3", m, lambda: htok[seq][:], lambda: [("dve", c_o)])
        done[("store", m)] = emit("sq", lambda s, seq=seq, blk=blk: s.dma_start(
            out=out_h[seq, TOK * blk:TOK * (blk + 1), :], in_=htok[seq][:]), inc=True,
            waits=[("dve", c_o)])
        return False

    if stop_after is None:
        def emit_front_all(m):
            emit_F1(m); emit_F2a(m); emit_F2b(m); emit_F2c(m); emit_F3(m); emit_F4(m)

        if nmt >= 1:
            emit_front_all(0)
        for it in range(nmt + 1):
            b = it - 1
            f = it + 1
            Bv = 0 <= b < nmt
            Fv = f < nmt
            if it < nmt:
                emit_Ascores(it, 0)
                emit_Ascores(it, 1)
                emit_Ar(it, 0)
                emit_Ar(it, 1)
                emit_Ar(it, 2)
                emit_Ar(it, 3)
                emit_Aflush(it)
                if Bv:
                    emit_B1(b)
                emit_As(it, 0)
                if Bv:
                    emit_B1b(b)
                    emit_B2(b)
                emit_As(it, 1)
                if Fv:
                    emit_F1(f)
                if Bv:
                    emit_B3(b)
                if Fv:
                    emit_F2a(f)
                emit_As(it, 2)
                if Bv:
                    emit_B4(b)
                if Fv:
                    emit_F2b(f)
                if Bv:
                    emit_B5(b)
                emit_As(it, 3)
                if Fv:
                    emit_F2c(f)
                emit_Atail(it)
                if Fv:
                    emit_F3(f)
                    emit_F4(f)
                if Bv:
                    emit_B5b(b)
            else:
                if Bv:
                    emit_B1(b); emit_B1b(b); emit_B2(b); emit_B3(b)
                    emit_B4(b); emit_B5(b); emit_B5b(b)
    else:
        for m in range(nmt):
            if (emit_F1(m) or emit_F2(m) or emit_F3(m) or emit_F4(m)
                    or emit_scan(m) or emit_B1(m) or emit_B1b(m) or emit_B2(m)
                    or emit_B3(m) or emit_B4(m) or emit_B5(m) or emit_B5b(m)):
                break

    total_sq = cnt["sq"]
    emit("sq", lambda s: None, waits=[("sq", total_sq)])

    with nc.Block() as block:
        block.tensor(_mk_runner(ops["pe"]))
        block.vector(_mk_runner(ops["dve"]))
        block.scalar(_mk_runner(ops["scalar"]))
        block.gpsimd(_mk_runner(ops["gps"]))
        block.sync(_mk_runner(ops["sync"]))

    ctx.close()
    return nc, tap_h


def _mk_runner(stream):
    def _run(eng):
        last_instr = [None]
        for (wl, fn, sem, step, logical) in stream:
            for (s, v) in wl:
                eng.wait_ge(s, v)
            instr = fn(eng)
            if instr is not None:
                last_instr[0] = instr
            if sem is not None:
                if logical == "actd":
                    eng.drain().then_inc(sem, step)
                else:
                    tgt = instr if instr is not None else last_instr[0]
                    tgt.then_inc(sem, step)
    return _run


_NC_CACHE = {}


def kernel(L0_attn_x, L0_attn_w_qkv, L0_attn_w_out, L0_attn_ln_g, L0_attn_ln_b,
           L0_attn_mem_beta, L0_ffn_W_r, L0_ffn_W_k, L0_ffn_W_v, L0_ffn_mu_r, L0_ffn_mu_k):
    x = np.ascontiguousarray(np.asarray(L0_attn_x, dtype=np.float32))
    if "nc" not in _NC_CACHE:
        _NC_CACHE["nc"] = build_nc()
    nc, _ = _NC_CACHE["nc"]
    shared = {
        "w_qkv": np.ascontiguousarray(np.asarray(L0_attn_w_qkv, np.float32)),
        "w_out": np.ascontiguousarray(np.asarray(L0_attn_w_out, np.float32)),
        "ln_g": np.ascontiguousarray(np.asarray(L0_attn_ln_g, np.float32)),
        "ln_b": np.ascontiguousarray(np.asarray(L0_attn_ln_b, np.float32)),
        "mem_beta": np.ascontiguousarray(np.asarray(L0_attn_mem_beta, np.float32).reshape(1, H)),
        "W_r": np.ascontiguousarray(np.asarray(L0_ffn_W_r, np.float32)),
        "W_k": np.ascontiguousarray(np.asarray(L0_ffn_W_k, np.float32)),
        "W_v": np.ascontiguousarray(np.asarray(L0_ffn_W_v, np.float32)),
        "mu_r": np.ascontiguousarray(np.asarray(L0_ffn_mu_r, np.float32)),
        "mu_k": np.ascontiguousarray(np.asarray(L0_ffn_mu_k, np.float32)),
    }
    in_maps = []
    for core in range(8):
        im = dict(shared)
        im["x"] = np.ascontiguousarray(x[2 * core:2 * core + 2])
        in_maps.append(im)
    res = run_bass_kernel_spmd(nc, in_maps, core_ids=list(range(8)))
    out = np.concatenate([r["out"] for r in res.results], axis=0)
    return np.ascontiguousarray(out.reshape(16, TSEQ, D).astype(np.float32))


# version 16
# speedup vs baseline: 1.3099x; 1.0130x over previous
import math
import numpy as np
import concourse.bass as bass
import concourse.mybir as mybir
from concourse.bass_utils import run_bass_kernel_spmd
from contextlib import ExitStack

F32 = mybir.dt.float32
BF16 = mybir.dt.bfloat16
I32 = mybir.dt.int32
AF = mybir.ActivationFunctionType
ALU = mybir.AluOpType

P = 128
D = 1024
NCH = 8
H = 16
SEG = 32
TOK = 128
NSEQ = 2
TSEQ = 2048
NBLK = TSEQ // TOK
NMT = NSEQ * NBLK
SEGB = TOK // SEG
EXPB = 20.0
LOG1E4 = math.log(10000.0)
TWO_PI = 2.0 * math.pi


def build_nc(taps=(), nmt=NMT, stop_after=None):
    nc = bass.Bass("TRN2", target_bir_lowering=False)
    x_h = nc.dram_tensor("x", [NSEQ, TSEQ, D], F32, kind="ExternalInput").ap()
    wqkv_h = nc.dram_tensor("w_qkv", [D, 3 * D], F32, kind="ExternalInput").ap()
    wout_h = nc.dram_tensor("w_out", [D, D], F32, kind="ExternalInput").ap()
    g_h = nc.dram_tensor("ln_g", [D], F32, kind="ExternalInput").ap()
    b_h = nc.dram_tensor("ln_b", [D], F32, kind="ExternalInput").ap()
    beta_h = nc.dram_tensor("mem_beta", [1, H], F32, kind="ExternalInput").ap()
    wr_h = nc.dram_tensor("W_r", [D, D], F32, kind="ExternalInput").ap()
    wk_h = nc.dram_tensor("W_k", [D, D], F32, kind="ExternalInput").ap()
    wv_h = nc.dram_tensor("W_v", [D, D], F32, kind="ExternalInput").ap()
    mur_h = nc.dram_tensor("mu_r", [D], F32, kind="ExternalInput").ap()
    muk_h = nc.dram_tensor("mu_k", [D], F32, kind="ExternalInput").ap()
    out_h = nc.dram_tensor("out", [NSEQ, TSEQ, D], F32, kind="ExternalOutput").ap()

    ctx = ExitStack()
    e = ctx.enter_context

    wq_all = e(nc.sbuf_tensor("wq_all", [P, NCH * 3072], BF16))
    wo_all = e(nc.sbuf_tensor("wo_all", [P, NCH * 1024], BF16))
    wrT = e(nc.sbuf_tensor("wrT", [P, NCH * 1024], BF16))
    wkT = e(nc.sbuf_tensor("wkT", [P, NCH * 1024], BF16))
    wvT = e(nc.sbuf_tensor("wvT", [P, NCH * 1024], BF16))
    cosT = e(nc.sbuf_tensor("cosT", [P, TSEQ], BF16))
    sinT = e(nc.sbuf_tensor("sinT", [P, TSEQ], BF16))
    mask01 = e(nc.sbuf_tensor("mask01", [P, P], BF16))
    smat = e(nc.sbuf_tensor("smat", [P, P], BF16))
    gcol = e(nc.sbuf_tensor("gcol", [P, 8], F32))
    bcol = e(nc.sbuf_tensor("bcol", [P, 8], BF16))
    murm1 = e(nc.sbuf_tensor("murm1", [P, 8], F32))
    mukm1 = e(nc.sbuf_tensor("mukm1", [P, 8], F32))
    murb = e(nc.sbuf_tensor("murb", [P, 8], BF16))
    mukb = e(nc.sbuf_tensor("mukb", [P, 8], BF16))
    ubqk = e(nc.sbuf_tensor("ubqk", [P, 16], F32))
    ubv = e(nc.sbuf_tensor("ubv", [P, 1024], BF16))
    btr = e(nc.sbuf_tensor("btr", [1, 16], F32))
    betab = e(nc.sbuf_tensor("betab", [1, 16], BF16))
    ombb = e(nc.sbuf_tensor("ombb", [1, 16], BF16))
    btile = e(nc.sbuf_tensor("btile", [P, 8], F32))
    omb = e(nc.sbuf_tensor("omb", [P, 16], F32))
    epst = e(nc.sbuf_tensor("epst", [P, 1], F32))
    cm20 = e(nc.sbuf_tensor("cm20", [P, 1], F32))
    ones1 = e(nc.sbuf_tensor("ones1", [1, P], BF16))
    pidx = e(nc.sbuf_tensor("pidx", [P, 1], F32))
    cidx = None
    par = e(nc.sbuf_tensor("par", [P, 2], F32))
    freq64 = e(nc.sbuf_tensor("freq64", [P, 1], F32))
    sti = e(nc.sbuf_tensor("sti", [P, 1], I32))
    gtmp = e(nc.sbuf_tensor("gtmp", [P, 8], F32))

    def perseq(name, cols, dt):
        return [e(nc.sbuf_tensor(f"{name}{s}", [P, cols], dt)) for s in range(NSEQ)]

    xs = perseq("xs", D, F32)
    xhat = perseq("xhat", D, BF16)
    xnfm = perseq("xnfm", D, BF16)
    qkfm = perseq("qkfm", 2 * D, BF16)
    qkbd = perseq("qkbd", 2 * D, BF16)
    skT = perseq("skT", D, BF16)
    vb = perseq("vb", 8 * 130, BF16)
    sdpa = perseq("sdpa", 16 * 65, BF16)
    shadow = perseq("shadow", 8 * 65, BF16)
    zf = perseq("zf", 8, F32)
    hh = perseq("hh", 8, BF16)
    otok = None
    htok = perseq("htok", D, F32)
    hbf = xhat
    otok = xhat
    hfm = perseq("hfm", D, BF16)
    skfm = xnfm
    kafm = perseq("kafm", D, BF16)
    h3 = None
    ofm = e(nc.sbuf_tensor("ofm", [P, D], BF16))
    ut = e(nc.sbuf_tensor("ut", [P, D], BF16))
    h3 = [qkbd[s_].bitcast(F32)[:, 0:1024] for s_ in range(NSEQ)]
    dtile_sv = [sdpa[s_][:, 0:1024] for s_ in range(NSEQ)]
    tanh_t = e(nc.sbuf_tensor("tanh_t", [P, D], F32))
    rop1 = e(nc.sbuf_tensor("rop1", [P, 2 * TOK], F32))
    rop2 = e(nc.sbuf_tensor("rop2", [P, 2 * TOK], F32))
    rop1w = rop1[:].rearrange("p (a f) -> p a f", f=TOK)
    rop2w = rop2[:].rearrange("p (a f) -> p a f", f=TOK)

    wsc = e(nc.sbuf_tensor("wsc", [P, 3072], BF16))
    wscf = wsc.bitcast(F32)
    xr_t = wsc[:, 0:1024]
    xk_t = wsc[:, 1024:2048]
    kact_t = wsc[:, 2048:3072]
    xr = [xr_t, xr_t]
    xk = [xk_t, xk_t]
    kact = [kact_t, kact_t]

    arena = e(nc.sbuf_tensor("arena", [P, 2048], BF16))
    emin = arena[:, 0:1024]
    eexp = arena[:, 1024:2048]
    lnsq = emin
    wraw = arena[:, 0:1536]
    probs_sv = [[arena[:, 0:512], arena[:, 512:1024]] for s_ in range(NSEQ)]
    ffsc = arena.bitcast(F32)[:, 0:512]
    krelu_of = [kafm[s_].bitcast(F32)[:, 0:512] for s_ in range(NSEQ)]

    ubvrow = ut[0:1, 0:1024]
    cidx = ut.bitcast(F32)[:, 0:128]
    ST = e(nc.sbuf_tensor("ST", [P, 9 * 6], F32))

    def stc(seq, lnid, j):
        c = 9 * (lnid + 3 * seq) + j
        return ST[:, c:c + 1]

    REG = e(nc.sbuf_tensor("REG", [P, 5 * 64], F32))
    W64 = 64
    r_nm, r_tt, r_yy, r_aa, r_hx = [REG[:, 64 * j:64 * j + 64] for j in range(5)]
    rc1A = e(nc.sbuf_tensor("rc1A", [P, 4], F32))
    rc1B = e(nc.sbuf_tensor("rc1B", [P, 4], F32))
    rc1 = rc1A
    zred = e(nc.sbuf_tensor("zred", [P, 8], F32))

    pA = e(nc.psum_tensor("pA", [P, 512], F32))
    pB = e(nc.psum_tensor("pB", [P, 512], F32))
    pC = e(nc.psum_tensor("pC", [P, 512], F32))
    pD = e(nc.psum_tensor("pD", [P, 512], F32))
    pE = e(nc.psum_tensor("pE", [P, 512], F32))
    pF = e(nc.psum_tensor("pF", [P, 512], F32))
    memPS = [e(nc.psum_tensor(f"memPS{s}", [P, 512], F32)) for s in range(NSEQ)]

    LOG2PHYS = {"pe": "pe", "dve": "dve", "act": "scalar", "tq": "sync",
                "gps": "gps", "gq": "gps", "sq": "sync", "actd": "scalar"}
    PHYS = ["pe", "dve", "scalar", "gps", "sync"]
    sems = {k: e(nc.semaphore(f"s_{k}")) for k in ("pe", "dve", "act", "gps", "sq", "tq", "gq")}
    sems["actd"] = sems["act"]
    cnt = {k: 0 for k in sems}
    cnt.pop("actd"); cnt["actd"] = None
    waitmark = {ph: {k: 0 for k in sems} for ph in PHYS}
    ops = {ph: [] for ph in PHYS}
    tap_bar = {"v": 0}

    def emit(logical, fn, inc=False, waits=()):
        ph = LOG2PHYS[logical]
        waits = list(waits)
        if tap_bar["v"]:
            waits.append(("sq", tap_bar["v"]))
        wl = []
        for (pr, val) in waits:
            if val and val > waitmark[ph][pr]:
                waitmark[ph][pr] = val
                wl.append((sems[pr], val))
        step = 16 if logical in ("sq", "tq", "gq") else 1
        cl = "act" if logical == "actd" else logical
        sem = None
        if inc:
            cnt[cl] += step
            sem = sems[cl]
        ops[ph].append((wl, fn, sem, step, logical))
        return cnt[cl]

    def now(l):
        return cnt[l]


    def ddrain():
        emit("dve", lambda v: v.drain())

    tap_h = {}

    def tap(name, view, waits):
        th = nc.dram_tensor(f"tap_{name}", list(view.shape), view.dtype,
                            kind="ExternalOutput").ap()
        tap_h[f"tap_{name}"] = str(view.dtype)
        emit("sq", lambda s, th=th, view=view: s.dma_start(out=th, in_=view),
             inc=True, waits=waits)
        tap_bar["v"] = cnt["sq"]

    def maybe_tap(stage, m, view_fn, waits_fn):
        for (tname, tmt) in taps:
            if tname == stage and tmt == m:
                tap(f"{stage}_{m}", view_fn(), waits_fn())

    CK = {}
    xload = {}
    done = {}

    def load_x(m, w=()):
        seq, blk = m % 2, m // 2
        xload[m] = emit("sq", lambda s, seq=seq, blk=blk: s.dma_start(
            out=xs[seq][:], in_=x_h[seq, TOK * blk:TOK * (blk + 1), :]), inc=True,
            waits=list(w))

    load_x(0)
    load_x(1)

    def vec_to_cols(vh, dest, first_waits=()):
        mk = 0
        for c in range(8):
            src = vh[128 * c:128 * c + 128].rearrange("(a b) -> a b", b=1)
            mk = emit("sq", lambda s, d=dest[:, c:c + 1], src=src: s.dma_start(out=d, in_=src),
                      inc=True, waits=(list(first_waits) if c == 0 else []))
        return mk

    CK["g"] = vec_to_cols(g_h, gcol)
    CK["b"] = vec_to_cols(b_h, gtmp)
    CK["bc"] = emit("dve", lambda v: v.tensor_copy(out=bcol[:], in_=gtmp[:]), inc=True,
                    waits=[("sq", CK["b"])])
    CK["mur"] = vec_to_cols(mur_h, gtmp, first_waits=[("dve", CK["bc"])])
    emit("dve", lambda v: v.tensor_scalar(out=murm1[:], in0=gtmp[:], scalar1=1.0,
                                          scalar2=-1.0, op0=ALU.mult, op1=ALU.add),
         waits=[("sq", CK["mur"])])
    ddrain()
    CK["m1"] = emit("dve", lambda v: v.tensor_copy(out=murb[:], in_=murm1[:]), inc=True)
    CK["muk"] = vec_to_cols(muk_h, gtmp, first_waits=[("dve", CK["m1"])])
    emit("dve", lambda v: v.tensor_scalar(out=mukm1[:], in0=gtmp[:], scalar1=1.0,
                                          scalar2=-1.0, op0=ALU.mult, op1=ALU.add),
         waits=[("sq", CK["muk"])])
    ddrain()
    CK["m2"] = emit("dve", lambda v: v.tensor_copy(out=mukb[:], in_=mukm1[:]), inc=True)
    CK["beta"] = emit("sq", lambda s: s.dma_start(out=btr[:], in_=beta_h), inc=True)

    emit("gps", lambda g: g.iota(pidx[:], pattern=[[0, 1]], channel_multiplier=1,
                                 allow_small_or_imprecise_dtypes=True))
    emit("gps", lambda g: g.iota(cidx[:], pattern=[[1, P]], channel_multiplier=0,
                                 allow_small_or_imprecise_dtypes=True))
    emit("gps", lambda g: g.memset(mask01[:], 1.0))
    mview = mask01.rearrange("p (a b) -> p a b", a=4)
    emit("gps", lambda g: g.drain())
    emit("gps", lambda g: g.affine_select(out=mview, in_=mview, pattern=[[-32, 4], [0, 32]],
                                          compare_op=ALU.is_ge, fill=0.0, base=0,
                                          channel_multiplier=1))
    emit("gps", lambda g: g.drain())
    emit("gps", lambda g: g.affine_select(out=mview, in_=mview, pattern=[[32, 4], [0, 32]],
                                          compare_op=ALU.is_ge, fill=0.0, base=31,
                                          channel_multiplier=-1))
    emit("gps", lambda g: g.drain())
    emit("gps", lambda g: g.affine_select(out=mview, in_=mview, pattern=[[32, 4], [1, 32]],
                                          compare_op=ALU.is_ge, fill=0.0, base=0,
                                          channel_multiplier=-1))
    emit("gps", lambda g: g.memset(ones1[:], 1.0))
    for s_ in range(NSEQ):
        vbv = vb[s_].rearrange("p (c f) -> p c f", f=130)
        emit("gps", lambda g, vbv=vbv: g.memset(vbv[:, :, 64:65], 1.0))
        emit("gps", lambda g, vbv=vbv: g.memset(vbv[:, :, 129:130], 1.0))
        emit("gps", lambda g, s_=s_: g.memset(hh[s_][:], 0.0))
        emit("gps", lambda g, s_=s_: g.memset(zf[s_][:], 1e-6))
        shv = shadow[s_].rearrange("p (c f) -> p c f", f=65)
        emit("gps", lambda g, s_=s_: g.memset(shadow[s_][:], 0.0))
        emit("gps", lambda g: g.drain())
        emit("gps", lambda g, shv=shv: g.memset(shv[:, :, 64:65], 1e-6))
    emit("gps", lambda g: g.memset(cm20[:], -EXPB))
    emit("gps", lambda g: g.memset(epst[:], 0.0))
    emit("gps", lambda g: g.drain())
    emit("gps", lambda g: g.memset(epst[0:32, :], 1e-6))
    CK["gps0"] = emit("gps", lambda g: g.memset(epst[64:96, :], 1e-6), inc=True)
    emit("dve", lambda v: v.memset(memPS[0][:], 0.0))
    CK["pszero"] = emit("dve", lambda v: v.memset(memPS[1][:], 0.0), inc=True)

    st0 = stc(0, 0, 6)
    emit("dve", lambda v: v.tensor_scalar(out=st0, in0=pidx[:], scalar1=0.5, scalar2=-0.25,
                                          op0=ALU.mult, op1=ALU.add),
         waits=[("gps", CK["gps0"])])
    ddrain()
    emit("dve", lambda v: v.tensor_copy(out=sti[:], in_=st0))
    ddrain()
    CK["flr"] = emit("dve", lambda v: v.tensor_copy(out=st0, in_=sti[:]), inc=True)
    ddrain()
    CK["freq"] = emit("act", lambda a: a.activation(out=freq64[:], in_=st0, func=AF.Exp,
                                                    scale=-LOG1E4 / 32.0), inc=True,
                      waits=[("dve", CK["flr"])])
    emit("dve", lambda v: v.tensor_scalar(out=par[:, 0:1], in0=st0, scalar1=-2.0, scalar2=None,
                                          op0=ALU.mult))
    ddrain()
    emit("dve", lambda v: v.tensor_tensor(out=par[:, 0:1], in0=par[:, 0:1], in1=pidx[:],
                                          op=ALU.add))
    ddrain()
    emit("dve", lambda v: v.tensor_scalar(out=par[:, 1:2], in0=par[:, 0:1], scalar1=-1.0,
                                          scalar2=1.0, op0=ALU.mult, op1=ALU.add))
    ddrain()
    emit("dve", lambda v: v.tensor_scalar(out=cidx[:], in0=cidx[:], scalar1=pidx[:], scalar2=None,
                                          op0=ALU.subtract))
    emit("dve", lambda v: v.tensor_scalar(out=smat[:], in0=cidx[:], scalar1=1.0, scalar2=None,
                                          op0=ALU.is_equal))
    emit("dve", lambda v: v.tensor_scalar(out=smat[:], in0=smat[:], scalar1=par[:, 1:2],
                                          scalar2=None, op0=ALU.mult))
    emit("dve", lambda v: v.tensor_scalar(out=lnsq[:, 0:128], in0=cidx[:], scalar1=-1.0,
                                          scalar2=None, op0=ALU.is_equal))
    emit("dve", lambda v: v.tensor_scalar(out=lnsq[:, 0:128], in0=lnsq[:, 0:128],
                                          scalar1=par[:, 0:1], scalar2=None, op0=ALU.mult))
    CK["smat"] = emit("dve", lambda v: v.tensor_tensor(out=smat[:], in0=smat[:],
                                                       in1=lnsq[:, 0:128], op=ALU.subtract),
                      inc=True)

    wq_f = wq_all.bitcast(F32)
    pos64 = wq_f[0:64, 0:2048]
    ang64 = wq_f[0:64, 2048:4096]
    u_s = wq_f[0:64, 4096:6144]
    n_f = wq_f[0:64, 6144:8192]
    n_i = wq_f[0:64, 8192:10240].bitcast(I32)
    CK["pos"] = emit("gps", lambda g: g.iota(pos64, pattern=[[1, 2048]], channel_multiplier=0,
                                             allow_small_or_imprecise_dtypes=True), inc=True)
    emit("dve", lambda v: v.tensor_scalar(out=ang64, in0=pos64, scalar1=freq64[0:64, 0:1],
                                          scalar2=None, op0=ALU.mult),
         waits=[("gps", CK["pos"]), ("act", CK["freq"])])

    def table_build(phase, dest):
        emit("dve", lambda v: v.tensor_scalar(out=u_s, in0=ang64, scalar1=1.0 / TWO_PI,
                                              scalar2=phase, op0=ALU.mult, op1=ALU.add))
        emit("dve", lambda v: v.tensor_copy(out=n_i, in_=u_s))
        emit("dve", lambda v: v.tensor_copy(out=n_f, in_=n_i))
        cc = emit("dve", lambda v: v.tensor_tensor(out=u_s, in0=u_s, in1=n_f, op=ALU.subtract),
                  inc=True)
        ca = emit("act", lambda a: a.activation(out=dest[0:64, :], in_=u_s, func=AF.Sin,
                                                scale=TWO_PI), inc=True, waits=[("dve", cc)])
        emit("dve", lambda v: v.tensor_copy(out=dest[64:128, :], in_=dest[0:64, :]),
             waits=[("act", ca)])

    table_build(0.25, cosT)
    table_build(0.0, sinT)
    CK["tables"] = emit("dve", lambda v: None, inc=True)

    ca = emit("act", lambda a: a.activation(out=btr[:], in_=btr[:], func=AF.Tanh, scale=0.5),
              inc=True, waits=[("sq", CK["beta"])])
    emit("dve", lambda v: v.tensor_scalar(out=betab[:], in0=btr[:], scalar1=0.5, scalar2=0.5,
                                          op0=ALU.mult, op1=ALU.add), waits=[("act", ca)])
    CK["betab"] = emit("dve", lambda v: v.tensor_scalar(out=ombb[:], in0=btr[:], scalar1=-0.5,
                                                        scalar2=0.5, op0=ALU.mult, op1=ALU.add),
                       inc=True)
    emit("pe", lambda t: t.matmul(out=pA[:, 0:16], lhsT=ones1[:], rhs=ombb[:],
                                  start=True, stop=True),
         waits=[("dve", CK["betab"]), ("gps", CK["gps0"])])
    CK["pe_beta"] = emit("pe", lambda t: t.matmul(out=pB[0:32, 0:16], lhsT=ones1[:, 0:32],
                                                  rhs=betab[:], start=True, stop=True), inc=True)
    emit("dve", lambda v: v.tensor_copy(out=omb[:], in_=pA[:, 0:16]),
         waits=[("pe", CK["pe_beta"])])
    emit("dve", lambda v: v.memset(btile[:], 1.0))
    ddrain()
    emit("dve", lambda v: v.tensor_copy(out=btile[0:32, :], in_=pB[0:32, 0:16:2]))
    CK["btile"] = emit("dve", lambda v: v.tensor_copy(out=btile[64:96, :], in_=pB[0:32, 1:16:2]),
                      inc=True)

    prev_sc = 0
    prev_ca = 0
    last_pe = 0
    for c in range(NCH):
        for hf in range(2):
            wl = emit("sq", lambda s, c=c, hf=hf: s.dma_start(
                out=wscf[:, 0:1536],
                in_=wqkv_h[128 * c:128 * c + 128, 1536 * hf:1536 * (hf + 1)]),
                inc=True, waits=[("dve", max(CK["tables"], prev_sc)), ("act", prev_ca),
                                 ("pe", last_pe)])
            prev_ca = emit("act", lambda a: a.activation(out=wraw, in_=wscf[:, 0:1536],
                                                         func=AF.Copy), inc=True,
                           waits=[("sq", wl)])
            prev_sc = emit("dve", lambda v, c=c, hf=hf: v.tensor_scalar(
                out=wq_all[:, 3072 * c + 1536 * hf:3072 * c + 1536 * (hf + 1)],
                in0=wscf[:, 0:1536], scalar1=gcol[:, c:c + 1], scalar2=None,
                op0=ALU.mult), inc=True,
                waits=[("sq", wl), ("sq", CK["g"]), ("dve", CK["tables"])])
            if hf == 0:
                mcs = range(0, 12)
            else:
                mcs = range(12, 16)
            for mc in mcs:
                off = 128 * mc - 1536 * hf
                emit("pe", lambda t, c=c, mc=mc, off=off: t.matmul(
                    out=pC[:, 32 + mc:33 + mc], lhsT=wraw[:, off:off + 128],
                    rhs=bcol[:, c:c + 1], start=(c == 0 and mc == 0),
                    stop=(c == NCH - 1)),
                    waits=[("act", prev_ca), ("dve", CK["bc"]),
                           ("dve", CK["btile"] if c == 0 else 0)])
            if hf == 1:
                emit("pe", lambda t, c=c: t.matmul(out=pD[0:1, 0:512], lhsT=bcol[:, c:c + 1],
                                                   rhs=wraw[:, 512:1024], start=(c == 0),
                                                   stop=(c == NCH - 1)))
                last_pe = emit("pe", lambda t, c=c: t.matmul(
                    out=pE[0:1, 0:512], lhsT=bcol[:, c:c + 1],
                    rhs=wraw[:, 1024:1536], start=(c == 0), stop=(c == NCH - 1)), inc=True)
    CK["pe_ub"] = last_pe
    emit("dve", lambda v: v.tensor_copy(out=ubqk[:], in_=pC[:, 32:48]),
         waits=[("pe", CK["pe_ub"])])
    emit("dve", lambda v: v.tensor_copy(out=ubvrow[0:1, 0:512], in_=pD[0:1, 0:512]))
    CK["ubvr"] = emit("dve", lambda v: v.tensor_copy(out=ubvrow[0:1, 512:1024],
                                                     in_=pE[0:1, 0:512]), inc=True)
    emit("pe", lambda t: t.matmul(out=pD[:, 0:512], lhsT=ones1[:], rhs=ubvrow[0:1, 0:512],
                                  start=True, stop=True), waits=[("dve", CK["ubvr"])])
    CK["pe_ubv"] = emit("pe", lambda t: t.matmul(out=pE[:, 0:512], lhsT=ones1[:],
                                                 rhs=ubvrow[0:1, 512:1024], start=True,
                                                 stop=True), inc=True)
    emit("dve", lambda v: v.tensor_copy(out=ubv[:, 0:512], in_=pD[:, 0:512]),
         waits=[("pe", CK["pe_ubv"])])
    CK["ubv"] = emit("dve", lambda v: v.tensor_copy(out=ubv[:, 512:1024], in_=pE[:, 0:512]),
                     inc=True)

    prev_gq = 0
    for c in range(NCH):
        prev_gq = emit("gq", lambda g, c=c: g.dma_start(
            out=wo_all[:, 1024 * c:1024 * (c + 1)], in_=wout_h[128 * c:128 * c + 128, :]),
            inc=True, waits=[("gps", CK["gps0"])])
    prev_tq = 0
    prev_dv = 0
    stg0 = qkbd[0][:, 0:1024]
    stg1 = qkbd[0][:, 1024:2048]
    for (whh, dstT, halfscale) in ((wr_h, wrT, False), (wk_h, wkT, False), (wv_h, wvT, True)):
        for dc in range(NCH):
            wl = emit("gq", lambda g, whh=whh, dc=dc: g.dma_start(
                out=stg0, in_=whh[128 * dc:128 * dc + 128, :]), inc=True,
                waits=[("tq", prev_tq), ("dve", prev_dv)])
            src = stg0
            if halfscale:
                prev_dv = emit("dve", lambda v: v.tensor_scalar(
                    out=stg1, in0=stg0, scalar1=0.5, scalar2=None, op0=ALU.mult), inc=True,
                    waits=[("gq", wl)])
                src = stg1
            dst = dstT.rearrange("p (a b f) -> p a b f", a=8, f=128)[:, :, dc, :]
            prev_tq = emit("tq", lambda s, dst=dst, src=src.rearrange(
                "p (c f) -> p c f", c=8): s.dma_start(out=dst, in_=src, transpose=True),
                inc=True, waits=([("gq", wl)] if not halfscale else [("dve", prev_dv)]))
    CK["wffn_dve"] = max(prev_sc, prev_dv)
    CK["wffn_gq"] = prev_gq
    CK["wffn_tq"] = prev_tq
    CK["wsc_free_dve"] = prev_sc
    CK["wsc_free_act"] = prev_ca
    CK["arena_free_pe"] = CK["pe_ubv"]


    def dget(key, default=0):
        return done.get(key, default)

    pe_big = [pA, pB, pC, pD]
    big_i = [0]
    big_evac_d = {id(pA): ("dve", CK["btile"]), id(pB): ("dve", CK["btile"]),
                  id(pC): ("dve", CK["ubv"]), id(pD): ("dve", CK["ubv"]),
                  id(pE): ("dve", CK["ubv"]), id(pF): ("dve", 0)}

    class _BE:
        def __getitem__(self, b):
            return big_evac_d[id(b)]

        def __setitem__(self, b, v):
            big_evac_d[id(b)] = v

    big_evac = _BE()

    def next_big():
        b = pe_big[big_i[0] % 2]
        big_i[0] += 1
        return b

    def rstd_chain(lnid, seq, steps, xtr_waits=()):
            r1, sq_ = stc(seq, lnid, 0), stc(seq, lnid, 1)
            emit("dve", lambda v: v.tensor_scalar(
                out=r_nm, in0=r1.broadcast_to([P, W64]), scalar1=-1.0 / D,
                scalar2=None, op0=ALU.mult), waits=list(xtr_waits))
            emit("dve", lambda v: v.tensor_scalar(
                out=r_tt, in0=sq_.broadcast_to([P, W64]), scalar1=1.0 / D,
                scalar2=1e-5, op0=ALU.mult, op1=ALU.add))
            emit("dve", lambda v: v.tensor_tensor(out=r_aa, in0=r_nm, in1=r_nm, op=ALU.mult))
            emit("dve", lambda v: v.tensor_tensor(out=r_tt, in0=r_tt, in1=r_aa,
                                                  op=ALU.subtract))
            emit("dve", lambda v: v.tensor_scalar(out=r_aa, in0=r_tt, scalar1=0.5, scalar2=0.5,
                                                  op0=ALU.mult, op1=ALU.add))
            emit("dve", lambda v: v.reciprocal(out=r_yy, in_=r_aa))
            for _ in range(steps):
                emit("dve", lambda v: v.tensor_tensor(out=r_aa, in0=r_yy, in1=r_yy,
                                                      op=ALU.mult))
                emit("dve", lambda v: v.tensor_tensor(out=r_hx, in0=r_aa, in1=r_tt,
                                                      op=ALU.mult))
                emit("dve", lambda v: v.tensor_scalar(out=r_aa, in0=r_hx, scalar1=-0.5,
                                                      scalar2=1.5, op0=ALU.mult, op1=ALU.add))
                emit("dve", lambda v: v.tensor_tensor(out=r_yy, in0=r_yy, in1=r_aa,
                                                      op=ALU.mult))

    def emit_F1(m):
        seq, blk = m % 2, m // 2
        r1, sq_ = stc(seq, 0, 0), stc(seq, 0, 1)
        wx = xload[m]
        emit("act", lambda a, seq=seq, r1=r1: a.activation(
            out=xhat[seq][:], in_=xs[seq][:], func=AF.Copy, accum_out=r1),
            waits=[("sq", wx),
                   ("tq", max(dget(("ht", m - 2)), dget(("xt", m - 2))))])
        c_sqa = emit("actd", lambda a, seq=seq, sq_=sq_: a.activation(
            out=xhat[seq][:], in_=xs[seq][:], func=AF.Square, accum_out=sq_),
            inc=True)

        rstd_chain(0, seq, 2, xtr_waits=[("act", c_sqa)])
        c_xh = emit("dve", lambda v, seq=seq: v.tensor_scalar(
            out=xhat[seq][:], in0=xs[seq][:], scalar1=r_nm[:, 0:1], scalar2=r_yy[:, 0:1],
            op0=ALU.add, op1=ALU.mult), inc=True, waits=[("tq", dget(("ht", m - 2)))])
        maybe_tap("xhat", m, lambda: xhat[seq][:], lambda: [("dve", c_xh)])

        if stop_after == "ln1":
            return True
        done[("xt", m)] = emit("tq", lambda s, seq=seq: s.dma_start(
            out=xnfm[seq].rearrange("p (c f) -> p c f", c=8),
            in_=xhat[seq].rearrange("p (c f) -> p c f", c=8), transpose=True), inc=True,
            waits=[("dve", c_xh), ("pe", dget(("qkv_done", m - 2))),
                   ("tq", dget(("skt", m - 2)))])
        maybe_tap("xnfm", m, lambda: xnfm[seq][:], lambda: [("tq", done[("xt", m)])])

        return stop_after == "xt"

    def emit_F2qk(m, mp0, mp1):
        seq, blk = m % 2, m // 2
        cev = 0
        for mp in range(mp0, mp1):
            bank = next_big()
            beng, bval = big_evac[bank]
            for half in range(2):
                mc = 2 * mp + half
                for kc in range(NCH):
                    wts = []
                    if half == 0 and kc == 0:
                        wts = [("tq", done[("xt", m)]), (beng, bval)]
                        if m < 2 and mp == 0:
                            wts.append(("dve", CK["wffn_dve"]))
                    emit("pe", lambda t, mc=mc, kc=kc, half=half, seq=seq, bank=bank: t.matmul(
                        out=bank[:, 128 * half:128 * (half + 1)],
                        lhsT=wq_all[:, 3072 * kc + 128 * mc:3072 * kc + 128 * (mc + 1)],
                        rhs=xnfm[seq][:, 128 * kc:128 * (kc + 1)],
                        start=(kc == 0), stop=(kc == NCH - 1)), waits=wts)
            cpe = emit("pe", lambda t: None, inc=True)
            ubv2 = ubqk[:, 2 * mp:2 * mp + 2].unsqueeze(2).broadcast_to([P, 2, 128])
            cev = emit("dve", lambda v, mp=mp, seq=seq, bank=bank, ubv2=ubv2: v.tensor_tensor(
                out=qkfm[seq][:, 256 * mp:256 * (mp + 1)].rearrange("p (a f) -> p a f", f=128),
                in0=bank[:, 0:256].rearrange("p (a f) -> p a f", f=128),
                in1=ubv2, op=ALU.add), inc=True,
                waits=[("pe", cpe), ("pe", dget(("scores_done", m - 2)))])
            big_evac[bank] = ("dve", cev)
        return cev

    def emit_F2a(m):
        done[("qk_evac_q", m)] = emit_F2qk(m, 0, 4)
        return False

    def emit_F2b(m):
        done[("qk_evac", m)] = emit_F2qk(m, 4, 8)
        return False

    def emit_F2c(m):
        seq, blk = m % 2, m // 2
        for nh in range(2):
            bank = next_big()
            beng, bval = big_evac[bank]
            for kc in range(NCH):
                wts = []
                if kc == 0:
                    wts = [("tq", done[("xt", m)]), (beng, bval)]
                emit("pe", lambda t, nh=nh, kc=kc, seq=seq, bank=bank: t.matmul(
                    out=bank[:, 0:512], lhsT=xnfm[seq][:, 128 * kc:128 * (kc + 1)],
                    rhs=wq_all[:, 3072 * kc + 2048 + 512 * nh:3072 * kc + 2048 + 512 * (nh + 1)],
                    start=(kc == 0), stop=(kc == NCH - 1)), waits=wts)
            cpe = emit("pe", lambda t: None, inc=True)
            vdst = vb[seq].rearrange("p (c g f) -> p c g f", g=2, f=65)[:, 4 * nh:4 * (nh + 1), :, 0:64]
            vsrc = bank[:, 0:512].rearrange("p (c g f) -> p c g f", c=4, g=2)
            usrc = ubv[:, 512 * nh:512 * (nh + 1)].rearrange("p (c g f) -> p c g f", c=4, g=2)
            cev = emit("dve", lambda v, vdst=vdst, vsrc=vsrc, usrc=usrc: v.tensor_tensor(
                out=vdst, in0=vsrc, in1=usrc, op=ALU.add), inc=True,
                waits=[("pe", cpe), ("dve", dget(("usub", m - 2, SEGB - 1))),
                       ("pe", dget(("sdpa_done", m - 2))), ("dve", CK["ubv"] if m < 2 else 0)])
            big_evac[bank] = ("dve", cev)
        done[("v_evac", m)] = cev
        done[("qkv_done", m)] = now("pe")
        maybe_tap("qkfm", m, lambda: qkfm[seq][:], lambda: [("dve", done[("qk_evac", m)])])
        maybe_tap("vb", m, lambda: vb[seq][:], lambda: [("dve", done[("v_evac", m)])])

        return stop_after == "qkv"

    def emit_F2(m):
        return emit_F2a(m) or emit_F2b(m) or emit_F2c(m)

    def emit_F3(m):
        seq, blk = m % 2, m // 2
        c_expq = emit("act", lambda a, seq=seq: a.activation(out=eexp[:], in_=qkfm[seq][:, 0:1024],
                                                             func=AF.Exp), inc=True,
                      waits=[("dve", done[("qk_evac_q", m)]), ("dve", dget(("elu_stt", m - 1))),
                             ("pe", CK["arena_free_pe"]),
                             ("dve", dget(("mix_done", m - 1)))])
        emit("dve", lambda v: v.tensor_scalar(out=emin[:], in0=eexp[:], scalar1=1.0,
                                              scalar2=None, op0=ALU.min),
             waits=[("act", c_expq), ("pe", dget(("sdpa_done", m - 1)))])
        qdst = qkbd[seq].rearrange("p (c s g f) -> p c s g f", c=8, s=4, g=2)[:, :, :, 0, :]
        c_sttq = emit("dve", lambda v, seq=seq, qdst=qdst: v.scalar_tensor_tensor(
            out=qdst, in0=qkfm[seq][:, 0:1024].rearrange("p (c s f) -> p c s f", c=8, f=32),
            scalar=0.0, in1=emin.rearrange("p (c s f) -> p c s f", c=8, f=32),
            op0=ALU.max, op1=ALU.add), inc=True,
            waits=[("pe", dget(("phb_done", m - 2))),
                   ("tq", CK["wffn_tq"] if m < 2 else 0)])
        c_expk = emit("act", lambda a, seq=seq: a.activation(out=eexp[:],
                                                             in_=qkfm[seq][:, 1024:2048],
                                                             func=AF.Exp), inc=True,
                      waits=[("dve", c_sttq), ("dve", done[("qk_evac", m)])])
        done[("elu_exp", m)] = c_expk
        c_min = c_expk
        emit("dve", lambda v: v.tensor_scalar(out=emin[:], in0=eexp[:], scalar1=1.0,
                                              scalar2=None, op0=ALU.min),
             waits=[("act", c_expk)])
        emit("dve", lambda v, seq=seq: v.scalar_tensor_tensor(
            out=skfm[seq][:], in0=qkfm[seq][:, 1024:2048], scalar=0.0,
            in1=emin[:], op0=ALU.max, op1=ALU.add),
            waits=[("pe", done[("qkv_done", m)])])
        kdst = qkbd[seq].rearrange("p (c s g f) -> p c s g f", c=8, s=4, g=2)[:, :, :, 1, :]
        c3 = emit("dve", lambda v, seq=seq, kdst=kdst: v.tensor_copy(
            out=kdst, in_=skfm[seq].rearrange("p (c s f) -> p c s f", c=8, f=32)), inc=True,
            waits=[("tq", CK["wffn_tq"] if m < 2 else 0)])
        done[("elu_stt", m)] = c3
        maybe_tap("qkbd", m, lambda: qkbd[seq][:], lambda: [("dve", c3)])
        done[("skt", m)] = emit("tq", lambda s, seq=seq: s.dma_start(
            out=skT[seq].rearrange("p (c f) -> p c f", c=8),
            in_=skfm[seq].rearrange("p (c f) -> p c f", c=8), transpose=True), inc=True,
            waits=[("dve", c3), ("pe", dget(("phb_done", m - 2)))])

        return stop_after == "elu"

    def emit_F4(m):
        seq, blk = m % 2, m // 2
        c_min = done[("elu_exp", m)]
        cosv = cosT[:, TOK * blk:TOK * (blk + 1)]
        sinv = sinT[:, TOK * blk:TOK * (blk + 1)]
        prev_rope_dve = 0
        for half in range(2):
            base = 1024 * half
            for cp in range(4):
                bank = next_big()
                beng, bval = big_evac[bank]
                for half2 in range(2):
                    c = 2 * cp + half2
                    cpe = emit("pe", lambda t, seq=seq, base=base, c=c, half2=half2, bank=bank: t.matmul(
                        out=bank[:, 128 * half2:128 * (half2 + 1)], lhsT=smat[:],
                        rhs=qkfm[seq][:, base + 128 * c:base + 128 * (c + 1)],
                        start=True, stop=True), inc=(half2 == 1),
                        waits=([("dve", done[("elu_stt", m)]), (beng, bval),
                                ("act", c_min), ("dve", CK["smat"])] if half2 == 0 else []))
                c2 = 2 * cp
                cosv2 = cosv.unsqueeze(1).broadcast_to([P, 2, TOK])
                sinv2 = sinv.unsqueeze(1).broadcast_to([P, 2, TOK])
                emit("dve", lambda v, bank=bank, sinv2=sinv2: v.tensor_tensor(
                    out=rop2[:].rearrange("p (a f) -> p a f", f=TOK) if False else rop2w,
                    in0=bank[:, 0:256].rearrange("p (a f) -> p a f", f=TOK),
                    in1=sinv2, op=ALU.mult), waits=[("pe", cpe)])
                emit("dve", lambda v, seq=seq, base=base, c2=c2, cosv2=cosv2: v.tensor_tensor(
                    out=rop1w,
                    in0=qkfm[seq][:, base + 128 * c2:base + 128 * (c2 + 2)].rearrange(
                        "p (a f) -> p a f", f=TOK),
                    in1=cosv2, op=ALU.mult))
                cev = emit("dve", lambda v, seq=seq, base=base, c2=c2: v.tensor_tensor(
                    out=qkfm[seq][:, base + 128 * c2:base + 128 * (c2 + 2)].rearrange(
                        "p (a f) -> p a f", f=TOK),
                    in0=rop2w, in1=rop1w, op=ALU.add), inc=True)
                big_evac[bank] = ("dve", cev)
                prev_rope_dve = cev
        done[("rope", m)] = prev_rope_dve
        maybe_tap("qkrope", m, lambda: qkfm[seq][:], lambda: [("dve", done[("rope", m)])])

        return stop_after == "rope"

    pa_state = {}

    def pa_lvl_of():
        return {"pa_scores": 1, "pa_exp": 2, "pa_mask": 3, "pa_sdpa": 4,
                "pa_recip": 5}.get(stop_after, 99)

    def emit_Ascores(m, r):
        seq, blk = m % 2, m // 2
        hf = r % 2
        c0 = 4 * (r // 2)
        sbank = pC if r % 2 == 0 else pD
        sbeng, sbval = big_evac[sbank]
        for ci in range(4):
            ch = c0 + ci
            wts = []
            if ci == 0:
                wts = [("dve", done[("rope", m)]), (sbeng, sbval),
                       ("act", dget(("exp_done", m, r - 2), dget(("exp_done", m - 1, r + 2))))]
            cpe = emit("pe", lambda t, seq=seq, ch=ch, hf=hf, ci=ci, sbank=sbank: t.matmul(
                out=sbank[:, 128 * ci:128 * (ci + 1)],
                lhsT=qkfm[seq][64 * hf:64 * (hf + 1), 1024 + 128 * ch:1024 + 128 * (ch + 1)],
                rhs=qkfm[seq][64 * hf:64 * (hf + 1), 128 * ch:128 * (ch + 1)],
                start=True, stop=True, tile_position=(64 * hf, 0)),
                inc=(ci == 3), waits=wts)
        done[("scorespe", m, r)] = cpe

    def emit_Ar(m, r):
        seq, blk = m % 2, m // 2
        pa_lvl = pa_lvl_of()
        probs = probs_sv[seq]
        pend_evac = pa_state.setdefault(m, [None])
        if True:
            hf = r % 2
            c0 = 4 * (r // 2)
            sbank = pC if r % 2 == 0 else pD
            obank = pE if r % 2 == 0 else pF
            rcX = rc1A if r % 2 == 0 else rc1B
            cpe = done[("scorespe", m, r)]
            if pa_lvl < 2:
                return
            c_ex = emit("act", lambda a, r=r, sbank=sbank: a.activation(
                out=probs[r % 2][:], in_=sbank[:], func=AF.Exp, scale=0.125, bias=cm20[:]),
                inc=True, waits=[("pe", cpe),
                                 ("dve", dget(("msk_done", m, r - 2),
                                              dget(("msk_done", m - 1, r + 2)))),
                                 ("dve", done[("elu_stt", m)] if r == 0 else 0),
                                 ("pe", CK["arena_free_pe"] if m < 2 else 0),
                                 ("gps", CK["gps0"] if m < 2 else 0)])
            done[("exp_done", m, r)] = c_ex
            big_evac[sbank] = ("act", c_ex)
            if pa_lvl < 3:
                return
            c_mk = emit("dve", lambda v, r=r: v.tensor_tensor(
                out=probs[r % 2].rearrange("p (a f) -> p a f", f=128),
                in0=probs[r % 2].rearrange("p (a f) -> p a f", f=128),
                in1=mask01[:].unsqueeze(1).broadcast_to([P, 4, P]),
                op=ALU.mult), inc=True, waits=[("act", c_ex),
                                               ("gps", CK["gps0"] if m < 2 else 0)])
            done[("msk_done", m, r)] = c_mk
            if pa_lvl < 4:
                return
            obeng, obval = big_evac[obank]
            for ci in range(4):
                ch = c0 + ci
                wts = []
                if ci == 0:
                    wts = [("dve", c_mk), ("dve", done[("v_evac", m)]), (obeng, obval),
                           ("dve", dget(("nds_mult", m - 1, SEGB - 1))),
                           ("gps", CK["gps0"] if m < 2 else 0)]
                cpe = emit("pe", lambda t, seq=seq, ch=ch, hf=hf, ci=ci, r=r, obank=obank: t.matmul(
                    out=obank[:, 65 * ci:65 * (ci + 1)],
                    lhsT=probs[r % 2][:, 128 * ci:128 * (ci + 1)],
                    rhs=vb[seq][:, 130 * ch + 65 * hf:130 * ch + 65 * (hf + 1)],
                    start=True, stop=True), inc=(ci == 3), waits=wts)
            done[("sdpape", m, r)] = cpe
            if r < 2:
                emit_Ascores(m, r + 2)
            if pa_lvl < 5:
                return
            emit("dve", lambda v, obank=obank, rcX=rcX: v.reciprocal(
                out=rcX[:], in_=obank[:, 0:260].rearrange("p (a b) -> p a b", b=65)[:, :, 64]),
                waits=[("pe", cpe), ("dve", CK["btile"] if m == 0 else 0)])
            if pend_evac[0] is None:
                ddrain()
            else:
                (pr, prcX, probank) = pend_evac[0]
                sdst = sdpa[seq].rearrange("p (c g b) -> p c g b", g=2, b=65)[
                    :, 4 * (pr // 2):4 * (pr // 2) + 4, pr % 2, :]
                c_se = emit("dve", lambda v, sdst=sdst, probank=probank, prcX=prcX: v.tensor_tensor(
                    out=sdst, in0=probank[:, 0:260].rearrange("p (a b) -> p a b", b=65),
                    in1=prcX[:].unsqueeze(2).broadcast_to([P, 4, 65]), op=ALU.mult),
                    inc=True, waits=[("dve", dget(("blend", m - 2, SEGB - 1)))])
                done[("sdpa_evac", m, pr)] = c_se
                big_evac[probank] = ("dve", c_se)
            ombv = omb.rearrange("p (c g) -> p c g", g=2)[:, c0:c0 + 4, hf]
            emit("dve", lambda v, ombv=ombv, rcX=rcX: v.tensor_tensor(
                out=rcX[:], in0=rcX[:], in1=ombv, op=ALU.mult))
            pend_evac[0] = (r, rcX, obank)

    def emit_Aflush(m):
        seq, blk = m % 2, m // 2
        pa_lvl = pa_lvl_of()
        pend_evac = pa_state.setdefault(m, [None])
        done[("sdpa_done", m)] = now("pe")
        done[("scores_done", m)] = now("pe")
        if pa_lvl >= 5 and pend_evac[0] is not None:
            (pr, prcX, probank) = pend_evac[0]
            ddrain()
            sdst = sdpa[seq].rearrange("p (c g b) -> p c g b", g=2, b=65)[
                :, 4 * (pr // 2):4 * (pr // 2) + 4, pr % 2, :]
            c_se = emit("dve", lambda v, sdst=sdst, probank=probank, prcX=prcX: v.tensor_tensor(
                out=sdst, in0=probank[:, 0:260].rearrange("p (a b) -> p a b", b=65),
                in1=prcX[:].unsqueeze(2).broadcast_to([P, 4, 65]), op=ALU.mult),
                inc=True, waits=[("dve", dget(("blend", m - 2, SEGB - 1)))])
            done[("sdpa_evac", m, pr)] = c_se
            big_evac[probank] = ("dve", c_se)
        if pa_lvl >= 99:
            maybe_tap("sdpa", m, lambda: sdpa[seq][:],
                      lambda: [("dve", done[("sdpa_evac", m, 3)])])

    def emit_As(m, s_):
        seq, blk = m % 2, m // 2
        if True:
            gseg = SEGB * blk + s_
            r0 = 32 * s_
            shdst = shadow[seq].rearrange("p (c f) -> p c f", f=65)[:, :, 0:64]
            if gseg > 0:
                c_shA = emit("act", lambda a, seq=seq, shdst=shdst: a.activation(
                    out=shdst[:, 0:4, :],
                    in_=memPS[seq].rearrange("p (c f) -> p c f", f=64)[:, 0:4, :],
                    func=AF.Copy), inc=True,
                    waits=[("pe", dget(("delta_h", m, s_ - 1),
                                       dget(("delta", m - 2, SEGB - 1))))])
                c_sh = emit("act", lambda a, seq=seq, shdst=shdst: a.activation(
                    out=shdst[:, 4:8, :],
                    in_=memPS[seq].rearrange("p (c f) -> p c f", f=64)[:, 4:8, :],
                    func=AF.Copy), inc=True,
                    waits=[("pe", dget(("delta", m, s_ - 1), dget(("delta", m - 2, SEGB - 1))))])
            else:
                c_shA = c_sh = now("act")
            done[("shadow", m, s_)] = c_sh
            emit("dve", lambda v, seq=seq, s_=s_: v.tensor_reduce(
                out=zred[:], in_=qkbd[seq].rearrange("p (c s g f) -> p c s g f",
                                                     c=8, s=4, g=2)[:, :, s_, 1, :],
                axis=mybir.AxisListType.X, op=ALU.add),
                waits=[("dve", done[("elu_stt", m)]),
                       ("gps", CK["gps0"] if m < 2 else 0)])
            for jj in range(8):
                bank = pE if jj < 4 else pF
                col = 65 * (jj % 4)
                for hf in range(2):
                    wts = []
                    if jj == 0 and hf == 0:
                        wts = [("dve", max(done[("elu_stt", m)],
                                           done[("sdpa_evac", m, 3)])),
                               ("act", c_shA),
                               ("dve", dget(("nds_mult", m, s_ - 1))),
                               ("gps", CK["gps0"] if m < 2 else 0)]
                    elif jj == 4 and hf == 0:
                        wts = [("act", c_sh)]
                    cpe = emit("pe", lambda t, seq=seq, s_=s_, jj=jj, hf=hf, bank=bank,
                               col=col: t.matmul(
                        out=bank[64 * hf:64 * (hf + 1), col:col + 65],
                        lhsT=qkbd[seq][64 * hf:64 * hf + 64,
                                       256 * jj + 64 * s_:256 * jj + 64 * (s_ + 1)],
                        rhs=shadow[seq][64 * hf:64 * hf + 64, 65 * jj:65 * (jj + 1)],
                        start=True, stop=True, tile_position=(64 * hf, 64 * hf)),
                        inc=(jj in (3, 7) and hf == 1), waits=wts)
                    if jj == 3 and hf == 1:
                        cpe_half = cpe
            done[("numskm", m, s_)] = cpe
            emit("dve", lambda v: v.reciprocal(
                out=rc1A[:], in_=pE[:, 0:260].rearrange("p (a b) -> p a b", b=65)[:, :, 64]),
                waits=[("pe", cpe_half),
                       ("dve", dget(("usub", m, s_ - 1), dget(("usub", m - 1, SEGB - 1)))),
                       ("dve", dget(("blend", m, s_ - 1)))])
            emit("dve", lambda v: v.reciprocal(
                out=rc1B[:], in_=pF[:, 0:260].rearrange("p (a b) -> p a b", b=65)[:, :, 64]),
                waits=[("pe", done[("numskm", m, s_)])])
            emit("dve", lambda v: v.tensor_tensor(out=rc1A[:], in0=rc1A[:],
                                                  in1=btile[:, 0:4], op=ALU.mult))
            emit("dve", lambda v: v.tensor_tensor(out=rc1B[:], in0=rc1B[:],
                                                  in1=btile[:, 4:8], op=ALU.mult))
            emit("dve", lambda v, seq=seq: v.tensor_tensor(out=zf[seq][:], in0=zf[seq][:],
                                                           in1=zred[:], op=ALU.add))
            for half in range(2):
                bank = pE if half == 0 else pF
                rcX = rc1A if half == 0 else rc1B
                for hf in range(2):
                    prow = 32 + 64 * hf
                    uview = ut[r0:r0 + 32, :].rearrange(
                        "p (c g f) -> p c g f", g=2, f=64)[:, 4 * half:4 * half + 4, hf, :]
                    emit("dve", lambda v, bank=bank, prow=prow, uview=uview, rcX=rcX: v.tensor_tensor(
                        out=uview,
                        in0=bank[prow:prow + 32, 0:260].rearrange("p (a b) -> p a b", b=65)[:, :, 0:64],
                        in1=rcX[prow:prow + 32, :].unsqueeze(2).broadcast_to([32, 4, 64]),
                        op=ALU.mult))
            uin = ut[r0:r0 + 32, :].rearrange("p (c g f) -> p c g f", g=2, f=64)
            vsrc = vb[seq][r0:r0 + 32, :].rearrange("p (c g f) -> p c g f", g=2, f=65)[:, :, :, 0:64]
            cu = emit("dve", lambda v, vsrc=vsrc, uin=uin: v.tensor_tensor(
                out=uin, in0=vsrc, in1=uin, op=ALU.subtract), inc=True,
                waits=[("pe", dget(("delta", m, s_ - 1)))])
            done[("usub", m, s_)] = cu
            emit("dve", lambda v, seq=seq: v.tensor_copy(
                out=shadow[seq].rearrange("p (c f) -> p c f", f=65)[:, :, 64],
                in_=zf[seq][:]))
            c_nds = 0
            for half in range(2):
                bank = pE if half == 0 else pF
                rcX = rc1A if half == 0 else rc1B
                for hf in range(2):
                    nrow = 64 * hf
                    oview = otok[seq][r0:r0 + 32, :].rearrange(
                        "p (c g f) -> p c g f", g=2, f=64)[:, 4 * half:4 * half + 4, hf, :]
                    c_nds = emit("dve", lambda v, bank=bank, nrow=nrow, oview=oview, rcX=rcX: v.tensor_tensor(
                        out=oview,
                        in0=bank[nrow:nrow + 32, 0:260].rearrange("p (a b) -> p a b", b=65)[:, :, 0:64],
                        in1=rcX[nrow:nrow + 32, :].unsqueeze(2).broadcast_to([32, 4, 64]),
                        op=ALU.mult), inc=(half == 1 and hf == 1))
            done[("nds_mult", m, s_)] = c_nds
            oview = otok[seq][r0:r0 + 32, :].rearrange("p (c g f) -> p c g f", g=2, f=64)
            sdv = sdpa[seq][r0:r0 + 32, :].rearrange("p (c g b) -> p c g b", g=2, b=65)[:, :, :, 0:64]
            cb = emit("dve", lambda v, oview=oview, sdv=sdv: v.tensor_tensor(
                out=oview, in0=oview, in1=sdv, op=ALU.add), inc=True)
            done[("blend", m, s_)] = cb
            first = (gseg == 0)
            for jj in range(8):
                for hf in range(2):
                    wts = []
                    if jj == 0 and hf == 0:
                        wts = [("dve", cu), ("act", c_sh), ("tq", done[("skt", m)]),
                               ("dve", CK["pszero"] if m < 2 else 0)]
                    cpe = emit("pe", lambda t, seq=seq, s_=s_, jj=jj, hf=hf, first=first: t.matmul(
                        out=memPS[seq][64 * hf:64 * (hf + 1), 64 * jj:64 * (jj + 1)],
                        lhsT=skT[seq][32 * s_:32 * (s_ + 1),
                                      128 * jj + 64 * hf:128 * jj + 64 * (hf + 1)],
                        rhs=ut[32 * s_:32 * (s_ + 1),
                               128 * jj + 64 * hf:128 * jj + 64 * (hf + 1)],
                        start=False, stop=False,
                        tile_position=(32 * s_, 64 * hf), skip_group_check=True),
                        inc=(jj in (3, 7) and hf == 1), waits=wts)
                    if jj == 3 and hf == 1:
                        done[("delta_h", m, s_)] = cpe
            done[("delta", m, s_)] = cpe

    def emit_Atail(m):
        seq, blk = m % 2, m // 2
        done[("phb_done", m)] = now("pe")
        maybe_tap("otok", m, lambda: otok[seq][:],
                  lambda: [("dve", done[("blend", m, SEGB - 1)])])
        maybe_tap("shadow", m, lambda: shadow[seq][:],
                  lambda: [("act", done[("shadow", m, SEGB - 1)])])
        done[("ot", m)] = emit("tq", lambda s, seq=seq: s.dma_start(
            out=ofm.rearrange("p (c f) -> p c f", c=8),
            in_=otok[seq].rearrange("p (c f) -> p c f", c=8), transpose=True), inc=True,
            waits=[("dve", done[("blend", m, SEGB - 1)]), ("pe", dget(("wout_done", m - 1)))])
        return stop_after == "ot"

    def emit_scan(m):
        emit_Ascores(m, 0)
        emit_Ascores(m, 1)
        for r in range(4):
            emit_Ar(m, r)
        emit_Aflush(m)
        if pa_lvl_of() < 99 or stop_after == "phasea":
            return True
        for s_ in range(SEGB):
            emit_As(m, s_)
        if stop_after == "phaseb":
            return True
        return emit_Atail(m)

    def emit_B1(m):
        seq, blk = m % 2, m // 2
        for nh in range(2):
            bank = next_big()
            beng, bval = big_evac[bank]
            for ic in range(NCH):
                wts = []
                if ic == 0:
                    wts = [("tq", done[("ot", m)]), (beng, bval)]
                emit("pe", lambda t, nh=nh, ic=ic, bank=bank: t.matmul(
                    out=bank[:, 0:512], lhsT=ofm[:, 128 * ic:128 * (ic + 1)],
                    rhs=wo_all[:, 1024 * ic + 512 * nh:1024 * ic + 512 * (nh + 1)],
                    start=(ic == 0), stop=(ic == NCH - 1)), waits=wts)
            cpe = emit("pe", lambda t: None, inc=True)
            cev = emit("dve", lambda v, seq=seq, nh=nh, bank=bank: v.scalar_tensor_tensor(
                out=xs[seq][:, 512 * nh:512 * (nh + 1)], in0=bank[:, 0:512], scalar=1.0,
                in1=xs[seq][:, 512 * nh:512 * (nh + 1)], op0=ALU.mult, op1=ALU.add), inc=True,
                waits=[("pe", cpe)])
            big_evac[bank] = ("dve", cev)
        done[("wout_done", m)] = now("pe")
        done[("s2", m)] = cev
        maybe_tap("s2", m, lambda: xs[seq][:], lambda: [("dve", cev)])
        return False

    def emit_B1b(m):
        seq, blk = m % 2, m // 2
        c_s2 = done[("s2", m)]
        r1, sq_ = stc(seq, 1, 0), stc(seq, 1, 1)
        emit("act", lambda a, seq=seq, r1=r1: a.activation(
            out=xnfm[seq][:], in_=xs[seq][:], func=AF.Copy, accum_out=r1),
            waits=[("dve", c_s2), ("tq", done[("skt", m)])])
        c_sqa2 = emit("actd", lambda a, seq=seq, sq_=sq_: a.activation(
            out=xnfm[seq][:], in_=xs[seq][:], func=AF.Square, accum_out=sq_), inc=True)
        rstd_chain(1, seq, 3, xtr_waits=[("act", c_sqa2)])
        c_h = emit("dve", lambda v, seq=seq: v.tensor_scalar(
            out=htok[seq][:], in0=xs[seq][:], scalar1=r_nm[:, 0:1], scalar2=r_yy[:, 0:1],
            op0=ALU.add, op1=ALU.mult), inc=True,
            waits=[("dve", dget(("ln3", m - 2))), ("sq", dget(("store", m - 2)))])
        c_hb = emit("act", lambda a, seq=seq: a.activation(
            out=hbf[seq][:], in_=htok[seq][:], func=AF.Copy), inc=True,
            waits=[("dve", c_h), ("tq", done[("ot", m)])])
        done[("chb", m)] = c_hb
        maybe_tap("htok", m, lambda: htok[seq][:], lambda: [("dve", c_h)])
        if m + 2 < nmt:
            load_x(m + 2, w=[("dve", c_h), ("act", c_sqa2)])

        return stop_after == "ln2"

    def emit_B2(m):
        seq, blk = m % 2, m // 2
        if blk > 0:
            c_halo = emit("dve", lambda v, seq=seq: v.tensor_copy(
                out=hh[seq][:], in_=hfm[seq].rearrange("p (c f) -> p c f", f=128)[:, :, 127]),
                inc=True, waits=[("dve", dget(("mix_done", m - 2)))])
        else:
            c_halo = now("dve")
        done[("ht", m)] = emit("tq", lambda s, seq=seq: s.dma_start(
            out=hfm[seq].rearrange("p (c f) -> p c f", c=8),
            in_=hbf[seq].rearrange("p (c f) -> p c f", c=8), transpose=True), inc=True,
            waits=[("act", done[("chb", m)]), ("dve", c_halo)])

        if stop_after == "ht":
            return True
        dtile = dtile_sv[seq]
        c_d2 = emit("dve", lambda v, seq=seq, dtile=dtile: v.tensor_tensor(
            out=dtile[:, 0:1024].rearrange("p (c f) -> p c f", f=128)[:, :, 0],
            in0=hfm[seq].rearrange("p (c f) -> p c f", f=128)[:, :, 0],
            in1=hh[seq][:], op=ALU.subtract), inc=True,
            waits=[("tq", done[("ht", m)]), ("dve", done[("blend", m, SEGB - 1)]),
                   ("gps", CK["gps0"] if m < 2 else 0)])
        emit("dve", lambda v, seq=seq, dtile=dtile: v.tensor_tensor(
            out=dtile[:, 0:1024].rearrange("p (c f) -> p c f", f=128)[:, :, 1:128],
            in0=hfm[seq].rearrange("p (c f) -> p c f", f=128)[:, :, 1:128],
            in1=hfm[seq].rearrange("p (c f) -> p c f", f=128)[:, :, 0:127], op=ALU.subtract))
        murb_v = murb[:].unsqueeze(2).broadcast_to([P, 8, 128])
        mukb_v = mukb[:].unsqueeze(2).broadcast_to([P, 8, 128])
        dv = dtile[:, 0:1024].rearrange("p (c f) -> p c f", f=128)
        emit("dve", lambda v, dv=dv, murb_v=murb_v: v.tensor_tensor(
            out=xr[0].rearrange("p (c f) -> p c f", f=128), in0=dv, in1=murb_v,
            op=ALU.mult),
            waits=[("dve", c_d2), ("dve", CK["wsc_free_dve"] if m < 2 else 0),
                   ("act", CK["wsc_free_act"] if m < 2 else 0),
                   ("pe", dget(("ffnmm_done", m - 1))),
                   ("act", dget(("tanh", m - 1, 1)))])
        emit("dve", lambda v, dv=dv, mukb_v=mukb_v: v.tensor_tensor(
            out=xk[0].rearrange("p (c f) -> p c f", f=128), in0=dv, in1=mukb_v,
            op=ALU.mult))
        emit("dve", lambda v, seq=seq: v.tensor_tensor(
            out=xr[0][:], in0=xr[0][:], in1=hfm[seq][:], op=ALU.add))
        cm = emit("dve", lambda v, seq=seq: v.tensor_tensor(
            out=xk[0][:], in0=xk[0][:], in1=hfm[seq][:], op=ALU.add), inc=True)
        done[("mix_done", m)] = cm
        maybe_tap("xr", m, lambda: xr[seq][:], lambda: [("dve", cm)])

        return stop_after == "mix"

    def emit_B3(m):
        seq, blk = m % 2, m // 2
        cm = done[("mix_done", m)]
        for nh in range(2):
            bank = next_big()
            beng, bval = big_evac[bank]
            for cc in range(NCH):
                wts = []
                if cc == 0:
                    wts = [("dve", cm), (beng, bval)]
                emit("pe", lambda t, seq=seq, nh=nh, cc=cc, bank=bank: t.matmul(
                    out=bank[:, 0:512], lhsT=xr[seq][:, 128 * cc:128 * (cc + 1)],
                    rhs=wrT[:, 1024 * cc + 512 * nh:1024 * cc + 512 * (nh + 1)],
                    start=(cc == 0), stop=(cc == NCH - 1)), waits=wts)
            cpe = emit("pe", lambda t: None, inc=True)
            ca = emit("act", lambda a, nh=nh, bank=bank: a.activation(
                out=tanh_t[:, 512 * nh:512 * (nh + 1)], in_=bank[:, 0:512], func=AF.Tanh,
                scale=0.5), inc=True,
                waits=[("pe", cpe), ("dve", dget(("ffno", m - 1)))])
            done[("tanh", m, nh)] = ca
            big_evac[bank] = ("act", ca)
        return False

    def emit_B4(m):
        seq, blk = m % 2, m // 2
        cm = done[("mix_done", m)]
        for nh in range(2):
            bank = next_big()
            beng, bval = big_evac[bank]
            for cc in range(NCH):
                wts = []
                if cc == 0:
                    wts = [("dve", cm), (beng, bval)]
                emit("pe", lambda t, seq=seq, nh=nh, cc=cc, bank=bank: t.matmul(
                    out=bank[:, 0:512], lhsT=xk[seq][:, 128 * cc:128 * (cc + 1)],
                    rhs=wkT[:, 1024 * cc + 512 * nh:1024 * cc + 512 * (nh + 1)],
                    start=(cc == 0), stop=(cc == NCH - 1)), waits=wts)
            cpe = emit("pe", lambda t: None, inc=True)
            emit("dve", lambda v, seq=seq, bank=bank: v.tensor_scalar(
                out=krelu_of[seq][:], in0=bank[:, 0:512], scalar1=0.0, scalar2=None,
                op0=ALU.max), waits=[("pe", cpe), ("pe", dget(("kv_done", m - 2))),
                                     ("tq", dget(("kt", m - 1))),
                                     ("act", dget(("ln3sq", m - 2))),
                                     ("dve", CK["wsc_free_dve"] if m < 2 else 0),
                                     ("act", CK["wsc_free_act"] if m < 2 else 0)])
            cev = emit("dve", lambda v, seq=seq, nh=nh: v.tensor_tensor(
                out=kact[seq][:, 512 * nh:512 * (nh + 1)], in0=krelu_of[seq][:],
                in1=krelu_of[seq][:], op=ALU.mult), inc=True)
            big_evac[bank] = ("dve", cev)
        c_ka = cev
        done[("kact", m)] = c_ka
        maybe_tap("kact", m, lambda: kact[seq][:], lambda: [("dve", c_ka)])
        done[("kt", m)] = emit("tq", lambda s, seq=seq: s.dma_start(
            out=kafm[seq].rearrange("p (c f) -> p c f", c=8),
            in_=kact[seq].rearrange("p (c f) -> p c f", c=8), transpose=True), inc=True,
            waits=[("dve", c_ka)])

        return stop_after == "kact"

    def emit_B5(m):
        seq, blk = m % 2, m // 2
        for nh in range(2):
            bank = next_big()
            beng, bval = big_evac[bank]
            for cc in range(NCH):
                wts = []
                if cc == 0:
                    wts = [("tq", done[("kt", m)]), (beng, bval)]
                emit("pe", lambda t, seq=seq, nh=nh, cc=cc, bank=bank: t.matmul(
                    out=bank[:, 0:512], lhsT=kafm[seq][:, 128 * cc:128 * (cc + 1)],
                    rhs=wvT[:, 1024 * cc + 512 * nh:1024 * cc + 512 * (nh + 1)],
                    start=(cc == 0), stop=(cc == NCH - 1)), waits=wts)
            cpe = emit("pe", lambda t: None, inc=True)
            emit("dve", lambda v, nh=nh, bank=bank: v.scalar_tensor_tensor(
                out=tanh_t[:, 512 * nh:512 * (nh + 1)], in0=tanh_t[:, 512 * nh:512 * (nh + 1)],
                scalar=1.0, in1=bank[:, 0:512], op0=ALU.add, op1=ALU.mult),
                waits=[("pe", cpe), ("act", done[("tanh", m, nh)])])
            cev = emit("dve", lambda v, seq=seq, nh=nh: v.tensor_tensor(
                out=htok[seq][:, 512 * nh:512 * (nh + 1)],
                in0=htok[seq][:, 512 * nh:512 * (nh + 1)],
                in1=tanh_t[:, 512 * nh:512 * (nh + 1)], op=ALU.add), inc=True,
                waits=[("tq", done[("ht", m)])])
            big_evac[bank] = ("dve", cev)
        done[("ffnmm_done", m)] = now("pe")
        done[("kv_done", m)] = now("pe")
        done[("ffno", m)] = cev
        maybe_tap("s3", m, lambda: htok[seq][:], lambda: [("dve", cev)])
        return False

    def emit_B5b(m):
        seq, blk = m % 2, m // 2
        c_s3 = done[("ffno", m)]
        r1, sq_ = stc(seq, 2, 0), stc(seq, 2, 1)
        emit("act", lambda a, seq=seq, r1=r1: a.activation(
            out=kafm[seq][:], in_=htok[seq][:], func=AF.Copy, accum_out=r1),
            waits=[("dve", c_s3), ("pe", done[("kv_done", m)])])
        c_sqa3 = emit("actd", lambda a, seq=seq, sq_=sq_: a.activation(
            out=kafm[seq][:], in_=htok[seq][:], func=AF.Square, accum_out=sq_), inc=True)
        done[("ln3sq", m)] = c_sqa3
        rstd_chain(2, seq, 3, xtr_waits=[("act", c_sqa3)])
        c_o = emit("dve", lambda v, seq=seq: v.tensor_scalar(
            out=htok[seq][:], in0=htok[seq][:], scalar1=r_nm[:, 0:1], scalar2=r_yy[:, 0:1],
            op0=ALU.add, op1=ALU.mult), inc=True,
            waits=[("sq", dget(("store", m - 2)))])
        done[("ln3", m)] = c_o
        maybe_tap("h3", m, lambda: htok[seq][:], lambda: [("dve", c_o)])
        done[("store", m)] = emit("sq", lambda s, seq=seq, blk=blk: s.dma_start(
            out=out_h[seq, TOK * blk:TOK * (blk + 1), :], in_=htok[seq][:]), inc=True,
            waits=[("dve", c_o)])
        return False

    if stop_after is None:
        def emit_front_all(m):
            emit_F1(m); emit_F2a(m); emit_F2b(m); emit_F2c(m); emit_F3(m); emit_F4(m)

        if nmt >= 1:
            emit_front_all(0)
        for it in range(nmt + 1):
            b = it - 1
            f = it + 1
            Bv = 0 <= b < nmt
            Fv = f < nmt
            if it < nmt:
                emit_Ascores(it, 0)
                emit_Ascores(it, 1)
                emit_Ar(it, 0)
                emit_Ar(it, 1)
                emit_Ar(it, 2)
                emit_Ar(it, 3)
                emit_Aflush(it)
                if Bv:
                    emit_B1(b)
                emit_As(it, 0)
                if Bv:
                    emit_B1b(b)
                    emit_B2(b)
                emit_As(it, 1)
                if Fv:
                    emit_F1(f)
                if Bv:
                    emit_B3(b)
                if Fv:
                    emit_F2a(f)
                emit_As(it, 2)
                if Bv:
                    emit_B4(b)
                if Fv:
                    emit_F2b(f)
                if Bv:
                    emit_B5(b)
                emit_As(it, 3)
                if Fv:
                    emit_F2c(f)
                emit_Atail(it)
                if Fv:
                    emit_F3(f)
                    emit_F4(f)
                if Bv:
                    emit_B5b(b)
            else:
                if Bv:
                    emit_B1(b); emit_B1b(b); emit_B2(b); emit_B3(b)
                    emit_B4(b); emit_B5(b); emit_B5b(b)
    else:
        for m in range(nmt):
            if (emit_F1(m) or emit_F2(m) or emit_F3(m) or emit_F4(m)
                    or emit_scan(m) or emit_B1(m) or emit_B1b(m) or emit_B2(m)
                    or emit_B3(m) or emit_B4(m) or emit_B5(m) or emit_B5b(m)):
                break

    total_sq = cnt["sq"]
    emit("sq", lambda s: None, waits=[("sq", total_sq)])

    with nc.Block() as block:
        block.tensor(_mk_runner(ops["pe"]))
        block.vector(_mk_runner(ops["dve"]))
        block.scalar(_mk_runner(ops["scalar"]))
        block.gpsimd(_mk_runner(ops["gps"]))
        block.sync(_mk_runner(ops["sync"]))

    ctx.close()
    return nc, tap_h


def _mk_runner(stream):
    def _run(eng):
        last_instr = [None]
        for (wl, fn, sem, step, logical) in stream:
            for (s, v) in wl:
                eng.wait_ge(s, v)
            instr = fn(eng)
            if instr is not None:
                last_instr[0] = instr
            if sem is not None:
                if logical == "actd":
                    eng.drain().then_inc(sem, step)
                else:
                    tgt = instr if instr is not None else last_instr[0]
                    tgt.then_inc(sem, step)
    return _run


_NC_CACHE = {}


def kernel(L0_attn_x, L0_attn_w_qkv, L0_attn_w_out, L0_attn_ln_g, L0_attn_ln_b,
           L0_attn_mem_beta, L0_ffn_W_r, L0_ffn_W_k, L0_ffn_W_v, L0_ffn_mu_r, L0_ffn_mu_k):
    x = np.ascontiguousarray(np.asarray(L0_attn_x, dtype=np.float32))
    if "nc" not in _NC_CACHE:
        _NC_CACHE["nc"] = build_nc()
    nc, _ = _NC_CACHE["nc"]
    shared = {
        "w_qkv": np.ascontiguousarray(np.asarray(L0_attn_w_qkv, np.float32)),
        "w_out": np.ascontiguousarray(np.asarray(L0_attn_w_out, np.float32)),
        "ln_g": np.ascontiguousarray(np.asarray(L0_attn_ln_g, np.float32)),
        "ln_b": np.ascontiguousarray(np.asarray(L0_attn_ln_b, np.float32)),
        "mem_beta": np.ascontiguousarray(np.asarray(L0_attn_mem_beta, np.float32).reshape(1, H)),
        "W_r": np.ascontiguousarray(np.asarray(L0_ffn_W_r, np.float32)),
        "W_k": np.ascontiguousarray(np.asarray(L0_ffn_W_k, np.float32)),
        "W_v": np.ascontiguousarray(np.asarray(L0_ffn_W_v, np.float32)),
        "mu_r": np.ascontiguousarray(np.asarray(L0_ffn_mu_r, np.float32)),
        "mu_k": np.ascontiguousarray(np.asarray(L0_ffn_mu_k, np.float32)),
    }
    in_maps = []
    for core in range(8):
        im = dict(shared)
        im["x"] = np.ascontiguousarray(x[2 * core:2 * core + 2])
        in_maps.append(im)
    res = run_bass_kernel_spmd(nc, in_maps, core_ids=list(range(8)))
    out = np.concatenate([r["out"] for r in res.results], axis=0)
    return np.ascontiguousarray(out.reshape(16, TSEQ, D).astype(np.float32))


# version 17
# speedup vs baseline: 1.3358x; 1.0198x over previous
import math
import numpy as np
import concourse.bass as bass
import concourse.mybir as mybir
from concourse.bass_utils import run_bass_kernel_spmd
from contextlib import ExitStack

F32 = mybir.dt.float32
BF16 = mybir.dt.bfloat16
I32 = mybir.dt.int32
AF = mybir.ActivationFunctionType
ALU = mybir.AluOpType

P = 128
D = 1024
NCH = 8
H = 16
SEG = 32
TOK = 128
NSEQ = 2
TSEQ = 2048
NBLK = TSEQ // TOK
NMT = NSEQ * NBLK
SEGB = TOK // SEG
EXPB = 20.0
LOG1E4 = math.log(10000.0)
TWO_PI = 2.0 * math.pi


def build_nc(taps=(), nmt=NMT, stop_after=None):
    nc = bass.Bass("TRN2", target_bir_lowering=False)
    x_h = nc.dram_tensor("x", [NSEQ, TSEQ, D], F32, kind="ExternalInput").ap()
    wqkv_h = nc.dram_tensor("w_qkv", [D, 3 * D], F32, kind="ExternalInput").ap()
    wout_h = nc.dram_tensor("w_out", [D, D], F32, kind="ExternalInput").ap()
    g_h = nc.dram_tensor("ln_g", [D], F32, kind="ExternalInput").ap()
    b_h = nc.dram_tensor("ln_b", [D], F32, kind="ExternalInput").ap()
    beta_h = nc.dram_tensor("mem_beta", [1, H], F32, kind="ExternalInput").ap()
    wr_h = nc.dram_tensor("W_r", [D, D], F32, kind="ExternalInput").ap()
    wk_h = nc.dram_tensor("W_k", [D, D], F32, kind="ExternalInput").ap()
    wv_h = nc.dram_tensor("W_v", [D, D], F32, kind="ExternalInput").ap()
    mur_h = nc.dram_tensor("mu_r", [D], F32, kind="ExternalInput").ap()
    muk_h = nc.dram_tensor("mu_k", [D], F32, kind="ExternalInput").ap()
    out_h = nc.dram_tensor("out", [NSEQ, TSEQ, D], F32, kind="ExternalOutput").ap()

    ctx = ExitStack()
    e = ctx.enter_context

    wq_all = e(nc.sbuf_tensor("wq_all", [P, NCH * 3072], BF16))
    wo_all = e(nc.sbuf_tensor("wo_all", [P, NCH * 1024], BF16))
    wrT = e(nc.sbuf_tensor("wrT", [P, NCH * 1024], BF16))
    wkT = e(nc.sbuf_tensor("wkT", [P, NCH * 1024], BF16))
    wvT = e(nc.sbuf_tensor("wvT", [P, NCH * 1024], BF16))
    cosT = e(nc.sbuf_tensor("cosT", [P, TSEQ], BF16))
    sinT = e(nc.sbuf_tensor("sinT", [P, TSEQ], BF16))
    mask01 = e(nc.sbuf_tensor("mask01", [P, P], BF16))
    smat = e(nc.sbuf_tensor("smat", [P, P], BF16))
    gcol = e(nc.sbuf_tensor("gcol", [P, 8], F32))
    bcol = e(nc.sbuf_tensor("bcol", [P, 8], BF16))
    murm1 = e(nc.sbuf_tensor("murm1", [P, 8], F32))
    mukm1 = e(nc.sbuf_tensor("mukm1", [P, 8], F32))
    murb = e(nc.sbuf_tensor("murb", [P, 8], BF16))
    mukb = e(nc.sbuf_tensor("mukb", [P, 8], BF16))
    ubqk = e(nc.sbuf_tensor("ubqk", [P, 16], F32))
    ubv = e(nc.sbuf_tensor("ubv", [P, 1024], BF16))
    btr = e(nc.sbuf_tensor("btr", [1, 16], F32))
    betab = e(nc.sbuf_tensor("betab", [1, 16], BF16))
    ombb = e(nc.sbuf_tensor("ombb", [1, 16], BF16))
    btile = e(nc.sbuf_tensor("btile", [P, 8], F32))
    omb = e(nc.sbuf_tensor("omb", [P, 16], F32))
    epst = e(nc.sbuf_tensor("epst", [P, 1], F32))
    cm20 = e(nc.sbuf_tensor("cm20", [P, 1], F32))
    ones1 = e(nc.sbuf_tensor("ones1", [1, P], BF16))
    pidx = e(nc.sbuf_tensor("pidx", [P, 1], F32))
    cidx = None
    par = e(nc.sbuf_tensor("par", [P, 2], F32))
    freq64 = e(nc.sbuf_tensor("freq64", [P, 1], F32))
    sti = e(nc.sbuf_tensor("sti", [P, 1], I32))
    gtmp = e(nc.sbuf_tensor("gtmp", [P, 8], F32))

    def perseq(name, cols, dt):
        return [e(nc.sbuf_tensor(f"{name}{s}", [P, cols], dt)) for s in range(NSEQ)]

    xs = perseq("xs", D, F32)
    xhat = perseq("xhat", D, BF16)
    xnfm = perseq("xnfm", D, BF16)
    qkfm = perseq("qkfm", 2 * D, BF16)
    qkbd = perseq("qkbd", 2 * D, BF16)
    skT = perseq("skT", D, BF16)
    vb = perseq("vb", 8 * 130, BF16)
    sdpa = perseq("sdpa", 16 * 65, BF16)
    shadow = perseq("shadow", 8 * 65, BF16)
    zf = perseq("zf", 8, F32)
    hh = perseq("hh", 8, BF16)
    otok = None
    htok = perseq("htok", D, F32)
    hbf = xhat
    otok = xhat
    hfm = perseq("hfm", D, BF16)
    skfm = xnfm
    kafm = perseq("kafm", D, BF16)
    h3 = None
    ofm = e(nc.sbuf_tensor("ofm", [P, D], BF16))
    ut = e(nc.sbuf_tensor("ut", [P, D], BF16))
    h3 = [qkbd[s_].bitcast(F32)[:, 0:1024] for s_ in range(NSEQ)]
    dtile_sv = [sdpa[s_][:, 0:1024] for s_ in range(NSEQ)]
    tanh_t = e(nc.sbuf_tensor("tanh_t", [P, D], F32))
    rop1 = e(nc.sbuf_tensor("rop1", [P, 2 * TOK], F32))
    rop2 = e(nc.sbuf_tensor("rop2", [P, 2 * TOK], F32))
    rop1w = rop1[:].rearrange("p (a f) -> p a f", f=TOK)
    rop2w = rop2[:].rearrange("p (a f) -> p a f", f=TOK)

    wsc = e(nc.sbuf_tensor("wsc", [P, 3072], BF16))
    wscf = wsc.bitcast(F32)
    xr_t = wsc[:, 0:1024]
    xk_t = wsc[:, 1024:2048]
    kact_t = wsc[:, 2048:3072]
    xr = [xr_t, xr_t]
    xk = [xk_t, xk_t]
    kact = [kact_t, kact_t]

    arena = e(nc.sbuf_tensor("arena", [P, 2048], BF16))
    emin = arena[:, 0:1024]
    eexp = arena[:, 1024:2048]
    lnsq = emin
    wraw = arena[:, 0:1536]
    probs_sv = [[arena[:, 0:512], arena[:, 512:1024]] for s_ in range(NSEQ)]
    ffsc = arena.bitcast(F32)[:, 0:512]
    krelu_of = [kafm[s_].bitcast(F32)[:, 0:512] for s_ in range(NSEQ)]

    ubvrow = ut[0:1, 0:1024]
    cidx = ut.bitcast(F32)[:, 0:128]
    ST = e(nc.sbuf_tensor("ST", [P, 9 * 6], F32))

    def stc(seq, lnid, j):
        c = 9 * (lnid + 3 * seq) + j
        return ST[:, c:c + 1]

    REG = e(nc.sbuf_tensor("REG", [P, 5 * 64], F32))
    W64 = 64
    r_nm, r_tt, r_yy, r_aa, r_hx = [REG[:, 64 * j:64 * j + 64] for j in range(5)]
    rc1A = e(nc.sbuf_tensor("rc1A", [P, 4], F32))
    rc1B = e(nc.sbuf_tensor("rc1B", [P, 4], F32))
    rc1 = rc1A
    zred = e(nc.sbuf_tensor("zred", [P, 8], F32))

    pA = e(nc.psum_tensor("pA", [P, 512], F32))
    pB = e(nc.psum_tensor("pB", [P, 512], F32))
    pC = e(nc.psum_tensor("pC", [P, 512], F32))
    pD = e(nc.psum_tensor("pD", [P, 512], F32))
    pE = e(nc.psum_tensor("pE", [P, 512], F32))
    pF = e(nc.psum_tensor("pF", [P, 512], F32))
    memPS = [e(nc.psum_tensor(f"memPS{s}", [P, 512], F32)) for s in range(NSEQ)]

    LOG2PHYS = {"pe": "pe", "dve": "dve", "act": "scalar", "tq": "sync",
                "gps": "gps", "gq": "gps", "sq": "sync", "actd": "scalar"}
    PHYS = ["pe", "dve", "scalar", "gps", "sync"]
    sems = {k: e(nc.semaphore(f"s_{k}")) for k in ("pe", "dve", "act", "gps", "sq", "tq", "gq")}
    sems["actd"] = sems["act"]
    cnt = {k: 0 for k in sems}
    cnt.pop("actd"); cnt["actd"] = None
    waitmark = {ph: {k: 0 for k in sems} for ph in PHYS}
    ops = {ph: [] for ph in PHYS}
    tap_bar = {"v": 0}

    def emit(logical, fn, inc=False, waits=()):
        ph = LOG2PHYS[logical]
        waits = list(waits)
        if tap_bar["v"]:
            waits.append(("sq", tap_bar["v"]))
        wl = []
        for (pr, val) in waits:
            if val and val > waitmark[ph][pr]:
                waitmark[ph][pr] = val
                wl.append((sems[pr], val))
        step = 16 if logical in ("sq", "tq", "gq") else 1
        cl = "act" if logical == "actd" else logical
        sem = None
        if inc:
            cnt[cl] += step
            sem = sems[cl]
        ops[ph].append((wl, fn, sem, step, logical))
        return cnt[cl]

    def now(l):
        return cnt[l]


    def ddrain():
        emit("dve", lambda v: v.drain())

    tap_h = {}

    def tap(name, view, waits):
        th = nc.dram_tensor(f"tap_{name}", list(view.shape), view.dtype,
                            kind="ExternalOutput").ap()
        tap_h[f"tap_{name}"] = str(view.dtype)
        emit("sq", lambda s, th=th, view=view: s.dma_start(out=th, in_=view),
             inc=True, waits=waits)
        tap_bar["v"] = cnt["sq"]

    def maybe_tap(stage, m, view_fn, waits_fn):
        for (tname, tmt) in taps:
            if tname == stage and tmt == m:
                tap(f"{stage}_{m}", view_fn(), waits_fn())

    CK = {}
    xload = {}
    done = {}

    def load_x(m, w=()):
        seq, blk = m % 2, m // 2
        xload[m] = emit("sq", lambda s, seq=seq, blk=blk: s.dma_start(
            out=xs[seq][:], in_=x_h[seq, TOK * blk:TOK * (blk + 1), :]), inc=True,
            waits=list(w))

    load_x(0)
    load_x(1)

    def vec_to_cols(vh, dest, first_waits=()):
        mk = 0
        for c in range(8):
            src = vh[128 * c:128 * c + 128].rearrange("(a b) -> a b", b=1)
            mk = emit("sq", lambda s, d=dest[:, c:c + 1], src=src: s.dma_start(out=d, in_=src),
                      inc=True, waits=(list(first_waits) if c == 0 else []))
        return mk

    CK["g"] = vec_to_cols(g_h, gcol)
    CK["b"] = vec_to_cols(b_h, gtmp)
    CK["bc"] = emit("dve", lambda v: v.tensor_copy(out=bcol[:], in_=gtmp[:]), inc=True,
                    waits=[("sq", CK["b"])])
    CK["mur"] = vec_to_cols(mur_h, gtmp, first_waits=[("dve", CK["bc"])])
    emit("dve", lambda v: v.tensor_scalar(out=murm1[:], in0=gtmp[:], scalar1=1.0,
                                          scalar2=-1.0, op0=ALU.mult, op1=ALU.add),
         waits=[("sq", CK["mur"])])
    ddrain()
    CK["m1"] = emit("dve", lambda v: v.tensor_copy(out=murb[:], in_=murm1[:]), inc=True)
    CK["muk"] = vec_to_cols(muk_h, gtmp, first_waits=[("dve", CK["m1"])])
    emit("dve", lambda v: v.tensor_scalar(out=mukm1[:], in0=gtmp[:], scalar1=1.0,
                                          scalar2=-1.0, op0=ALU.mult, op1=ALU.add),
         waits=[("sq", CK["muk"])])
    ddrain()
    CK["m2"] = emit("dve", lambda v: v.tensor_copy(out=mukb[:], in_=mukm1[:]), inc=True)
    CK["beta"] = emit("sq", lambda s: s.dma_start(out=btr[:], in_=beta_h), inc=True)

    emit("gps", lambda g: g.iota(pidx[:], pattern=[[0, 1]], channel_multiplier=1,
                                 allow_small_or_imprecise_dtypes=True))
    emit("gps", lambda g: g.iota(cidx[:], pattern=[[1, P]], channel_multiplier=0,
                                 allow_small_or_imprecise_dtypes=True))
    emit("gps", lambda g: g.memset(mask01[:], 1.0))
    mview = mask01.rearrange("p (a b) -> p a b", a=4)
    emit("gps", lambda g: g.drain())
    emit("gps", lambda g: g.affine_select(out=mview, in_=mview, pattern=[[-32, 4], [0, 32]],
                                          compare_op=ALU.is_ge, fill=0.0, base=0,
                                          channel_multiplier=1))
    emit("gps", lambda g: g.drain())
    emit("gps", lambda g: g.affine_select(out=mview, in_=mview, pattern=[[32, 4], [0, 32]],
                                          compare_op=ALU.is_ge, fill=0.0, base=31,
                                          channel_multiplier=-1))
    emit("gps", lambda g: g.drain())
    emit("gps", lambda g: g.affine_select(out=mview, in_=mview, pattern=[[32, 4], [1, 32]],
                                          compare_op=ALU.is_ge, fill=0.0, base=0,
                                          channel_multiplier=-1))
    emit("gps", lambda g: g.memset(ones1[:], 1.0))
    for s_ in range(NSEQ):
        vbv = vb[s_].rearrange("p (c f) -> p c f", f=130)
        emit("gps", lambda g, vbv=vbv: g.memset(vbv[:, :, 64:65], 1.0))
        emit("gps", lambda g, vbv=vbv: g.memset(vbv[:, :, 129:130], 1.0))
        emit("gps", lambda g, s_=s_: g.memset(hh[s_][:], 0.0))
        emit("gps", lambda g, s_=s_: g.memset(zf[s_][:], 1e-6))
        shv = shadow[s_].rearrange("p (c f) -> p c f", f=65)
        emit("gps", lambda g, s_=s_: g.memset(shadow[s_][:], 0.0))
        emit("gps", lambda g: g.drain())
        emit("gps", lambda g, shv=shv: g.memset(shv[:, :, 64:65], 1e-6))
    emit("gps", lambda g: g.memset(cm20[:], -EXPB))
    emit("gps", lambda g: g.memset(epst[:], 0.0))
    emit("gps", lambda g: g.drain())
    emit("gps", lambda g: g.memset(epst[0:32, :], 1e-6))
    CK["gps0"] = emit("gps", lambda g: g.memset(epst[64:96, :], 1e-6), inc=True)
    emit("dve", lambda v: v.memset(memPS[0][:], 0.0))
    CK["pszero"] = emit("dve", lambda v: v.memset(memPS[1][:], 0.0), inc=True)

    st0 = stc(0, 0, 6)
    emit("dve", lambda v: v.tensor_scalar(out=st0, in0=pidx[:], scalar1=0.5, scalar2=-0.25,
                                          op0=ALU.mult, op1=ALU.add),
         waits=[("gps", CK["gps0"])])
    ddrain()
    emit("dve", lambda v: v.tensor_copy(out=sti[:], in_=st0))
    ddrain()
    CK["flr"] = emit("dve", lambda v: v.tensor_copy(out=st0, in_=sti[:]), inc=True)
    ddrain()
    CK["freq"] = emit("act", lambda a: a.activation(out=freq64[:], in_=st0, func=AF.Exp,
                                                    scale=-LOG1E4 / 32.0), inc=True,
                      waits=[("dve", CK["flr"])])
    emit("dve", lambda v: v.tensor_scalar(out=par[:, 0:1], in0=st0, scalar1=-2.0, scalar2=None,
                                          op0=ALU.mult))
    ddrain()
    emit("dve", lambda v: v.tensor_tensor(out=par[:, 0:1], in0=par[:, 0:1], in1=pidx[:],
                                          op=ALU.add))
    ddrain()
    emit("dve", lambda v: v.tensor_scalar(out=par[:, 1:2], in0=par[:, 0:1], scalar1=-1.0,
                                          scalar2=1.0, op0=ALU.mult, op1=ALU.add))
    ddrain()
    emit("dve", lambda v: v.tensor_scalar(out=cidx[:], in0=cidx[:], scalar1=pidx[:], scalar2=None,
                                          op0=ALU.subtract))
    emit("dve", lambda v: v.tensor_scalar(out=smat[:], in0=cidx[:], scalar1=1.0, scalar2=None,
                                          op0=ALU.is_equal))
    emit("dve", lambda v: v.tensor_scalar(out=smat[:], in0=smat[:], scalar1=par[:, 1:2],
                                          scalar2=None, op0=ALU.mult))
    emit("dve", lambda v: v.tensor_scalar(out=lnsq[:, 0:128], in0=cidx[:], scalar1=-1.0,
                                          scalar2=None, op0=ALU.is_equal))
    emit("dve", lambda v: v.tensor_scalar(out=lnsq[:, 0:128], in0=lnsq[:, 0:128],
                                          scalar1=par[:, 0:1], scalar2=None, op0=ALU.mult))
    CK["smat"] = emit("dve", lambda v: v.tensor_tensor(out=smat[:], in0=smat[:],
                                                       in1=lnsq[:, 0:128], op=ALU.subtract),
                      inc=True)

    wq_f = wq_all.bitcast(F32)
    pos64 = wq_f[0:64, 0:2048]
    ang64 = wq_f[0:64, 2048:4096]
    u_s = wq_f[0:64, 4096:6144]
    n_f = wq_f[0:64, 6144:8192]
    n_i = wq_f[0:64, 8192:10240].bitcast(I32)
    CK["pos"] = emit("gps", lambda g: g.iota(pos64, pattern=[[1, 2048]], channel_multiplier=0,
                                             allow_small_or_imprecise_dtypes=True), inc=True)
    emit("dve", lambda v: v.tensor_scalar(out=ang64, in0=pos64, scalar1=freq64[0:64, 0:1],
                                          scalar2=None, op0=ALU.mult),
         waits=[("gps", CK["pos"]), ("act", CK["freq"])])

    def table_build(phase, dest):
        emit("dve", lambda v: v.tensor_scalar(out=u_s, in0=ang64, scalar1=1.0 / TWO_PI,
                                              scalar2=phase, op0=ALU.mult, op1=ALU.add))
        emit("dve", lambda v: v.tensor_copy(out=n_i, in_=u_s))
        emit("dve", lambda v: v.tensor_copy(out=n_f, in_=n_i))
        cc = emit("dve", lambda v: v.tensor_tensor(out=u_s, in0=u_s, in1=n_f, op=ALU.subtract),
                  inc=True)
        ca = emit("act", lambda a: a.activation(out=dest[0:64, :], in_=u_s, func=AF.Sin,
                                                scale=TWO_PI), inc=True, waits=[("dve", cc)])
        emit("dve", lambda v: v.tensor_copy(out=dest[64:128, :], in_=dest[0:64, :]),
             waits=[("act", ca)])

    table_build(0.25, cosT)
    table_build(0.0, sinT)
    CK["tables"] = emit("dve", lambda v: None, inc=True)

    ca = emit("act", lambda a: a.activation(out=btr[:], in_=btr[:], func=AF.Tanh, scale=0.5),
              inc=True, waits=[("sq", CK["beta"])])
    emit("dve", lambda v: v.tensor_scalar(out=betab[:], in0=btr[:], scalar1=0.5, scalar2=0.5,
                                          op0=ALU.mult, op1=ALU.add), waits=[("act", ca)])
    CK["betab"] = emit("dve", lambda v: v.tensor_scalar(out=ombb[:], in0=btr[:], scalar1=-0.5,
                                                        scalar2=0.5, op0=ALU.mult, op1=ALU.add),
                       inc=True)
    emit("pe", lambda t: t.matmul(out=pA[:, 0:16], lhsT=ones1[:], rhs=ombb[:],
                                  start=True, stop=True),
         waits=[("dve", CK["betab"]), ("gps", CK["gps0"])])
    CK["pe_beta"] = emit("pe", lambda t: t.matmul(out=pB[0:32, 0:16], lhsT=ones1[:, 0:32],
                                                  rhs=betab[:], start=True, stop=True), inc=True)
    emit("dve", lambda v: v.tensor_copy(out=omb[:], in_=pA[:, 0:16]),
         waits=[("pe", CK["pe_beta"])])
    emit("dve", lambda v: v.memset(btile[:], 1.0))
    ddrain()
    emit("dve", lambda v: v.tensor_copy(out=btile[0:32, :], in_=pB[0:32, 0:16:2]))
    CK["btile"] = emit("dve", lambda v: v.tensor_copy(out=btile[64:96, :], in_=pB[0:32, 1:16:2]),
                      inc=True)

    prev_sc = 0
    prev_ca = 0
    last_pe = 0
    for c in range(NCH):
        for hf in range(2):
            wl = emit("sq", lambda s, c=c, hf=hf: s.dma_start(
                out=wscf[:, 0:1536],
                in_=wqkv_h[128 * c:128 * c + 128, 1536 * hf:1536 * (hf + 1)]),
                inc=True, waits=[("dve", max(CK["tables"], prev_sc)), ("act", prev_ca),
                                 ("pe", last_pe)])
            prev_ca = emit("act", lambda a: a.activation(out=wraw, in_=wscf[:, 0:1536],
                                                         func=AF.Copy), inc=True,
                           waits=[("sq", wl)])
            prev_sc = emit("dve", lambda v, c=c, hf=hf: v.tensor_scalar(
                out=wq_all[:, 3072 * c + 1536 * hf:3072 * c + 1536 * (hf + 1)],
                in0=wscf[:, 0:1536], scalar1=gcol[:, c:c + 1], scalar2=None,
                op0=ALU.mult), inc=True,
                waits=[("sq", wl), ("sq", CK["g"]), ("dve", CK["tables"])])
            if hf == 0:
                mcs = range(0, 12)
            else:
                mcs = range(12, 16)
            for mc in mcs:
                off = 128 * mc - 1536 * hf
                emit("pe", lambda t, c=c, mc=mc, off=off: t.matmul(
                    out=pC[:, 32 + mc:33 + mc], lhsT=wraw[:, off:off + 128],
                    rhs=bcol[:, c:c + 1], start=(c == 0 and mc == 0),
                    stop=(c == NCH - 1)),
                    waits=[("act", prev_ca), ("dve", CK["bc"]),
                           ("dve", CK["btile"] if c == 0 else 0)])
            if hf == 1:
                emit("pe", lambda t, c=c: t.matmul(out=pD[0:1, 0:512], lhsT=bcol[:, c:c + 1],
                                                   rhs=wraw[:, 512:1024], start=(c == 0),
                                                   stop=(c == NCH - 1)))
                last_pe = emit("pe", lambda t, c=c: t.matmul(
                    out=pE[0:1, 0:512], lhsT=bcol[:, c:c + 1],
                    rhs=wraw[:, 1024:1536], start=(c == 0), stop=(c == NCH - 1)), inc=True)
    CK["pe_ub"] = last_pe
    emit("dve", lambda v: v.tensor_copy(out=ubqk[:], in_=pC[:, 32:48]),
         waits=[("pe", CK["pe_ub"])])
    emit("dve", lambda v: v.tensor_copy(out=ubvrow[0:1, 0:512], in_=pD[0:1, 0:512]))
    CK["ubvr"] = emit("dve", lambda v: v.tensor_copy(out=ubvrow[0:1, 512:1024],
                                                     in_=pE[0:1, 0:512]), inc=True)
    emit("pe", lambda t: t.matmul(out=pD[:, 0:512], lhsT=ones1[:], rhs=ubvrow[0:1, 0:512],
                                  start=True, stop=True), waits=[("dve", CK["ubvr"])])
    CK["pe_ubv"] = emit("pe", lambda t: t.matmul(out=pE[:, 0:512], lhsT=ones1[:],
                                                 rhs=ubvrow[0:1, 512:1024], start=True,
                                                 stop=True), inc=True)
    emit("dve", lambda v: v.tensor_copy(out=ubv[:, 0:512], in_=pD[:, 0:512]),
         waits=[("pe", CK["pe_ubv"])])
    CK["ubv"] = emit("dve", lambda v: v.tensor_copy(out=ubv[:, 512:1024], in_=pE[:, 0:512]),
                     inc=True)

    prev_gq = 0
    for c in range(NCH):
        prev_gq = emit("gq", lambda g, c=c: g.dma_start(
            out=wo_all[:, 1024 * c:1024 * (c + 1)], in_=wout_h[128 * c:128 * c + 128, :]),
            inc=True, waits=[("gps", CK["gps0"])])
    prev_tq = 0
    prev_dv = 0
    stg0 = qkbd[0][:, 0:1024]
    stg1 = qkbd[0][:, 1024:2048]
    for (whh, dstT, halfscale) in ((wr_h, wrT, False), (wk_h, wkT, False), (wv_h, wvT, True)):
        for dc in range(NCH):
            wl = emit("gq", lambda g, whh=whh, dc=dc: g.dma_start(
                out=stg0, in_=whh[128 * dc:128 * dc + 128, :]), inc=True,
                waits=[("tq", prev_tq), ("dve", prev_dv)])
            src = stg0
            if halfscale:
                prev_dv = emit("dve", lambda v: v.tensor_scalar(
                    out=stg1, in0=stg0, scalar1=0.5, scalar2=None, op0=ALU.mult), inc=True,
                    waits=[("gq", wl)])
                src = stg1
            dst = dstT.rearrange("p (a b f) -> p a b f", a=8, f=128)[:, :, dc, :]
            prev_tq = emit("tq", lambda s, dst=dst, src=src.rearrange(
                "p (c f) -> p c f", c=8): s.dma_start(out=dst, in_=src, transpose=True),
                inc=True, waits=([("gq", wl)] if not halfscale else [("dve", prev_dv)]))
    CK["wffn_dve"] = max(prev_sc, prev_dv)
    CK["wffn_gq"] = prev_gq
    CK["wffn_tq"] = prev_tq
    CK["wsc_free_dve"] = prev_sc
    CK["wsc_free_act"] = prev_ca
    CK["arena_free_pe"] = CK["pe_ubv"]


    def dget(key, default=0):
        return done.get(key, default)

    pe_big = [pA, pB, pC, pD]
    big_i = [0]
    big_evac_d = {id(pA): ("dve", CK["btile"]), id(pB): ("dve", CK["btile"]),
                  id(pC): ("dve", CK["ubv"]), id(pD): ("dve", CK["ubv"]),
                  id(pE): ("dve", CK["ubv"]), id(pF): ("dve", 0)}

    class _BE:
        def __getitem__(self, b):
            return big_evac_d[id(b)]

        def __setitem__(self, b, v):
            big_evac_d[id(b)] = v

    big_evac = _BE()

    def next_big():
        b = pe_big[big_i[0] % 2]
        big_i[0] += 1
        return b

    def rstd_chain(lnid, seq, steps, xtr_waits=()):
            r1, sq_ = stc(seq, lnid, 0), stc(seq, lnid, 1)
            emit("dve", lambda v: v.tensor_scalar(
                out=r_nm, in0=r1.broadcast_to([P, W64]), scalar1=-1.0 / D,
                scalar2=None, op0=ALU.mult), waits=list(xtr_waits))
            emit("dve", lambda v: v.tensor_scalar(
                out=r_tt, in0=sq_.broadcast_to([P, W64]), scalar1=1.0 / D,
                scalar2=1e-5, op0=ALU.mult, op1=ALU.add))
            emit("dve", lambda v: v.tensor_tensor(out=r_aa, in0=r_nm, in1=r_nm, op=ALU.mult))
            emit("dve", lambda v: v.tensor_tensor(out=r_tt, in0=r_tt, in1=r_aa,
                                                  op=ALU.subtract))
            emit("dve", lambda v: v.tensor_scalar(out=r_aa, in0=r_tt, scalar1=0.5, scalar2=0.5,
                                                  op0=ALU.mult, op1=ALU.add))
            emit("dve", lambda v: v.reciprocal(out=r_yy, in_=r_aa))
            for _ in range(steps):
                emit("dve", lambda v: v.tensor_tensor(out=r_aa, in0=r_yy, in1=r_yy,
                                                      op=ALU.mult))
                emit("dve", lambda v: v.tensor_tensor(out=r_hx, in0=r_aa, in1=r_tt,
                                                      op=ALU.mult))
                emit("dve", lambda v: v.tensor_scalar(out=r_aa, in0=r_hx, scalar1=-0.5,
                                                      scalar2=1.5, op0=ALU.mult, op1=ALU.add))
                emit("dve", lambda v: v.tensor_tensor(out=r_yy, in0=r_yy, in1=r_aa,
                                                      op=ALU.mult))

    def emit_F1(m):
        seq, blk = m % 2, m // 2
        r1, sq_ = stc(seq, 0, 0), stc(seq, 0, 1)
        wx = xload[m]
        emit("act", lambda a, seq=seq, r1=r1: a.activation(
            out=xhat[seq][:], in_=xs[seq][:], func=AF.Copy, accum_out=r1),
            waits=[("sq", wx),
                   ("tq", max(dget(("ht", m - 2)), dget(("xt", m - 2))))])
        c_sqa = emit("actd", lambda a, seq=seq, sq_=sq_: a.activation(
            out=xhat[seq][:], in_=xs[seq][:], func=AF.Square, accum_out=sq_),
            inc=True)

        rstd_chain(0, seq, 2, xtr_waits=[("act", c_sqa)])
        c_xh = emit("dve", lambda v, seq=seq: v.tensor_scalar(
            out=xhat[seq][:], in0=xs[seq][:], scalar1=r_nm[:, 0:1], scalar2=r_yy[:, 0:1],
            op0=ALU.add, op1=ALU.mult), inc=True, waits=[("tq", dget(("ht", m - 2)))])
        maybe_tap("xhat", m, lambda: xhat[seq][:], lambda: [("dve", c_xh)])

        if stop_after == "ln1":
            return True
        done[("xt", m)] = emit("tq", lambda s, seq=seq: s.dma_start(
            out=xnfm[seq].rearrange("p (c f) -> p c f", c=8),
            in_=xhat[seq].rearrange("p (c f) -> p c f", c=8), transpose=True), inc=True,
            waits=[("dve", c_xh), ("pe", dget(("qkv_done", m - 2))),
                   ("tq", dget(("skt", m - 2)))])
        maybe_tap("xnfm", m, lambda: xnfm[seq][:], lambda: [("tq", done[("xt", m)])])

        return stop_after == "xt"

    def emit_F2qk(m, mp0, mp1):
        seq, blk = m % 2, m // 2
        cev = 0
        for mp in range(mp0, mp1):
            bank = next_big()
            beng, bval = big_evac[bank]
            for half in range(2):
                mc = 2 * mp + half
                for kc in range(NCH):
                    wts = []
                    if half == 0 and kc == 0:
                        wts = [("tq", done[("xt", m)]), (beng, bval)]
                        if m < 2 and mp == 0:
                            wts.append(("dve", CK["wffn_dve"]))
                    emit("pe", lambda t, mc=mc, kc=kc, half=half, seq=seq, bank=bank: t.matmul(
                        out=bank[:, 128 * half:128 * (half + 1)],
                        lhsT=wq_all[:, 3072 * kc + 128 * mc:3072 * kc + 128 * (mc + 1)],
                        rhs=xnfm[seq][:, 128 * kc:128 * (kc + 1)],
                        start=(kc == 0), stop=(kc == NCH - 1)), waits=wts)
            cpe = emit("pe", lambda t: None, inc=True)
            ubv2 = ubqk[:, 2 * mp:2 * mp + 2].unsqueeze(2).broadcast_to([P, 2, 128])
            cev = emit("dve", lambda v, mp=mp, seq=seq, bank=bank, ubv2=ubv2: v.tensor_tensor(
                out=qkfm[seq][:, 256 * mp:256 * (mp + 1)].rearrange("p (a f) -> p a f", f=128),
                in0=bank[:, 0:256].rearrange("p (a f) -> p a f", f=128),
                in1=ubv2, op=ALU.add), inc=True,
                waits=[("pe", cpe), ("pe", dget(("scores_done", m - 2)))])
            big_evac[bank] = ("dve", cev)
        return cev

    def emit_F2a(m):
        done[("qk_evac_q", m)] = emit_F2qk(m, 0, 4)
        return False

    def emit_F2b(m):
        done[("qk_evac", m)] = emit_F2qk(m, 4, 8)
        return False

    def emit_F2c(m):
        seq, blk = m % 2, m // 2
        for nh in range(2):
            bank = next_big()
            beng, bval = big_evac[bank]
            for kc in range(NCH):
                wts = []
                if kc == 0:
                    wts = [("tq", done[("xt", m)]), (beng, bval)]
                emit("pe", lambda t, nh=nh, kc=kc, seq=seq, bank=bank: t.matmul(
                    out=bank[:, 0:512], lhsT=xnfm[seq][:, 128 * kc:128 * (kc + 1)],
                    rhs=wq_all[:, 3072 * kc + 2048 + 512 * nh:3072 * kc + 2048 + 512 * (nh + 1)],
                    start=(kc == 0), stop=(kc == NCH - 1)), waits=wts)
            cpe = emit("pe", lambda t: None, inc=True)
            vdst = vb[seq].rearrange("p (c g f) -> p c g f", g=2, f=65)[:, 4 * nh:4 * (nh + 1), :, 0:64]
            vsrc = bank[:, 0:512].rearrange("p (c g f) -> p c g f", c=4, g=2)
            usrc = ubv[:, 512 * nh:512 * (nh + 1)].rearrange("p (c g f) -> p c g f", c=4, g=2)
            cev = emit("dve", lambda v, vdst=vdst, vsrc=vsrc, usrc=usrc: v.tensor_tensor(
                out=vdst, in0=vsrc, in1=usrc, op=ALU.add), inc=True,
                waits=[("pe", cpe), ("dve", dget(("usub", m - 2, SEGB - 1))),
                       ("pe", dget(("sdpa_done", m - 2))), ("dve", CK["ubv"] if m < 2 else 0)])
            big_evac[bank] = ("dve", cev)
        done[("v_evac", m)] = cev
        done[("qkv_done", m)] = now("pe")
        maybe_tap("qkfm", m, lambda: qkfm[seq][:], lambda: [("dve", done[("qk_evac", m)])])
        maybe_tap("vb", m, lambda: vb[seq][:], lambda: [("dve", done[("v_evac", m)])])

        return stop_after == "qkv"

    def emit_F2(m):
        return emit_F2a(m) or emit_F2b(m) or emit_F2c(m)

    def emit_F3(m):
        seq, blk = m % 2, m // 2
        c_expq = emit("act", lambda a, seq=seq: a.activation(out=eexp[:], in_=qkfm[seq][:, 0:1024],
                                                             func=AF.Exp), inc=True,
                      waits=[("dve", done[("qk_evac_q", m)]), ("dve", dget(("elu_stt", m - 1))),
                             ("pe", CK["arena_free_pe"]),
                             ("dve", dget(("mix_done", m - 1)))])
        emit("dve", lambda v: v.tensor_scalar(out=emin[:], in0=eexp[:], scalar1=1.0,
                                              scalar2=None, op0=ALU.min),
             waits=[("act", c_expq), ("pe", dget(("sdpa_done", m - 1)))])
        qdst = qkbd[seq].rearrange("p (c s g f) -> p c s g f", c=8, s=4, g=2)[:, :, :, 0, :]
        c_sttq = emit("dve", lambda v, seq=seq, qdst=qdst: v.scalar_tensor_tensor(
            out=qdst, in0=qkfm[seq][:, 0:1024].rearrange("p (c s f) -> p c s f", c=8, f=32),
            scalar=0.0, in1=emin.rearrange("p (c s f) -> p c s f", c=8, f=32),
            op0=ALU.max, op1=ALU.add), inc=True,
            waits=[("pe", dget(("phb_done", m - 2))),
                   ("tq", CK["wffn_tq"] if m < 2 else 0)])
        c_expk = emit("act", lambda a, seq=seq: a.activation(out=eexp[:],
                                                             in_=qkfm[seq][:, 1024:2048],
                                                             func=AF.Exp), inc=True,
                      waits=[("dve", c_sttq), ("dve", done[("qk_evac", m)])])
        done[("elu_exp", m)] = c_expk
        c_min = c_expk
        emit("dve", lambda v: v.tensor_scalar(out=emin[:], in0=eexp[:], scalar1=1.0,
                                              scalar2=None, op0=ALU.min),
             waits=[("act", c_expk)])
        emit("dve", lambda v, seq=seq: v.scalar_tensor_tensor(
            out=skfm[seq][:], in0=qkfm[seq][:, 1024:2048], scalar=0.0,
            in1=emin[:], op0=ALU.max, op1=ALU.add),
            waits=[("pe", done[("qkv_done", m)])])
        kdst = qkbd[seq].rearrange("p (c s g f) -> p c s g f", c=8, s=4, g=2)[:, :, :, 1, :]
        c3 = emit("dve", lambda v, seq=seq, kdst=kdst: v.tensor_copy(
            out=kdst, in_=skfm[seq].rearrange("p (c s f) -> p c s f", c=8, f=32)), inc=True,
            waits=[("tq", CK["wffn_tq"] if m < 2 else 0)])
        done[("elu_stt", m)] = c3
        maybe_tap("qkbd", m, lambda: qkbd[seq][:], lambda: [("dve", c3)])
        done[("skt", m)] = emit("tq", lambda s, seq=seq: s.dma_start(
            out=skT[seq].rearrange("p (c f) -> p c f", c=8),
            in_=skfm[seq].rearrange("p (c f) -> p c f", c=8), transpose=True), inc=True,
            waits=[("dve", c3), ("pe", dget(("phb_done", m - 2)))])

        return stop_after == "elu"

    def emit_F4(m):
        seq, blk = m % 2, m // 2
        c_min = done[("elu_exp", m)]
        cosv = cosT[:, TOK * blk:TOK * (blk + 1)]
        sinv = sinT[:, TOK * blk:TOK * (blk + 1)]
        prev_rope_dve = 0
        for half in range(2):
            base = 1024 * half
            for cp in range(4):
                bank = next_big()
                beng, bval = big_evac[bank]
                for half2 in range(2):
                    c = 2 * cp + half2
                    cpe = emit("pe", lambda t, seq=seq, base=base, c=c, half2=half2, bank=bank: t.matmul(
                        out=bank[:, 128 * half2:128 * (half2 + 1)], lhsT=smat[:],
                        rhs=qkfm[seq][:, base + 128 * c:base + 128 * (c + 1)],
                        start=True, stop=True), inc=(half2 == 1),
                        waits=([("dve", done[("elu_stt", m)]), (beng, bval),
                                ("act", c_min), ("dve", CK["smat"])] if half2 == 0 else []))
                c2 = 2 * cp
                cosv2 = cosv.unsqueeze(1).broadcast_to([P, 2, TOK])
                sinv2 = sinv.unsqueeze(1).broadcast_to([P, 2, TOK])
                emit("dve", lambda v, bank=bank, sinv2=sinv2: v.tensor_tensor(
                    out=rop2[:].rearrange("p (a f) -> p a f", f=TOK) if False else rop2w,
                    in0=bank[:, 0:256].rearrange("p (a f) -> p a f", f=TOK),
                    in1=sinv2, op=ALU.mult), waits=[("pe", cpe)])
                emit("dve", lambda v, seq=seq, base=base, c2=c2, cosv2=cosv2: v.tensor_tensor(
                    out=rop1w,
                    in0=qkfm[seq][:, base + 128 * c2:base + 128 * (c2 + 2)].rearrange(
                        "p (a f) -> p a f", f=TOK),
                    in1=cosv2, op=ALU.mult))
                cev = emit("dve", lambda v, seq=seq, base=base, c2=c2: v.tensor_tensor(
                    out=qkfm[seq][:, base + 128 * c2:base + 128 * (c2 + 2)].rearrange(
                        "p (a f) -> p a f", f=TOK),
                    in0=rop2w, in1=rop1w, op=ALU.add), inc=True)
                big_evac[bank] = ("dve", cev)
                prev_rope_dve = cev
        done[("rope", m)] = prev_rope_dve
        maybe_tap("qkrope", m, lambda: qkfm[seq][:], lambda: [("dve", done[("rope", m)])])

        return stop_after == "rope"

    pa_state = {}

    def pa_lvl_of():
        return {"pa_scores": 1, "pa_exp": 2, "pa_mask": 3, "pa_sdpa": 4,
                "pa_recip": 5}.get(stop_after, 99)

    def emit_Ascores(m, r):
        seq, blk = m % 2, m // 2
        hf = r % 2
        c0 = 4 * (r // 2)
        sbank = pC if r % 2 == 0 else pD
        sbeng, sbval = big_evac[sbank]
        for ci in range(4):
            ch = c0 + ci
            wts = []
            if ci == 0:
                wts = [("dve", done[("rope", m)]), (sbeng, sbval),
                       ("act", dget(("exp_done", m, r - 2), dget(("exp_done", m - 1, r + 2))))]
            cpe = emit("pe", lambda t, seq=seq, ch=ch, hf=hf, ci=ci, sbank=sbank: t.matmul(
                out=sbank[:, 128 * ci:128 * (ci + 1)],
                lhsT=qkfm[seq][64 * hf:64 * (hf + 1), 1024 + 128 * ch:1024 + 128 * (ch + 1)],
                rhs=qkfm[seq][64 * hf:64 * (hf + 1), 128 * ch:128 * (ch + 1)],
                start=True, stop=True, tile_position=(64 * hf, 0)),
                inc=(ci == 3), waits=wts)
        done[("scorespe", m, r)] = cpe

    def emit_Ar(m, r):
        seq, blk = m % 2, m // 2
        pa_lvl = pa_lvl_of()
        probs = probs_sv[seq]
        pend_evac = pa_state.setdefault(m, [None])
        if True:
            hf = r % 2
            c0 = 4 * (r // 2)
            sbank = pC if r % 2 == 0 else pD
            obank = pE if r % 2 == 0 else pF
            rcX = rc1A if r % 2 == 0 else rc1B
            cpe = done[("scorespe", m, r)]
            if pa_lvl < 2:
                return
            c_ex = emit("act", lambda a, r=r, sbank=sbank: a.activation(
                out=probs[r % 2][:], in_=sbank[:], func=AF.Exp, scale=0.125, bias=cm20[:]),
                inc=True, waits=[("pe", cpe),
                                 ("dve", dget(("msk_done", m, r - 2),
                                              dget(("msk_done", m - 1, r + 2)))),
                                 ("dve", done[("elu_stt", m)] if r == 0 else 0),
                                 ("pe", CK["arena_free_pe"] if m < 2 else 0),
                                 ("gps", CK["gps0"] if m < 2 else 0)])
            done[("exp_done", m, r)] = c_ex
            big_evac[sbank] = ("act", c_ex)
            if pa_lvl < 3:
                return
            c_mk = emit("dve", lambda v, r=r: v.tensor_tensor(
                out=probs[r % 2].rearrange("p (a f) -> p a f", f=128),
                in0=probs[r % 2].rearrange("p (a f) -> p a f", f=128),
                in1=mask01[:].unsqueeze(1).broadcast_to([P, 4, P]),
                op=ALU.mult), inc=True, waits=[("act", c_ex),
                                               ("gps", CK["gps0"] if m < 2 else 0)])
            done[("msk_done", m, r)] = c_mk
            if pa_lvl < 4:
                return
            obeng, obval = big_evac[obank]
            for ci in range(4):
                ch = c0 + ci
                wts = []
                if ci == 0:
                    wts = [("dve", c_mk), ("dve", done[("v_evac", m)]), (obeng, obval),
                           ("dve", dget(("nds_mult", m - 1, SEGB - 1))),
                           ("gps", CK["gps0"] if m < 2 else 0)]
                cpe = emit("pe", lambda t, seq=seq, ch=ch, hf=hf, ci=ci, r=r, obank=obank: t.matmul(
                    out=obank[:, 65 * ci:65 * (ci + 1)],
                    lhsT=probs[r % 2][:, 128 * ci:128 * (ci + 1)],
                    rhs=vb[seq][:, 130 * ch + 65 * hf:130 * ch + 65 * (hf + 1)],
                    start=True, stop=True), inc=(ci == 3), waits=wts)
            done[("sdpape", m, r)] = cpe
            if r < 2:
                emit_Ascores(m, r + 2)
            if pa_lvl < 5:
                return
            emit("dve", lambda v, obank=obank, rcX=rcX: v.reciprocal(
                out=rcX[:], in_=obank[:, 0:260].rearrange("p (a b) -> p a b", b=65)[:, :, 64]),
                waits=[("pe", cpe), ("dve", CK["btile"] if m == 0 else 0)])
            if pend_evac[0] is None:
                ddrain()
            else:
                (pr, prcX, probank) = pend_evac[0]
                sdst = sdpa[seq].rearrange("p (c g b) -> p c g b", g=2, b=65)[
                    :, 4 * (pr // 2):4 * (pr // 2) + 4, pr % 2, :]
                c_se = emit("dve", lambda v, sdst=sdst, probank=probank, prcX=prcX: v.tensor_tensor(
                    out=sdst, in0=probank[:, 0:260].rearrange("p (a b) -> p a b", b=65),
                    in1=prcX[:].unsqueeze(2).broadcast_to([P, 4, 65]), op=ALU.mult),
                    inc=True, waits=[("dve", dget(("blend", m - 2, SEGB - 1)))])
                done[("sdpa_evac", m, pr)] = c_se
                big_evac[probank] = ("dve", c_se)
            ombv = omb.rearrange("p (c g) -> p c g", g=2)[:, c0:c0 + 4, hf]
            emit("dve", lambda v, ombv=ombv, rcX=rcX: v.tensor_tensor(
                out=rcX[:], in0=rcX[:], in1=ombv, op=ALU.mult))
            pend_evac[0] = (r, rcX, obank)

    def emit_Aflush(m):
        seq, blk = m % 2, m // 2
        pa_lvl = pa_lvl_of()
        pend_evac = pa_state.setdefault(m, [None])
        done[("sdpa_done", m)] = now("pe")
        done[("scores_done", m)] = now("pe")
        if pa_lvl >= 5 and pend_evac[0] is not None:
            (pr, prcX, probank) = pend_evac[0]
            ddrain()
            sdst = sdpa[seq].rearrange("p (c g b) -> p c g b", g=2, b=65)[
                :, 4 * (pr // 2):4 * (pr // 2) + 4, pr % 2, :]
            c_se = emit("dve", lambda v, sdst=sdst, probank=probank, prcX=prcX: v.tensor_tensor(
                out=sdst, in0=probank[:, 0:260].rearrange("p (a b) -> p a b", b=65),
                in1=prcX[:].unsqueeze(2).broadcast_to([P, 4, 65]), op=ALU.mult),
                inc=True, waits=[("dve", dget(("blend", m - 2, SEGB - 1)))])
            done[("sdpa_evac", m, pr)] = c_se
            big_evac[probank] = ("dve", c_se)
        if pa_lvl >= 99:
            maybe_tap("sdpa", m, lambda: sdpa[seq][:],
                      lambda: [("dve", done[("sdpa_evac", m, 3)])])

    def emit_As(m, s_):
        seq, blk = m % 2, m // 2
        if True:
            gseg = SEGB * blk + s_
            r0 = 32 * s_
            shdst = shadow[seq].rearrange("p (c f) -> p c f", f=65)[:, :, 0:64]
            if gseg > 0:
                c_sh = emit("act", lambda a, seq=seq, shdst=shdst: a.activation(
                    out=shdst, in_=memPS[seq].rearrange("p (c f) -> p c f", f=64),
                    func=AF.Copy), inc=True,
                    waits=[("pe", dget(("delta", m, s_ - 1), dget(("delta", m - 2, SEGB - 1))))])
            else:
                c_sh = now("act")
            done[("shadow", m, s_)] = c_sh
            emit("dve", lambda v, seq=seq, s_=s_: v.tensor_reduce(
                out=zred[:], in_=qkbd[seq].rearrange("p (c s g f) -> p c s g f",
                                                     c=8, s=4, g=2)[:, :, s_, 1, :],
                axis=mybir.AxisListType.X, op=ALU.add),
                waits=[("dve", done[("elu_stt", m)]),
                       ("gps", CK["gps0"] if m < 2 else 0)])
            for jj in range(8):
                bank = pE if jj < 4 else pF
                col = 65 * (jj % 4)
                for hf in range(2):
                    wts = []
                    if jj == 0 and hf == 0:
                        wts = [("dve", max(done[("elu_stt", m)],
                                           done[("sdpa_evac", m, 3)])),
                               ("act", c_sh),
                               ("dve", dget(("nds_mult", m, s_ - 1))),
                               ("gps", CK["gps0"] if m < 2 else 0)]
                    cpe = emit("pe", lambda t, seq=seq, s_=s_, jj=jj, hf=hf, bank=bank,
                               col=col: t.matmul(
                        out=bank[64 * hf:64 * (hf + 1), col:col + 65],
                        lhsT=qkbd[seq][64 * hf:64 * hf + 64,
                                       256 * jj + 64 * s_:256 * jj + 64 * (s_ + 1)],
                        rhs=shadow[seq][64 * hf:64 * hf + 64, 65 * jj:65 * (jj + 1)],
                        start=True, stop=True, tile_position=(64 * hf, 64 * hf)),
                        inc=(jj in (3, 7) and hf == 1), waits=wts)
                    if jj == 3 and hf == 1:
                        cpe_half = cpe
            done[("numskm", m, s_)] = cpe
            emit("dve", lambda v: v.reciprocal(
                out=rc1A[:], in_=pE[:, 0:260].rearrange("p (a b) -> p a b", b=65)[:, :, 64]),
                waits=[("pe", cpe_half),
                       ("dve", dget(("usub", m, s_ - 1), dget(("usub", m - 1, SEGB - 1)))),
                       ("dve", dget(("blend", m, s_ - 1)))])
            emit("dve", lambda v: v.reciprocal(
                out=rc1B[:], in_=pF[:, 0:260].rearrange("p (a b) -> p a b", b=65)[:, :, 64]),
                waits=[("pe", done[("numskm", m, s_)])])
            emit("dve", lambda v: v.tensor_tensor(out=rc1A[:], in0=rc1A[:],
                                                  in1=btile[:, 0:4], op=ALU.mult))
            emit("dve", lambda v: v.tensor_tensor(out=rc1B[:], in0=rc1B[:],
                                                  in1=btile[:, 4:8], op=ALU.mult))
            emit("dve", lambda v, seq=seq: v.tensor_tensor(out=zf[seq][:], in0=zf[seq][:],
                                                           in1=zred[:], op=ALU.add))
            for half in range(2):
                bank = pE if half == 0 else pF
                rcX = rc1A if half == 0 else rc1B
                for hf in range(2):
                    prow = 32 + 64 * hf
                    uview = ut[r0:r0 + 32, :].rearrange(
                        "p (c g f) -> p c g f", g=2, f=64)[:, 4 * half:4 * half + 4, hf, :]
                    emit("dve", lambda v, bank=bank, prow=prow, uview=uview, rcX=rcX: v.tensor_tensor(
                        out=uview,
                        in0=bank[prow:prow + 32, 0:260].rearrange("p (a b) -> p a b", b=65)[:, :, 0:64],
                        in1=rcX[prow:prow + 32, :].unsqueeze(2).broadcast_to([32, 4, 64]),
                        op=ALU.mult))
            uin = ut[r0:r0 + 32, :].rearrange("p (c g f) -> p c g f", g=2, f=64)
            vsrc = vb[seq][r0:r0 + 32, :].rearrange("p (c g f) -> p c g f", g=2, f=65)[:, :, :, 0:64]
            cu = emit("dve", lambda v, vsrc=vsrc, uin=uin: v.tensor_tensor(
                out=uin, in0=vsrc, in1=uin, op=ALU.subtract), inc=True,
                waits=[("pe", dget(("delta", m, s_ - 1)))])
            done[("usub", m, s_)] = cu
            emit("dve", lambda v, seq=seq: v.tensor_copy(
                out=shadow[seq].rearrange("p (c f) -> p c f", f=65)[:, :, 64],
                in_=zf[seq][:]))
            c_nds = 0
            for half in range(2):
                bank = pE if half == 0 else pF
                rcX = rc1A if half == 0 else rc1B
                for hf in range(2):
                    nrow = 64 * hf
                    oview = otok[seq][r0:r0 + 32, :].rearrange(
                        "p (c g f) -> p c g f", g=2, f=64)[:, 4 * half:4 * half + 4, hf, :]
                    c_nds = emit("dve", lambda v, bank=bank, nrow=nrow, oview=oview, rcX=rcX: v.tensor_tensor(
                        out=oview,
                        in0=bank[nrow:nrow + 32, 0:260].rearrange("p (a b) -> p a b", b=65)[:, :, 0:64],
                        in1=rcX[nrow:nrow + 32, :].unsqueeze(2).broadcast_to([32, 4, 64]),
                        op=ALU.mult), inc=(half == 1 and hf == 1))
            done[("nds_mult", m, s_)] = c_nds
            oview = otok[seq][r0:r0 + 32, :].rearrange("p (c g f) -> p c g f", g=2, f=64)
            sdv = sdpa[seq][r0:r0 + 32, :].rearrange("p (c g b) -> p c g b", g=2, b=65)[:, :, :, 0:64]
            cb = emit("dve", lambda v, oview=oview, sdv=sdv: v.tensor_tensor(
                out=oview, in0=oview, in1=sdv, op=ALU.add), inc=True)
            done[("blend", m, s_)] = cb
            first = (gseg == 0)
            for jj in range(8):
                for hf in range(2):
                    wts = []
                    if jj == 0 and hf == 0:
                        wts = [("dve", cu), ("act", c_sh), ("tq", done[("skt", m)]),
                               ("dve", CK["pszero"] if m < 2 else 0)]
                    cpe = emit("pe", lambda t, seq=seq, s_=s_, jj=jj, hf=hf, first=first: t.matmul(
                        out=memPS[seq][64 * hf:64 * (hf + 1), 64 * jj:64 * (jj + 1)],
                        lhsT=skT[seq][32 * s_:32 * (s_ + 1),
                                      128 * jj + 64 * hf:128 * jj + 64 * (hf + 1)],
                        rhs=ut[32 * s_:32 * (s_ + 1),
                               128 * jj + 64 * hf:128 * jj + 64 * (hf + 1)],
                        start=False, stop=False,
                        tile_position=(32 * s_, 64 * hf), skip_group_check=True),
                        inc=(jj == 7 and hf == 1), waits=wts)
            done[("delta", m, s_)] = cpe

    def emit_Atail(m):
        seq, blk = m % 2, m // 2
        done[("phb_done", m)] = now("pe")
        maybe_tap("otok", m, lambda: otok[seq][:],
                  lambda: [("dve", done[("blend", m, SEGB - 1)])])
        maybe_tap("shadow", m, lambda: shadow[seq][:],
                  lambda: [("act", done[("shadow", m, SEGB - 1)])])
        done[("ot", m)] = emit("tq", lambda s, seq=seq: s.dma_start(
            out=ofm.rearrange("p (c f) -> p c f", c=8),
            in_=otok[seq].rearrange("p (c f) -> p c f", c=8), transpose=True), inc=True,
            waits=[("dve", done[("blend", m, SEGB - 1)]), ("pe", dget(("wout_done", m - 1)))])
        return stop_after == "ot"

    def emit_scan(m):
        emit_Ascores(m, 0)
        emit_Ascores(m, 1)
        for r in range(4):
            emit_Ar(m, r)
        emit_Aflush(m)
        if pa_lvl_of() < 99 or stop_after == "phasea":
            return True
        for s_ in range(SEGB):
            emit_As(m, s_)
        if stop_after == "phaseb":
            return True
        return emit_Atail(m)

    def emit_B1(m):
        seq, blk = m % 2, m // 2
        for nh in range(2):
            bank = next_big()
            beng, bval = big_evac[bank]
            for ic in range(NCH):
                wts = []
                if ic == 0:
                    wts = [("tq", done[("ot", m)]), (beng, bval)]
                emit("pe", lambda t, nh=nh, ic=ic, bank=bank: t.matmul(
                    out=bank[:, 0:512], lhsT=ofm[:, 128 * ic:128 * (ic + 1)],
                    rhs=wo_all[:, 1024 * ic + 512 * nh:1024 * ic + 512 * (nh + 1)],
                    start=(ic == 0), stop=(ic == NCH - 1)), waits=wts)
            cpe = emit("pe", lambda t: None, inc=True)
            cev = emit("dve", lambda v, seq=seq, nh=nh, bank=bank: v.scalar_tensor_tensor(
                out=xs[seq][:, 512 * nh:512 * (nh + 1)], in0=bank[:, 0:512], scalar=1.0,
                in1=xs[seq][:, 512 * nh:512 * (nh + 1)], op0=ALU.mult, op1=ALU.add), inc=True,
                waits=[("pe", cpe)])
            big_evac[bank] = ("dve", cev)
        done[("wout_done", m)] = now("pe")
        done[("s2", m)] = cev
        maybe_tap("s2", m, lambda: xs[seq][:], lambda: [("dve", cev)])
        return False

    def emit_B1b(m):
        seq, blk = m % 2, m // 2
        c_s2 = done[("s2", m)]
        r1, sq_ = stc(seq, 1, 0), stc(seq, 1, 1)
        emit("act", lambda a, seq=seq, r1=r1: a.activation(
            out=xnfm[seq][:], in_=xs[seq][:], func=AF.Copy, accum_out=r1),
            waits=[("dve", c_s2), ("tq", done[("skt", m)])])
        c_sqa2 = emit("actd", lambda a, seq=seq, sq_=sq_: a.activation(
            out=xnfm[seq][:], in_=xs[seq][:], func=AF.Square, accum_out=sq_), inc=True)
        rstd_chain(1, seq, 3, xtr_waits=[("act", c_sqa2)])
        c_h = emit("dve", lambda v, seq=seq: v.tensor_scalar(
            out=htok[seq][:], in0=xs[seq][:], scalar1=r_nm[:, 0:1], scalar2=r_yy[:, 0:1],
            op0=ALU.add, op1=ALU.mult), inc=True,
            waits=[("dve", dget(("ln3", m - 2))), ("sq", dget(("store", m - 2)))])
        c_hb = emit("act", lambda a, seq=seq: a.activation(
            out=hbf[seq][:], in_=htok[seq][:], func=AF.Copy), inc=True,
            waits=[("dve", c_h), ("tq", done[("ot", m)])])
        done[("chb", m)] = c_hb
        maybe_tap("htok", m, lambda: htok[seq][:], lambda: [("dve", c_h)])
        if m + 2 < nmt:
            load_x(m + 2, w=[("dve", c_h), ("act", c_sqa2)])

        return stop_after == "ln2"

    def emit_B2(m):
        seq, blk = m % 2, m // 2
        if blk > 0:
            c_halo = emit("dve", lambda v, seq=seq: v.tensor_copy(
                out=hh[seq][:], in_=hfm[seq].rearrange("p (c f) -> p c f", f=128)[:, :, 127]),
                inc=True, waits=[("dve", dget(("mix_done", m - 2)))])
        else:
            c_halo = now("dve")
        done[("ht", m)] = emit("tq", lambda s, seq=seq: s.dma_start(
            out=hfm[seq].rearrange("p (c f) -> p c f", c=8),
            in_=hbf[seq].rearrange("p (c f) -> p c f", c=8), transpose=True), inc=True,
            waits=[("act", done[("chb", m)]), ("dve", c_halo)])

        return stop_after == "ht"

    def emit_B2b(m):
        seq, blk = m % 2, m // 2
        dtile = dtile_sv[seq]
        c_d2 = emit("dve", lambda v, seq=seq, dtile=dtile: v.tensor_tensor(
            out=dtile[:, 0:1024].rearrange("p (c f) -> p c f", f=128)[:, :, 0],
            in0=hfm[seq].rearrange("p (c f) -> p c f", f=128)[:, :, 0],
            in1=hh[seq][:], op=ALU.subtract), inc=True,
            waits=[("tq", done[("ht", m)]), ("dve", done[("blend", m, SEGB - 1)]),
                   ("gps", CK["gps0"] if m < 2 else 0)])
        emit("dve", lambda v, seq=seq, dtile=dtile: v.tensor_tensor(
            out=dtile[:, 0:1024].rearrange("p (c f) -> p c f", f=128)[:, :, 1:128],
            in0=hfm[seq].rearrange("p (c f) -> p c f", f=128)[:, :, 1:128],
            in1=hfm[seq].rearrange("p (c f) -> p c f", f=128)[:, :, 0:127], op=ALU.subtract))
        murb_v = murb[:].unsqueeze(2).broadcast_to([P, 8, 128])
        mukb_v = mukb[:].unsqueeze(2).broadcast_to([P, 8, 128])
        dv = dtile[:, 0:1024].rearrange("p (c f) -> p c f", f=128)
        emit("dve", lambda v, dv=dv, murb_v=murb_v: v.tensor_tensor(
            out=xr[0].rearrange("p (c f) -> p c f", f=128), in0=dv, in1=murb_v,
            op=ALU.mult),
            waits=[("dve", c_d2), ("dve", CK["wsc_free_dve"] if m < 2 else 0),
                   ("act", CK["wsc_free_act"] if m < 2 else 0),
                   ("pe", dget(("ffnmm_done", m - 1))),
                   ("act", dget(("tanh", m - 1, 1)))])
        emit("dve", lambda v, dv=dv, mukb_v=mukb_v: v.tensor_tensor(
            out=xk[0].rearrange("p (c f) -> p c f", f=128), in0=dv, in1=mukb_v,
            op=ALU.mult))
        emit("dve", lambda v, seq=seq: v.tensor_tensor(
            out=xr[0][:], in0=xr[0][:], in1=hfm[seq][:], op=ALU.add))
        cm = emit("dve", lambda v, seq=seq: v.tensor_tensor(
            out=xk[0][:], in0=xk[0][:], in1=hfm[seq][:], op=ALU.add), inc=True)
        done[("mix_done", m)] = cm
        maybe_tap("xr", m, lambda: xr[seq][:], lambda: [("dve", cm)])

        return stop_after == "mix"

    def emit_B3(m):
        seq, blk = m % 2, m // 2
        cm = done[("mix_done", m)]
        for nh in range(2):
            bank = next_big()
            beng, bval = big_evac[bank]
            for cc in range(NCH):
                wts = []
                if cc == 0:
                    wts = [("dve", cm), (beng, bval)]
                emit("pe", lambda t, seq=seq, nh=nh, cc=cc, bank=bank: t.matmul(
                    out=bank[:, 0:512], lhsT=xr[seq][:, 128 * cc:128 * (cc + 1)],
                    rhs=wrT[:, 1024 * cc + 512 * nh:1024 * cc + 512 * (nh + 1)],
                    start=(cc == 0), stop=(cc == NCH - 1)), waits=wts)
            cpe = emit("pe", lambda t: None, inc=True)
            ca = emit("act", lambda a, nh=nh, bank=bank: a.activation(
                out=tanh_t[:, 512 * nh:512 * (nh + 1)], in_=bank[:, 0:512], func=AF.Tanh,
                scale=0.5), inc=True,
                waits=[("pe", cpe), ("dve", dget(("ffno", m - 1)))])
            done[("tanh", m, nh)] = ca
            big_evac[bank] = ("act", ca)
        return False

    def emit_B4(m):
        seq, blk = m % 2, m // 2
        cm = done[("mix_done", m)]
        for nh in range(2):
            bank = next_big()
            beng, bval = big_evac[bank]
            for cc in range(NCH):
                wts = []
                if cc == 0:
                    wts = [("dve", cm), (beng, bval)]
                emit("pe", lambda t, seq=seq, nh=nh, cc=cc, bank=bank: t.matmul(
                    out=bank[:, 0:512], lhsT=xk[seq][:, 128 * cc:128 * (cc + 1)],
                    rhs=wkT[:, 1024 * cc + 512 * nh:1024 * cc + 512 * (nh + 1)],
                    start=(cc == 0), stop=(cc == NCH - 1)), waits=wts)
            cpe = emit("pe", lambda t: None, inc=True)
            emit("dve", lambda v, seq=seq, bank=bank: v.tensor_scalar(
                out=krelu_of[seq][:], in0=bank[:, 0:512], scalar1=0.0, scalar2=None,
                op0=ALU.max), waits=[("pe", cpe), ("pe", dget(("kv_done", m - 2))),
                                     ("tq", dget(("kt", m - 1))),
                                     ("act", dget(("ln3sq", m - 2))),
                                     ("dve", CK["wsc_free_dve"] if m < 2 else 0),
                                     ("act", CK["wsc_free_act"] if m < 2 else 0)])
            cev = emit("dve", lambda v, seq=seq, nh=nh: v.tensor_tensor(
                out=kact[seq][:, 512 * nh:512 * (nh + 1)], in0=krelu_of[seq][:],
                in1=krelu_of[seq][:], op=ALU.mult), inc=True)
            big_evac[bank] = ("dve", cev)
        c_ka = cev
        done[("kact", m)] = c_ka
        maybe_tap("kact", m, lambda: kact[seq][:], lambda: [("dve", c_ka)])
        done[("kt", m)] = emit("tq", lambda s, seq=seq: s.dma_start(
            out=kafm[seq].rearrange("p (c f) -> p c f", c=8),
            in_=kact[seq].rearrange("p (c f) -> p c f", c=8), transpose=True), inc=True,
            waits=[("dve", c_ka)])

        return stop_after == "kact"

    def emit_B5(m):
        seq, blk = m % 2, m // 2
        for nh in range(2):
            bank = next_big()
            beng, bval = big_evac[bank]
            for cc in range(NCH):
                wts = []
                if cc == 0:
                    wts = [("tq", done[("kt", m)]), (beng, bval)]
                emit("pe", lambda t, seq=seq, nh=nh, cc=cc, bank=bank: t.matmul(
                    out=bank[:, 0:512], lhsT=kafm[seq][:, 128 * cc:128 * (cc + 1)],
                    rhs=wvT[:, 1024 * cc + 512 * nh:1024 * cc + 512 * (nh + 1)],
                    start=(cc == 0), stop=(cc == NCH - 1)), waits=wts)
            cpe = emit("pe", lambda t: None, inc=True)
            emit("dve", lambda v, nh=nh, bank=bank: v.scalar_tensor_tensor(
                out=tanh_t[:, 512 * nh:512 * (nh + 1)], in0=tanh_t[:, 512 * nh:512 * (nh + 1)],
                scalar=1.0, in1=bank[:, 0:512], op0=ALU.add, op1=ALU.mult),
                waits=[("pe", cpe), ("act", done[("tanh", m, nh)])])
            cev = emit("dve", lambda v, seq=seq, nh=nh: v.tensor_tensor(
                out=htok[seq][:, 512 * nh:512 * (nh + 1)],
                in0=htok[seq][:, 512 * nh:512 * (nh + 1)],
                in1=tanh_t[:, 512 * nh:512 * (nh + 1)], op=ALU.add), inc=True,
                waits=[("tq", done[("ht", m)])])
            big_evac[bank] = ("dve", cev)
        done[("ffnmm_done", m)] = now("pe")
        done[("kv_done", m)] = now("pe")
        done[("ffno", m)] = cev
        maybe_tap("s3", m, lambda: htok[seq][:], lambda: [("dve", cev)])
        return False

    def emit_B5b(m):
        seq, blk = m % 2, m // 2
        c_s3 = done[("ffno", m)]
        r1, sq_ = stc(seq, 2, 0), stc(seq, 2, 1)
        emit("act", lambda a, seq=seq, r1=r1: a.activation(
            out=kafm[seq][:], in_=htok[seq][:], func=AF.Copy, accum_out=r1),
            waits=[("dve", c_s3), ("pe", done[("kv_done", m)])])
        c_sqa3 = emit("actd", lambda a, seq=seq, sq_=sq_: a.activation(
            out=kafm[seq][:], in_=htok[seq][:], func=AF.Square, accum_out=sq_), inc=True)
        done[("ln3sq", m)] = c_sqa3
        rstd_chain(2, seq, 3, xtr_waits=[("act", c_sqa3)])
        c_o = emit("dve", lambda v, seq=seq: v.tensor_scalar(
            out=htok[seq][:], in0=htok[seq][:], scalar1=r_nm[:, 0:1], scalar2=r_yy[:, 0:1],
            op0=ALU.add, op1=ALU.mult), inc=True,
            waits=[("sq", dget(("store", m - 2)))])
        done[("ln3", m)] = c_o
        maybe_tap("h3", m, lambda: htok[seq][:], lambda: [("dve", c_o)])
        done[("store", m)] = emit("sq", lambda s, seq=seq, blk=blk: s.dma_start(
            out=out_h[seq, TOK * blk:TOK * (blk + 1), :], in_=htok[seq][:]), inc=True,
            waits=[("dve", c_o)])
        return False

    if stop_after is None:
        def emit_front_all(m):
            emit_F1(m); emit_F2a(m); emit_F2b(m); emit_F2c(m); emit_F3(m); emit_F4(m)

        if nmt >= 1:
            emit_front_all(0)
        for it in range(nmt + 1):
            b = it - 1
            f = it + 1
            Bv = 0 <= b < nmt
            Fv = f < nmt
            if it < nmt:
                emit_Ascores(it, 0)
                emit_Ascores(it, 1)
                emit_Ar(it, 0)
                emit_Ar(it, 1)
                emit_Ar(it, 2)
                emit_Ar(it, 3)
                emit_Aflush(it)
                if Bv:
                    emit_B1(b)
                emit_As(it, 0)
                if Bv:
                    emit_B1b(b)
                    emit_B2(b)
                emit_As(it, 1)
                if Bv:
                    emit_B2b(b)
                if Fv:
                    emit_F1(f)
                if Bv:
                    emit_B3(b)
                if Fv:
                    emit_F2a(f)
                emit_As(it, 2)
                if Bv:
                    emit_B4(b)
                if Fv:
                    emit_F2b(f)
                if Bv:
                    emit_B5(b)
                emit_As(it, 3)
                if Fv:
                    emit_F2c(f)
                emit_Atail(it)
                if Fv:
                    emit_F3(f)
                    emit_F4(f)
                if Bv:
                    emit_B5b(b)
            else:
                if Bv:
                    emit_B1(b); emit_B1b(b); emit_B2(b); emit_B2b(b); emit_B3(b)
                    emit_B4(b); emit_B5(b); emit_B5b(b)
    else:
        for m in range(nmt):
            if (emit_F1(m) or emit_F2(m) or emit_F3(m) or emit_F4(m)
                    or emit_scan(m) or emit_B1(m) or emit_B1b(m) or emit_B2(m)
                    or emit_B2b(m) or emit_B3(m) or emit_B4(m) or emit_B5(m)
                    or emit_B5b(m)):
                break

    total_sq = cnt["sq"]
    emit("sq", lambda s: None, waits=[("sq", total_sq)])

    with nc.Block() as block:
        block.tensor(_mk_runner(ops["pe"]))
        block.vector(_mk_runner(ops["dve"]))
        block.scalar(_mk_runner(ops["scalar"]))
        block.gpsimd(_mk_runner(ops["gps"]))
        block.sync(_mk_runner(ops["sync"]))

    ctx.close()
    return nc, tap_h


def _mk_runner(stream):
    def _run(eng):
        last_instr = [None]
        for (wl, fn, sem, step, logical) in stream:
            for (s, v) in wl:
                eng.wait_ge(s, v)
            instr = fn(eng)
            if instr is not None:
                last_instr[0] = instr
            if sem is not None:
                if logical == "actd":
                    eng.drain().then_inc(sem, step)
                else:
                    tgt = instr if instr is not None else last_instr[0]
                    tgt.then_inc(sem, step)
    return _run


_NC_CACHE = {}


def kernel(L0_attn_x, L0_attn_w_qkv, L0_attn_w_out, L0_attn_ln_g, L0_attn_ln_b,
           L0_attn_mem_beta, L0_ffn_W_r, L0_ffn_W_k, L0_ffn_W_v, L0_ffn_mu_r, L0_ffn_mu_k):
    x = np.ascontiguousarray(np.asarray(L0_attn_x, dtype=np.float32))
    if "nc" not in _NC_CACHE:
        _NC_CACHE["nc"] = build_nc()
    nc, _ = _NC_CACHE["nc"]
    shared = {
        "w_qkv": np.ascontiguousarray(np.asarray(L0_attn_w_qkv, np.float32)),
        "w_out": np.ascontiguousarray(np.asarray(L0_attn_w_out, np.float32)),
        "ln_g": np.ascontiguousarray(np.asarray(L0_attn_ln_g, np.float32)),
        "ln_b": np.ascontiguousarray(np.asarray(L0_attn_ln_b, np.float32)),
        "mem_beta": np.ascontiguousarray(np.asarray(L0_attn_mem_beta, np.float32).reshape(1, H)),
        "W_r": np.ascontiguousarray(np.asarray(L0_ffn_W_r, np.float32)),
        "W_k": np.ascontiguousarray(np.asarray(L0_ffn_W_k, np.float32)),
        "W_v": np.ascontiguousarray(np.asarray(L0_ffn_W_v, np.float32)),
        "mu_r": np.ascontiguousarray(np.asarray(L0_ffn_mu_r, np.float32)),
        "mu_k": np.ascontiguousarray(np.asarray(L0_ffn_mu_k, np.float32)),
    }
    in_maps = []
    for core in range(8):
        im = dict(shared)
        im["x"] = np.ascontiguousarray(x[2 * core:2 * core + 2])
        in_maps.append(im)
    res = run_bass_kernel_spmd(nc, in_maps, core_ids=list(range(8)))
    out = np.concatenate([r["out"] for r in res.results], axis=0)
    return np.ascontiguousarray(out.reshape(16, TSEQ, D).astype(np.float32))
